# version 18
# baseline (speedup 1.0000x reference)
import math
from contextlib import ExitStack

import numpy as np

import concourse.bass as bass
import concourse.mybir as mybir
from concourse.bass_utils import run_bass_kernel_spmd

F32 = mybir.dt.float32
BF16 = mybir.dt.bfloat16
AF = mybir.ActivationFunctionType
ALU = mybir.AluOpType
AX = mybir.AxisListType.X

SEQ, T, C, H, HD = 2, 2048, 1024, 16, 64
NT = SEQ * T
FF2, INNER = 2048, 4096
NTT = NT // 128
WIN = 512
C1 = float(1.5 * 2.0 ** 23)
CD = float(1.5 * 2.0 ** 52)
GN_EPS, LN_EPS = 1e-5, 1e-5
S1B_MODE = "full"
CARVE = 49152
CARVE_END = 229376

ENGINES = ("tensor", "vector", "scalar", "gpsimd", "sync")


class _Buf:
    __slots__ = ("name", "lw", "rd", "psum")

    def __init__(self, name, psum=False):
        self.name, self.lw, self.rd, self.psum = name, None, [], psum


class _Chan:
    __slots__ = ("name", "eng", "cum", "ops", "last", "waiters")

    def __init__(self, name, eng):
        assert eng in ("sync", "gpsimd")
        self.name, self.eng, self.cum, self.ops, self.last = name, eng, 0, [], None
        self.waiters = {}


class _Op:
    __slots__ = ("eng", "fn", "idx", "deps", "need_inc", "inc_val", "chan",
                 "cum_after", "waits", "tag")

    def __init__(self, eng, fn, idx, tag):
        self.eng, self.fn, self.idx, self.tag = eng, fn, idx, tag
        self.deps, self.need_inc, self.inc_val = [], False, None
        self.chan, self.cum_after, self.waits = None, None, None


class Sched:
    def __init__(self):
        self.ops, self.chans, self.last_on = [], [], {e: None for e in ENGINES}

    def buf(self, name="", psum=False):
        return _Buf(name, psum)

    def chan(self, name, eng):
        c = _Chan(name, eng)
        self.chans.append(c)
        return c

    def _record(self, op, reads, writes):
        deps = set()
        for b in reads:
            if b.lw is not None:
                deps.add(b.lw)
            if b.psum:
                for r in b.rd:
                    if r.eng != op.eng:
                        deps.add(r)
        for b in writes:
            if b.lw is not None:
                deps.add(b.lw)
            for r in b.rd:
                deps.add(r)
        op.deps = [d for d in deps if d is not op and not (
            d.eng == "tensor" and op.eng == "tensor"
            and d.chan is None and op.chan is None)]
        for d in op.deps:
            if d.chan is not None:
                d.chan.waiters[op.eng] = op
        for b in writes:
            b.lw, b.rd = op, []
        for b in reads:
            b.rd.append(op)
        self.ops.append(op)
        self.last_on[op.eng] = op
        return op

    def op(self, eng, fn, reads=(), writes=(), tag=""):
        assert eng in ("tensor", "vector", "scalar", "gpsimd")
        return self._record(_Op(eng, fn, len(self.ops), tag), reads, writes)

    def dma(self, chan, fn, reads=(), writes=(), tag=""):
        op = _Op(chan.eng, fn, len(self.ops), tag)
        op.chan = chan
        chan.cum += 16
        op.cum_after = chan.cum
        chan.ops.append(op)
        chan.last = op
        extra = [w for w in chan.waiters.values() if w is not op]
        ret = self._record(op, reads, writes)
        for w in extra:
            if w not in op.deps:
                op.deps.append(w)
        return ret

    def barrier(self, tag="barrier"):
        prods = [o for o in self.last_on.values() if o is not None]
        prods += [c.last for c in self.chans if c.last is not None]
        for eng in ENGINES:
            op = _Op(eng, None, len(self.ops), tag)
            op.deps = [p for p in prods if p is not op]
            self.ops.append(op)

    def emit(self, block, sems):
        for op in self.ops:
            for d in op.deps:
                d.need_inc = True
        inc_count = {e: 0 for e in ENGINES}
        for op in self.ops:
            if op.chan is None and op.need_inc:
                inc_count[op.eng] += 1
                op.inc_val = inc_count[op.eng]
        for op in self.ops:
            waits = {}
            for d in op.deps:
                if d.chan is not None:
                    key = ("chan", d.chan.name)
                    val = d.cum_after
                    for other in d.chan.ops:
                        if other.idx < op.idx and other.cum_after > val:
                            val = other.cum_after
                else:
                    key, val = ("eng", d.eng), d.inc_val
                waits[key] = max(waits.get(key, 0), val)
            op.waits = waits
        self._verify_schedulable()

        self._emit_streams(block, sems)

    def _verify_schedulable(self):
        queues = {e: [op for op in self.ops if op.eng == e] for e in ENGINES}
        heads = {e: 0 for e in ENGINES}
        esem = {e: 0 for e in ENGINES}
        csem = {c.name: 0 for c in self.chans}

        def ok(op):
            for (kind, nm), val in op.waits.items():
                if kind == "eng" and esem[nm] < val:
                    return False
                if kind == "chan" and csem[nm] < val:
                    return False
            return True

        progress = True
        while progress:
            progress = False
            for e in ENGINES:
                q = queues[e]
                while heads[e] < len(q):
                    op = q[heads[e]]
                    if not ok(op):
                        break
                    if op.chan is not None:
                        csem[op.chan.name] += 16
                    elif op.inc_val is not None:
                        esem[e] = op.inc_val
                    heads[e] += 1
                    progress = True
        stuck = [(e, queues[e][heads[e]]) for e in ENGINES if heads[e] < len(queues[e])]
        if stuck:
            msg = "; ".join(
                f"{e}:{op.tag}@{op.idx} waits {op.waits}" for e, op in stuck[:4])
            raise RuntimeError("schedule deadlock: " + msg)

    def _emit_streams(self, block, sems):
        def stream(engobj, ename):
            hw = {}
            for op in self.ops:
                if op.eng != ename:
                    continue
                for key, val in sorted(op.waits.items(), key=lambda kv: str(kv[0])):
                    if hw.get(key, 0) >= val:
                        continue
                    engobj.wait_ge(sems[key], val)
                    hw[key] = val
                if op.fn is not None:
                    inst = op.fn(engobj)
                    if op.chan is not None:
                        inst.then_inc(sems[("chan", op.chan.name)], 16)
                    elif op.need_inc:
                        inst.then_inc(sems[("eng", op.eng)], 1)
                elif op.need_inc:
                    raise RuntimeError(f"fn-less op {op.tag} cannot carry inc")

        block.tensor(lambda e: stream(e, "tensor"))
        block.vector(lambda e: stream(e, "vector"))
        block.scalar(lambda e: stream(e, "scalar"))
        block.gpsimd(lambda e: stream(e, "gpsimd"))
        block.sync(lambda e: stream(e, "sync"))


def _host_consts():
    c = {}
    c["identm"] = np.eye(128, dtype=np.float32)
    P = np.zeros((128, 128), dtype=np.float32)
    for i in range(64):
        P[2 * i, 2 * i + 1] = 1.0
        P[2 * i + 1, 2 * i] = 1.0
    c["pswapm"] = P
    tri = np.zeros((128, 4 * WIN), dtype=np.float32)
    for r in range(4):
        for j in range(4):
            if j > r:
                tri[:, WIN * r + 128 * j:WIN * r + 128 * (j + 1)] = 1.0
            elif j == r:
                tri[:, WIN * r + 128 * j:WIN * r + 128 * (j + 1)] = \
                    np.tril(np.ones((128, 128), dtype=np.float32)).T
    c["trim"] = tri * 0.125
    p = ((np.arange(T, dtype=np.float64) - 1024.0) / 512.0).astype(np.float32)
    c["pbcm"] = np.broadcast_to(p[None, :], (128, T)).copy()
    gam = (1.0 - 2.0 ** (-5.0 - np.arange(H, dtype=np.float64))).astype(np.float32)
    gcols = np.zeros((128, 8), dtype=np.float32)
    for cc in range(8):
        for pp in range(128):
            gcols[pp, cc] = gam[(128 * cc + pp) // 64]
    c["gcolsm"] = gcols
    c["hrampm"] = (np.arange(128, dtype=np.float32) // 2).reshape(128, 1)
    c["sgnm"] = np.where(np.arange(128) % 2 == 0, -1.0, 1.0
                         ).astype(np.float32).reshape(128, 1)
    c["onesm"] = np.ones((128, 1), dtype=np.float32)
    c["halfpim"] = np.full((128, 1), np.float32(np.pi / 2), dtype=np.float32)
    gl = np.zeros((128, 8 * 64), dtype=np.float32)
    for hp in range(8):
        gl[0:64, 64 * hp + 2 * hp] = 1.0 / 64.0
        gl[64:128, 64 * hp + 2 * hp + 1] = 1.0 / 64.0
    c["gnlm"] = gl
    return c


def _tile_oc(w_t, nblk_o, nblk_c):
    ci, co = w_t.shape
    assert ci == nblk_c * 128 and co == nblk_o * 128
    return np.ascontiguousarray(
        w_t.reshape(nblk_c, 128, nblk_o, 128).transpose(2, 1, 0, 3))



def _xpos_decay_tables():
    import jax
    cpu = jax.local_devices(backend="cpu")[0]
    import jax.numpy as jnp
    with jax.default_device(cpu):
        half = C // 2
        sv = (jnp.arange(0, C, 2, dtype=jnp.float32) + 0.4 * C) / (1.4 * C)
        mn = -(T) // 2
        power = jnp.arange(mn, T + mn, dtype=jnp.float32) / 512
        scale = sv[None, :] ** power[:, None]
        invf = 1.0 / 10000.0 ** (jnp.arange(half, dtype=jnp.float32) / half)
        sinus = jnp.arange(T, dtype=jnp.float32)[:, None] * invf[None, :]
        sin = jnp.sin(sinus)
        cos = jnp.cos(sinus)
        cos_q = jnp.repeat(cos * scale, 2, axis=-1)
        sin_q = jnp.repeat(sin * scale, 2, axis=-1)
        iscale = 1.0 / scale
        cos_k = jnp.repeat(cos * iscale, 2, axis=-1)
        sin_k = jnp.repeat(sin * iscale, 2, axis=-1)
        cos_q = np.asarray(cos_q, dtype=np.float32)
        sin_q = np.asarray(sin_q, dtype=np.float32)
        cos_k = np.asarray(cos_k, dtype=np.float32)
        sin_k = np.asarray(sin_k, dtype=np.float32)
    sgn = np.where(np.arange(C) % 2 == 0, -1.0, 1.0).astype(np.float32)
    gam = (1.0 - 2.0 ** (-5.0 - np.arange(H, dtype=np.float64)))
    gc = gam[np.arange(C) // 64]
    tpow = np.arange(T, dtype=np.float64)
    g_t = (gc[:, None] ** tpow[None, :])
    g_ti = (gc[:, None] ** (-tpow[None, :]))
    out = {
        "cqm": (cos_q.T.astype(np.float64) * g_t).astype(np.float32),
        "sqm": ((sin_q.T * sgn[:, None]).astype(np.float64) * g_t).astype(np.float32),
        "ckm": (cos_k.T.astype(np.float64) * g_ti).astype(np.float32),
        "skm": ((sin_k.T * sgn[:, None]).astype(np.float64) * g_ti).astype(np.float32),
    }
    return {k: np.ascontiguousarray(v) for k, v in out.items()}


def build_program(upto="S6", debug=()):
    nc = bass.Bass("TRN2", target_bir_lowering=False)
    s = Sched()
    dr, st, B = {}, {}, {}

    def din(name, shape, dtype=F32):
        dr[name] = nc.dram_tensor(name, list(shape), dtype, kind="ExternalInput").ap()
        return dr[name]

    def dsc(name, shape, dtype):
        dr[name] = nc.dram_tensor(name, list(shape), dtype, kind="Internal").ap()
        return dr[name]

    x_d = din("x", (NT, C))
    w1t_d = din("w1t", (40, 128, 8, 128))
    wgt_d = din("wgt", (8, 128, 8, 128))
    wpt_d = din("wpt", (C, C))
    wft_d = din("wft", (FF2, C))
    wg_d = din("wg", (C, INNER))
    wu_d = din("wu", (C, INNER))
    wd_d = din("wd", (INNER, C))
    bgat_d = din("b_gated", (1, C))
    bprj_d = din("b_proj", (1, C))
    gnw_d = din("gn_w", (1, C))
    gnb_d = din("gn_b", (1, C))
    rms_d = din("rms_w", (1, C))
    identm_d = din("identm", (128, 128))
    pswap_d = din("pswapm", (128, 128))
    tri_d = din("trim", (128, 4 * WIN))
    pbc_d = din("pbcm", (128, T))
    gcols_d = din("gcolsm", (128, 8))
    hramp_d = din("hrampm", (128, 1))
    sgn_d = din("sgnm", (128, 1))
    ones_d = din("onesm", (128, 1))
    halfpi_d = din("halfpim", (128, 1))
    gnl_d = din("gnlm", (128, 8 * 64))
    cqm_d = din("cqm", (C, T))
    sqm_d = din("sqm", (C, T))
    ckm_d = din("ckm", (C, T))
    skm_d = din("skm", (C, T))

    out_d = nc.dram_tensor("out", [NT, C], F32, kind="ExternalOutput").ap()

    scq_d = dsc("s_cq", (C, T), F32)
    ssq_d = dsc("s_sq", (C, T), F32)
    sck_d = dsc("s_ck", (C, T), F32)
    ssk_d = dsc("s_sk", (C, T), F32)
    sq_d = dsc("s_q", (SEQ, C, T), BF16)
    sk_d = dsc("s_k", (SEQ, C, T), BF16)
    sv_d = dsc("s_v", (SEQ, T, C), BF16)
    sgel_d = dsc("s_gelu", (SEQ, 128, 16, 16, 128), BF16)
    sxn_d = dsc("s_xn", (128, 8, NT), BF16)
    sxq_d = dsc("s_xq", (NTT, 128, 8 * 128), BF16)
    sy2a_d = dsc("s_y2a", (SEQ, 128, 8, T), BF16)
    sy2_d = dsc("s_y2", (128, NTT, 8, 128), BF16)
    shq_d = dsc("s_hq", (NTT, 4, 128, 8 * 128), BF16)
    swgq_d = dsc("s_wgq", (C, INNER), BF16)
    swuq_d = dsc("s_wuq", (C, INNER), BF16)
    swdq_d = dsc("s_wdq", (INNER, C), BF16)
    sdbg_d = dsc("s_dbgspill", (40, 128, NT), BF16)

    def sb(key, shape, dtype=F32):
        st[key] = nc.alloc_sbuf_tensor("sb_" + key, list(shape), dtype)
        B[key] = s.buf(key)
        return st[key]

    def sb_at(key, shape, dtype, off):
        nbytes = int(np.prod(shape[1:])) * mybir.dt.size(dtype)
        assert CARVE + off + nbytes <= CARVE_END, (key, off, nbytes)
        st[key] = nc.alloc_sbuf_tensor_at("cv_" + key, list(shape), dtype,
                                          offset=CARVE + off)
        B[key] = s.buf(key)
        return st[key]

    sb("identb", (128, 128), BF16)
    sb("identf", (128, 128), F32)
    sb("pswapb", (128, 128), BF16)
    sb("gcols", (128, 8))
    sb("hramp", (128, 1))
    sb("sgncol", (128, 1))
    sb("onescol", (128, 1))
    sb("halfpi", (128, 1))
    sb("vecs", (128, 5 * 8))
    sb("rowbuf", (1, 2 * C))
    sb("rmsb", (128, C))
    sb("ceff", (128, 4))
    sb("wsc", (128, 16))
    sb("stats", (128, 9 * NTT))
    sb("tmpc", (128, 64))
    sb("gnl", (128, 8 * 64))
    sb("tri", (128, 4 * WIN))
    sentinel = nc.alloc_sbuf_tensor("sb_sentinel", [128, 4], F32)
    assert nc.lookup_mloc(sentinel).addr + 16 <= CARVE, \
        f"persistent sbuf overflow: {nc.lookup_mloc(sentinel).addr}"

    ps = {}
    for nm, shape, dt in (("PA", (128, 512), F32), ("PB", (128, 512), F32),
                          ("PC", (128, 512), F32), ("PD", (128, 512), F32),
                          ("PE2", (128, 1024), F32), ("PT", (128, 1024), BF16),
                          ("PG", (128, 512), F32)):
        ps[nm] = nc.alloc_psum_tensor(nm, list(shape), dt)
        B[nm] = s.buf(nm, psum=True)
    B["PE2A"] = s.buf("PE2A", psum=True)
    B["PE2B"] = s.buf("PE2B", psum=True)

    def pap(pname):
        if pname == "PE2A":
            return ps["PE2"][:, 0:512]
        if pname == "PE2B":
            return ps["PE2"][:, 512:1024]
        return ps[pname][:]

    ch_in = [s.chan(f"in{i}", "sync") for i in range(2)]
    ch_w = [s.chan(f"w{i}", "gpsimd") for i in range(2)]
    ch_sp = [s.chan(f"sp{i}", "gpsimd") for i in range(8)]
    ch_rd = [s.chan(f"rd{i}", "sync") for i in range(4)]
    ch_out = [s.chan(f"out{i}", "sync") for i in range(2)]
    ch_cst = s.chan("cst", "sync")
    ch_cw = s.chan("cw", "gpsimd")

    def S0_consts():
        for key, d, chn in (("identb", identm_d, ch_cw), ("pswapb", pswap_d, ch_cw),
                            ("identf", identm_d, ch_cst),
                            ("tri", tri_d, ch_cst), ("gcols", gcols_d, ch_cst),
                            ("hramp", hramp_d, ch_cst), ("sgncol", sgn_d, ch_cst),
                            ("onescol", ones_d, ch_cst), ("gnl", gnl_d, ch_cst),
                            ("halfpi", halfpi_d, ch_cst)):
            s.dma(chn, lambda e, key=key, d=d: e.dma_start(out=st[key][:], in_=d),
                  writes=[B[key]], tag=f"ld_{key}")
        for k, d in enumerate((bgat_d, bprj_d, gnw_d, gnb_d, rms_d)):
            s.dma(ch_cst, lambda e, k=k, d=d: e.dma_start(
                out=st["vecs"][:, 8 * k:8 * (k + 1)],
                in_=bass.AP(d.tensor, 0, [[1, 128], [128, 8]]),
                allow_slow_non_contiguous=True),
                writes=[B["vecs"]], tag=f"veccol{k}")
        s.dma(ch_cst, lambda e: e.dma_start(out=st["rowbuf"][0:1, 0:C], in_=bprj_d),
              writes=[B["rowbuf"]], tag="bprow")
        s.dma(ch_cst, lambda e: e.dma_start(out=st["rowbuf"][0:1, C:2 * C], in_=rms_d),
              writes=[B["rowbuf"]], tag="rmsrow")

        s.op("vector", lambda e: e.tensor_scalar(
            st["ceff"][:, 0:1], st["onescol"][:], 2.0 ** -30, 1.0, ALU.add, ALU.subtract),
            reads=[B["onescol"]], writes=[B["ceff"]], tag="ind")
        s.op("vector", lambda e: e.tensor_scalar(
            st["ceff"][:, 1:2], st["ceff"][:, 0:1], 2.0 ** 30 * CD, C1, ALU.mult, ALU.add),
            reads=[B["ceff"]], writes=[B["ceff"]], tag="ceff")

        s.op("vector", lambda e: e.memset(st["tmpc"][0:1, 0:16], 1.0),
             writes=[B["tmpc"]], tag="onesrow16")
        onesrow = sb("onesrow", (1, 128))
        s.op("vector", lambda e: e.memset(st["onesrow"][:], 1.0),
             writes=[B["onesrow"]], tag="onesrow")
        for half in range(2):
            s.op("tensor", lambda e, half=half: e.matmul(
                ps["PE2"][:, 512 * half:512 * (half + 1)], st["onesrow"][0:1, :],
                st["rowbuf"][0:1, C + 512 * half:C + 512 * (half + 1)],
                start=True, stop=True),
                reads=[B["onesrow"], B["rowbuf"]],
                writes=[B["PE2A"] if half == 0 else B["PE2B"]], tag=f"rmsmm{half}")
        s.op("vector", lambda e: e.tensor_copy(st["rmsb"][:], ps["PE2"][:, 0:C]),
             reads=[B["PE2A"], B["PE2B"]], writes=[B["rmsb"]], tag="rmsbc")

    def S0_wquant_gen():
        sb_at("wload0", (128, 4096), F32, 132 * 1024)
        sb_at("wload1", (128, 4096), F32, 148 * 1024)
        sb_at("wqout0", (128, 4096), BF16, 164 * 1024)
        acc = st["wsc"]
        for wname, wsrc, wdst, nchunk, wcols, wi in (
                ("wg", wg_d, swgq_d, 8, INNER, 0),
                ("wu", wu_d, swuq_d, 8, INNER, 1),
                ("wd", wd_d, swdq_d, 32, C, 2)):
            for k in range(nchunk):
                par = k % 2
                bw = B[f"wload{par}"]
                s.dma(ch_rd[par], lambda e, k=k, par=par, wsrc=wsrc, wcols=wcols:
                      e.dma_start(out=st[f"wload{par}"][:, 0:wcols],
                                  in_=wsrc[128 * k:128 * (k + 1), :]),
                      writes=[bw], tag=f"{wname}ld{k}")
                s.op("vector", lambda e, par=par, wcols=wcols: e.reduce_sum(
                    acc[:, 8:9], st[f"wload{par}"][:, 0:wcols], AX,
                    apply_absolute_value=True),
                    reads=[bw, B["wsc"]], writes=[B["wsc"]], tag=f"{wname}abs{k}")
                if k == 0:
                    s.op("vector", lambda e: e.tensor_copy(acc[:, 9:10], acc[:, 8:9]),
                         reads=[B["wsc"]], writes=[B["wsc"]], tag=f"{wname}a0")
                else:
                    s.op("vector", lambda e: e.tensor_add(acc[:, 9:10], acc[:, 9:10], acc[:, 8:9]),
                         reads=[B["wsc"]], writes=[B["wsc"]], tag=f"{wname}a{k}")
                yield
            s.op("tensor", lambda e: e.matmul(
                ps["PE2"][0:1, 0:1], st["onescol"][:], acc[:, 9:10], start=True, stop=True),
                reads=[B["onescol"], B["wsc"]], writes=[B["PE2A"]], tag=f"{wname}tot")
            ncount = float(nchunk * 128 * wcols)
            s.op("vector", lambda e, ncount=ncount: e.tensor_scalar(
                acc[0:1, 10:11], ps["PE2"][0:1, 0:1], 1.0 / ncount, 1e-5, ALU.mult, ALU.max),
                reads=[B["PE2A"]], writes=[B["wsc"]], tag=f"{wname}mean")
            s.op("tensor", lambda e: e.matmul(
                ps["PE2"][:, 4:5], st["onesrow"][0:1, :], acc[0:1, 10:11],
                start=True, stop=True),
                reads=[B["wsc"], B["onesrow"]], writes=[B["PE2A"]], tag=f"{wname}bc")
            s.op("vector", lambda e, wi=wi: e.tensor_copy(
                st["wsc"][:, wi:wi + 1], ps["PE2"][:, 4:5]),
                reads=[B["PE2A"]], writes=[B["wsc"]], tag=f"{wname}wsc")
            s.op("scalar", lambda e, wi=wi: e.activation(
                st["wsc"][:, 4 + wi:5 + wi], st["wsc"][:, wi:wi + 1], AF.Ln),
                reads=[B["wsc"]], writes=[B["wsc"]], tag=f"{wname}ln")
            s.op("scalar", lambda e, wi=wi: e.activation(
                st["wsc"][:, 4 + wi:5 + wi], st["wsc"][:, 4 + wi:5 + wi], AF.Exp, scale=-1.0),
                reads=[B["wsc"]], writes=[B["wsc"]], tag=f"{wname}exp")
            for k in range(nchunk):
                par = k % 2
                bw, bq = B[f"wload{par}"], B["wqout0"]
                s.dma(ch_rd[2 + par], lambda e, k=k, par=par, wsrc=wsrc, wcols=wcols:
                      e.dma_start(out=st[f"wload{par}"][:, 0:wcols],
                                  in_=wsrc[128 * k:128 * (k + 1), :]),
                      writes=[bw], tag=f"{wname}l2_{k}")
                s.op("vector", lambda e, par=par, wcols=wcols, wi=wi: e.tensor_scalar(
                    st[f"wload{par}"][:, 0:wcols], st[f"wload{par}"][:, 0:wcols],
                    st["wsc"][:, 4 + wi:5 + wi], None, ALU.mult),
                    reads=[bw, B["wsc"]], writes=[bw], tag=f"{wname}sc{k}")
                s.op("vector", lambda e, par=par, wcols=wcols: e.tensor_scalar(
                    st[f"wload{par}"][:, 0:wcols], st[f"wload{par}"][:, 0:wcols],
                    st["ceff"][:, 1:2], st["ceff"][:, 1:2], ALU.add, ALU.subtract),
                    reads=[bw, B["ceff"]], writes=[bw], tag=f"{wname}ri{k}")
                s.op("vector", lambda e, par=par, wcols=wcols: e.tensor_scalar(
                    st["wqout0"][:, 0:wcols], st[f"wload{par}"][:, 0:wcols],
                    -1.0, 1.0, ALU.max, ALU.min),
                    reads=[bw], writes=[bq], tag=f"{wname}cl{k}")
                s.dma(ch_sp[4 + par], lambda e, k=k, par=par, wdst=wdst, wcols=wcols:
                      e.dma_start(out=wdst[128 * k:128 * (k + 1), :],
                                  in_=st["wqout0"][:, 0:wcols]),
                      reads=[bq], tag=f"{wname}st{k}")
                yield

    def S0_wquant():
        for _ in S0_wquant_gen():
            pass

    def S0b():
        sb_at("pbc", (128, T), F32, 56 * 1024)
        for nm, off in (("tbC", 64), ("tbS", 72), ("tbU", 80), ("tbA", 88),
                        ("tbB", 96), ("tbG", 104), ("tbH", 112)):
            sb_at(nm, (128, T), F32, off * 1024)
        sb_at("tcol", (128, 16), F32, 120 * 1024)
        s.dma(ch_cst, lambda e: e.dma_start(out=st["pbc"][:], in_=pbc_d),
              writes=[B["pbc"]], tag="ld_pbc")
        LNIF = -math.log(10000.0) / 512.0
        tc = st["tcol"]
        bcol = B["tcol"]
        for cc in range(8):
            bC, bS, bU, bA, bB_, bG, bH = (B["tbC"], B["tbS"], B["tbU"], B["tbA"],
                                           B["tbB"], B["tbG"], B["tbH"])
            s.op("vector", lambda e, cc=cc: e.tensor_scalar_add(
                tc[:, 0:1], st["hramp"][:], float(64 * cc)),
                reads=[B["hramp"]], writes=[bcol], tag=f"t{cc}jc")
            s.op("scalar", lambda e: e.activation(tc[:, 1:2], tc[:, 0:1], AF.Exp, scale=LNIF),
                 reads=[bcol], writes=[bcol], tag=f"t{cc}f")
            s.op("vector", lambda e: e.tensor_scalar(
                tc[:, 2:3], tc[:, 0:1], 2.0 / 1433.6, 409.6 / 1433.6, ALU.mult, ALU.add),
                reads=[bcol], writes=[bcol], tag=f"t{cc}sv")
            s.op("scalar", lambda e: e.activation(tc[:, 3:4], tc[:, 2:3], AF.Ln),
                 reads=[bcol], writes=[bcol], tag=f"t{cc}L")
            s.op("vector", lambda e: e.tensor_scalar(
                st["tbU"][:], st["pbc"][:], tc[:, 3:4], None, ALU.mult),
                reads=[bcol, B["pbc"]], writes=[bU], tag=f"t{cc}U")
            s.op("scalar", lambda e: e.activation(st["tbA"][:], st["tbU"][:], AF.Exp),
                 reads=[bU], writes=[bA], tag=f"t{cc}scq")
            s.op("scalar", lambda e: e.activation(st["tbB"][:], st["tbU"][:], AF.Exp, scale=-1.0),
                 reads=[bU], writes=[bB_], tag=f"t{cc}sck")
            s.op("vector", lambda e: e.memset(st["tbC"][:, 0:1], 1.0),
                 writes=[bC], tag=f"t{cc}c0")
            s.op("vector", lambda e: e.memset(st["tbS"][:, 0:1], 0.0),
                 writes=[bS], tag=f"t{cc}s0")
            s.op("vector", lambda e: e.tensor_scalar(
                tc[:, 12:13], tc[:, 1:2], -1.0, 1.5707963705062866, ALU.mult, ALU.add),
                reads=[bcol], writes=[bcol], tag=f"t{cc}ch1")
            s.op("vector", lambda e: e.tensor_scalar_add(
                tc[:, 12:13], tc[:, 12:13], -4.371138828673793e-08),
                reads=[bcol], writes=[bcol], tag=f"t{cc}ch2")
            s.op("scalar", lambda e: e.activation(st["tbC"][:, 1:2], tc[:, 12:13], AF.Sin),
                 reads=[bcol], writes=[bC], tag=f"t{cc}c1")
            s.op("scalar", lambda e: e.activation(st["tbS"][:, 1:2], tc[:, 1:2], AF.Sin),
                 reads=[bcol], writes=[bS], tag=f"t{cc}s1")
            s.op("vector", lambda e: e.tensor_copy(tc[:, 4:5], st["tbC"][:, 1:2]),
                 reads=[bC], writes=[bcol], tag=f"t{cc}r0")
            s.op("vector", lambda e: e.tensor_copy(tc[:, 5:6], st["tbS"][:, 1:2]),
                 reads=[bS], writes=[bcol], tag=f"t{cc}r1")
            w = 2
            while w < T:
                s.op("vector", lambda e: e.tensor_mul(tc[:, 6:7], tc[:, 4:5], tc[:, 4:5]),
                     reads=[bcol], writes=[bcol], tag=f"t{cc}ra{w}")
                s.op("vector", lambda e: e.tensor_mul(tc[:, 7:8], tc[:, 5:6], tc[:, 5:6]),
                     reads=[bcol], writes=[bcol], tag=f"t{cc}rb{w}")
                s.op("vector", lambda e: e.tensor_mul(tc[:, 8:9], tc[:, 4:5], tc[:, 5:6]),
                     reads=[bcol], writes=[bcol], tag=f"t{cc}rc{w}")
                s.op("vector", lambda e: e.tensor_sub(tc[:, 4:5], tc[:, 6:7], tc[:, 7:8]),
                     reads=[bcol], writes=[bcol], tag=f"t{cc}rd{w}")
                s.op("vector", lambda e: e.tensor_scalar_mul(tc[:, 5:6], tc[:, 8:9], 2.0),
                     reads=[bcol], writes=[bcol], tag=f"t{cc}re{w}")
                s.op("vector", lambda e, w=w: e.tensor_scalar(
                    st["tbH"][:, 0:w], st["tbS"][:, 0:w], tc[:, 5:6], None, ALU.mult),
                    reads=[bS, bcol], writes=[bH], tag=f"t{cc}da{w}")
                s.op("vector", lambda e, w=w: e.scalar_tensor_tensor(
                    st["tbC"][:, w:2 * w], st["tbC"][:, 0:w], tc[:, 4:5],
                    st["tbH"][:, 0:w], ALU.mult, ALU.subtract),
                    reads=[bC, bcol, bH], writes=[bC], tag=f"t{cc}db{w}")
                s.op("vector", lambda e, w=w: e.tensor_scalar(
                    st["tbH"][:, 0:w], st["tbC"][:, 0:w], tc[:, 5:6], None, ALU.mult),
                    reads=[bC, bcol], writes=[bH], tag=f"t{cc}dc{w}")
                s.op("vector", lambda e, w=w: e.scalar_tensor_tensor(
                    st["tbS"][:, w:2 * w], st["tbS"][:, 0:w], tc[:, 4:5],
                    st["tbH"][:, 0:w], ALU.mult, ALU.add),
                    reads=[bS, bcol, bH], writes=[bS], tag=f"t{cc}dd{w}")
                w *= 2
            s.op("vector", lambda e, cc=cc: e.tensor_copy(
                tc[:, 9:10], st["gcols"][:, cc:cc + 1]),
                reads=[B["gcols"]], writes=[bcol], tag=f"t{cc}g")
            s.op("vector", lambda e: e.reciprocal(tc[:, 10:11], tc[:, 9:10]),
                 reads=[bcol], writes=[bcol], tag=f"t{cc}gi")
            for dst, bD, col in ((st["tbG"], bG, 9), (st["tbH"], bH, 10)):
                s.op("vector", lambda e, dst=dst: e.memset(dst[:, 0:1], 1.0),
                     writes=[bD], tag=f"t{cc}g0_{col}")
                s.op("vector", lambda e, dst=dst, col=col: e.tensor_copy(
                    dst[:, 1:2], tc[:, col:col + 1]),
                    reads=[bcol], writes=[bD], tag=f"t{cc}g1_{col}")
                s.op("vector", lambda e, col=col: e.tensor_copy(
                    tc[:, 11:12], tc[:, col:col + 1]),
                    reads=[bcol], writes=[bcol], tag=f"t{cc}gr_{col}")
                w = 2
                while w < T:
                    s.op("vector", lambda e: e.tensor_mul(
                        tc[:, 11:12], tc[:, 11:12], tc[:, 11:12]),
                        reads=[bcol], writes=[bcol], tag=f"t{cc}gs{w}_{col}")
                    s.op("vector", lambda e, w=w, dst=dst: e.tensor_scalar(
                        dst[:, w:2 * w], dst[:, 0:w], tc[:, 11:12], None, ALU.mult),
                        reads=[bD, bcol], writes=[bD], tag=f"t{cc}gm{w}_{col}")
                    w *= 2
            s.op("vector", lambda e: e.tensor_mul(st["tbA"][:], st["tbA"][:], st["tbG"][:]),
                 reads=[bA, bG], writes=[bA], tag=f"t{cc}aq")
            s.op("vector", lambda e: e.tensor_mul(st["tbB"][:], st["tbB"][:], st["tbH"][:]),
                 reads=[bB_, bH], writes=[bB_], tag=f"t{cc}ak")
            for ji, (dst_d, srcCS, bsrc, srcA, bamp, use_sgn) in enumerate((
                    (scq_d, st["tbC"], bC, st["tbA"], bA, False),
                    (ssq_d, st["tbS"], bS, st["tbA"], bA, True),
                    (sck_d, st["tbC"], bC, st["tbB"], bB_, False),
                    (ssk_d, st["tbS"], bS, st["tbB"], bB_, True))):
                s.op("vector", lambda e, srcCS=srcCS, srcA=srcA: e.tensor_mul(
                    st["tbU"][:], srcCS[:], srcA[:]),
                    reads=[bsrc, bamp], writes=[bU], tag=f"t{cc}f{ji}")
                if use_sgn:
                    s.op("vector", lambda e: e.tensor_scalar(
                        st["tbU"][:], st["tbU"][:], st["sgncol"][:], None, ALU.mult),
                        reads=[bU, B["sgncol"]], writes=[bU], tag=f"t{cc}fs{ji}")
                s.dma(ch_sp[ji % 2], lambda e, dst_d=dst_d, cc=cc: e.dma_start(
                    out=dst_d[128 * cc:128 * (cc + 1), :], in_=st["tbU"][:]),
                    reads=[bU], tag=f"t{cc}st{ji}")

    def carve_S1():
        sb_at("XHFM", (128, 8 * NT), BF16, 0)
        for g in range(8):
            B[f"XHFM{g}"] = s.buf(f"XHFM{g}")
        sb_at("xt", (128, 2 * C), F32, 64 * 1024)
        sb_at("xnf", (128, 2 * C), F32, 72 * 1024)
        sb_at("tmp1", (128, 2 * C), F32, 80 * 1024)
        sb_at("xps", (128, 2 * WIN), F32, 88 * 1024)
        sb_at("xhb", (128, 2 * C), BF16, 92 * 1024)
        sb_at("xnb", (128, 2 * C), BF16, 96 * 1024)
        sb_at("xqb", (128, 2 * C), BF16, 100 * 1024)
        sb_at("stg", (128, 2 * 2 * C), BF16, 104 * 1024)
        sb_at("wbuf", (128, 2 * 1024), BF16, 112 * 1024)
        sb_at("tabs", (128, 2 * 2 * WIN), F32, 116 * 1024)
        sb_at("qkst", (128, 2 * 2 * WIN), BF16, 124 * 1024)
        sb_at("xsq", (128, C), F32, 128 * 1024)
        for nm in ("xt", "xnf", "tmp1", "xps", "xhb", "xnb", "xqb", "stg",
                   "qkst", "tabs", "wbuf"):
            B[nm + "0"] = s.buf(nm + "0")
            B[nm + "1"] = s.buf(nm + "1")

    def S1a():
        for _ in S1a_gen():
            pass

    def S1a_gen():
        xhfm = st["XHFM"]
        cS = st["stats"]
        tc = st["tmpc"]
        for fam in ("stSUM", "stSSQ", "stXMX", "stXMN"):
            if fam not in B:
                B[fam] = s.buf(fam)
        G_ORDER = [g for gp in zip(range(4), range(4, 8)) for g in gp]
        SUM, SSQ, XMX, XMN = 5 * NTT, 6 * NTT, 7 * NTT, 8 * NTT
        for g in G_ORDER:
            tiles = list(range(4 * g, 4 * g + 4))
            for i in tiles:
                par = i % 2
                bxt = B[f"xt{par}"]
                s.dma(ch_in[par], lambda e, i=i, par=par: e.dma_start(
                    out=st["xt"][:, C * par:C * (par + 1)],
                    in_=x_d[128 * i:128 * (i + 1), :]),
                    writes=[bxt], tag=f"xld{i}")
                s.op("scalar", lambda e, par=par, i=i: e.activation(
                    st["xsq"][:], st["xt"][:, C * par:C * (par + 1)], AF.Copy,
                    accum_out=cS[:, SUM + i:SUM + i + 1]),
                    reads=[bxt], writes=[B["xsq"], B["stSUM"]], tag=f"rs{i}")
                s.op("scalar", lambda e, par=par, i=i: e.activation(
                    st["xsq"][:], st["xt"][:, C * par:C * (par + 1)], AF.Square,
                    accum_out=cS[:, SSQ + i:SSQ + i + 1]),
                    reads=[bxt], writes=[B["xsq"], B["stSSQ"]], tag=f"sq{i}")
                s.op("vector", lambda e, par=par, i=i: e.reduce_max(
                    cS[:, XMX + i:XMX + i + 1],
                    st["xt"][:, C * par:C * (par + 1)], AX),
                    reads=[bxt], writes=[B["stXMX"]], tag=f"mx{i}")
                s.op("vector", lambda e, par=par, i=i: e.tensor_reduce(
                    cS[:, XMN + i:XMN + i + 1],
                    st["xt"][:, C * par:C * (par + 1)], AX, ALU.min),
                    reads=[bxt], writes=[B["stXMN"]], tag=f"mn{i}")
            a, b = 4 * g, 4 * g + 4
            ch = [
                ("vector", [B["stSUM"]], lambda e, a=a, b=b: e.tensor_scalar(
                    cS[:, a:b], cS[:, SUM + a:SUM + b], 1.0 / C, None, ALU.mult), "m"),
                ("vector", [B["stSSQ"]], lambda e, a=a, b=b: e.tensor_scalar(
                    tc[:, 16:20], cS[:, SSQ + a:SSQ + b], 1.0 / C, None, ALU.mult), "e2"),
                ("vector", [], lambda e, a=a, b=b: e.tensor_mul(
                    tc[:, 20:24], cS[:, a:b], cS[:, a:b]), "mm"),
                ("vector", [], lambda e: e.tensor_sub(
                    tc[:, 24:28], tc[:, 16:20], tc[:, 20:24]), "v"),
                ("vector", [], lambda e: e.tensor_scalar_add(
                    tc[:, 28:32], tc[:, 24:28], LN_EPS), "u"),
                ("scalar", [], lambda e: e.activation(
                    tc[:, 32:36], tc[:, 28:32], AF.Sqrt), "sqr"),
                ("vector", [], lambda e: e.reciprocal(tc[:, 36:40], tc[:, 32:36]), "b0"),
                ("vector", [], lambda e: e.tensor_mul(
                    tc[:, 40:44], tc[:, 36:40], tc[:, 36:40]), "n1"),
                ("vector", [], lambda e: e.tensor_mul(
                    tc[:, 40:44], tc[:, 40:44], tc[:, 28:32]), "n2"),
                ("vector", [], lambda e: e.tensor_scalar(
                    tc[:, 40:44], tc[:, 40:44], -0.5, 1.5, ALU.mult, ALU.add), "n3"),
                ("vector", [], lambda e, a=a, b=b: e.tensor_mul(
                    cS[:, NTT + a:NTT + b], tc[:, 40:44], tc[:, 36:40]), "beta"),
                ("vector", [], lambda e: e.reciprocal(tc[:, 44:48], tc[:, 28:32]), "ru"),
                ("vector", [], lambda e: e.tensor_mul(
                    tc[:, 44:48], tc[:, 44:48], tc[:, 24:28]), "w"),
                ("scalar", [], lambda e: e.activation(
                    tc[:, 48:52], tc[:, 44:48], AF.Sqrt), "nr"),
                ("vector", [], lambda e: e.tensor_scalar_max(
                    tc[:, 48:52], tc[:, 48:52], 1e-8), "mxr"),
                ("vector", [], lambda e: e.reciprocal(tc[:, 52:56], tc[:, 48:52]), "rg"),
                ("vector", [], lambda e, a=a, b=b: e.tensor_mul(
                    cS[:, 2 * NTT + a:2 * NTT + b],
                    cS[:, NTT + a:NTT + b], tc[:, 52:56]), "al"),
                ("vector", [B["stXMX"]], lambda e, a=a, b=b: e.tensor_sub(
                    tc[:, 56:60], cS[:, XMX + a:XMX + b], cS[:, a:b]), "t1"),
                ("vector", [B["stXMN"]], lambda e, a=a, b=b: e.tensor_sub(
                    tc[:, 60:64], cS[:, a:b], cS[:, XMN + a:XMN + b]), "t2"),
                ("vector", [], lambda e: e.tensor_max(
                    tc[:, 56:60], tc[:, 56:60], tc[:, 60:64]), "t3"),
                ("vector", [], lambda e, a=a, b=b: e.tensor_mul(
                    tc[:, 56:60], tc[:, 56:60], cS[:, NTT + a:NTT + b]), "axn"),
                ("vector", [], lambda e: e.tensor_scalar_max(
                    tc[:, 56:60], tc[:, 56:60], 1e-5), "cl"),
                ("vector", [], lambda e: e.reciprocal(tc[:, 60:64], tc[:, 56:60]), "rc"),
                ("vector", [], lambda e, a=a, b=b: e.tensor_scalar(
                    cS[:, 3 * NTT + a:3 * NTT + b],
                    tc[:, 60:64], 127.0, None, ALU.mult), "sa"),
                ("vector", [], lambda e, a=a, b=b: e.tensor_scalar(
                    cS[:, 4 * NTT + a:4 * NTT + b],
                    tc[:, 56:60], 1.0 / 127.0, None, ALU.mult), "sar"),
            ]
            for eng, xreads, fn, nm in ch:
                s.op(eng, fn, reads=[B["stats"], B["tmpc"]] + xreads,
                     writes=[B["stats"], B["tmpc"]], tag=f"gc_{nm}{g}")
            for i in tiles:
                par = i % 2
                bxt, bxn, btm = B[f"xt{par}"], B[f"xnf{par}"], B[f"tmp1{par}"]
                bxh, bxq, bnb = B[f"xhb{par}"], B[f"xqb{par}"], B[f"xnb{par}"]
                s.dma(ch_in[par], lambda e, i=i, par=par: e.dma_start(
                    out=st["xt"][:, C * par:C * (par + 1)],
                    in_=x_d[128 * i:128 * (i + 1), :]),
                    writes=[bxt], tag=f"xl2_{i}")
                s.op("vector", lambda e, i=i, par=par: e.tensor_scalar(
                    st["xnf"][:, C * par:C * (par + 1)],
                    st["xt"][:, C * par:C * (par + 1)],
                    cS[:, i:i + 1], cS[:, NTT + i:NTT + i + 1],
                    ALU.subtract, ALU.mult),
                    reads=[bxt, B["stats"]], writes=[bxn], tag=f"xn{i}")
                s.op("vector", lambda e, i=i, par=par: e.tensor_scalar(
                    st["tmp1"][:, C * par:C * (par + 1)],
                    st["xt"][:, C * par:C * (par + 1)],
                    cS[:, i:i + 1], cS[:, 2 * NTT + i:2 * NTT + i + 1],
                    ALU.subtract, ALU.mult),
                    reads=[bxt, B["stats"]], writes=[btm], tag=f"xh{i}")
                s.op("vector", lambda e, par=par: e.tensor_mul(
                    st["xhb"][:, C * par:C * (par + 1)],
                    st["tmp1"][:, C * par:C * (par + 1)], st["rmsb"][:]),
                    reads=[btm, B["rmsb"]], writes=[bxh], tag=f"xhr{i}")
                s.op("vector", lambda e, i=i, par=par: e.tensor_scalar(
                    st["tmp1"][:, C * par:C * (par + 1)],
                    st["xnf"][:, C * par:C * (par + 1)],
                    cS[:, 3 * NTT + i:3 * NTT + i + 1], None, ALU.mult),
                    reads=[bxn, B["stats"]], writes=[btm], tag=f"xqs{i}")
                s.op("vector", lambda e, par=par: e.tensor_scalar(
                    st["xqb"][:, C * par:C * (par + 1)],
                    st["tmp1"][:, C * par:C * (par + 1)],
                    st["ceff"][:, 1:2], st["ceff"][:, 1:2], ALU.add, ALU.subtract),
                    reads=[btm, B["ceff"]], writes=[bxq], tag=f"xqr{i}")
                s.op("scalar", lambda e, par=par: e.activation(
                    st["xnb"][:, C * par:C * (par + 1)],
                    st["xnf"][:, C * par:C * (par + 1)], AF.Copy),
                    reads=[bxn], writes=[bnb], tag=f"xnb{i}")
                for kind, src2, bsrc in (("fm", st["xhb"], bxh), ("xn", st["xnb"], bnb),
                                         ("xq", st["xqb"], bxq)):
                    for cchunk in range(8):
                        s.op("tensor", lambda e, src2=src2, par=par, cchunk=cchunk:
                             e.transpose(ps["PT"][:, 128 * cchunk:128 * (cchunk + 1)],
                                         src2[:, C * par + 128 * cchunk:
                                              C * par + 128 * (cchunk + 1)],
                                         st["identb"][:]),
                             reads=[bsrc, B["identb"]], writes=[B["PT"]],
                             tag=f"tr_{kind}{i}_{cchunk}")
                    if kind == "fm":
                        s.op("vector", lambda e, i=i: e.tensor_copy(
                            bass.AP(xhfm, 128 * i, [[8 * NT, 128], [NT, 8], [1, 128]]),
                            bass.AP(ps["PT"], 0, [[1024, 128], [128, 8], [1, 128]])),
                            reads=[B["PT"]], writes=[B[f"XHFM{i // 4}"]],
                            tag=f"ev_{kind}{i}")
                    else:
                        koff = 0 if kind == "xn" else 1
                        reg = 2 * C * par + C * koff
                        bstg = B[f"stg{par}"]
                        s.op("scalar", lambda e, reg=reg: e.activation(
                            st["stg"][:, reg:reg + 1024], ps["PT"][:], AF.Copy),
                            reads=[B["PT"]], writes=[bstg], tag=f"sg_{kind}{i}")
                        if kind == "xq":
                            s.dma(ch_sp[2 + (i % 2)],
                                  lambda e, i=i, reg=reg: e.dma_start(
                                      out=sxq_d[i], in_=st["stg"][:, reg:reg + 1024]),
                                  reads=[bstg], tag=f"sp_xq{i}")
                        else:
                            s.dma(ch_sp[(i % 2)],
                                  lambda e, i=i, reg=reg: e.dma_start(
                                      out=sxn_d[:, :, 128 * i:128 * (i + 1)],
                                      in_=st["stg"][:, reg:reg + 1024]),
                                  reads=[bstg], tag=f"sp_xn{i}")
            yield

    def S1b2(wq=None, s1a=None):
        xhfm = st["XHFM"]
        o_order = []
        for cp in range(8):
            o_order += [cp, cp + 8]
        o_order += list(range(16, 40))
        prot = ["PA", "PB", "PC"]
        rrot = ["PD", "PG"]
        cnt = [0]
        for oi, o in enumerate(o_order):
            par = oi % 2
            bw = B[f"wbuf{par}"]
            s.dma(ch_w[par], lambda e, o=o, par=par: e.dma_start(
                out=st["wbuf"][:, 1024 * par:1024 * (par + 1)], in_=w1t_d[o]),
                writes=[bw], tag=f"wld{o}")
            is_q, is_k, is_v = o < 8, 8 <= o < 16, 16 <= o < 24
            cchunk = o % 8 if (o < 16) else None
            for w in range(4):
                if s1a is not None and oi == 0:
                    next(s1a, None)
                    next(s1a, None)
                tpar = w % 2
                btab = B[f"tabs{tpar}"]
                if is_q or is_k:
                    ctab_d = cqm_d if is_q else ckm_d
                    stab_d = sqm_d if is_q else skm_d
                    s.dma(ch_rd[tpar], lambda e, ctab_d=ctab_d, cchunk=cchunk, w=w, tpar=tpar:
                          e.dma_start(
                              out=st["tabs"][:, 2 * WIN * tpar:2 * WIN * tpar + WIN],
                              in_=ctab_d[128 * cchunk:128 * (cchunk + 1),
                                         WIN * w:WIN * (w + 1)]),
                          writes=[btab], tag=f"tc{o}_{w}")
                    s.dma(ch_rd[2 + tpar], lambda e, stab_d=stab_d, cchunk=cchunk, w=w, tpar=tpar:
                          e.dma_start(
                              out=st["tabs"][:, 2 * WIN * tpar + WIN:2 * WIN * (tpar + 1)],
                              in_=stab_d[128 * cchunk:128 * (cchunk + 1),
                                         WIN * w:WIN * (w + 1)]),
                          writes=[btab], tag=f"ts{o}_{w}")
                for sq in range(SEQ):
                    tw = T * sq + WIN * w
                    twin = WIN * w
                    pname = prot[cnt[0] % 3]
                    rname = rrot[cnt[0] % 2]
                    cnt[0] += 1
                    bp = B[pname]
                    for cci in range(8):
                        s.op("tensor", lambda e, pname=pname, par=par, cci=cci, tw=tw:
                             e.matmul(pap(pname),
                                      st["wbuf"][:, 1024 * par + 128 * cci:
                                                 1024 * par + 128 * (cci + 1)],
                                      bass.AP(xhfm, cci * NT + tw, [[8 * NT, 128], [1, WIN]]),
                                      start=(cci == 0), stop=(cci == 7)),
                             reads=[bw, B[f"XHFM{tw // WIN}"]], writes=[bp],
                             tag=f"mm{o}_{w}_{sq}_{cci}")
                    spar = (2 * w + sq) % 2
                    bqk = B[f"qkst{spar}"]
                    bxp = B[f"xps{spar}"]
                    if S1B_MODE == "mm_only":
                        s.op("scalar", lambda e, pname=pname, spar=spar: e.activation(
                            st["qkst"][:, 2 * WIN * spar:2 * WIN * spar + WIN],
                            ps[pname][:], AF.Copy),
                            reads=[bp], writes=[bqk], tag=f"dqb{o}_{w}_{sq}")
                        s.dma(ch_sp[4 + ((2 * w + sq) % 2)],
                              lambda e, o=o, tw=tw, spar=spar: e.dma_start(
                                  out=sdbg_d[o, :, tw:tw + WIN],
                                  in_=st["qkst"][:, 2 * WIN * spar:2 * WIN * spar + WIN]),
                              reads=[bqk], tag=f"dsp{o}_{w}_{sq}")
                        continue
                    if S1B_MODE == "rot" and (is_q or is_k):
                        s.op("scalar", lambda e, pname=pname, spar=spar: e.activation(
                            st["qkst"][:, 2 * WIN * spar:2 * WIN * spar + WIN],
                            ps[pname][:], AF.Copy),
                            reads=[bp], writes=[bqk], tag=f"dqb{o}_{w}_{sq}")
                        s.op("tensor", lambda e, spar=spar: e.matmul(
                            ps["PD"][:], st["pswapb"][:],
                            st["qkst"][:, 2 * WIN * spar:2 * WIN * spar + WIN],
                            start=True, stop=True),
                            reads=[bqk, B["pswapb"]], writes=[B["PD"]],
                            tag=f"drot{o}_{w}_{sq}")
                        s.op("vector", lambda e, spar=spar: e.tensor_copy(
                            st["qkst"][:, 2 * WIN * spar + WIN:2 * WIN * (spar + 1)],
                            bass.AP(ps["PD"], 0, [[512, 128], [1, WIN]])),
                            reads=[B["PD"]], writes=[bqk], tag=f"dre{o}_{w}_{sq}")
                        s.dma(ch_sp[4 + ((2 * w + sq) % 2)],
                              lambda e, o=o, tw=tw, spar=spar: e.dma_start(
                                  out=sdbg_d[o, :, tw:tw + WIN],
                                  in_=st["qkst"][:, 2 * WIN * spar + WIN:2 * WIN * (spar + 1)]),
                              reads=[bqk], tag=f"dsp{o}_{w}_{sq}")
                        continue
                    if S1B_MODE == "rot":
                        continue
                    if S1B_MODE in ("xpos", "xpos1", "xpos2", "xpos3", "xpos4") and not (is_q or is_k):
                        continue
                    if S1B_MODE == "xpos3" and (is_q or is_k):
                        s.op("vector", lambda e, pname=pname, tpar=tpar, spar=spar: e.tensor_mul(
                            st["xps"][:, WIN * spar:WIN * (spar + 1)], ps[pname][:],
                            st["tabs"][:, 2 * WIN * tpar:2 * WIN * tpar + WIN]),
                            reads=[bp, btab], writes=[bxp], tag=f"3xc{o}_{w}_{sq}")
                        btm = B[f"tmp1{spar}"]
                        s.op("vector", lambda e, pname=pname, tpar=tpar, spar=spar: e.tensor_mul(
                            st["tmp1"][:, C * spar:C * spar + WIN], ps[pname][:],
                            st["tabs"][:, 2 * WIN * tpar + WIN:2 * WIN * (tpar + 1)]),
                            reads=[bp, btab], writes=[btm], tag=f"3xs{o}_{w}_{sq}")
                        s.op("vector", lambda e, spar=spar: e.tensor_add(
                            st["qkst"][:, 2 * WIN * spar + WIN:2 * WIN * (spar + 1)],
                            st["xps"][:, WIN * spar:WIN * (spar + 1)],
                            st["tmp1"][:, C * spar:C * spar + WIN]),
                            reads=[bxp, btm], writes=[bqk], tag=f"3xa{o}_{w}_{sq}")
                        s.dma(ch_sp[4 + ((2 * w + sq) % 2)],
                              lambda e, o=o, tw=tw, spar=spar: e.dma_start(
                                  out=sdbg_d[o, :, tw:tw + WIN],
                                  in_=st["qkst"][:, 2 * WIN * spar + WIN:2 * WIN * (spar + 1)]),
                              reads=[bqk], tag=f"dsp{o}_{w}_{sq}")
                        continue
                    if S1B_MODE == "xpos4" and (is_q or is_k):
                        s.op("scalar", lambda e, pname=pname, spar=spar: e.activation(
                            st["qkst"][:, 2 * WIN * spar:2 * WIN * spar + WIN],
                            ps[pname][:], AF.Copy),
                            reads=[bp], writes=[bqk], tag=f"4qb{o}_{w}_{sq}")
                        s.op("tensor", lambda e, spar=spar: e.matmul(
                            ps["PD"][:], st["pswapb"][:],
                            st["qkst"][:, 2 * WIN * spar:2 * WIN * spar + WIN],
                            start=True, stop=True),
                            reads=[bqk, B["pswapb"]], writes=[B["PD"]],
                            tag=f"4rot{o}_{w}_{sq}")
                        btm = B[f"tmp1{spar}"]
                        s.op("vector", lambda e, tpar=tpar, spar=spar: e.tensor_mul(
                            st["tmp1"][:, C * spar:C * spar + WIN],
                            bass.AP(ps["PD"], 0, [[512, 128], [1, WIN]]),
                            st["tabs"][:, 2 * WIN * tpar + WIN:2 * WIN * (tpar + 1)]),
                            reads=[B["PD"], btab], writes=[btm], tag=f"4xs{o}_{w}_{sq}")
                        s.op("vector", lambda e, spar=spar: e.tensor_copy(
                            st["qkst"][:, 2 * WIN * spar + WIN:2 * WIN * (spar + 1)],
                            st["tmp1"][:, C * spar:C * spar + WIN]),
                            reads=[btm], writes=[bqk], tag=f"4cp{o}_{w}_{sq}")
                        s.dma(ch_sp[4 + ((2 * w + sq) % 2)],
                              lambda e, o=o, tw=tw, spar=spar: e.dma_start(
                                  out=sdbg_d[o, :, tw:tw + WIN],
                                  in_=st["qkst"][:, 2 * WIN * spar + WIN:2 * WIN * (spar + 1)]),
                              reads=[bqk], tag=f"dsp{o}_{w}_{sq}")
                        continue
                    if S1B_MODE == "xpos1" and (is_q or is_k):
                        s.op("vector", lambda e, pname=pname, tpar=tpar, spar=spar: e.tensor_mul(
                            st["xps"][:, WIN * spar:WIN * (spar + 1)], ps[pname][:],
                            st["tabs"][:, 2 * WIN * tpar:2 * WIN * tpar + WIN]),
                            reads=[bp, btab], writes=[bxp], tag=f"dxc{o}_{w}_{sq}")
                        s.op("scalar", lambda e, spar=spar: e.activation(
                            st["qkst"][:, 2 * WIN * spar:2 * WIN * spar + WIN],
                            st["xps"][:, WIN * spar:WIN * (spar + 1)], AF.Copy),
                            reads=[bxp], writes=[bqk], tag=f"dcp{o}_{w}_{sq}")
                        s.dma(ch_sp[4 + ((2 * w + sq) % 2)],
                              lambda e, o=o, tw=tw, spar=spar: e.dma_start(
                                  out=sdbg_d[o, :, tw:tw + WIN],
                                  in_=st["qkst"][:, 2 * WIN * spar:2 * WIN * spar + WIN]),
                              reads=[bqk], tag=f"dsp{o}_{w}_{sq}")
                        continue
                    if S1B_MODE == "xpos2" and (is_q or is_k):
                        s.op("scalar", lambda e, pname=pname, spar=spar: e.activation(
                            st["qkst"][:, 2 * WIN * spar:2 * WIN * spar + WIN],
                            ps[pname][:], AF.Copy),
                            reads=[bp], writes=[bqk], tag=f"qb{o}_{w}_{sq}")
                        s.op("tensor", lambda e, spar=spar, rname=rname: e.matmul(
                            ps[rname][:], st["pswapb"][:],
                            st["qkst"][:, 2 * WIN * spar:2 * WIN * spar + WIN],
                            start=True, stop=True),
                            reads=[bqk, B["pswapb"]], writes=[B[rname]],
                            tag=f"rot{o}_{w}_{sq}")
                        s.op("vector", lambda e, pname=pname, tpar=tpar, spar=spar: e.tensor_mul(
                            st["xps"][:, WIN * spar:WIN * (spar + 1)], ps[pname][:],
                            st["tabs"][:, 2 * WIN * tpar:2 * WIN * tpar + WIN]),
                            reads=[bp, btab], writes=[bxp], tag=f"xc{o}_{w}_{sq}")
                        btm = B[f"tmp1{spar}"]
                        s.op("vector", lambda e, tpar=tpar, spar=spar: e.tensor_mul(
                            st["tmp1"][:, C * spar:C * spar + WIN],
                            bass.AP(ps["PD"], 0, [[512, 128], [1, WIN]]),
                            st["tabs"][:, 2 * WIN * tpar + WIN:2 * WIN * (tpar + 1)]),
                            reads=[B["PD"], btab], writes=[btm], tag=f"xs{o}_{w}_{sq}")
                        s.op("vector", lambda e, spar=spar: e.tensor_add(
                            st["qkst"][:, 2 * WIN * spar + WIN:2 * WIN * (spar + 1)],
                            st["xps"][:, WIN * spar:WIN * (spar + 1)],
                            st["tmp1"][:, C * spar:C * spar + WIN]),
                            reads=[bxp, btm], writes=[bqk], tag=f"xa{o}_{w}_{sq}")
                        s.dma(ch_sp[4 + ((2 * w + sq) % 2)],
                              lambda e, o=o, tw=tw, spar=spar: e.dma_start(
                                  out=sdbg_d[o, :, tw:tw + WIN],
                                  in_=st["qkst"][:, 2 * WIN * spar + WIN:2 * WIN * (spar + 1)]),
                              reads=[bqk], tag=f"dsp{o}_{w}_{sq}")
                        continue
                    if S1B_MODE == "vff" and (is_q or is_k):
                        s.op("scalar", lambda e, pname=pname, spar=spar: e.activation(
                            st["qkst"][:, 2 * WIN * spar:2 * WIN * spar + WIN],
                            ps[pname][:], AF.Copy),
                            reads=[bp], writes=[bqk], tag=f"dqb{o}_{w}_{sq}")
                        s.dma(ch_sp[4 + ((2 * w + sq) % 2)],
                              lambda e, o=o, tw=tw, spar=spar: e.dma_start(
                                  out=sdbg_d[o, :, tw:tw + WIN],
                                  in_=st["qkst"][:, 2 * WIN * spar:2 * WIN * spar + WIN]),
                              reads=[bqk], tag=f"dsp{o}_{w}_{sq}")
                        continue
                    if is_q or is_k:
                        evac_op = s.op("scalar", lambda e, pname=pname, spar=spar: e.activation(
                            st["qkst"][:, 2 * WIN * spar:2 * WIN * spar + WIN],
                            pap(pname), AF.Copy),
                            reads=[bp], writes=[bqk], tag=f"qb{o}_{w}_{sq}")
                        s.op("tensor", lambda e, spar=spar, rname=rname: e.matmul(
                            ps[rname][:], st["pswapb"][:],
                            st["qkst"][:, 2 * WIN * spar:2 * WIN * spar + WIN],
                            start=True, stop=True),
                            reads=[bqk, B["pswapb"]], writes=[B[rname]],
                            tag=f"rot{o}_{w}_{sq}")
                        xc_op = s.op("vector", lambda e, pname=pname, tpar=tpar, spar=spar: e.tensor_mul(
                            st["xps"][:, WIN * spar:WIN * (spar + 1)], pap(pname),
                            st["tabs"][:, 2 * WIN * tpar:2 * WIN * tpar + WIN]),
                            reads=[bp, btab], writes=[bxp], tag=f"xc{o}_{w}_{sq}")

                        btm = B[f"tmp1{spar}"]
                        s.op("vector", lambda e, tpar=tpar, spar=spar, rname=rname: e.tensor_mul(
                            st["tmp1"][:, C * spar:C * spar + WIN], ps[rname][:],
                            st["tabs"][:, 2 * WIN * tpar + WIN:2 * WIN * (tpar + 1)]),
                            reads=[B[rname], btab], writes=[btm], tag=f"xs{o}_{w}_{sq}")
                        s.op("vector", lambda e, spar=spar: e.tensor_add(
                            st["qkst"][:, 2 * WIN * spar + WIN:2 * WIN * (spar + 1)],
                            st["xps"][:, WIN * spar:WIN * (spar + 1)],
                            st["tmp1"][:, C * spar:C * spar + WIN]),
                            reads=[bxp, btm], writes=[bqk], tag=f"xa{o}_{w}_{sq}")
                        dst_d = sq_d if is_q else sk_d
                        s.dma(ch_sp[4 + ((2 * w + sq) % 2)],
                              lambda e, dst_d=dst_d, sq=sq, cchunk=cchunk, twin=twin, spar=spar:
                              e.dma_start(
                                  out=dst_d[sq, 128 * cchunk:128 * (cchunk + 1),
                                            twin:twin + WIN],
                                  in_=st["qkst"][:, 2 * WIN * spar + WIN:
                                                 2 * WIN * (spar + 1)]),
                              reads=[bqk], tag=f"qko{o}_{w}_{sq}")
                    elif is_v:
                        s.op("scalar", lambda e, pname=pname, spar=spar: e.activation(
                            st["qkst"][:, 2 * WIN * spar:2 * WIN * spar + WIN],
                            pap(pname), AF.Copy),
                            reads=[bp], writes=[bqk], tag=f"vb{o}_{w}_{sq}")
                        for j in range(4):
                            s.op("tensor", lambda e, j=j, spar=spar: e.transpose(
                                ps["PT"][:, 128 * j:128 * (j + 1)],
                                st["qkst"][:, 2 * WIN * spar + 128 * j:
                                           2 * WIN * spar + 128 * (j + 1)],
                                st["identb"][:]),
                                reads=[bqk, B["identb"]], writes=[B["PT"]],
                                tag=f"vt{o}_{w}_{sq}_{j}")
                        s.op("scalar", lambda e, spar=spar: e.activation(
                            st["qkst"][:, 2 * WIN * spar + WIN:2 * WIN * (spar + 1)],
                            ps["PT"][:, 0:512], AF.Copy),
                            reads=[B["PT"]], writes=[bqk], tag=f"ve{o}_{w}_{sq}")
                        for j in range(4):
                            s.dma(ch_sp[6 + ((2 * w + sq + j) % 2)],
                                  lambda e, sq=sq, o=o, twin=twin, j=j, spar=spar:
                                  e.dma_start(
                                      out=sv_d[sq, twin + 128 * j:twin + 128 * (j + 1),
                                               128 * (o - 16):128 * (o - 15)],
                                      in_=st["qkst"][:, 2 * WIN * spar + WIN + 128 * j:
                                                     2 * WIN * spar + WIN + 128 * (j + 1)]),
                                  reads=[bqk], tag=f"vo{o}_{w}_{sq}_{j}")
                    else:
                        s.op("scalar", lambda e, pname=pname, spar=spar: e.activation(
                            st["qkst"][:, 2 * WIN * spar:2 * WIN * spar + WIN],
                            pap(pname), AF.Gelu),
                            reads=[bp], writes=[bqk], tag=f"gel{o}_{w}_{sq}")
                        s.dma(ch_sp[4 + ((2 * w + sq) % 2)],
                              lambda e, sq=sq, o=o, twin=twin, spar=spar: e.dma_start(
                                  out=sgel_d[sq, :, twin // 128:twin // 128 + 4,
                                             o - 24, :],
                                  in_=st["qkst"][:, 2 * WIN * spar:2 * WIN * spar + WIN]),
                              reads=[bqk], tag=f"go{o}_{w}_{sq}")
                if wq is not None and oi >= 16:
                    next(wq, None)


    def carve_S2():
        sb_at("KK", (128, 8 * T), BF16, 0)
        sb_at("VV", (128, 16 * C), BF16, 32 * 1024)
        sb_at("Y", (128, 8 * T), F32, 64 * 1024)
        sb_at("QT", (128, 2 * WIN), BF16, 128 * 1024)
        sb_at("QB", (128, 2 * WIN), BF16, 159 * 1024)
        sb_at("SC", (128, 4 * WIN), BF16, 130 * 1024)
        sb_at("YT", (128, 2 * C), F32, 134 * 1024)
        sb_at("YSQT", (128, 2 * C), F32, 142 * 1024)
        sb_at("YN", (128, 2 * C), BF16, 150 * 1024)
        sb_at("MU", (128, 16 * 5), F32, 154 * 1024)
        sb_at("Y2S", (128, 2 * C), BF16, 155 * 1024)
        for nm in ("QT", "QB", "SC", "YT", "YSQT", "YN", "Y2S"):
            B[nm + "0"] = s.buf(nm + "0")
            B[nm + "1"] = s.buf(nm + "1")
        for nm in ("KK", "VV", "Y", "MU"):
            if nm not in B:
                B[nm] = s.buf(nm)
        for g in range(4):
            if f"Ycol{g}" not in B:
                B[f"Ycol{g}"] = s.buf(f"Ycol{g}")
        for sq in range(SEQ):
            for g in range(4):
                if f"Y2Asp{sq}_{g}" not in B:
                    B[f"Y2Asp{sq}_{g}"] = s.buf(f"Y2Asp{sq}_{g}")

    def S2(sq, gn_prev=None):
        if sq == 0:
            s.op("vector", lambda e: e.memset(st["QT"][64:128, :], 0.0),
                 writes=[B["QT0"], B["QT1"]], tag="qz0")
            s.op("vector", lambda e: e.memset(st["QB"][0:64, :], 0.0),
                 writes=[B["QB0"], B["QB1"]], tag="qz1")
        for cc in range(8):
            s.dma(ch_rd[cc % 2], lambda e, cc=cc, sq=sq: e.dma_start(
                out=st["KK"][:, T * cc:T * (cc + 1)],
                in_=sk_d[sq, 128 * cc:128 * (cc + 1), :]),
                writes=[B["KK"]], tag=f"Kld{sq}_{cc}")
        for stile in range(16):
            s.dma(ch_rd[2 + stile % 2], lambda e, stile=stile, sq=sq: e.dma_start(
                out=st["VV"][:, C * stile:C * (stile + 1)],
                in_=sv_d[sq, 128 * stile:128 * (stile + 1), :]),
                writes=[B["VV"]], tag=f"Vld{sq}_{stile}")
        scA = ["PA", "PB"]
        scB = ["PC", "PC"]
        for w in range(4):
            if gn_prev is not None:
                next(gn_prev, None)
            for hp in range(8):
                qpar = (4 * w + hp) % 2
                bq = B[f"QT{qpar}"]
                bqb = B[f"QB{qpar}"]
                s.dma(ch_rd[qpar], lambda e, sq=sq, hp=hp, w=w, qpar=qpar: e.dma_start(
                    out=st["QT"][0:64, WIN * qpar:WIN * (qpar + 1)],
                    in_=sq_d[sq, 128 * hp:128 * hp + 64, WIN * w:WIN * (w + 1)]),
                    writes=[bq], tag=f"Qld{sq}_{w}_{hp}")
                s.dma(ch_rd[qpar], lambda e, sq=sq, hp=hp, w=w, qpar=qpar: e.dma_start(
                    out=st["QB"][64:128, WIN * qpar:WIN * (qpar + 1)],
                    in_=sq_d[sq, 128 * hp + 64:128 * (hp + 1), WIN * w:WIN * (w + 1)]),
                    writes=[bqb], tag=f"Qlb{sq}_{w}_{hp}")
                nst = 4 * w + 4
                avdef = [None]
                for stt in range(nst):
                    pa = scA[stt % 2]
                    pb = scB[stt % 2]
                    s.op("tensor", lambda e, pa=pa, hp=hp, stt=stt, qpar=qpar: e.matmul(
                        ps[pa][:], st["KK"][:, T * hp + 128 * stt:T * hp + 128 * (stt + 1)],
                        st["QT"][:, WIN * qpar:WIN * (qpar + 1)],
                        start=True, stop=True),
                        reads=[B["KK"], bq], writes=[B[pa]], tag=f"sMA{sq}_{w}_{hp}_{stt}")
                    s.op("tensor", lambda e, pb=pb, hp=hp, stt=stt, qpar=qpar: e.matmul(
                        ps[pb][:], st["KK"][:, T * hp + 128 * stt:T * hp + 128 * (stt + 1)],
                        st["QB"][:, WIN * qpar:WIN * (qpar + 1)],
                        start=True, stop=True),
                        reads=[B["KK"], bqb], writes=[B[pb]], tag=f"sMB{sq}_{w}_{hp}_{stt}")
                    diag = stt >= 4 * w
                    r = stt - 4 * w
                    bsc = B[f"SC{stt % 2}"]
                    if diag:
                        s.op("vector", lambda e, pa=pa, stt=stt, r=r: e.tensor_mul(
                            st["SC"][:, WIN * (stt % 2):WIN * (stt % 2 + 1)],
                            ps[pa][:], st["tri"][:, WIN * r:WIN * (r + 1)]),
                            reads=[B[pa], B["tri"]], writes=[bsc], tag=f"eA{sq}_{w}_{hp}_{stt}")
                        s.op("vector", lambda e, pb=pb, stt=stt, r=r: e.tensor_mul(
                            st["SC"][:, WIN * (2 + stt % 2):WIN * (3 + stt % 2)],
                            ps[pb][:], st["tri"][:, WIN * r:WIN * (r + 1)]),
                            reads=[B[pb], B["tri"]], writes=[bsc], tag=f"eB{sq}_{w}_{hp}_{stt}")
                    else:
                        s.op("scalar", lambda e, pa=pa, stt=stt: e.activation(
                            st["SC"][:, WIN * (stt % 2):WIN * (stt % 2 + 1)],
                            ps[pa][:], AF.Copy, scale=0.125),
                            reads=[B[pa]], writes=[bsc], tag=f"eA{sq}_{w}_{hp}_{stt}")
                        s.op("vector", lambda e, pb=pb, stt=stt: e.tensor_scalar(
                            st["SC"][:, WIN * (2 + stt % 2):WIN * (3 + stt % 2)],
                            ps[pb][:], 0.125, None, ALU.mult),
                            reads=[B[pb]], writes=[bsc], tag=f"eB{sq}_{w}_{hp}_{stt}")
                    if avdef[0] is not None:
                        avdef[0]()
                    def avblock(hp=hp, stt=stt, nst=nst):
                        s.op("tensor", lambda e, hp=hp, stt=stt, nst=nst: e.matmul(
                            ps["PE2"][0:64, 0:512],
                            st["VV"][:, C * stt + 128 * hp:C * stt + 128 * hp + 64],
                            st["SC"][:, WIN * (stt % 2):WIN * (stt % 2 + 1)],
                            start=(stt == 0), stop=(stt == nst - 1)),
                            reads=[B["VV"], B[f"SC{stt % 2}"]], writes=[B["PE2A"]],
                            tag=f"aMA{sq}_{w}_{hp}_{stt}")
                        s.op("tensor", lambda e, hp=hp, stt=stt, nst=nst: e.matmul(
                            ps["PE2"][0:64, 512:1024],
                            st["VV"][:, C * stt + 128 * hp + 64:C * stt + 128 * (hp + 1)],
                            st["SC"][:, WIN * (2 + stt % 2):WIN * (3 + stt % 2)],
                            start=(stt == 0), stop=(stt == nst - 1)),
                            reads=[B["VV"], B[f"SC{stt % 2}"]], writes=[B["PE2B"]],
                            tag=f"aMB{sq}_{w}_{hp}_{stt}")
                    avdef[0] = avblock
                avdef[0]()
                s.op("vector", lambda e, hp=hp, w=w: e.tensor_copy(
                    st["Y"][0:64, T * hp + WIN * w:T * hp + WIN * (w + 1)],
                    ps["PE2"][0:64, 0:512]),
                    reads=[B["PE2A"]], writes=[B[f"Ycol{w}"]], tag=f"yA{sq}_{w}_{hp}")
                s.op("vector", lambda e, hp=hp, w=w: e.tensor_copy(
                    st["Y"][64:128, T * hp + WIN * w:T * hp + WIN * (w + 1)],
                    ps["PE2"][0:64, 512:1024]),
                    reads=[B["PE2B"]], writes=[B[f"Ycol{w}"]], tag=f"yB{sq}_{w}_{hp}")
    def GN_gen(sq):
        for tt in range(16):
            par = tt % 2
            byt, bysq, byn, by2 = (B[f"YT{par}"], B[f"YSQT{par}"],
                                   B[f"YN{par}"], B[f"Y2S{par}"])
            byc = B[f"Ycol{tt // 4}"]
            for gg in range(2):
                pname = "PD" if gg == 0 else "PG"
                for j in range(4):
                    cc = 4 * gg + j
                    s.op("tensor", lambda e, cc=cc, j=j, tt=tt, pname=pname: e.transpose(
                        ps[pname][:, 128 * j:128 * (j + 1)],
                        st["Y"][:, T * cc + 128 * tt:T * cc + 128 * (tt + 1)],
                        st["identf"][:]),
                        reads=[byc, B["identf"]], writes=[B[pname]],
                        tag=f"gt{sq}_{tt}_{cc}")
                s.op("vector", lambda e, gg=gg, par=par, pname=pname: e.tensor_copy(
                    st["YT"][:, C * par + 512 * gg:C * par + 512 * (gg + 1)],
                    ps[pname][:]),
                    reads=[B[pname]], writes=[byt], tag=f"ge{sq}_{tt}_{gg}")
            s.op("vector", lambda e, par=par, tt=tt: e.reduce_sum(
                st["MU"][:, 16 * (tt % 2):16 * (tt % 2) + 16],
                bass.AP(st["YT"], C * par, [[2 * C, 128], [64, 16], [1, 64]]), AX),
                reads=[byt, B["MU"]], writes=[B["MU"]], tag=f"gmu{sq}_{tt}")
            s.op("scalar", lambda e, par=par: e.activation(
                st["YSQT"][:, C * par:C * (par + 1)],
                st["YT"][:, C * par:C * (par + 1)], AF.Square),
                reads=[byt], writes=[bysq], tag=f"gsq{sq}_{tt}")
            s.op("vector", lambda e, par=par, tt=tt: e.reduce_sum(
                st["MU"][:, 32 + 16 * (tt % 2):32 + 16 * (tt % 2) + 16],
                bass.AP(st["YSQT"], C * par, [[2 * C, 128], [64, 16], [1, 64]]), AX),
                reads=[bysq, B["MU"]], writes=[B["MU"]], tag=f"gs2{sq}_{tt}")
            mcol = slice(16 * (tt % 2), 16 * (tt % 2) + 16)
            scol = slice(32 + 16 * (tt % 2), 32 + 16 * (tt % 2) + 16)
            s.op("vector", lambda e, mcol=mcol: e.tensor_scalar(
                st["MU"][:, mcol], st["MU"][:, mcol], 1.0 / 64.0, None, ALU.mult),
                reads=[B["MU"]], writes=[B["MU"]], tag=f"gm1{sq}_{tt}")
            s.op("vector", lambda e, mcol=mcol, scol=scol: e.tensor_mul(
                st["MU"][:, 64:80], st["MU"][:, mcol], st["MU"][:, mcol]),
                reads=[B["MU"]], writes=[B["MU"]], tag=f"gm2{sq}_{tt}")
            s.op("vector", lambda e, scol=scol: e.tensor_scalar(
                st["MU"][:, scol], st["MU"][:, scol], 1.0 / 64.0, GN_EPS, ALU.mult, ALU.add),
                reads=[B["MU"]], writes=[B["MU"]], tag=f"gm3{sq}_{tt}")
            s.op("vector", lambda e, scol=scol: e.tensor_sub(
                st["MU"][:, scol], st["MU"][:, scol], st["MU"][:, 64:80]),
                reads=[B["MU"]], writes=[B["MU"]], tag=f"gm4{sq}_{tt}")
            s.op("scalar", lambda e, scol=scol: e.activation(
                st["MU"][:, 64:80], st["MU"][:, scol], AF.Sqrt),
                reads=[B["MU"]], writes=[B["MU"]], tag=f"gm7{sq}_{tt}")
            s.op("vector", lambda e: e.reciprocal(st["MU"][:, 64:80], st["MU"][:, 64:80]),
                reads=[B["MU"]], writes=[B["MU"]], tag=f"gm8{sq}_{tt}")
            for h in range(16):
                s.op("vector", lambda e, h=h, par=par, mcol=mcol: e.tensor_scalar(
                    st["YN"][:, C * par + 64 * h:C * par + 64 * (h + 1)],
                    st["YT"][:, C * par + 64 * h:C * par + 64 * (h + 1)],
                    st["MU"][:, mcol.start + h:mcol.start + h + 1],
                    st["MU"][:, 64 + h:64 + h + 1], ALU.subtract, ALU.mult),
                    reads=[byt, B["MU"]], writes=[byn], tag=f"gn{sq}_{tt}_{h}")
            for cc in range(8):
                s.op("tensor", lambda e, cc=cc, par=par: e.transpose(
                    ps["PT"][:, 128 * cc:128 * (cc + 1)],
                    st["YN"][:, C * par + 128 * cc:C * par + 128 * (cc + 1)],
                    st["identb"][:]),
                    reads=[byn, B["identb"]], writes=[B["PT"]], tag=f"gb{sq}_{tt}_{cc}")
            for cc in range(8):
                s.op("vector", lambda e, cc=cc, par=par: e.tensor_scalar(
                    st["Y2S"][:, C * par + 128 * cc:C * par + 128 * (cc + 1)],
                    ps["PT"][:, 128 * cc:128 * (cc + 1)],
                    st["vecs"][:, 16 + cc:17 + cc], st["vecs"][:, 24 + cc:25 + cc],
                    ALU.mult, ALU.add),
                    reads=[B["PT"], B["vecs"]], writes=[by2], tag=f"ga{sq}_{tt}_{cc}")
            s.dma(ch_sp[4 + (tt % 2)], lambda e, tt=tt, sq=sq, par=par:
                  e.dma_start(
                      out=sy2a_d[sq, :, :, 128 * tt:128 * (tt + 1)],
                      in_=st["Y2S"][:, C * par:C * (par + 1)]),
                  reads=[by2], writes=[B[f"Y2Asp{sq}_{tt // 4}"]], tag=f"gsp{sq}_{tt}")
            if tt % 4 == 3:
                yield

    def carve_S4():
        sb_at("WGT", (128, 64 * 128), BF16, 0)
        sb_at("XNM", (128, 2 * 8 * WIN), BF16, 16 * 1024)
        sb_at("YH", (128, 2 * WIN), BF16, 32 * 1024)
        sb_at("GS", (128, 2 * WIN), BF16, 34 * 1024)
        sb_at("Y2O", (128, 2 * WIN), BF16, 36 * 1024)
        for nm in ("XNM", "YH", "GS", "Y2O"):
            B[nm + "0"] = s.buf(nm + "0")
            B[nm + "1"] = s.buf(nm + "1")
        B["WGT"] = s.buf("WGT")

    def S4(gn=None):
        for o in range(8):
            s.dma(ch_w[o % 2], lambda e, o=o: e.dma_start(
                out=st["WGT"][:, 1024 * o:1024 * (o + 1)], in_=wgt_d[o]),
                writes=[B["WGT"]], tag=f"wgld{o}")
        pp = ["PA", "PB"]
        for sq in range(SEQ):
            for w in range(4):
                if gn is not None and sq == 0:
                    next(gn, None)
                wpar = (4 * sq + w) % 2
                tw = T * sq + WIN * w
                bxn = B[f"XNM{wpar}"]
                for cc in range(8):
                    s.dma(ch_rd[cc % 2], lambda e, cc=cc, tw=tw, wpar=wpar: e.dma_start(
                        out=st["XNM"][:, 8 * WIN * wpar + WIN * cc:
                                      8 * WIN * wpar + WIN * (cc + 1)],
                        in_=sxn_d[:, cc, tw:tw + WIN]),
                        writes=[bxn], tag=f"xnl{sq}_{w}_{cc}")
                for o in range(8):
                    par = o % 2
                    byh, bgs, by2 = B[f"YH{par}"], B[f"GS{par}"], B[f"Y2O{par}"]
                    s.dma(ch_rd[2 + par], lambda e, o=o, sq=sq, w=w, par=par: e.dma_start(
                        out=st["YH"][:, WIN * par:WIN * (par + 1)],
                        in_=sy2a_d[sq, :, o, WIN * w:WIN * (w + 1)]),
                        reads=[B[f"Y2Asp{sq}_{w}"]],
                        writes=[byh], tag=f"yhl{o}_{sq}_{w}")
                    pname = pp[par]
                    for cc in range(8):
                        s.op("tensor", lambda e, pname=pname, o=o, cc=cc, wpar=wpar:
                             e.matmul(ps[pname][:],
                                      st["WGT"][:, 1024 * o + 128 * cc:
                                                 1024 * o + 128 * (cc + 1)],
                                      st["XNM"][:, 8 * WIN * wpar + WIN * cc:
                                                 8 * WIN * wpar + WIN * (cc + 1)],
                                      start=(cc == 0), stop=(cc == 7)),
                             reads=[B["WGT"], bxn], writes=[B[pname]],
                             tag=f"gmm{o}_{sq}_{w}_{cc}")
                    s.op("scalar", lambda e, pname=pname, o=o, par=par: e.activation(
                        st["GS"][:, WIN * par:WIN * (par + 1)], ps[pname][:],
                        AF.Silu, bias=st["vecs"][:, o:o + 1]),
                        reads=[B[pname], B["vecs"]], writes=[bgs], tag=f"gsl{o}_{sq}_{w}")
                    s.op("vector", lambda e, par=par: e.tensor_mul(
                        st["Y2O"][:, WIN * par:WIN * (par + 1)],
                        st["GS"][:, WIN * par:WIN * (par + 1)],
                        st["YH"][:, WIN * par:WIN * (par + 1)]),
                        reads=[bgs, byh], writes=[by2], tag=f"gy2{o}_{sq}_{w}")
                    s.dma(ch_sp[par], lambda e, o=o, tw=tw, par=par: e.dma_start(
                        out=sy2_d[:, tw // 128:tw // 128 + 4, o, :],
                        in_=st["Y2O"][:, WIN * par:WIN * (par + 1)]),
                        reads=[by2], tag=f"y2o{o}_{sq}_{w}")

    def carve_S5():
        sb_at("WGQ", (128, 8 * INNER), BF16, 0)
        sb_at("WUQ", (128, 8 * INNER), BF16, 64 * 1024)
        sb_at("XQS", (128, 2 * 1024), BF16, 128 * 1024)
        sb_at("HH", (128, INNER), F32, 132 * 1024)
        sb_at("VT", (128, INNER), F32, 148 * 1024)
        sb_at("HQT", (128, INNER), BF16, 164 * 1024)
        sb_at("HQS", (128, 1024), BF16, 172 * 1024)
        for nm in ("XQS",):
            B[nm + "0"] = s.buf(nm + "0")
            B[nm + "1"] = s.buf(nm + "1")
        for nm in ("WGQ", "WUQ", "HH", "VT", "HQT", "HQS"):
            B[nm] = s.buf(nm)

    def S5():
        for cc in range(8):
            s.dma(ch_rd[cc % 2], lambda e, cc=cc: e.dma_start(
                out=st["WGQ"][:, INNER * cc:INNER * (cc + 1)],
                in_=swgq_d[128 * cc:128 * (cc + 1), :]),
                writes=[B["WGQ"]], tag=f"wgq{cc}")
            s.dma(ch_rd[2 + cc % 2], lambda e, cc=cc: e.dma_start(
                out=st["WUQ"][:, INNER * cc:INNER * (cc + 1)],
                in_=swuq_d[128 * cc:128 * (cc + 1), :]),
                writes=[B["WUQ"]], tag=f"wuq{cc}")
        tc = st["tmpc"]
        tdefer = [None]
        for i in range(NTT):
            par = i % 2
            bxq = B[f"XQS{par}"]
            s.dma(ch_in[par], lambda e, i=i, par=par: e.dma_start(
                out=st["XQS"][:, 1024 * par:1024 * (par + 1)], in_=sxq_d[i]),
                writes=[bxq], tag=f"xql{i}")
            s.op("vector", lambda e, i=i: e.tensor_scalar(
                tc[:, 0:1], st["stats"][:, 4 * NTT + i:4 * NTT + i + 1],
                st["wsc"][:, 0:1], None, ALU.mult),
                reads=[B["stats"], B["wsc"], B["tmpc"]], writes=[B["tmpc"]], tag=f"dqg{i}")
            s.op("vector", lambda e, i=i: e.tensor_scalar(
                tc[:, 1:2], st["stats"][:, 4 * NTT + i:4 * NTT + i + 1],
                st["wsc"][:, 1:2], None, ALU.mult),
                reads=[B["stats"], B["wsc"], B["tmpc"]], writes=[B["tmpc"]], tag=f"dqu{i}")
            pg = ["PA", "PC"]
            pu = ["PB", "PD"]
            for iw in range(8):
                q = iw % 2
                psg, psu = pg[q], pu[q]
                for cc in range(8):
                    s.op("tensor", lambda e, psg=psg, cc=cc, iw=iw, par=par: e.matmul(
                        pap(psg),
                        st["XQS"][:, 1024 * par + 128 * cc:1024 * par + 128 * (cc + 1)],
                        st["WGQ"][:, INNER * cc + WIN * iw:INNER * cc + WIN * (iw + 1)],
                        start=(cc == 0), stop=(cc == 7)),
                        reads=[bxq, B["WGQ"]], writes=[B[psg]], tag=f"fg{i}_{iw}_{cc}")
                    s.op("tensor", lambda e, psu=psu, cc=cc, iw=iw, par=par: e.matmul(
                        pap(psu),
                        st["XQS"][:, 1024 * par + 128 * cc:1024 * par + 128 * (cc + 1)],
                        st["WUQ"][:, INNER * cc + WIN * iw:INNER * cc + WIN * (iw + 1)],
                        start=(cc == 0), stop=(cc == 7)),
                        reads=[bxq, B["WUQ"]], writes=[B[psu]], tag=f"fu{i}_{iw}_{cc}")
                s.op("scalar", lambda e, psg=psg, iw=iw: e.activation(
                    st["VT"][:, WIN * iw:WIN * (iw + 1)], pap(psg), AF.Silu,
                    scale=tc[:, 0:1]),
                    reads=[B[psg], B["tmpc"]], writes=[B["VT"]], tag=f"fsl{i}_{iw}")
                s.op("vector", lambda e, psu=psu, iw=iw: e.tensor_mul(
                    st["HH"][:, WIN * iw:WIN * (iw + 1)],
                    st["VT"][:, WIN * iw:WIN * (iw + 1)], pap(psu)),
                    reads=[B[psu], B["VT"]], writes=[B["HH"]], tag=f"fh{i}_{iw}")
            if tdefer[0] is not None:
                tdefer[0]()
            s.op("vector", lambda e: e.reduce_max(tc[:, 2:3], st["HH"][:], AX,
                                                  apply_absolute_value=True),
                reads=[B["HH"], B["tmpc"]], writes=[B["tmpc"]], tag=f"fam{i}")
            s.op("vector", lambda e: e.tensor_mul(tc[:, 2:3], tc[:, 2:3], tc[:, 1:2]),
                reads=[B["tmpc"]], writes=[B["tmpc"]], tag=f"fm1{i}")
            s.op("vector", lambda e: e.tensor_scalar_max(tc[:, 2:3], tc[:, 2:3], 1e-5),
                reads=[B["tmpc"]], writes=[B["tmpc"]], tag=f"fm2{i}")
            s.op("vector", lambda e: e.reciprocal(tc[:, 3:4], tc[:, 2:3]),
                reads=[B["tmpc"]], writes=[B["tmpc"]], tag=f"fm3{i}")
            s.op("vector", lambda e: e.tensor_scalar(
                tc[:, 4:5], tc[:, 3:4], 127.0, None, ALU.mult),
                reads=[B["tmpc"]], writes=[B["tmpc"]], tag=f"fm4{i}")
            s.op("vector", lambda e: e.tensor_mul(tc[:, 4:5], tc[:, 4:5], tc[:, 1:2]),
                reads=[B["tmpc"]], writes=[B["tmpc"]], tag=f"fm5{i}")
            s.op("vector", lambda e: e.tensor_scalar(
                tc[:, 5:6], tc[:, 2:3], 1.0 / 127.0, None, ALU.mult),
                reads=[B["tmpc"]], writes=[B["tmpc"]], tag=f"fm6{i}")
            s.op("vector", lambda e: e.tensor_scalar(
                tc[:, 5:6], tc[:, 5:6], st["wsc"][:, 2:3], None, ALU.mult),
                reads=[B["tmpc"], B["wsc"]], writes=[B["tmpc"]], tag=f"fm7{i}")
            s.op("vector", lambda e: e.tensor_scalar(
                st["VT"][:], st["HH"][:], tc[:, 4:5], st["ceff"][:, 1:2],
                ALU.mult, ALU.add),
                reads=[B["HH"], B["tmpc"], B["ceff"]], writes=[B["VT"]], tag=f"fq1{i}")
            s.op("vector", lambda e: e.tensor_scalar(
                st["HH"][:], st["VT"][:], st["ceff"][:, 1:2], None, ALU.subtract),
                reads=[B["VT"], B["ceff"]], writes=[B["HH"]], tag=f"fq2{i}")
            hqt_t = st["HQT"]
            hqt_b = B["HQT"]
            s.op("vector", lambda e, hqt_t=hqt_t: e.tensor_scalar(
                hqt_t[:], st["HH"][:], tc[:, 5:6], None, ALU.mult),
                reads=[B["HH"], B["tmpc"]], writes=[hqt_b], tag=f"fq3{i}")

            def tblock(i=i, hqt_t=hqt_t, hqt_b=hqt_b):
                for r in range(4):
                    for g in range(8):
                        ic = 8 * r + g
                        s.op("tensor", lambda e, ic=ic, g=g, hqt_t=hqt_t: e.transpose(
                            ps["PT"][:, 128 * g:128 * (g + 1)],
                            hqt_t[:, 128 * ic:128 * (ic + 1)], st["identb"][:]),
                            reads=[hqt_b, B["identb"]], writes=[B["PT"]],
                            tag=f"ftr{i}_{ic}")
                    s.op("vector", lambda e: e.tensor_copy(st["HQS"][:], ps["PT"][:]),
                         reads=[B["PT"]], writes=[B["HQS"]], tag=f"fte{i}_{r}")
                    s.dma(ch_sp[4 + r % 2], lambda e, i=i, r=r: e.dma_start(
                        out=shq_d[i, r], in_=st["HQS"][:]),
                        reads=[B["HQS"]], tag=f"fsp{i}_{r}")
            tdefer[0] = tblock

        tdefer[0]()

    def carve_S6():
        sb_at("WDQ", (128, 32 * 1024), BF16, 0)
        sb_at("WPT", (128, 8 * 1024), BF16, 64 * 1024)
        sb_at("WFT", (128, 16 * 1024), BF16, 80 * 1024)
        sb_at("HQ6", (128, 2 * 4096), BF16, 112 * 1024)
        sb_at("Y2T", (128, 2 * 1024), BF16, 128 * 1024)
        sb_at("GET", (128, 2 * 2048), BF16, 132 * 1024)
        sb_at("XS", (128, 2 * 1024), F32, 140 * 1024)
        sb_at("OUTS", (128, 2 * 1024), F32, 148 * 1024)
        for nm in ("HQ6", "Y2T", "GET", "XS", "OUTS"):
            B[nm + "0"] = s.buf(nm + "0")
            B[nm + "1"] = s.buf(nm + "1")
        for nm in ("WDQ", "WPT", "WFT"):
            B[nm] = s.buf(nm)

    def S6():
        for ic in range(32):
            s.dma(ch_rd[ic % 2], lambda e, ic=ic: e.dma_start(
                out=st["WDQ"][:, 1024 * ic:1024 * (ic + 1)],
                in_=swdq_d[128 * ic:128 * (ic + 1), :]),
                writes=[B["WDQ"]], tag=f"wdq{ic}")
        for cc in range(8):
            s.dma(ch_w[cc % 2], lambda e, cc=cc: e.dma_start(
                out=st["WPT"][:, 1024 * cc:1024 * (cc + 1)],
                in_=wpt_d[128 * cc:128 * (cc + 1), :]),
                writes=[B["WPT"]], tag=f"wptl{cc}")
        for ec in range(16):
            s.dma(ch_w[ec % 2], lambda e, ec=ec: e.dma_start(
                out=st["WFT"][:, 1024 * ec:1024 * (ec + 1)],
                in_=wft_d[128 * ec:128 * (ec + 1), :]),
                writes=[B["WFT"]], tag=f"wftl{ec}")
        tc = st["tmpc"]
        for i in range(NTT):
            par = i % 2
            sq, tt = i // 16, i % 16
            bhq, by2, bge, bxs, bo = (B[f"HQ6{par}"], B[f"Y2T{par}"],
                                      B[f"GET{par}"], B[f"XS{par}"], B[f"OUTS{par}"])
            s.dma(ch_in[par], lambda e, i=i, par=par: e.dma_start(
                out=st["XS"][:, 1024 * par:1024 * (par + 1)],
                in_=x_d[128 * i:128 * (i + 1), :]),
                writes=[bxs], tag=f"oxl{i}")
            for r in range(4):
                s.dma(ch_rd[2 + r % 2], lambda e, i=i, r=r, par=par: e.dma_start(
                    out=st["HQ6"][:, 4096 * par + 1024 * r:4096 * par + 1024 * (r + 1)],
                    in_=shq_d[i, r]),
                    writes=[bhq], tag=f"ohl{i}_{r}")
            s.dma(ch_rd[i % 2], lambda e, i=i, par=par: e.dma_start(
                out=st["Y2T"][:, 1024 * par:1024 * (par + 1)],
                in_=sy2_d[:, i, :, :]),
                writes=[by2], tag=f"oyl{i}")
            s.dma(ch_sp[6 + i % 2], lambda e, sq=sq, tt=tt, par=par: e.dma_start(
                out=st["GET"][:, 2048 * par:2048 * (par + 1)],
                in_=sgel_d[sq, :, tt, :, :]),
                writes=[bge], tag=f"ogl{i}")
            if par == 0:
                ph = [("PE2", 0), ("PE2", 512)]
                bps = [B["PE2A"], B["PE2B"]]
            else:
                ph = [("PA", 0), ("PB", 0)]
                bps = [B["PA"], B["PB"]]

            def mmpair(lhs_ap, rhs_base, rhs_tensor, brd, bstat, start, stop, tag):
                for hh in range(2):
                    pn, poff = ph[hh]
                    s.op("tensor", lambda e, pn=pn, poff=poff, lhs_ap=lhs_ap,
                         rhs_base=rhs_base, rhs_tensor=rhs_tensor, hh=hh,
                         start=start, stop=stop:
                         e.matmul(ps[pn][:, poff:poff + 512], lhs_ap(),
                                  rhs_tensor[:, rhs_base + 512 * hh:rhs_base + 512 * (hh + 1)],
                                  start=start, stop=stop),
                         reads=[brd, bstat],
                         writes=[bps[hh]], tag=f"{tag}_{hh}")

            for hh in range(2):
                pn, poff = ph[hh]
                s.op("tensor", lambda e, pn=pn, poff=poff, hh=hh: e.matmul(
                    ps[pn][:, poff:poff + 512], st["onesrow"][0:1, :],
                    st["rowbuf"][0:1, 512 * hh:512 * (hh + 1)],
                    start=True, stop=False),
                    reads=[B["onesrow"], B["rowbuf"]], writes=[bps[hh]],
                    tag=f"obias{i}_{hh}")
            for cc in range(8):
                mmpair(lambda cc=cc, par=par: st["Y2T"][:, 1024 * par + 128 * cc:
                                                        1024 * par + 128 * (cc + 1)],
                       1024 * cc, st["WPT"], B["WPT"], by2, False, False, f"omp{i}_{cc}")
            for ec in range(16):
                mmpair(lambda ec=ec, par=par: st["GET"][:, 2048 * par + 128 * ec:
                                                        2048 * par + 128 * (ec + 1)],
                       1024 * ec, st["WFT"], B["WFT"], bge, False, False, f"omf{i}_{ec}")
            for ic in range(32):
                mmpair(lambda ic=ic, par=par: st["HQ6"][:, 4096 * par + 128 * ic:
                                                         4096 * par + 128 * (ic + 1)],
                       1024 * ic, st["WDQ"], B["WDQ"], bhq, False, ic == 31, f"omd{i}_{ic}")
            s.op("vector", lambda e, i=i: e.tensor_scalar(
                tc[:, 8:9], st["stats"][:, NTT + i:NTT + i + 1], 1.0, None, ALU.add),
                reads=[B["stats"], B["tmpc"]], writes=[B["tmpc"]], tag=f"oc1{i}")
            s.op("vector", lambda e, i=i: e.tensor_mul(
                tc[:, 9:10], st["stats"][:, i:i + 1], st["stats"][:, NTT + i:NTT + i + 1]),
                reads=[B["stats"], B["tmpc"]], writes=[B["tmpc"]], tag=f"oc2{i}")
            s.op("vector", lambda e, par=par: e.tensor_scalar(
                st["OUTS"][:, 1024 * par:1024 * (par + 1)],
                st["XS"][:, 1024 * par:1024 * (par + 1)],
                tc[:, 8:9], tc[:, 9:10], ALU.mult, ALU.subtract),
                reads=[bxs, B["tmpc"]], writes=[bo], tag=f"oc3{i}")
            for hh in range(2):
                pn, poff = ph[hh]
                s.op("vector", lambda e, pn=pn, poff=poff, par=par, hh=hh: e.tensor_add(
                    st["OUTS"][:, 1024 * par + 512 * hh:1024 * par + 512 * (hh + 1)],
                    st["OUTS"][:, 1024 * par + 512 * hh:1024 * par + 512 * (hh + 1)],
                    ps[pn][:, poff:poff + 512]),
                    reads=[bps[hh], bo], writes=[bo], tag=f"oadd{i}_{hh}")
            s.dma(ch_out[par], lambda e, i=i, par=par: e.dma_start(
                out=out_d[128 * i:128 * (i + 1), :],
                in_=st["OUTS"][:, 1024 * par:1024 * (par + 1)]),
                reads=[bo], tag=f"oout{i}")

    ORDER = ["S0", "S1a", "S1b", "S2", "S4", "S5", "S6"]
    stage_n = ORDER.index(upto)
    S0_consts()
    S0_wquant_late = stage_n >= 2
    if not S0_wquant_late:
        S0_wquant()
    s.barrier("bar_S0")
    carve_S1()
    if stage_n == 1:
        S1a()
    if stage_n >= 2:
        wqg = S0_wquant_gen()
        s1ag = S1a_gen()
        S1b2(wq=wqg, s1a=s1ag)
        for _ in s1ag:
            pass
        for _ in wqg:
            pass
    if stage_n >= 3:
        s.barrier("bar_S2")
        carve_S2()
        S2(0)
        gn0 = GN_gen(0)
        S2(1, gn_prev=gn0)
        for _ in gn0:
            pass
        gn1 = GN_gen(1)
        if stage_n == 3:
            for _ in gn1:
                pass
    if stage_n >= 4:
        s.barrier("bar_S4")
        carve_S4()
        S4(gn=gn1)
        for _ in gn1:
            pass
    if stage_n >= 5:
        s.barrier("bar_S5")
        carve_S5()
        S5()
    if stage_n >= 6:
        s.barrier("bar_S6")
        carve_S6()
        S6()

    if debug:
        s.barrier("bar_dbg")
        bdb = s.buf("dbg")
        for k, name in enumerate(debug):
            src = dr[name]
            shp = list(src.tensor.shape)
            dt = src.tensor.dtype
            o = nc.dram_tensor("dbg_" + name, shp, dt, kind="ExternalOutput").ap()
            n = int(np.prod(shp))
            s.dma(ch_out[k % 2], lambda e, src=src, o=o, n=n: e.dma_start(
                out=bass.AP(o.tensor, 0, [[n, 1], [1, n]]),
                in_=bass.AP(src.tensor, 0, [[n, 1], [1, n]])),
                reads=[bdb], tag=f"dbg_{name}")

    if stage_n < 6:
        s.barrier("bar_out")
        bz = s.buf("zout")
        s.op("vector", lambda e: e.memset(st["rmsb"][:], 0.0),
             reads=[B["rmsb"]], writes=[bz], tag="zz")
        for i in range(NTT):
            s.dma(ch_out[i % 2], lambda e, i=i: e.dma_start(
                out=bass.AP(out_d.tensor, 128 * i * C, [[C, 128], [1, C]]),
                in_=st["rmsb"][:]),
                reads=[bz], tag=f"zo{i}")

    s.barrier("bar_final")

    with ExitStack() as stack:
        e = stack.enter_context
        sems = {}
        for eng in ENGINES:
            sems[("eng", eng)] = e(nc.semaphore(f"sem_{eng}"))
        for c in s.chans:
            sems[("chan", c.name)] = e(nc.semaphore(f"semc_{c.name}"))
        block = e(nc.Block())
        s.emit(block, sems)
    return nc


def _prep(inputs):
    consts = _host_consts()
    tiled = {
        "w1t": _tile_oc(np.ascontiguousarray(
            np.asarray(inputs["L0_attn_w_qkvff"], dtype=np.float32).T), 40, 8),
        "wgt": _tile_oc(np.ascontiguousarray(
            np.asarray(inputs["L0_attn_w_gated"], dtype=np.float32).T), 8, 8),
        "wpt": np.ascontiguousarray(np.asarray(inputs["L0_attn_w_proj"], dtype=np.float32).T),
        "wft": np.ascontiguousarray(np.asarray(inputs["L0_attn_w_ff"], dtype=np.float32).T),
        "wg": np.ascontiguousarray(np.asarray(inputs["L0_ffn_w_gate"], dtype=np.float32).T),
        "wu": np.ascontiguousarray(np.asarray(inputs["L0_ffn_w_up"], dtype=np.float32).T),
        "wd": np.ascontiguousarray(np.asarray(inputs["L0_ffn_w_down"], dtype=np.float32).T),
        "b_gated": np.asarray(inputs["L0_attn_b_gated"], dtype=np.float32).reshape(1, C),
        "b_proj": np.asarray(inputs["L0_attn_b_proj"], dtype=np.float32).reshape(1, C),
        "gn_w": np.asarray(inputs["L0_attn_gn_w"], dtype=np.float32).reshape(1, C),
        "gn_b": np.asarray(inputs["L0_attn_gn_b"], dtype=np.float32).reshape(1, C),
        "rms_w": np.asarray(inputs["L0_attn_rms_w"], dtype=np.float32).reshape(1, C),
    }
    tiled.update(_xpos_decay_tables())
    return consts, tiled


def kernel(**inputs):
    consts, tiled = _prep(inputs)
    nc = build_program()
    in_maps = []
    for core in range(8):
        m = {"x": np.ascontiguousarray(
            np.asarray(inputs["L0_attn_x"][2 * core:2 * core + 2],
                       dtype=np.float32).reshape(NT, C))}
        m.update(tiled)
        m.update(consts)
        in_maps.append(m)
    res = run_bass_kernel_spmd(nc, in_maps, core_ids=list(range(8)))
    out = np.concatenate([np.asarray(r["out"]).reshape(2, T, C)
                          for r in res.results], axis=0)
    return out.astype(np.float32)


# version 19
# speedup vs baseline: 1.0131x; 1.0131x over previous
import math
from contextlib import ExitStack

import numpy as np

import concourse.bass as bass
import concourse.mybir as mybir
from concourse.bass_utils import run_bass_kernel_spmd

F32 = mybir.dt.float32
BF16 = mybir.dt.bfloat16
AF = mybir.ActivationFunctionType
ALU = mybir.AluOpType
AX = mybir.AxisListType.X

SEQ, T, C, H, HD = 2, 2048, 1024, 16, 64
NT = SEQ * T
FF2, INNER = 2048, 4096
NTT = NT // 128
WIN = 512
C1 = float(1.5 * 2.0 ** 23)
CD = float(1.5 * 2.0 ** 52)
GN_EPS, LN_EPS = 1e-5, 1e-5
S1B_MODE = "full"
CARVE = 49152
CARVE_END = 229376

ENGINES = ("tensor", "vector", "scalar", "gpsimd", "sync")


class _Buf:
    __slots__ = ("name", "lw", "rd", "psum")

    def __init__(self, name, psum=False):
        self.name, self.lw, self.rd, self.psum = name, None, [], psum


class _Chan:
    __slots__ = ("name", "eng", "cum", "ops", "last", "waiters")

    def __init__(self, name, eng):
        assert eng in ("sync", "gpsimd")
        self.name, self.eng, self.cum, self.ops, self.last = name, eng, 0, [], None
        self.waiters = {}


class _Op:
    __slots__ = ("eng", "fn", "idx", "deps", "need_inc", "inc_val", "chan",
                 "cum_after", "waits", "tag")

    def __init__(self, eng, fn, idx, tag):
        self.eng, self.fn, self.idx, self.tag = eng, fn, idx, tag
        self.deps, self.need_inc, self.inc_val = [], False, None
        self.chan, self.cum_after, self.waits = None, None, None


class Sched:
    def __init__(self):
        self.ops, self.chans, self.last_on = [], [], {e: None for e in ENGINES}

    def buf(self, name="", psum=False):
        return _Buf(name, psum)

    def chan(self, name, eng):
        c = _Chan(name, eng)
        self.chans.append(c)
        return c

    def _record(self, op, reads, writes):
        deps = set()
        for b in reads:
            if b.lw is not None:
                deps.add(b.lw)
            if b.psum:
                for r in b.rd:
                    if r.eng != op.eng:
                        deps.add(r)
        for b in writes:
            if b.lw is not None:
                deps.add(b.lw)
            for r in b.rd:
                deps.add(r)
        op.deps = [d for d in deps if d is not op and not (
            d.eng == "tensor" and op.eng == "tensor"
            and d.chan is None and op.chan is None)]
        for d in op.deps:
            if d.chan is not None:
                d.chan.waiters[op.eng] = op
        for b in writes:
            b.lw, b.rd = op, []
        for b in reads:
            b.rd.append(op)
        self.ops.append(op)
        self.last_on[op.eng] = op
        return op

    def op(self, eng, fn, reads=(), writes=(), tag=""):
        assert eng in ("tensor", "vector", "scalar", "gpsimd")
        return self._record(_Op(eng, fn, len(self.ops), tag), reads, writes)

    def dma(self, chan, fn, reads=(), writes=(), tag=""):
        op = _Op(chan.eng, fn, len(self.ops), tag)
        op.chan = chan
        chan.cum += 16
        op.cum_after = chan.cum
        chan.ops.append(op)
        chan.last = op
        extra = [w for w in chan.waiters.values() if w is not op]
        ret = self._record(op, reads, writes)
        for w in extra:
            if w not in op.deps:
                op.deps.append(w)
        return ret

    def barrier(self, tag="barrier"):
        prods = [o for o in self.last_on.values() if o is not None]
        prods += [c.last for c in self.chans if c.last is not None]
        for eng in ENGINES:
            op = _Op(eng, None, len(self.ops), tag)
            op.deps = [p for p in prods if p is not op]
            self.ops.append(op)

    def emit(self, block, sems):
        for op in self.ops:
            for d in op.deps:
                d.need_inc = True
        inc_count = {e: 0 for e in ENGINES}
        for op in self.ops:
            if op.chan is None and op.need_inc:
                inc_count[op.eng] += 1
                op.inc_val = inc_count[op.eng]
        for op in self.ops:
            waits = {}
            for d in op.deps:
                if d.chan is not None:
                    key = ("chan", d.chan.name)
                    val = d.cum_after
                    for other in d.chan.ops:
                        if other.idx < op.idx and other.cum_after > val:
                            val = other.cum_after
                else:
                    key, val = ("eng", d.eng), d.inc_val
                waits[key] = max(waits.get(key, 0), val)
            op.waits = waits
        self._verify_schedulable()

        self._emit_streams(block, sems)

    def _verify_schedulable(self):
        queues = {e: [op for op in self.ops if op.eng == e] for e in ENGINES}
        heads = {e: 0 for e in ENGINES}
        esem = {e: 0 for e in ENGINES}
        csem = {c.name: 0 for c in self.chans}

        def ok(op):
            for (kind, nm), val in op.waits.items():
                if kind == "eng" and esem[nm] < val:
                    return False
                if kind == "chan" and csem[nm] < val:
                    return False
            return True

        progress = True
        while progress:
            progress = False
            for e in ENGINES:
                q = queues[e]
                while heads[e] < len(q):
                    op = q[heads[e]]
                    if not ok(op):
                        break
                    if op.chan is not None:
                        csem[op.chan.name] += 16
                    elif op.inc_val is not None:
                        esem[e] = op.inc_val
                    heads[e] += 1
                    progress = True
        stuck = [(e, queues[e][heads[e]]) for e in ENGINES if heads[e] < len(queues[e])]
        if stuck:
            msg = "; ".join(
                f"{e}:{op.tag}@{op.idx} waits {op.waits}" for e, op in stuck[:4])
            raise RuntimeError("schedule deadlock: " + msg)

    def _emit_streams(self, block, sems):
        def stream(engobj, ename):
            hw = {}
            for op in self.ops:
                if op.eng != ename:
                    continue
                for key, val in sorted(op.waits.items(), key=lambda kv: str(kv[0])):
                    if hw.get(key, 0) >= val:
                        continue
                    engobj.wait_ge(sems[key], val)
                    hw[key] = val
                if op.fn is not None:
                    inst = op.fn(engobj)
                    if op.chan is not None:
                        inst.then_inc(sems[("chan", op.chan.name)], 16)
                    elif op.need_inc:
                        inst.then_inc(sems[("eng", op.eng)], 1)
                elif op.need_inc:
                    raise RuntimeError(f"fn-less op {op.tag} cannot carry inc")

        block.tensor(lambda e: stream(e, "tensor"))
        block.vector(lambda e: stream(e, "vector"))
        block.scalar(lambda e: stream(e, "scalar"))
        block.gpsimd(lambda e: stream(e, "gpsimd"))
        block.sync(lambda e: stream(e, "sync"))


def _host_consts():
    c = {}
    c["identm"] = np.eye(128, dtype=np.float32)
    P = np.zeros((128, 128), dtype=np.float32)
    for i in range(64):
        P[2 * i, 2 * i + 1] = 1.0
        P[2 * i + 1, 2 * i] = 1.0
    c["pswapm"] = P
    tri = np.zeros((128, 4 * WIN), dtype=np.float32)
    for r in range(4):
        for j in range(4):
            if j > r:
                tri[:, WIN * r + 128 * j:WIN * r + 128 * (j + 1)] = 1.0
            elif j == r:
                tri[:, WIN * r + 128 * j:WIN * r + 128 * (j + 1)] = \
                    np.tril(np.ones((128, 128), dtype=np.float32)).T
    c["trim"] = tri * 0.125
    p = ((np.arange(T, dtype=np.float64) - 1024.0) / 512.0).astype(np.float32)
    c["pbcm"] = np.broadcast_to(p[None, :], (128, T)).copy()
    gam = (1.0 - 2.0 ** (-5.0 - np.arange(H, dtype=np.float64))).astype(np.float32)
    gcols = np.zeros((128, 8), dtype=np.float32)
    for cc in range(8):
        for pp in range(128):
            gcols[pp, cc] = gam[(128 * cc + pp) // 64]
    c["gcolsm"] = gcols
    c["hrampm"] = (np.arange(128, dtype=np.float32) // 2).reshape(128, 1)
    c["sgnm"] = np.where(np.arange(128) % 2 == 0, -1.0, 1.0
                         ).astype(np.float32).reshape(128, 1)
    c["onesm"] = np.ones((128, 1), dtype=np.float32)
    c["halfpim"] = np.full((128, 1), np.float32(np.pi / 2), dtype=np.float32)
    gl = np.zeros((128, 8 * 64), dtype=np.float32)
    for hp in range(8):
        gl[0:64, 64 * hp + 2 * hp] = 1.0 / 64.0
        gl[64:128, 64 * hp + 2 * hp + 1] = 1.0 / 64.0
    c["gnlm"] = gl
    return c


def _tile_oc(w_t, nblk_o, nblk_c):
    ci, co = w_t.shape
    assert ci == nblk_c * 128 and co == nblk_o * 128
    return np.ascontiguousarray(
        w_t.reshape(nblk_c, 128, nblk_o, 128).transpose(2, 1, 0, 3))



def _xpos_decay_tables():
    import jax
    cpu = jax.local_devices(backend="cpu")[0]
    import jax.numpy as jnp
    with jax.default_device(cpu):
        half = C // 2
        sv = (jnp.arange(0, C, 2, dtype=jnp.float32) + 0.4 * C) / (1.4 * C)
        mn = -(T) // 2
        power = jnp.arange(mn, T + mn, dtype=jnp.float32) / 512
        scale = sv[None, :] ** power[:, None]
        invf = 1.0 / 10000.0 ** (jnp.arange(half, dtype=jnp.float32) / half)
        sinus = jnp.arange(T, dtype=jnp.float32)[:, None] * invf[None, :]
        sin = jnp.sin(sinus)
        cos = jnp.cos(sinus)
        cos_q = jnp.repeat(cos * scale, 2, axis=-1)
        sin_q = jnp.repeat(sin * scale, 2, axis=-1)
        iscale = 1.0 / scale
        cos_k = jnp.repeat(cos * iscale, 2, axis=-1)
        sin_k = jnp.repeat(sin * iscale, 2, axis=-1)
        cos_q = np.asarray(cos_q, dtype=np.float32)
        sin_q = np.asarray(sin_q, dtype=np.float32)
        cos_k = np.asarray(cos_k, dtype=np.float32)
        sin_k = np.asarray(sin_k, dtype=np.float32)
    sgn = np.where(np.arange(C) % 2 == 0, -1.0, 1.0).astype(np.float32)
    gam = (1.0 - 2.0 ** (-5.0 - np.arange(H, dtype=np.float64)))
    gc = gam[np.arange(C) // 64]
    tpow = np.arange(T, dtype=np.float64)
    g_t = (gc[:, None] ** tpow[None, :])
    g_ti = (gc[:, None] ** (-tpow[None, :]))
    out = {
        "cqm": (cos_q.T.astype(np.float64) * g_t).astype(np.float32),
        "sqm": ((sin_q.T * sgn[:, None]).astype(np.float64) * g_t).astype(np.float32),
        "ckm": (cos_k.T.astype(np.float64) * g_ti).astype(np.float32),
        "skm": ((sin_k.T * sgn[:, None]).astype(np.float64) * g_ti).astype(np.float32),
    }
    return {k: np.ascontiguousarray(v) for k, v in out.items()}


def build_program(upto="S6", debug=()):
    nc = bass.Bass("TRN2", target_bir_lowering=False)
    s = Sched()
    dr, st, B = {}, {}, {}

    def din(name, shape, dtype=F32):
        dr[name] = nc.dram_tensor(name, list(shape), dtype, kind="ExternalInput").ap()
        return dr[name]

    def dsc(name, shape, dtype):
        dr[name] = nc.dram_tensor(name, list(shape), dtype, kind="Internal").ap()
        return dr[name]

    x_d = din("x", (NT, C))
    w1t_d = din("w1t", (40, 128, 8, 128))
    wgt_d = din("wgt", (8, 128, 8, 128))
    wpt_d = din("wpt", (C, C))
    wft_d = din("wft", (FF2, C))
    wg_d = din("wg", (C, INNER))
    wu_d = din("wu", (C, INNER))
    wd_d = din("wd", (INNER, C))
    bgat_d = din("b_gated", (1, C))
    bprj_d = din("b_proj", (1, C))
    gnw_d = din("gn_w", (1, C))
    gnb_d = din("gn_b", (1, C))
    rms_d = din("rms_w", (1, C))
    identm_d = din("identm", (128, 128))
    pswap_d = din("pswapm", (128, 128))
    tri_d = din("trim", (128, 4 * WIN))
    pbc_d = din("pbcm", (128, T))
    gcols_d = din("gcolsm", (128, 8))
    hramp_d = din("hrampm", (128, 1))
    sgn_d = din("sgnm", (128, 1))
    ones_d = din("onesm", (128, 1))
    halfpi_d = din("halfpim", (128, 1))
    gnl_d = din("gnlm", (128, 8 * 64))
    cqm_d = din("cqm", (C, T))
    sqm_d = din("sqm", (C, T))
    ckm_d = din("ckm", (C, T))
    skm_d = din("skm", (C, T))

    out_d = nc.dram_tensor("out", [NT, C], F32, kind="ExternalOutput").ap()

    scq_d = dsc("s_cq", (C, T), F32)
    ssq_d = dsc("s_sq", (C, T), F32)
    sck_d = dsc("s_ck", (C, T), F32)
    ssk_d = dsc("s_sk", (C, T), F32)
    sq_d = dsc("s_q", (SEQ, C, T), BF16)
    sk_d = dsc("s_k", (SEQ, C, T), BF16)
    sv_d = dsc("s_v", (SEQ, T, C), BF16)
    sgel_d = dsc("s_gelu", (SEQ, 128, 16, 16, 128), BF16)
    sxn_d = dsc("s_xn", (128, 8, NT), BF16)
    sxq_d = dsc("s_xq", (NTT, 128, 8 * 128), BF16)
    sy2a_d = dsc("s_y2a", (SEQ, 128, 8, T), BF16)
    sy2_d = dsc("s_y2", (128, NTT, 8, 128), BF16)
    shq_d = dsc("s_hq", (NTT, 4, 128, 8 * 128), BF16)
    swgq_d = dsc("s_wgq", (C, INNER), BF16)
    swuq_d = dsc("s_wuq", (C, INNER), BF16)
    swdq_d = dsc("s_wdq", (INNER, C), BF16)
    sdbg_d = dsc("s_dbgspill", (40, 128, NT), BF16)

    def sb(key, shape, dtype=F32):
        st[key] = nc.alloc_sbuf_tensor("sb_" + key, list(shape), dtype)
        B[key] = s.buf(key)
        return st[key]

    def sb_at(key, shape, dtype, off):
        nbytes = int(np.prod(shape[1:])) * mybir.dt.size(dtype)
        assert CARVE + off + nbytes <= CARVE_END, (key, off, nbytes)
        st[key] = nc.alloc_sbuf_tensor_at("cv_" + key, list(shape), dtype,
                                          offset=CARVE + off)
        B[key] = s.buf(key)
        return st[key]

    sb("identb", (128, 128), BF16)
    sb("identf", (128, 128), F32)
    sb("pswapb", (128, 128), BF16)
    sb("gcols", (128, 8))
    sb("hramp", (128, 1))
    sb("sgncol", (128, 1))
    sb("onescol", (128, 1))
    sb("halfpi", (128, 1))
    sb("vecs", (128, 5 * 8))
    sb("rowbuf", (1, 2 * C))
    sb("rmsb", (128, C))
    sb("ceff", (128, 4))
    sb("wsc", (128, 16))
    sb("stats", (128, 9 * NTT))
    sb("tmpc", (128, 64))
    sb("gnl", (128, 8 * 64))
    sb("tri", (128, 4 * WIN))
    sentinel = nc.alloc_sbuf_tensor("sb_sentinel", [128, 4], F32)
    assert nc.lookup_mloc(sentinel).addr + 16 <= CARVE, \
        f"persistent sbuf overflow: {nc.lookup_mloc(sentinel).addr}"

    ps = {}
    for nm, shape, dt in (("PA", (128, 512), F32), ("PB", (128, 512), F32),
                          ("PC", (128, 512), F32), ("PD", (128, 512), F32),
                          ("PE2", (128, 1024), F32), ("PT", (128, 1024), BF16),
                          ("PG", (128, 512), F32)):
        ps[nm] = nc.alloc_psum_tensor(nm, list(shape), dt)
        B[nm] = s.buf(nm, psum=True)
    B["PE2A"] = s.buf("PE2A", psum=True)
    B["PE2B"] = s.buf("PE2B", psum=True)

    def pap(pname):
        if pname == "PE2A":
            return ps["PE2"][:, 0:512]
        if pname == "PE2B":
            return ps["PE2"][:, 512:1024]
        return ps[pname][:]

    ch_in = [s.chan(f"in{i}", "sync") for i in range(2)]
    ch_w = [s.chan(f"w{i}", "gpsimd") for i in range(2)]
    ch_sp = [s.chan(f"sp{i}", "gpsimd") for i in range(8)]
    ch_rd = [s.chan(f"rd{i}", "sync") for i in range(4)]
    ch_out = [s.chan(f"out{i}", "sync") for i in range(2)]
    ch_cst = s.chan("cst", "sync")
    ch_cw = s.chan("cw", "gpsimd")

    def S0_consts():
        for key, d, chn in (("identb", identm_d, ch_cw), ("pswapb", pswap_d, ch_cw),
                            ("identf", identm_d, ch_cst),
                            ("tri", tri_d, ch_cst), ("gcols", gcols_d, ch_cst),
                            ("hramp", hramp_d, ch_cst), ("sgncol", sgn_d, ch_cst),
                            ("onescol", ones_d, ch_cst), ("gnl", gnl_d, ch_cst),
                            ("halfpi", halfpi_d, ch_cst)):
            s.dma(chn, lambda e, key=key, d=d: e.dma_start(out=st[key][:], in_=d),
                  writes=[B[key]], tag=f"ld_{key}")
        for k, d in enumerate((bgat_d, bprj_d, gnw_d, gnb_d, rms_d)):
            s.dma(ch_cst, lambda e, k=k, d=d: e.dma_start(
                out=st["vecs"][:, 8 * k:8 * (k + 1)],
                in_=bass.AP(d.tensor, 0, [[1, 128], [128, 8]]),
                allow_slow_non_contiguous=True),
                writes=[B["vecs"]], tag=f"veccol{k}")
        s.dma(ch_cst, lambda e: e.dma_start(out=st["rowbuf"][0:1, 0:C], in_=bprj_d),
              writes=[B["rowbuf"]], tag="bprow")
        s.dma(ch_cst, lambda e: e.dma_start(out=st["rowbuf"][0:1, C:2 * C], in_=rms_d),
              writes=[B["rowbuf"]], tag="rmsrow")

        s.op("vector", lambda e: e.tensor_scalar(
            st["ceff"][:, 0:1], st["onescol"][:], 2.0 ** -30, 1.0, ALU.add, ALU.subtract),
            reads=[B["onescol"]], writes=[B["ceff"]], tag="ind")
        s.op("vector", lambda e: e.tensor_scalar(
            st["ceff"][:, 1:2], st["ceff"][:, 0:1], 2.0 ** 30 * CD, C1, ALU.mult, ALU.add),
            reads=[B["ceff"]], writes=[B["ceff"]], tag="ceff")

        s.op("vector", lambda e: e.memset(st["tmpc"][0:1, 0:16], 1.0),
             writes=[B["tmpc"]], tag="onesrow16")
        onesrow = sb("onesrow", (1, 128))
        s.op("vector", lambda e: e.memset(st["onesrow"][:], 1.0),
             writes=[B["onesrow"]], tag="onesrow")
        for half in range(2):
            s.op("tensor", lambda e, half=half: e.matmul(
                ps["PE2"][:, 512 * half:512 * (half + 1)], st["onesrow"][0:1, :],
                st["rowbuf"][0:1, C + 512 * half:C + 512 * (half + 1)],
                start=True, stop=True),
                reads=[B["onesrow"], B["rowbuf"]],
                writes=[B["PE2A"] if half == 0 else B["PE2B"]], tag=f"rmsmm{half}")
        s.op("vector", lambda e: e.tensor_copy(st["rmsb"][:], ps["PE2"][:, 0:C]),
             reads=[B["PE2A"], B["PE2B"]], writes=[B["rmsb"]], tag="rmsbc")

    def S0_wquant_gen():
        sb_at("wload0", (128, 4096), F32, 132 * 1024)
        sb_at("wload1", (128, 4096), F32, 148 * 1024)
        sb_at("wqout0", (128, 4096), BF16, 164 * 1024)
        acc = st["wsc"]
        for wname, wsrc, wdst, nchunk, wcols, wi in (
                ("wg", wg_d, swgq_d, 8, INNER, 0),
                ("wu", wu_d, swuq_d, 8, INNER, 1),
                ("wd", wd_d, swdq_d, 32, C, 2)):
            for k in range(nchunk):
                par = k % 2
                bw = B[f"wload{par}"]
                s.dma(ch_rd[par], lambda e, k=k, par=par, wsrc=wsrc, wcols=wcols:
                      e.dma_start(out=st[f"wload{par}"][:, 0:wcols],
                                  in_=wsrc[128 * k:128 * (k + 1), :]),
                      writes=[bw], tag=f"{wname}ld{k}")
                s.op("vector", lambda e, par=par, wcols=wcols: e.reduce_sum(
                    acc[:, 8:9], st[f"wload{par}"][:, 0:wcols], AX,
                    apply_absolute_value=True),
                    reads=[bw, B["wsc"]], writes=[B["wsc"]], tag=f"{wname}abs{k}")
                if k == 0:
                    s.op("vector", lambda e: e.tensor_copy(acc[:, 9:10], acc[:, 8:9]),
                         reads=[B["wsc"]], writes=[B["wsc"]], tag=f"{wname}a0")
                else:
                    s.op("vector", lambda e: e.tensor_add(acc[:, 9:10], acc[:, 9:10], acc[:, 8:9]),
                         reads=[B["wsc"]], writes=[B["wsc"]], tag=f"{wname}a{k}")
                yield
            s.op("tensor", lambda e: e.matmul(
                ps["PE2"][0:1, 0:1], st["onescol"][:], acc[:, 9:10], start=True, stop=True),
                reads=[B["onescol"], B["wsc"]], writes=[B["PE2A"]], tag=f"{wname}tot")
            ncount = float(nchunk * 128 * wcols)
            s.op("vector", lambda e, ncount=ncount: e.tensor_scalar(
                acc[0:1, 10:11], ps["PE2"][0:1, 0:1], 1.0 / ncount, 1e-5, ALU.mult, ALU.max),
                reads=[B["PE2A"]], writes=[B["wsc"]], tag=f"{wname}mean")
            s.op("tensor", lambda e: e.matmul(
                ps["PE2"][:, 4:5], st["onesrow"][0:1, :], acc[0:1, 10:11],
                start=True, stop=True),
                reads=[B["wsc"], B["onesrow"]], writes=[B["PE2A"]], tag=f"{wname}bc")
            s.op("vector", lambda e, wi=wi: e.tensor_copy(
                st["wsc"][:, wi:wi + 1], ps["PE2"][:, 4:5]),
                reads=[B["PE2A"]], writes=[B["wsc"]], tag=f"{wname}wsc")
            s.op("scalar", lambda e, wi=wi: e.activation(
                st["wsc"][:, 4 + wi:5 + wi], st["wsc"][:, wi:wi + 1], AF.Ln),
                reads=[B["wsc"]], writes=[B["wsc"]], tag=f"{wname}ln")
            s.op("scalar", lambda e, wi=wi: e.activation(
                st["wsc"][:, 4 + wi:5 + wi], st["wsc"][:, 4 + wi:5 + wi], AF.Exp, scale=-1.0),
                reads=[B["wsc"]], writes=[B["wsc"]], tag=f"{wname}exp")
            for k in range(nchunk):
                par = k % 2
                bw, bq = B[f"wload{par}"], B["wqout0"]
                s.dma(ch_rd[2 + par], lambda e, k=k, par=par, wsrc=wsrc, wcols=wcols:
                      e.dma_start(out=st[f"wload{par}"][:, 0:wcols],
                                  in_=wsrc[128 * k:128 * (k + 1), :]),
                      writes=[bw], tag=f"{wname}l2_{k}")
                s.op("scalar", lambda e, par=par, wcols=wcols, wi=wi: e.activation(
                    st[f"wload{par}"][:, 0:wcols], st[f"wload{par}"][:, 0:wcols],
                    AF.Copy, scale=st["wsc"][:, 4 + wi:5 + wi]),
                    reads=[bw, B["wsc"]], writes=[bw], tag=f"{wname}sc{k}")
                s.op("vector", lambda e, par=par, wcols=wcols: e.tensor_scalar(
                    st[f"wload{par}"][:, 0:wcols], st[f"wload{par}"][:, 0:wcols],
                    st["ceff"][:, 1:2], st["ceff"][:, 1:2], ALU.add, ALU.subtract),
                    reads=[bw, B["ceff"]], writes=[bw], tag=f"{wname}ri{k}")
                s.op("vector", lambda e, par=par, wcols=wcols: e.tensor_scalar(
                    st["wqout0"][:, 0:wcols], st[f"wload{par}"][:, 0:wcols],
                    -1.0, 1.0, ALU.max, ALU.min),
                    reads=[bw], writes=[bq], tag=f"{wname}cl{k}")
                s.dma(ch_sp[4 + par], lambda e, k=k, par=par, wdst=wdst, wcols=wcols:
                      e.dma_start(out=wdst[128 * k:128 * (k + 1), :],
                                  in_=st["wqout0"][:, 0:wcols]),
                      reads=[bq], tag=f"{wname}st{k}")
                yield

    def S0_wquant():
        for _ in S0_wquant_gen():
            pass

    def S0b():
        sb_at("pbc", (128, T), F32, 56 * 1024)
        for nm, off in (("tbC", 64), ("tbS", 72), ("tbU", 80), ("tbA", 88),
                        ("tbB", 96), ("tbG", 104), ("tbH", 112)):
            sb_at(nm, (128, T), F32, off * 1024)
        sb_at("tcol", (128, 16), F32, 120 * 1024)
        s.dma(ch_cst, lambda e: e.dma_start(out=st["pbc"][:], in_=pbc_d),
              writes=[B["pbc"]], tag="ld_pbc")
        LNIF = -math.log(10000.0) / 512.0
        tc = st["tcol"]
        bcol = B["tcol"]
        for cc in range(8):
            bC, bS, bU, bA, bB_, bG, bH = (B["tbC"], B["tbS"], B["tbU"], B["tbA"],
                                           B["tbB"], B["tbG"], B["tbH"])
            s.op("vector", lambda e, cc=cc: e.tensor_scalar_add(
                tc[:, 0:1], st["hramp"][:], float(64 * cc)),
                reads=[B["hramp"]], writes=[bcol], tag=f"t{cc}jc")
            s.op("scalar", lambda e: e.activation(tc[:, 1:2], tc[:, 0:1], AF.Exp, scale=LNIF),
                 reads=[bcol], writes=[bcol], tag=f"t{cc}f")
            s.op("vector", lambda e: e.tensor_scalar(
                tc[:, 2:3], tc[:, 0:1], 2.0 / 1433.6, 409.6 / 1433.6, ALU.mult, ALU.add),
                reads=[bcol], writes=[bcol], tag=f"t{cc}sv")
            s.op("scalar", lambda e: e.activation(tc[:, 3:4], tc[:, 2:3], AF.Ln),
                 reads=[bcol], writes=[bcol], tag=f"t{cc}L")
            s.op("vector", lambda e: e.tensor_scalar(
                st["tbU"][:], st["pbc"][:], tc[:, 3:4], None, ALU.mult),
                reads=[bcol, B["pbc"]], writes=[bU], tag=f"t{cc}U")
            s.op("scalar", lambda e: e.activation(st["tbA"][:], st["tbU"][:], AF.Exp),
                 reads=[bU], writes=[bA], tag=f"t{cc}scq")
            s.op("scalar", lambda e: e.activation(st["tbB"][:], st["tbU"][:], AF.Exp, scale=-1.0),
                 reads=[bU], writes=[bB_], tag=f"t{cc}sck")
            s.op("vector", lambda e: e.memset(st["tbC"][:, 0:1], 1.0),
                 writes=[bC], tag=f"t{cc}c0")
            s.op("vector", lambda e: e.memset(st["tbS"][:, 0:1], 0.0),
                 writes=[bS], tag=f"t{cc}s0")
            s.op("vector", lambda e: e.tensor_scalar(
                tc[:, 12:13], tc[:, 1:2], -1.0, 1.5707963705062866, ALU.mult, ALU.add),
                reads=[bcol], writes=[bcol], tag=f"t{cc}ch1")
            s.op("vector", lambda e: e.tensor_scalar_add(
                tc[:, 12:13], tc[:, 12:13], -4.371138828673793e-08),
                reads=[bcol], writes=[bcol], tag=f"t{cc}ch2")
            s.op("scalar", lambda e: e.activation(st["tbC"][:, 1:2], tc[:, 12:13], AF.Sin),
                 reads=[bcol], writes=[bC], tag=f"t{cc}c1")
            s.op("scalar", lambda e: e.activation(st["tbS"][:, 1:2], tc[:, 1:2], AF.Sin),
                 reads=[bcol], writes=[bS], tag=f"t{cc}s1")
            s.op("vector", lambda e: e.tensor_copy(tc[:, 4:5], st["tbC"][:, 1:2]),
                 reads=[bC], writes=[bcol], tag=f"t{cc}r0")
            s.op("vector", lambda e: e.tensor_copy(tc[:, 5:6], st["tbS"][:, 1:2]),
                 reads=[bS], writes=[bcol], tag=f"t{cc}r1")
            w = 2
            while w < T:
                s.op("vector", lambda e: e.tensor_mul(tc[:, 6:7], tc[:, 4:5], tc[:, 4:5]),
                     reads=[bcol], writes=[bcol], tag=f"t{cc}ra{w}")
                s.op("vector", lambda e: e.tensor_mul(tc[:, 7:8], tc[:, 5:6], tc[:, 5:6]),
                     reads=[bcol], writes=[bcol], tag=f"t{cc}rb{w}")
                s.op("vector", lambda e: e.tensor_mul(tc[:, 8:9], tc[:, 4:5], tc[:, 5:6]),
                     reads=[bcol], writes=[bcol], tag=f"t{cc}rc{w}")
                s.op("vector", lambda e: e.tensor_sub(tc[:, 4:5], tc[:, 6:7], tc[:, 7:8]),
                     reads=[bcol], writes=[bcol], tag=f"t{cc}rd{w}")
                s.op("vector", lambda e: e.tensor_scalar_mul(tc[:, 5:6], tc[:, 8:9], 2.0),
                     reads=[bcol], writes=[bcol], tag=f"t{cc}re{w}")
                s.op("vector", lambda e, w=w: e.tensor_scalar(
                    st["tbH"][:, 0:w], st["tbS"][:, 0:w], tc[:, 5:6], None, ALU.mult),
                    reads=[bS, bcol], writes=[bH], tag=f"t{cc}da{w}")
                s.op("vector", lambda e, w=w: e.scalar_tensor_tensor(
                    st["tbC"][:, w:2 * w], st["tbC"][:, 0:w], tc[:, 4:5],
                    st["tbH"][:, 0:w], ALU.mult, ALU.subtract),
                    reads=[bC, bcol, bH], writes=[bC], tag=f"t{cc}db{w}")
                s.op("vector", lambda e, w=w: e.tensor_scalar(
                    st["tbH"][:, 0:w], st["tbC"][:, 0:w], tc[:, 5:6], None, ALU.mult),
                    reads=[bC, bcol], writes=[bH], tag=f"t{cc}dc{w}")
                s.op("vector", lambda e, w=w: e.scalar_tensor_tensor(
                    st["tbS"][:, w:2 * w], st["tbS"][:, 0:w], tc[:, 4:5],
                    st["tbH"][:, 0:w], ALU.mult, ALU.add),
                    reads=[bS, bcol, bH], writes=[bS], tag=f"t{cc}dd{w}")
                w *= 2
            s.op("vector", lambda e, cc=cc: e.tensor_copy(
                tc[:, 9:10], st["gcols"][:, cc:cc + 1]),
                reads=[B["gcols"]], writes=[bcol], tag=f"t{cc}g")
            s.op("vector", lambda e: e.reciprocal(tc[:, 10:11], tc[:, 9:10]),
                 reads=[bcol], writes=[bcol], tag=f"t{cc}gi")
            for dst, bD, col in ((st["tbG"], bG, 9), (st["tbH"], bH, 10)):
                s.op("vector", lambda e, dst=dst: e.memset(dst[:, 0:1], 1.0),
                     writes=[bD], tag=f"t{cc}g0_{col}")
                s.op("vector", lambda e, dst=dst, col=col: e.tensor_copy(
                    dst[:, 1:2], tc[:, col:col + 1]),
                    reads=[bcol], writes=[bD], tag=f"t{cc}g1_{col}")
                s.op("vector", lambda e, col=col: e.tensor_copy(
                    tc[:, 11:12], tc[:, col:col + 1]),
                    reads=[bcol], writes=[bcol], tag=f"t{cc}gr_{col}")
                w = 2
                while w < T:
                    s.op("vector", lambda e: e.tensor_mul(
                        tc[:, 11:12], tc[:, 11:12], tc[:, 11:12]),
                        reads=[bcol], writes=[bcol], tag=f"t{cc}gs{w}_{col}")
                    s.op("vector", lambda e, w=w, dst=dst: e.tensor_scalar(
                        dst[:, w:2 * w], dst[:, 0:w], tc[:, 11:12], None, ALU.mult),
                        reads=[bD, bcol], writes=[bD], tag=f"t{cc}gm{w}_{col}")
                    w *= 2
            s.op("vector", lambda e: e.tensor_mul(st["tbA"][:], st["tbA"][:], st["tbG"][:]),
                 reads=[bA, bG], writes=[bA], tag=f"t{cc}aq")
            s.op("vector", lambda e: e.tensor_mul(st["tbB"][:], st["tbB"][:], st["tbH"][:]),
                 reads=[bB_, bH], writes=[bB_], tag=f"t{cc}ak")
            for ji, (dst_d, srcCS, bsrc, srcA, bamp, use_sgn) in enumerate((
                    (scq_d, st["tbC"], bC, st["tbA"], bA, False),
                    (ssq_d, st["tbS"], bS, st["tbA"], bA, True),
                    (sck_d, st["tbC"], bC, st["tbB"], bB_, False),
                    (ssk_d, st["tbS"], bS, st["tbB"], bB_, True))):
                s.op("vector", lambda e, srcCS=srcCS, srcA=srcA: e.tensor_mul(
                    st["tbU"][:], srcCS[:], srcA[:]),
                    reads=[bsrc, bamp], writes=[bU], tag=f"t{cc}f{ji}")
                if use_sgn:
                    s.op("vector", lambda e: e.tensor_scalar(
                        st["tbU"][:], st["tbU"][:], st["sgncol"][:], None, ALU.mult),
                        reads=[bU, B["sgncol"]], writes=[bU], tag=f"t{cc}fs{ji}")
                s.dma(ch_sp[ji % 2], lambda e, dst_d=dst_d, cc=cc: e.dma_start(
                    out=dst_d[128 * cc:128 * (cc + 1), :], in_=st["tbU"][:]),
                    reads=[bU], tag=f"t{cc}st{ji}")

    def carve_S1():
        sb_at("XHFM", (128, 8 * NT), BF16, 0)
        for g in range(8):
            B[f"XHFM{g}"] = s.buf(f"XHFM{g}")
        sb_at("xt", (128, 2 * C), F32, 64 * 1024)
        sb_at("xnf", (128, 2 * C), F32, 72 * 1024)
        sb_at("tmp1", (128, 2 * C), F32, 80 * 1024)
        sb_at("xps", (128, 2 * WIN), F32, 88 * 1024)
        sb_at("xhb", (128, 2 * C), BF16, 92 * 1024)
        sb_at("xnb", (128, 2 * C), BF16, 96 * 1024)
        sb_at("xqb", (128, 2 * C), BF16, 100 * 1024)
        sb_at("stg", (128, 2 * 2 * C), BF16, 104 * 1024)
        sb_at("wbuf", (128, 2 * 1024), BF16, 112 * 1024)
        sb_at("tabs", (128, 2 * 2 * WIN), F32, 116 * 1024)
        sb_at("qkst", (128, 2 * 2 * WIN), BF16, 124 * 1024)
        sb_at("xsq", (128, C), F32, 128 * 1024)
        for nm in ("xt", "xnf", "tmp1", "xps", "xhb", "xnb", "xqb", "stg",
                   "qkst", "tabs", "wbuf"):
            B[nm + "0"] = s.buf(nm + "0")
            B[nm + "1"] = s.buf(nm + "1")

    def S1a():
        for _ in S1a_gen():
            pass

    def S1a_gen():
        xhfm = st["XHFM"]
        cS = st["stats"]
        tc = st["tmpc"]
        for fam in ("stSUM", "stSSQ", "stXMX", "stXMN"):
            if fam not in B:
                B[fam] = s.buf(fam)
        G_ORDER = [g for gp in zip(range(4), range(4, 8)) for g in gp]
        SUM, SSQ, XMX, XMN = 5 * NTT, 6 * NTT, 7 * NTT, 8 * NTT
        for g in G_ORDER:
            tiles = list(range(4 * g, 4 * g + 4))
            for i in tiles:
                par = i % 2
                bxt = B[f"xt{par}"]
                s.dma(ch_in[par], lambda e, i=i, par=par: e.dma_start(
                    out=st["xt"][:, C * par:C * (par + 1)],
                    in_=x_d[128 * i:128 * (i + 1), :]),
                    writes=[bxt], tag=f"xld{i}")
                s.op("scalar", lambda e, par=par, i=i: e.activation(
                    st["xsq"][:], st["xt"][:, C * par:C * (par + 1)], AF.Copy,
                    accum_out=cS[:, SUM + i:SUM + i + 1]),
                    reads=[bxt], writes=[B["xsq"], B["stSUM"]], tag=f"rs{i}")
                s.op("scalar", lambda e, par=par, i=i: e.activation(
                    st["xsq"][:], st["xt"][:, C * par:C * (par + 1)], AF.Square,
                    accum_out=cS[:, SSQ + i:SSQ + i + 1]),
                    reads=[bxt], writes=[B["xsq"], B["stSSQ"]], tag=f"sq{i}")
                s.op("vector", lambda e, par=par, i=i: e.reduce_max(
                    cS[:, XMX + i:XMX + i + 1],
                    st["xt"][:, C * par:C * (par + 1)], AX),
                    reads=[bxt], writes=[B["stXMX"]], tag=f"mx{i}")
                s.op("vector", lambda e, par=par, i=i: e.tensor_reduce(
                    cS[:, XMN + i:XMN + i + 1],
                    st["xt"][:, C * par:C * (par + 1)], AX, ALU.min),
                    reads=[bxt], writes=[B["stXMN"]], tag=f"mn{i}")
            a, b = 4 * g, 4 * g + 4
            ch = [
                ("vector", [B["stSUM"]], lambda e, a=a, b=b: e.tensor_scalar(
                    cS[:, a:b], cS[:, SUM + a:SUM + b], 1.0 / C, None, ALU.mult), "m"),
                ("vector", [B["stSSQ"]], lambda e, a=a, b=b: e.tensor_scalar(
                    tc[:, 16:20], cS[:, SSQ + a:SSQ + b], 1.0 / C, None, ALU.mult), "e2"),
                ("vector", [], lambda e, a=a, b=b: e.tensor_mul(
                    tc[:, 20:24], cS[:, a:b], cS[:, a:b]), "mm"),
                ("vector", [], lambda e: e.tensor_sub(
                    tc[:, 24:28], tc[:, 16:20], tc[:, 20:24]), "v"),
                ("vector", [], lambda e: e.tensor_scalar_add(
                    tc[:, 28:32], tc[:, 24:28], LN_EPS), "u"),
                ("scalar", [], lambda e: e.activation(
                    tc[:, 32:36], tc[:, 28:32], AF.Sqrt), "sqr"),
                ("vector", [], lambda e: e.reciprocal(tc[:, 36:40], tc[:, 32:36]), "b0"),
                ("vector", [], lambda e: e.tensor_mul(
                    tc[:, 40:44], tc[:, 36:40], tc[:, 36:40]), "n1"),
                ("vector", [], lambda e: e.tensor_mul(
                    tc[:, 40:44], tc[:, 40:44], tc[:, 28:32]), "n2"),
                ("vector", [], lambda e: e.tensor_scalar(
                    tc[:, 40:44], tc[:, 40:44], -0.5, 1.5, ALU.mult, ALU.add), "n3"),
                ("vector", [], lambda e, a=a, b=b: e.tensor_mul(
                    cS[:, NTT + a:NTT + b], tc[:, 40:44], tc[:, 36:40]), "beta"),
                ("vector", [], lambda e: e.reciprocal(tc[:, 44:48], tc[:, 28:32]), "ru"),
                ("vector", [], lambda e: e.tensor_mul(
                    tc[:, 44:48], tc[:, 44:48], tc[:, 24:28]), "w"),
                ("scalar", [], lambda e: e.activation(
                    tc[:, 48:52], tc[:, 44:48], AF.Sqrt), "nr"),
                ("vector", [], lambda e: e.tensor_scalar_max(
                    tc[:, 48:52], tc[:, 48:52], 1e-8), "mxr"),
                ("vector", [], lambda e: e.reciprocal(tc[:, 52:56], tc[:, 48:52]), "rg"),
                ("vector", [], lambda e, a=a, b=b: e.tensor_mul(
                    cS[:, 2 * NTT + a:2 * NTT + b],
                    cS[:, NTT + a:NTT + b], tc[:, 52:56]), "al"),
                ("vector", [B["stXMX"]], lambda e, a=a, b=b: e.tensor_sub(
                    tc[:, 56:60], cS[:, XMX + a:XMX + b], cS[:, a:b]), "t1"),
                ("vector", [B["stXMN"]], lambda e, a=a, b=b: e.tensor_sub(
                    tc[:, 60:64], cS[:, a:b], cS[:, XMN + a:XMN + b]), "t2"),
                ("vector", [], lambda e: e.tensor_max(
                    tc[:, 56:60], tc[:, 56:60], tc[:, 60:64]), "t3"),
                ("vector", [], lambda e, a=a, b=b: e.tensor_mul(
                    tc[:, 56:60], tc[:, 56:60], cS[:, NTT + a:NTT + b]), "axn"),
                ("vector", [], lambda e: e.tensor_scalar_max(
                    tc[:, 56:60], tc[:, 56:60], 1e-5), "cl"),
                ("vector", [], lambda e: e.reciprocal(tc[:, 60:64], tc[:, 56:60]), "rc"),
                ("vector", [], lambda e, a=a, b=b: e.tensor_scalar(
                    cS[:, 3 * NTT + a:3 * NTT + b],
                    tc[:, 60:64], 127.0, None, ALU.mult), "sa"),
                ("vector", [], lambda e, a=a, b=b: e.tensor_scalar(
                    cS[:, 4 * NTT + a:4 * NTT + b],
                    tc[:, 56:60], 1.0 / 127.0, None, ALU.mult), "sar"),
            ]
            for eng, xreads, fn, nm in ch:
                s.op(eng, fn, reads=[B["stats"], B["tmpc"]] + xreads,
                     writes=[B["stats"], B["tmpc"]], tag=f"gc_{nm}{g}")
            for i in tiles:
                par = i % 2
                bxt, bxn, btm = B[f"xt{par}"], B[f"xnf{par}"], B[f"tmp1{par}"]
                bxh, bxq, bnb = B[f"xhb{par}"], B[f"xqb{par}"], B[f"xnb{par}"]
                s.dma(ch_in[par], lambda e, i=i, par=par: e.dma_start(
                    out=st["xt"][:, C * par:C * (par + 1)],
                    in_=x_d[128 * i:128 * (i + 1), :]),
                    writes=[bxt], tag=f"xl2_{i}")
                s.op("vector", lambda e, i=i, par=par: e.tensor_scalar(
                    st["xnf"][:, C * par:C * (par + 1)],
                    st["xt"][:, C * par:C * (par + 1)],
                    cS[:, i:i + 1], cS[:, NTT + i:NTT + i + 1],
                    ALU.subtract, ALU.mult),
                    reads=[bxt, B["stats"]], writes=[bxn], tag=f"xn{i}")
                s.op("vector", lambda e, i=i, par=par: e.tensor_scalar(
                    st["tmp1"][:, C * par:C * (par + 1)],
                    st["xt"][:, C * par:C * (par + 1)],
                    cS[:, i:i + 1], cS[:, 2 * NTT + i:2 * NTT + i + 1],
                    ALU.subtract, ALU.mult),
                    reads=[bxt, B["stats"]], writes=[btm], tag=f"xh{i}")
                s.op("vector", lambda e, par=par: e.tensor_mul(
                    st["xhb"][:, C * par:C * (par + 1)],
                    st["tmp1"][:, C * par:C * (par + 1)], st["rmsb"][:]),
                    reads=[btm, B["rmsb"]], writes=[bxh], tag=f"xhr{i}")
                s.op("vector", lambda e, i=i, par=par: e.tensor_scalar(
                    st["tmp1"][:, C * par:C * (par + 1)],
                    st["xnf"][:, C * par:C * (par + 1)],
                    cS[:, 3 * NTT + i:3 * NTT + i + 1], None, ALU.mult),
                    reads=[bxn, B["stats"]], writes=[btm], tag=f"xqs{i}")
                s.op("vector", lambda e, par=par: e.tensor_scalar(
                    st["xqb"][:, C * par:C * (par + 1)],
                    st["tmp1"][:, C * par:C * (par + 1)],
                    st["ceff"][:, 1:2], st["ceff"][:, 1:2], ALU.add, ALU.subtract),
                    reads=[btm, B["ceff"]], writes=[bxq], tag=f"xqr{i}")
                s.op("scalar", lambda e, par=par: e.activation(
                    st["xnb"][:, C * par:C * (par + 1)],
                    st["xnf"][:, C * par:C * (par + 1)], AF.Copy),
                    reads=[bxn], writes=[bnb], tag=f"xnb{i}")
                for kind, src2, bsrc in (("fm", st["xhb"], bxh), ("xn", st["xnb"], bnb),
                                         ("xq", st["xqb"], bxq)):
                    for cchunk in range(8):
                        s.op("tensor", lambda e, src2=src2, par=par, cchunk=cchunk:
                             e.transpose(ps["PT"][:, 128 * cchunk:128 * (cchunk + 1)],
                                         src2[:, C * par + 128 * cchunk:
                                              C * par + 128 * (cchunk + 1)],
                                         st["identb"][:]),
                             reads=[bsrc, B["identb"]], writes=[B["PT"]],
                             tag=f"tr_{kind}{i}_{cchunk}")
                    if kind == "fm":
                        s.op("vector", lambda e, i=i: e.tensor_copy(
                            bass.AP(xhfm, 128 * i, [[8 * NT, 128], [NT, 8], [1, 128]]),
                            bass.AP(ps["PT"], 0, [[1024, 128], [128, 8], [1, 128]])),
                            reads=[B["PT"]], writes=[B[f"XHFM{i // 4}"]],
                            tag=f"ev_{kind}{i}")
                    else:
                        koff = 0 if kind == "xn" else 1
                        reg = 2 * C * par + C * koff
                        bstg = B[f"stg{par}"]
                        s.op("scalar", lambda e, reg=reg: e.activation(
                            st["stg"][:, reg:reg + 1024], ps["PT"][:], AF.Copy),
                            reads=[B["PT"]], writes=[bstg], tag=f"sg_{kind}{i}")
                        if kind == "xq":
                            s.dma(ch_sp[2 + (i % 2)],
                                  lambda e, i=i, reg=reg: e.dma_start(
                                      out=sxq_d[i], in_=st["stg"][:, reg:reg + 1024]),
                                  reads=[bstg], tag=f"sp_xq{i}")
                        else:
                            s.dma(ch_sp[(i % 2)],
                                  lambda e, i=i, reg=reg: e.dma_start(
                                      out=sxn_d[:, :, 128 * i:128 * (i + 1)],
                                      in_=st["stg"][:, reg:reg + 1024]),
                                  reads=[bstg], tag=f"sp_xn{i}")
            yield

    def S1b2(wq=None, s1a=None):
        xhfm = st["XHFM"]
        o_order = []
        for cp in range(8):
            o_order += [cp, cp + 8]
        o_order += list(range(16, 40))
        prot = ["PA", "PB", "PC"]
        rrot = ["PD", "PG"]
        cnt = [0]
        for oi, o in enumerate(o_order):
            par = oi % 2
            bw = B[f"wbuf{par}"]
            s.dma(ch_w[par], lambda e, o=o, par=par: e.dma_start(
                out=st["wbuf"][:, 1024 * par:1024 * (par + 1)], in_=w1t_d[o]),
                writes=[bw], tag=f"wld{o}")
            is_q, is_k, is_v = o < 8, 8 <= o < 16, 16 <= o < 24
            cchunk = o % 8 if (o < 16) else None
            for w in range(4):
                if s1a is not None and oi == 0:
                    next(s1a, None)
                    next(s1a, None)
                tpar = w % 2
                btab = B[f"tabs{tpar}"]
                if is_q or is_k:
                    ctab_d = cqm_d if is_q else ckm_d
                    stab_d = sqm_d if is_q else skm_d
                    s.dma(ch_rd[tpar], lambda e, ctab_d=ctab_d, cchunk=cchunk, w=w, tpar=tpar:
                          e.dma_start(
                              out=st["tabs"][:, 2 * WIN * tpar:2 * WIN * tpar + WIN],
                              in_=ctab_d[128 * cchunk:128 * (cchunk + 1),
                                         WIN * w:WIN * (w + 1)]),
                          writes=[btab], tag=f"tc{o}_{w}")
                    s.dma(ch_rd[2 + tpar], lambda e, stab_d=stab_d, cchunk=cchunk, w=w, tpar=tpar:
                          e.dma_start(
                              out=st["tabs"][:, 2 * WIN * tpar + WIN:2 * WIN * (tpar + 1)],
                              in_=stab_d[128 * cchunk:128 * (cchunk + 1),
                                         WIN * w:WIN * (w + 1)]),
                          writes=[btab], tag=f"ts{o}_{w}")
                for sq in range(SEQ):
                    tw = T * sq + WIN * w
                    twin = WIN * w
                    pname = prot[cnt[0] % 3]
                    rname = rrot[cnt[0] % 2]
                    cnt[0] += 1
                    bp = B[pname]
                    for cci in range(8):
                        s.op("tensor", lambda e, pname=pname, par=par, cci=cci, tw=tw:
                             e.matmul(pap(pname),
                                      st["wbuf"][:, 1024 * par + 128 * cci:
                                                 1024 * par + 128 * (cci + 1)],
                                      bass.AP(xhfm, cci * NT + tw, [[8 * NT, 128], [1, WIN]]),
                                      start=(cci == 0), stop=(cci == 7)),
                             reads=[bw, B[f"XHFM{tw // WIN}"]], writes=[bp],
                             tag=f"mm{o}_{w}_{sq}_{cci}")
                    spar = (2 * w + sq) % 2
                    bqk = B[f"qkst{spar}"]
                    bxp = B[f"xps{spar}"]
                    if S1B_MODE == "mm_only":
                        s.op("scalar", lambda e, pname=pname, spar=spar: e.activation(
                            st["qkst"][:, 2 * WIN * spar:2 * WIN * spar + WIN],
                            ps[pname][:], AF.Copy),
                            reads=[bp], writes=[bqk], tag=f"dqb{o}_{w}_{sq}")
                        s.dma(ch_sp[4 + ((2 * w + sq) % 2)],
                              lambda e, o=o, tw=tw, spar=spar: e.dma_start(
                                  out=sdbg_d[o, :, tw:tw + WIN],
                                  in_=st["qkst"][:, 2 * WIN * spar:2 * WIN * spar + WIN]),
                              reads=[bqk], tag=f"dsp{o}_{w}_{sq}")
                        continue
                    if S1B_MODE == "rot" and (is_q or is_k):
                        s.op("scalar", lambda e, pname=pname, spar=spar: e.activation(
                            st["qkst"][:, 2 * WIN * spar:2 * WIN * spar + WIN],
                            ps[pname][:], AF.Copy),
                            reads=[bp], writes=[bqk], tag=f"dqb{o}_{w}_{sq}")
                        s.op("tensor", lambda e, spar=spar: e.matmul(
                            ps["PD"][:], st["pswapb"][:],
                            st["qkst"][:, 2 * WIN * spar:2 * WIN * spar + WIN],
                            start=True, stop=True),
                            reads=[bqk, B["pswapb"]], writes=[B["PD"]],
                            tag=f"drot{o}_{w}_{sq}")
                        s.op("vector", lambda e, spar=spar: e.tensor_copy(
                            st["qkst"][:, 2 * WIN * spar + WIN:2 * WIN * (spar + 1)],
                            bass.AP(ps["PD"], 0, [[512, 128], [1, WIN]])),
                            reads=[B["PD"]], writes=[bqk], tag=f"dre{o}_{w}_{sq}")
                        s.dma(ch_sp[4 + ((2 * w + sq) % 2)],
                              lambda e, o=o, tw=tw, spar=spar: e.dma_start(
                                  out=sdbg_d[o, :, tw:tw + WIN],
                                  in_=st["qkst"][:, 2 * WIN * spar + WIN:2 * WIN * (spar + 1)]),
                              reads=[bqk], tag=f"dsp{o}_{w}_{sq}")
                        continue
                    if S1B_MODE == "rot":
                        continue
                    if S1B_MODE in ("xpos", "xpos1", "xpos2", "xpos3", "xpos4") and not (is_q or is_k):
                        continue
                    if S1B_MODE == "xpos3" and (is_q or is_k):
                        s.op("vector", lambda e, pname=pname, tpar=tpar, spar=spar: e.tensor_mul(
                            st["xps"][:, WIN * spar:WIN * (spar + 1)], ps[pname][:],
                            st["tabs"][:, 2 * WIN * tpar:2 * WIN * tpar + WIN]),
                            reads=[bp, btab], writes=[bxp], tag=f"3xc{o}_{w}_{sq}")
                        btm = B[f"tmp1{spar}"]
                        s.op("vector", lambda e, pname=pname, tpar=tpar, spar=spar: e.tensor_mul(
                            st["tmp1"][:, C * spar:C * spar + WIN], ps[pname][:],
                            st["tabs"][:, 2 * WIN * tpar + WIN:2 * WIN * (tpar + 1)]),
                            reads=[bp, btab], writes=[btm], tag=f"3xs{o}_{w}_{sq}")
                        s.op("vector", lambda e, spar=spar: e.tensor_add(
                            st["qkst"][:, 2 * WIN * spar + WIN:2 * WIN * (spar + 1)],
                            st["xps"][:, WIN * spar:WIN * (spar + 1)],
                            st["tmp1"][:, C * spar:C * spar + WIN]),
                            reads=[bxp, btm], writes=[bqk], tag=f"3xa{o}_{w}_{sq}")
                        s.dma(ch_sp[4 + ((2 * w + sq) % 2)],
                              lambda e, o=o, tw=tw, spar=spar: e.dma_start(
                                  out=sdbg_d[o, :, tw:tw + WIN],
                                  in_=st["qkst"][:, 2 * WIN * spar + WIN:2 * WIN * (spar + 1)]),
                              reads=[bqk], tag=f"dsp{o}_{w}_{sq}")
                        continue
                    if S1B_MODE == "xpos4" and (is_q or is_k):
                        s.op("scalar", lambda e, pname=pname, spar=spar: e.activation(
                            st["qkst"][:, 2 * WIN * spar:2 * WIN * spar + WIN],
                            ps[pname][:], AF.Copy),
                            reads=[bp], writes=[bqk], tag=f"4qb{o}_{w}_{sq}")
                        s.op("tensor", lambda e, spar=spar: e.matmul(
                            ps["PD"][:], st["pswapb"][:],
                            st["qkst"][:, 2 * WIN * spar:2 * WIN * spar + WIN],
                            start=True, stop=True),
                            reads=[bqk, B["pswapb"]], writes=[B["PD"]],
                            tag=f"4rot{o}_{w}_{sq}")
                        btm = B[f"tmp1{spar}"]
                        s.op("vector", lambda e, tpar=tpar, spar=spar: e.tensor_mul(
                            st["tmp1"][:, C * spar:C * spar + WIN],
                            bass.AP(ps["PD"], 0, [[512, 128], [1, WIN]]),
                            st["tabs"][:, 2 * WIN * tpar + WIN:2 * WIN * (tpar + 1)]),
                            reads=[B["PD"], btab], writes=[btm], tag=f"4xs{o}_{w}_{sq}")
                        s.op("vector", lambda e, spar=spar: e.tensor_copy(
                            st["qkst"][:, 2 * WIN * spar + WIN:2 * WIN * (spar + 1)],
                            st["tmp1"][:, C * spar:C * spar + WIN]),
                            reads=[btm], writes=[bqk], tag=f"4cp{o}_{w}_{sq}")
                        s.dma(ch_sp[4 + ((2 * w + sq) % 2)],
                              lambda e, o=o, tw=tw, spar=spar: e.dma_start(
                                  out=sdbg_d[o, :, tw:tw + WIN],
                                  in_=st["qkst"][:, 2 * WIN * spar + WIN:2 * WIN * (spar + 1)]),
                              reads=[bqk], tag=f"dsp{o}_{w}_{sq}")
                        continue
                    if S1B_MODE == "xpos1" and (is_q or is_k):
                        s.op("vector", lambda e, pname=pname, tpar=tpar, spar=spar: e.tensor_mul(
                            st["xps"][:, WIN * spar:WIN * (spar + 1)], ps[pname][:],
                            st["tabs"][:, 2 * WIN * tpar:2 * WIN * tpar + WIN]),
                            reads=[bp, btab], writes=[bxp], tag=f"dxc{o}_{w}_{sq}")
                        s.op("scalar", lambda e, spar=spar: e.activation(
                            st["qkst"][:, 2 * WIN * spar:2 * WIN * spar + WIN],
                            st["xps"][:, WIN * spar:WIN * (spar + 1)], AF.Copy),
                            reads=[bxp], writes=[bqk], tag=f"dcp{o}_{w}_{sq}")
                        s.dma(ch_sp[4 + ((2 * w + sq) % 2)],
                              lambda e, o=o, tw=tw, spar=spar: e.dma_start(
                                  out=sdbg_d[o, :, tw:tw + WIN],
                                  in_=st["qkst"][:, 2 * WIN * spar:2 * WIN * spar + WIN]),
                              reads=[bqk], tag=f"dsp{o}_{w}_{sq}")
                        continue
                    if S1B_MODE == "xpos2" and (is_q or is_k):
                        s.op("scalar", lambda e, pname=pname, spar=spar: e.activation(
                            st["qkst"][:, 2 * WIN * spar:2 * WIN * spar + WIN],
                            ps[pname][:], AF.Copy),
                            reads=[bp], writes=[bqk], tag=f"qb{o}_{w}_{sq}")
                        s.op("tensor", lambda e, spar=spar, rname=rname: e.matmul(
                            ps[rname][:], st["pswapb"][:],
                            st["qkst"][:, 2 * WIN * spar:2 * WIN * spar + WIN],
                            start=True, stop=True),
                            reads=[bqk, B["pswapb"]], writes=[B[rname]],
                            tag=f"rot{o}_{w}_{sq}")
                        s.op("vector", lambda e, pname=pname, tpar=tpar, spar=spar: e.tensor_mul(
                            st["xps"][:, WIN * spar:WIN * (spar + 1)], ps[pname][:],
                            st["tabs"][:, 2 * WIN * tpar:2 * WIN * tpar + WIN]),
                            reads=[bp, btab], writes=[bxp], tag=f"xc{o}_{w}_{sq}")
                        btm = B[f"tmp1{spar}"]
                        s.op("vector", lambda e, tpar=tpar, spar=spar: e.tensor_mul(
                            st["tmp1"][:, C * spar:C * spar + WIN],
                            bass.AP(ps["PD"], 0, [[512, 128], [1, WIN]]),
                            st["tabs"][:, 2 * WIN * tpar + WIN:2 * WIN * (tpar + 1)]),
                            reads=[B["PD"], btab], writes=[btm], tag=f"xs{o}_{w}_{sq}")
                        s.op("vector", lambda e, spar=spar: e.tensor_add(
                            st["qkst"][:, 2 * WIN * spar + WIN:2 * WIN * (spar + 1)],
                            st["xps"][:, WIN * spar:WIN * (spar + 1)],
                            st["tmp1"][:, C * spar:C * spar + WIN]),
                            reads=[bxp, btm], writes=[bqk], tag=f"xa{o}_{w}_{sq}")
                        s.dma(ch_sp[4 + ((2 * w + sq) % 2)],
                              lambda e, o=o, tw=tw, spar=spar: e.dma_start(
                                  out=sdbg_d[o, :, tw:tw + WIN],
                                  in_=st["qkst"][:, 2 * WIN * spar + WIN:2 * WIN * (spar + 1)]),
                              reads=[bqk], tag=f"dsp{o}_{w}_{sq}")
                        continue
                    if S1B_MODE == "vff" and (is_q or is_k):
                        s.op("scalar", lambda e, pname=pname, spar=spar: e.activation(
                            st["qkst"][:, 2 * WIN * spar:2 * WIN * spar + WIN],
                            ps[pname][:], AF.Copy),
                            reads=[bp], writes=[bqk], tag=f"dqb{o}_{w}_{sq}")
                        s.dma(ch_sp[4 + ((2 * w + sq) % 2)],
                              lambda e, o=o, tw=tw, spar=spar: e.dma_start(
                                  out=sdbg_d[o, :, tw:tw + WIN],
                                  in_=st["qkst"][:, 2 * WIN * spar:2 * WIN * spar + WIN]),
                              reads=[bqk], tag=f"dsp{o}_{w}_{sq}")
                        continue
                    if is_q or is_k:
                        evac_op = s.op("scalar", lambda e, pname=pname, spar=spar: e.activation(
                            st["qkst"][:, 2 * WIN * spar:2 * WIN * spar + WIN],
                            pap(pname), AF.Copy),
                            reads=[bp], writes=[bqk], tag=f"qb{o}_{w}_{sq}")
                        s.op("tensor", lambda e, spar=spar, rname=rname: e.matmul(
                            ps[rname][:], st["pswapb"][:],
                            st["qkst"][:, 2 * WIN * spar:2 * WIN * spar + WIN],
                            start=True, stop=True),
                            reads=[bqk, B["pswapb"]], writes=[B[rname]],
                            tag=f"rot{o}_{w}_{sq}")
                        xc_op = s.op("vector", lambda e, pname=pname, tpar=tpar, spar=spar: e.tensor_mul(
                            st["xps"][:, WIN * spar:WIN * (spar + 1)], pap(pname),
                            st["tabs"][:, 2 * WIN * tpar:2 * WIN * tpar + WIN]),
                            reads=[bp, btab], writes=[bxp], tag=f"xc{o}_{w}_{sq}")

                        btm = B[f"tmp1{spar}"]
                        s.op("vector", lambda e, tpar=tpar, spar=spar, rname=rname: e.tensor_mul(
                            st["tmp1"][:, C * spar:C * spar + WIN], ps[rname][:],
                            st["tabs"][:, 2 * WIN * tpar + WIN:2 * WIN * (tpar + 1)]),
                            reads=[B[rname], btab], writes=[btm], tag=f"xs{o}_{w}_{sq}")
                        s.op("vector", lambda e, spar=spar: e.tensor_add(
                            st["qkst"][:, 2 * WIN * spar + WIN:2 * WIN * (spar + 1)],
                            st["xps"][:, WIN * spar:WIN * (spar + 1)],
                            st["tmp1"][:, C * spar:C * spar + WIN]),
                            reads=[bxp, btm], writes=[bqk], tag=f"xa{o}_{w}_{sq}")
                        dst_d = sq_d if is_q else sk_d
                        s.dma(ch_sp[4 + ((2 * w + sq) % 2)],
                              lambda e, dst_d=dst_d, sq=sq, cchunk=cchunk, twin=twin, spar=spar:
                              e.dma_start(
                                  out=dst_d[sq, 128 * cchunk:128 * (cchunk + 1),
                                            twin:twin + WIN],
                                  in_=st["qkst"][:, 2 * WIN * spar + WIN:
                                                 2 * WIN * (spar + 1)]),
                              reads=[bqk], tag=f"qko{o}_{w}_{sq}")
                    elif is_v:
                        s.op("scalar", lambda e, pname=pname, spar=spar: e.activation(
                            st["qkst"][:, 2 * WIN * spar:2 * WIN * spar + WIN],
                            pap(pname), AF.Copy),
                            reads=[bp], writes=[bqk], tag=f"vb{o}_{w}_{sq}")
                        for j in range(4):
                            s.op("tensor", lambda e, j=j, spar=spar: e.transpose(
                                ps["PT"][:, 128 * j:128 * (j + 1)],
                                st["qkst"][:, 2 * WIN * spar + 128 * j:
                                           2 * WIN * spar + 128 * (j + 1)],
                                st["identb"][:]),
                                reads=[bqk, B["identb"]], writes=[B["PT"]],
                                tag=f"vt{o}_{w}_{sq}_{j}")
                        s.op("scalar", lambda e, spar=spar: e.activation(
                            st["qkst"][:, 2 * WIN * spar + WIN:2 * WIN * (spar + 1)],
                            ps["PT"][:, 0:512], AF.Copy),
                            reads=[B["PT"]], writes=[bqk], tag=f"ve{o}_{w}_{sq}")
                        for j in range(4):
                            s.dma(ch_sp[6 + ((2 * w + sq + j) % 2)],
                                  lambda e, sq=sq, o=o, twin=twin, j=j, spar=spar:
                                  e.dma_start(
                                      out=sv_d[sq, twin + 128 * j:twin + 128 * (j + 1),
                                               128 * (o - 16):128 * (o - 15)],
                                      in_=st["qkst"][:, 2 * WIN * spar + WIN + 128 * j:
                                                     2 * WIN * spar + WIN + 128 * (j + 1)]),
                                  reads=[bqk], tag=f"vo{o}_{w}_{sq}_{j}")
                    else:
                        s.op("scalar", lambda e, pname=pname, spar=spar: e.activation(
                            st["qkst"][:, 2 * WIN * spar:2 * WIN * spar + WIN],
                            pap(pname), AF.Gelu),
                            reads=[bp], writes=[bqk], tag=f"gel{o}_{w}_{sq}")
                        s.dma(ch_sp[4 + ((2 * w + sq) % 2)],
                              lambda e, sq=sq, o=o, twin=twin, spar=spar: e.dma_start(
                                  out=sgel_d[sq, :, twin // 128:twin // 128 + 4,
                                             o - 24, :],
                                  in_=st["qkst"][:, 2 * WIN * spar:2 * WIN * spar + WIN]),
                              reads=[bqk], tag=f"go{o}_{w}_{sq}")
                if wq is not None and oi >= 28:
                    next(wq, None)
                    next(wq, None)


    def carve_S2():
        sb_at("KK", (128, 8 * T), BF16, 0)
        sb_at("VV", (128, 16 * C), BF16, 32 * 1024)
        sb_at("Y", (128, 8 * T), F32, 64 * 1024)
        sb_at("QT", (128, 2 * WIN), BF16, 128 * 1024)
        sb_at("QB", (128, 2 * WIN), BF16, 159 * 1024)
        sb_at("SC", (128, 4 * WIN), BF16, 130 * 1024)
        sb_at("YT", (128, 2 * C), F32, 134 * 1024)
        sb_at("YSQT", (128, 2 * C), F32, 142 * 1024)
        sb_at("YN", (128, 2 * C), BF16, 150 * 1024)
        sb_at("MU", (128, 16 * 5), F32, 154 * 1024)
        sb_at("Y2S", (128, 2 * C), BF16, 155 * 1024)
        for nm in ("QT", "QB", "SC", "YT", "YSQT", "YN", "Y2S"):
            B[nm + "0"] = s.buf(nm + "0")
            B[nm + "1"] = s.buf(nm + "1")
        for nm in ("KK", "VV", "Y", "MU"):
            if nm not in B:
                B[nm] = s.buf(nm)
        for g in range(4):
            if f"Ycol{g}" not in B:
                B[f"Ycol{g}"] = s.buf(f"Ycol{g}")
        for sq in range(SEQ):
            for g in range(4):
                if f"Y2Asp{sq}_{g}" not in B:
                    B[f"Y2Asp{sq}_{g}"] = s.buf(f"Y2Asp{sq}_{g}")

    def S2(sq, gn_prev=None):
        if sq == 0:
            s.op("vector", lambda e: e.memset(st["QT"][64:128, :], 0.0),
                 writes=[B["QT0"], B["QT1"]], tag="qz0")
            s.op("vector", lambda e: e.memset(st["QB"][0:64, :], 0.0),
                 writes=[B["QB0"], B["QB1"]], tag="qz1")
        for cc in range(8):
            s.dma(ch_rd[cc % 2], lambda e, cc=cc, sq=sq: e.dma_start(
                out=st["KK"][:, T * cc:T * (cc + 1)],
                in_=sk_d[sq, 128 * cc:128 * (cc + 1), :]),
                writes=[B["KK"]], tag=f"Kld{sq}_{cc}")
        for stile in range(16):
            s.dma(ch_rd[2 + stile % 2], lambda e, stile=stile, sq=sq: e.dma_start(
                out=st["VV"][:, C * stile:C * (stile + 1)],
                in_=sv_d[sq, 128 * stile:128 * (stile + 1), :]),
                writes=[B["VV"]], tag=f"Vld{sq}_{stile}")
        scA = ["PA", "PB"]
        scB = ["PC", "PC"]
        for w in range(4):
            if gn_prev is not None:
                next(gn_prev, None)
            for hp in range(8):
                qpar = (4 * w + hp) % 2
                bq = B[f"QT{qpar}"]
                bqb = B[f"QB{qpar}"]
                s.dma(ch_rd[qpar], lambda e, sq=sq, hp=hp, w=w, qpar=qpar: e.dma_start(
                    out=st["QT"][0:64, WIN * qpar:WIN * (qpar + 1)],
                    in_=sq_d[sq, 128 * hp:128 * hp + 64, WIN * w:WIN * (w + 1)]),
                    writes=[bq], tag=f"Qld{sq}_{w}_{hp}")
                s.dma(ch_rd[qpar], lambda e, sq=sq, hp=hp, w=w, qpar=qpar: e.dma_start(
                    out=st["QB"][64:128, WIN * qpar:WIN * (qpar + 1)],
                    in_=sq_d[sq, 128 * hp + 64:128 * (hp + 1), WIN * w:WIN * (w + 1)]),
                    writes=[bqb], tag=f"Qlb{sq}_{w}_{hp}")
                nst = 4 * w + 4
                avdef = [None]
                for stt in range(nst):
                    pa = scA[stt % 2]
                    pb = scB[stt % 2]
                    s.op("tensor", lambda e, pa=pa, hp=hp, stt=stt, qpar=qpar: e.matmul(
                        ps[pa][:], st["KK"][:, T * hp + 128 * stt:T * hp + 128 * (stt + 1)],
                        st["QT"][:, WIN * qpar:WIN * (qpar + 1)],
                        start=True, stop=True),
                        reads=[B["KK"], bq], writes=[B[pa]], tag=f"sMA{sq}_{w}_{hp}_{stt}")
                    s.op("tensor", lambda e, pb=pb, hp=hp, stt=stt, qpar=qpar: e.matmul(
                        ps[pb][:], st["KK"][:, T * hp + 128 * stt:T * hp + 128 * (stt + 1)],
                        st["QB"][:, WIN * qpar:WIN * (qpar + 1)],
                        start=True, stop=True),
                        reads=[B["KK"], bqb], writes=[B[pb]], tag=f"sMB{sq}_{w}_{hp}_{stt}")
                    diag = stt >= 4 * w
                    r = stt - 4 * w
                    bsc = B[f"SC{stt % 2}"]
                    if diag:
                        s.op("vector", lambda e, pa=pa, stt=stt, r=r: e.tensor_mul(
                            st["SC"][:, WIN * (stt % 2):WIN * (stt % 2 + 1)],
                            ps[pa][:], st["tri"][:, WIN * r:WIN * (r + 1)]),
                            reads=[B[pa], B["tri"]], writes=[bsc], tag=f"eA{sq}_{w}_{hp}_{stt}")
                        s.op("vector", lambda e, pb=pb, stt=stt, r=r: e.tensor_mul(
                            st["SC"][:, WIN * (2 + stt % 2):WIN * (3 + stt % 2)],
                            ps[pb][:], st["tri"][:, WIN * r:WIN * (r + 1)]),
                            reads=[B[pb], B["tri"]], writes=[bsc], tag=f"eB{sq}_{w}_{hp}_{stt}")
                    else:
                        s.op("scalar", lambda e, pa=pa, stt=stt: e.activation(
                            st["SC"][:, WIN * (stt % 2):WIN * (stt % 2 + 1)],
                            ps[pa][:], AF.Copy, scale=0.125),
                            reads=[B[pa]], writes=[bsc], tag=f"eA{sq}_{w}_{hp}_{stt}")
                        s.op("vector", lambda e, pb=pb, stt=stt: e.tensor_scalar(
                            st["SC"][:, WIN * (2 + stt % 2):WIN * (3 + stt % 2)],
                            ps[pb][:], 0.125, None, ALU.mult),
                            reads=[B[pb]], writes=[bsc], tag=f"eB{sq}_{w}_{hp}_{stt}")
                    if avdef[0] is not None:
                        avdef[0]()
                    def avblock(hp=hp, stt=stt, nst=nst):
                        s.op("tensor", lambda e, hp=hp, stt=stt, nst=nst: e.matmul(
                            ps["PE2"][0:64, 0:512],
                            st["VV"][:, C * stt + 128 * hp:C * stt + 128 * hp + 64],
                            st["SC"][:, WIN * (stt % 2):WIN * (stt % 2 + 1)],
                            start=(stt == 0), stop=(stt == nst - 1)),
                            reads=[B["VV"], B[f"SC{stt % 2}"]], writes=[B["PE2A"]],
                            tag=f"aMA{sq}_{w}_{hp}_{stt}")
                        s.op("tensor", lambda e, hp=hp, stt=stt, nst=nst: e.matmul(
                            ps["PE2"][0:64, 512:1024],
                            st["VV"][:, C * stt + 128 * hp + 64:C * stt + 128 * (hp + 1)],
                            st["SC"][:, WIN * (2 + stt % 2):WIN * (3 + stt % 2)],
                            start=(stt == 0), stop=(stt == nst - 1)),
                            reads=[B["VV"], B[f"SC{stt % 2}"]], writes=[B["PE2B"]],
                            tag=f"aMB{sq}_{w}_{hp}_{stt}")
                    avdef[0] = avblock
                avdef[0]()
                s.op("vector", lambda e, hp=hp, w=w: e.tensor_copy(
                    st["Y"][0:64, T * hp + WIN * w:T * hp + WIN * (w + 1)],
                    ps["PE2"][0:64, 0:512]),
                    reads=[B["PE2A"]], writes=[B[f"Ycol{w}"]], tag=f"yA{sq}_{w}_{hp}")
                s.op("vector", lambda e, hp=hp, w=w: e.tensor_copy(
                    st["Y"][64:128, T * hp + WIN * w:T * hp + WIN * (w + 1)],
                    ps["PE2"][0:64, 512:1024]),
                    reads=[B["PE2B"]], writes=[B[f"Ycol{w}"]], tag=f"yB{sq}_{w}_{hp}")
    def GN_gen(sq):
        for tt in range(16):
            par = tt % 2
            byt, bysq, byn, by2 = (B[f"YT{par}"], B[f"YSQT{par}"],
                                   B[f"YN{par}"], B[f"Y2S{par}"])
            byc = B[f"Ycol{tt // 4}"]
            for gg in range(2):
                pname = "PD" if gg == 0 else "PG"
                for j in range(4):
                    cc = 4 * gg + j
                    s.op("tensor", lambda e, cc=cc, j=j, tt=tt, pname=pname: e.transpose(
                        ps[pname][:, 128 * j:128 * (j + 1)],
                        st["Y"][:, T * cc + 128 * tt:T * cc + 128 * (tt + 1)],
                        st["identf"][:]),
                        reads=[byc, B["identf"]], writes=[B[pname]],
                        tag=f"gt{sq}_{tt}_{cc}")
                s.op("vector", lambda e, gg=gg, par=par, pname=pname: e.tensor_copy(
                    st["YT"][:, C * par + 512 * gg:C * par + 512 * (gg + 1)],
                    ps[pname][:]),
                    reads=[B[pname]], writes=[byt], tag=f"ge{sq}_{tt}_{gg}")
            s.op("vector", lambda e, par=par, tt=tt: e.reduce_sum(
                st["MU"][:, 16 * (tt % 2):16 * (tt % 2) + 16],
                bass.AP(st["YT"], C * par, [[2 * C, 128], [64, 16], [1, 64]]), AX),
                reads=[byt, B["MU"]], writes=[B["MU"]], tag=f"gmu{sq}_{tt}")
            s.op("scalar", lambda e, par=par: e.activation(
                st["YSQT"][:, C * par:C * (par + 1)],
                st["YT"][:, C * par:C * (par + 1)], AF.Square),
                reads=[byt], writes=[bysq], tag=f"gsq{sq}_{tt}")
            s.op("vector", lambda e, par=par, tt=tt: e.reduce_sum(
                st["MU"][:, 32 + 16 * (tt % 2):32 + 16 * (tt % 2) + 16],
                bass.AP(st["YSQT"], C * par, [[2 * C, 128], [64, 16], [1, 64]]), AX),
                reads=[bysq, B["MU"]], writes=[B["MU"]], tag=f"gs2{sq}_{tt}")
            mcol = slice(16 * (tt % 2), 16 * (tt % 2) + 16)
            scol = slice(32 + 16 * (tt % 2), 32 + 16 * (tt % 2) + 16)
            s.op("vector", lambda e, mcol=mcol: e.tensor_scalar(
                st["MU"][:, mcol], st["MU"][:, mcol], 1.0 / 64.0, None, ALU.mult),
                reads=[B["MU"]], writes=[B["MU"]], tag=f"gm1{sq}_{tt}")
            s.op("vector", lambda e, mcol=mcol, scol=scol: e.tensor_mul(
                st["MU"][:, 64:80], st["MU"][:, mcol], st["MU"][:, mcol]),
                reads=[B["MU"]], writes=[B["MU"]], tag=f"gm2{sq}_{tt}")
            s.op("vector", lambda e, scol=scol: e.tensor_scalar(
                st["MU"][:, scol], st["MU"][:, scol], 1.0 / 64.0, GN_EPS, ALU.mult, ALU.add),
                reads=[B["MU"]], writes=[B["MU"]], tag=f"gm3{sq}_{tt}")
            s.op("vector", lambda e, scol=scol: e.tensor_sub(
                st["MU"][:, scol], st["MU"][:, scol], st["MU"][:, 64:80]),
                reads=[B["MU"]], writes=[B["MU"]], tag=f"gm4{sq}_{tt}")
            s.op("scalar", lambda e, scol=scol: e.activation(
                st["MU"][:, 64:80], st["MU"][:, scol], AF.Sqrt),
                reads=[B["MU"]], writes=[B["MU"]], tag=f"gm7{sq}_{tt}")
            s.op("vector", lambda e: e.reciprocal(st["MU"][:, 64:80], st["MU"][:, 64:80]),
                reads=[B["MU"]], writes=[B["MU"]], tag=f"gm8{sq}_{tt}")
            for h in range(16):
                s.op("vector", lambda e, h=h, par=par, mcol=mcol: e.tensor_scalar(
                    st["YN"][:, C * par + 64 * h:C * par + 64 * (h + 1)],
                    st["YT"][:, C * par + 64 * h:C * par + 64 * (h + 1)],
                    st["MU"][:, mcol.start + h:mcol.start + h + 1],
                    st["MU"][:, 64 + h:64 + h + 1], ALU.subtract, ALU.mult),
                    reads=[byt, B["MU"]], writes=[byn], tag=f"gn{sq}_{tt}_{h}")
            for cc in range(8):
                s.op("tensor", lambda e, cc=cc, par=par: e.transpose(
                    ps["PT"][:, 128 * cc:128 * (cc + 1)],
                    st["YN"][:, C * par + 128 * cc:C * par + 128 * (cc + 1)],
                    st["identb"][:]),
                    reads=[byn, B["identb"]], writes=[B["PT"]], tag=f"gb{sq}_{tt}_{cc}")
            for cc in range(8):
                s.op("vector", lambda e, cc=cc, par=par: e.tensor_scalar(
                    st["Y2S"][:, C * par + 128 * cc:C * par + 128 * (cc + 1)],
                    ps["PT"][:, 128 * cc:128 * (cc + 1)],
                    st["vecs"][:, 16 + cc:17 + cc], st["vecs"][:, 24 + cc:25 + cc],
                    ALU.mult, ALU.add),
                    reads=[B["PT"], B["vecs"]], writes=[by2], tag=f"ga{sq}_{tt}_{cc}")
            s.dma(ch_sp[4 + (tt % 2)], lambda e, tt=tt, sq=sq, par=par:
                  e.dma_start(
                      out=sy2a_d[sq, :, :, 128 * tt:128 * (tt + 1)],
                      in_=st["Y2S"][:, C * par:C * (par + 1)]),
                  reads=[by2], writes=[B[f"Y2Asp{sq}_{tt // 4}"]], tag=f"gsp{sq}_{tt}")
            if tt % 4 == 3:
                yield

    def carve_S4():
        sb_at("WGT", (128, 64 * 128), BF16, 0)
        sb_at("XNM", (128, 2 * 8 * WIN), BF16, 16 * 1024)
        sb_at("YH", (128, 2 * WIN), BF16, 32 * 1024)
        sb_at("GS", (128, 2 * WIN), BF16, 34 * 1024)
        sb_at("Y2O", (128, 2 * WIN), BF16, 36 * 1024)
        for nm in ("XNM", "YH", "GS", "Y2O"):
            B[nm + "0"] = s.buf(nm + "0")
            B[nm + "1"] = s.buf(nm + "1")
        B["WGT"] = s.buf("WGT")

    def S4(gn=None):
        for o in range(8):
            s.dma(ch_w[o % 2], lambda e, o=o: e.dma_start(
                out=st["WGT"][:, 1024 * o:1024 * (o + 1)], in_=wgt_d[o]),
                writes=[B["WGT"]], tag=f"wgld{o}")
        pp = ["PA", "PB"]
        for sq in range(SEQ):
            for w in range(4):
                if gn is not None and sq == 0:
                    next(gn, None)
                wpar = (4 * sq + w) % 2
                tw = T * sq + WIN * w
                bxn = B[f"XNM{wpar}"]
                for cc in range(8):
                    s.dma(ch_rd[cc % 2], lambda e, cc=cc, tw=tw, wpar=wpar: e.dma_start(
                        out=st["XNM"][:, 8 * WIN * wpar + WIN * cc:
                                      8 * WIN * wpar + WIN * (cc + 1)],
                        in_=sxn_d[:, cc, tw:tw + WIN]),
                        writes=[bxn], tag=f"xnl{sq}_{w}_{cc}")
                for o in range(8):
                    par = o % 2
                    byh, bgs, by2 = B[f"YH{par}"], B[f"GS{par}"], B[f"Y2O{par}"]
                    s.dma(ch_rd[2 + par], lambda e, o=o, sq=sq, w=w, par=par: e.dma_start(
                        out=st["YH"][:, WIN * par:WIN * (par + 1)],
                        in_=sy2a_d[sq, :, o, WIN * w:WIN * (w + 1)]),
                        reads=[B[f"Y2Asp{sq}_{w}"]],
                        writes=[byh], tag=f"yhl{o}_{sq}_{w}")
                    pname = pp[par]
                    for cc in range(8):
                        s.op("tensor", lambda e, pname=pname, o=o, cc=cc, wpar=wpar:
                             e.matmul(ps[pname][:],
                                      st["WGT"][:, 1024 * o + 128 * cc:
                                                 1024 * o + 128 * (cc + 1)],
                                      st["XNM"][:, 8 * WIN * wpar + WIN * cc:
                                                 8 * WIN * wpar + WIN * (cc + 1)],
                                      start=(cc == 0), stop=(cc == 7)),
                             reads=[B["WGT"], bxn], writes=[B[pname]],
                             tag=f"gmm{o}_{sq}_{w}_{cc}")
                    s.op("scalar", lambda e, pname=pname, o=o, par=par: e.activation(
                        st["GS"][:, WIN * par:WIN * (par + 1)], ps[pname][:],
                        AF.Silu, bias=st["vecs"][:, o:o + 1]),
                        reads=[B[pname], B["vecs"]], writes=[bgs], tag=f"gsl{o}_{sq}_{w}")
                    s.op("vector", lambda e, par=par: e.tensor_mul(
                        st["Y2O"][:, WIN * par:WIN * (par + 1)],
                        st["GS"][:, WIN * par:WIN * (par + 1)],
                        st["YH"][:, WIN * par:WIN * (par + 1)]),
                        reads=[bgs, byh], writes=[by2], tag=f"gy2{o}_{sq}_{w}")
                    s.dma(ch_sp[par], lambda e, o=o, tw=tw, par=par: e.dma_start(
                        out=sy2_d[:, tw // 128:tw // 128 + 4, o, :],
                        in_=st["Y2O"][:, WIN * par:WIN * (par + 1)]),
                        reads=[by2], tag=f"y2o{o}_{sq}_{w}")

    def carve_S5():
        sb_at("WGQ", (128, 8 * INNER), BF16, 0)
        sb_at("WUQ", (128, 8 * INNER), BF16, 64 * 1024)
        sb_at("XQS", (128, 2 * 1024), BF16, 128 * 1024)
        sb_at("HH", (128, INNER), F32, 132 * 1024)
        sb_at("VT", (128, INNER), F32, 148 * 1024)
        sb_at("HQT", (128, INNER), BF16, 164 * 1024)
        sb_at("HQS", (128, 1024), BF16, 172 * 1024)
        for nm in ("XQS",):
            B[nm + "0"] = s.buf(nm + "0")
            B[nm + "1"] = s.buf(nm + "1")
        for nm in ("WGQ", "WUQ", "HH", "VT", "HQT", "HQS"):
            B[nm] = s.buf(nm)

    def S5():
        for cc in range(8):
            s.dma(ch_rd[cc % 2], lambda e, cc=cc: e.dma_start(
                out=st["WGQ"][:, INNER * cc:INNER * (cc + 1)],
                in_=swgq_d[128 * cc:128 * (cc + 1), :]),
                writes=[B["WGQ"]], tag=f"wgq{cc}")
            s.dma(ch_rd[2 + cc % 2], lambda e, cc=cc: e.dma_start(
                out=st["WUQ"][:, INNER * cc:INNER * (cc + 1)],
                in_=swuq_d[128 * cc:128 * (cc + 1), :]),
                writes=[B["WUQ"]], tag=f"wuq{cc}")
        tc = st["tmpc"]
        tdefer = [None]
        for i in range(NTT):
            par = i % 2
            bxq = B[f"XQS{par}"]
            s.dma(ch_in[par], lambda e, i=i, par=par: e.dma_start(
                out=st["XQS"][:, 1024 * par:1024 * (par + 1)], in_=sxq_d[i]),
                writes=[bxq], tag=f"xql{i}")
            s.op("vector", lambda e, i=i: e.tensor_scalar(
                tc[:, 0:1], st["stats"][:, 4 * NTT + i:4 * NTT + i + 1],
                st["wsc"][:, 0:1], None, ALU.mult),
                reads=[B["stats"], B["wsc"], B["tmpc"]], writes=[B["tmpc"]], tag=f"dqg{i}")
            s.op("vector", lambda e, i=i: e.tensor_scalar(
                tc[:, 1:2], st["stats"][:, 4 * NTT + i:4 * NTT + i + 1],
                st["wsc"][:, 1:2], None, ALU.mult),
                reads=[B["stats"], B["wsc"], B["tmpc"]], writes=[B["tmpc"]], tag=f"dqu{i}")
            pg = ["PA", "PC"]
            pu = ["PB", "PD"]
            for iw in range(8):
                q = iw % 2
                psg, psu = pg[q], pu[q]
                for cc in range(8):
                    s.op("tensor", lambda e, psg=psg, cc=cc, iw=iw, par=par: e.matmul(
                        pap(psg),
                        st["XQS"][:, 1024 * par + 128 * cc:1024 * par + 128 * (cc + 1)],
                        st["WGQ"][:, INNER * cc + WIN * iw:INNER * cc + WIN * (iw + 1)],
                        start=(cc == 0), stop=(cc == 7)),
                        reads=[bxq, B["WGQ"]], writes=[B[psg]], tag=f"fg{i}_{iw}_{cc}")
                    s.op("tensor", lambda e, psu=psu, cc=cc, iw=iw, par=par: e.matmul(
                        pap(psu),
                        st["XQS"][:, 1024 * par + 128 * cc:1024 * par + 128 * (cc + 1)],
                        st["WUQ"][:, INNER * cc + WIN * iw:INNER * cc + WIN * (iw + 1)],
                        start=(cc == 0), stop=(cc == 7)),
                        reads=[bxq, B["WUQ"]], writes=[B[psu]], tag=f"fu{i}_{iw}_{cc}")
                s.op("scalar", lambda e, psg=psg, iw=iw: e.activation(
                    st["VT"][:, WIN * iw:WIN * (iw + 1)], pap(psg), AF.Silu,
                    scale=tc[:, 0:1]),
                    reads=[B[psg], B["tmpc"]], writes=[B["VT"]], tag=f"fsl{i}_{iw}")
                s.op("vector", lambda e, psu=psu, iw=iw: e.tensor_mul(
                    st["HH"][:, WIN * iw:WIN * (iw + 1)],
                    st["VT"][:, WIN * iw:WIN * (iw + 1)], pap(psu)),
                    reads=[B[psu], B["VT"]], writes=[B["HH"]], tag=f"fh{i}_{iw}")
            if tdefer[0] is not None:
                tdefer[0]()
            s.op("vector", lambda e: e.reduce_max(tc[:, 2:3], st["HH"][:], AX,
                                                  apply_absolute_value=True),
                reads=[B["HH"], B["tmpc"]], writes=[B["tmpc"]], tag=f"fam{i}")
            s.op("vector", lambda e: e.tensor_mul(tc[:, 2:3], tc[:, 2:3], tc[:, 1:2]),
                reads=[B["tmpc"]], writes=[B["tmpc"]], tag=f"fm1{i}")
            s.op("vector", lambda e: e.tensor_scalar_max(tc[:, 2:3], tc[:, 2:3], 1e-5),
                reads=[B["tmpc"]], writes=[B["tmpc"]], tag=f"fm2{i}")
            s.op("vector", lambda e: e.reciprocal(tc[:, 3:4], tc[:, 2:3]),
                reads=[B["tmpc"]], writes=[B["tmpc"]], tag=f"fm3{i}")
            s.op("vector", lambda e: e.tensor_scalar(
                tc[:, 4:5], tc[:, 3:4], 127.0, None, ALU.mult),
                reads=[B["tmpc"]], writes=[B["tmpc"]], tag=f"fm4{i}")
            s.op("vector", lambda e: e.tensor_mul(tc[:, 4:5], tc[:, 4:5], tc[:, 1:2]),
                reads=[B["tmpc"]], writes=[B["tmpc"]], tag=f"fm5{i}")
            s.op("vector", lambda e: e.tensor_scalar(
                tc[:, 5:6], tc[:, 2:3], 1.0 / 127.0, None, ALU.mult),
                reads=[B["tmpc"]], writes=[B["tmpc"]], tag=f"fm6{i}")
            s.op("vector", lambda e: e.tensor_scalar(
                tc[:, 5:6], tc[:, 5:6], st["wsc"][:, 2:3], None, ALU.mult),
                reads=[B["tmpc"], B["wsc"]], writes=[B["tmpc"]], tag=f"fm7{i}")
            s.op("vector", lambda e: e.tensor_scalar(
                st["VT"][:], st["HH"][:], tc[:, 4:5], st["ceff"][:, 1:2],
                ALU.mult, ALU.add),
                reads=[B["HH"], B["tmpc"], B["ceff"]], writes=[B["VT"]], tag=f"fq1{i}")
            s.op("vector", lambda e: e.tensor_scalar(
                st["HH"][:], st["VT"][:], st["ceff"][:, 1:2], None, ALU.subtract),
                reads=[B["VT"], B["ceff"]], writes=[B["HH"]], tag=f"fq2{i}")
            hqt_t = st["HQT"]
            hqt_b = B["HQT"]
            s.op("vector", lambda e, hqt_t=hqt_t: e.tensor_scalar(
                hqt_t[:], st["HH"][:], tc[:, 5:6], None, ALU.mult),
                reads=[B["HH"], B["tmpc"]], writes=[hqt_b], tag=f"fq3{i}")

            def tblock(i=i, hqt_t=hqt_t, hqt_b=hqt_b):
                for r in range(4):
                    for g in range(8):
                        ic = 8 * r + g
                        s.op("tensor", lambda e, ic=ic, g=g, hqt_t=hqt_t: e.transpose(
                            ps["PT"][:, 128 * g:128 * (g + 1)],
                            hqt_t[:, 128 * ic:128 * (ic + 1)], st["identb"][:]),
                            reads=[hqt_b, B["identb"]], writes=[B["PT"]],
                            tag=f"ftr{i}_{ic}")
                    s.op("vector", lambda e: e.tensor_copy(st["HQS"][:], ps["PT"][:]),
                         reads=[B["PT"]], writes=[B["HQS"]], tag=f"fte{i}_{r}")
                    s.dma(ch_sp[4 + r % 2], lambda e, i=i, r=r: e.dma_start(
                        out=shq_d[i, r], in_=st["HQS"][:]),
                        reads=[B["HQS"]], tag=f"fsp{i}_{r}")
            tdefer[0] = tblock

        tdefer[0]()

    def carve_S6():
        sb_at("WDQ", (128, 32 * 1024), BF16, 0)
        sb_at("WPT", (128, 8 * 1024), BF16, 64 * 1024)
        sb_at("WFT", (128, 16 * 1024), BF16, 80 * 1024)
        sb_at("HQ6", (128, 2 * 4096), BF16, 112 * 1024)
        sb_at("Y2T", (128, 2 * 1024), BF16, 128 * 1024)
        sb_at("GET", (128, 2 * 2048), BF16, 132 * 1024)
        sb_at("XS", (128, 2 * 1024), F32, 140 * 1024)
        sb_at("OUTS", (128, 2 * 1024), F32, 148 * 1024)
        for nm in ("HQ6", "Y2T", "GET", "XS", "OUTS"):
            B[nm + "0"] = s.buf(nm + "0")
            B[nm + "1"] = s.buf(nm + "1")
        for nm in ("WDQ", "WPT", "WFT"):
            B[nm] = s.buf(nm)

    def S6():
        for ic in range(32):
            s.dma(ch_rd[ic % 2], lambda e, ic=ic: e.dma_start(
                out=st["WDQ"][:, 1024 * ic:1024 * (ic + 1)],
                in_=swdq_d[128 * ic:128 * (ic + 1), :]),
                writes=[B["WDQ"]], tag=f"wdq{ic}")
        for cc in range(8):
            s.dma(ch_w[cc % 2], lambda e, cc=cc: e.dma_start(
                out=st["WPT"][:, 1024 * cc:1024 * (cc + 1)],
                in_=wpt_d[128 * cc:128 * (cc + 1), :]),
                writes=[B["WPT"]], tag=f"wptl{cc}")
        for ec in range(16):
            s.dma(ch_w[ec % 2], lambda e, ec=ec: e.dma_start(
                out=st["WFT"][:, 1024 * ec:1024 * (ec + 1)],
                in_=wft_d[128 * ec:128 * (ec + 1), :]),
                writes=[B["WFT"]], tag=f"wftl{ec}")
        tc = st["tmpc"]
        odefer = [None]
        for i in range(NTT):
            par = i % 2
            sq, tt = i // 16, i % 16
            bhq, by2, bge, bxs, bo = (B[f"HQ6{par}"], B[f"Y2T{par}"],
                                      B[f"GET{par}"], B[f"XS{par}"], B[f"OUTS{par}"])
            s.dma(ch_in[par], lambda e, i=i, par=par: e.dma_start(
                out=st["XS"][:, 1024 * par:1024 * (par + 1)],
                in_=x_d[128 * i:128 * (i + 1), :]),
                writes=[bxs], tag=f"oxl{i}")
            for r in range(4):
                s.dma(ch_rd[2 + r % 2], lambda e, i=i, r=r, par=par: e.dma_start(
                    out=st["HQ6"][:, 4096 * par + 1024 * r:4096 * par + 1024 * (r + 1)],
                    in_=shq_d[i, r]),
                    writes=[bhq], tag=f"ohl{i}_{r}")
            s.dma(ch_rd[i % 2], lambda e, i=i, par=par: e.dma_start(
                out=st["Y2T"][:, 1024 * par:1024 * (par + 1)],
                in_=sy2_d[:, i, :, :]),
                writes=[by2], tag=f"oyl{i}")
            s.dma(ch_sp[6 + i % 2], lambda e, sq=sq, tt=tt, par=par: e.dma_start(
                out=st["GET"][:, 2048 * par:2048 * (par + 1)],
                in_=sgel_d[sq, :, tt, :, :]),
                writes=[bge], tag=f"ogl{i}")
            if par == 0:
                ph = [("PE2", 0), ("PE2", 512)]
                bps = [B["PE2A"], B["PE2B"]]
            else:
                ph = [("PA", 0), ("PB", 0)]
                bps = [B["PA"], B["PB"]]

            def mmpair(lhs_ap, rhs_base, rhs_tensor, brd, bstat, start, stop, tag):
                for hh in range(2):
                    pn, poff = ph[hh]
                    s.op("tensor", lambda e, pn=pn, poff=poff, lhs_ap=lhs_ap,
                         rhs_base=rhs_base, rhs_tensor=rhs_tensor, hh=hh,
                         start=start, stop=stop:
                         e.matmul(ps[pn][:, poff:poff + 512], lhs_ap(),
                                  rhs_tensor[:, rhs_base + 512 * hh:rhs_base + 512 * (hh + 1)],
                                  start=start, stop=stop),
                         reads=[brd, bstat],
                         writes=[bps[hh]], tag=f"{tag}_{hh}")

            for hh in range(2):
                pn, poff = ph[hh]
                s.op("tensor", lambda e, pn=pn, poff=poff, hh=hh: e.matmul(
                    ps[pn][:, poff:poff + 512], st["onesrow"][0:1, :],
                    st["rowbuf"][0:1, 512 * hh:512 * (hh + 1)],
                    start=True, stop=False),
                    reads=[B["onesrow"], B["rowbuf"]], writes=[bps[hh]],
                    tag=f"obias{i}_{hh}")
            for cc in range(8):
                mmpair(lambda cc=cc, par=par: st["Y2T"][:, 1024 * par + 128 * cc:
                                                        1024 * par + 128 * (cc + 1)],
                       1024 * cc, st["WPT"], B["WPT"], by2, False, False, f"omp{i}_{cc}")
            for ec in range(16):
                mmpair(lambda ec=ec, par=par: st["GET"][:, 2048 * par + 128 * ec:
                                                        2048 * par + 128 * (ec + 1)],
                       1024 * ec, st["WFT"], B["WFT"], bge, False, False, f"omf{i}_{ec}")
            for ic in range(32):
                mmpair(lambda ic=ic, par=par: st["HQ6"][:, 4096 * par + 128 * ic:
                                                         4096 * par + 128 * (ic + 1)],
                       1024 * ic, st["WDQ"], B["WDQ"], bhq, False, ic == 31, f"omd{i}_{ic}")
            if odefer[0] is not None:
                odefer[0]()
            def oblock(i=i, par=par, ph=ph, bps=bps, bxs=bxs, bo=bo):
                s.op("vector", lambda e, i=i: e.tensor_scalar(
                    tc[:, 8:9], st["stats"][:, NTT + i:NTT + i + 1], 1.0, None, ALU.add),
                    reads=[B["stats"], B["tmpc"]], writes=[B["tmpc"]], tag=f"oc1{i}")
                s.op("vector", lambda e, i=i: e.tensor_mul(
                    tc[:, 9:10], st["stats"][:, i:i + 1], st["stats"][:, NTT + i:NTT + i + 1]),
                    reads=[B["stats"], B["tmpc"]], writes=[B["tmpc"]], tag=f"oc2{i}")
                s.op("vector", lambda e, par=par: e.tensor_scalar(
                    st["OUTS"][:, 1024 * par:1024 * (par + 1)],
                    st["XS"][:, 1024 * par:1024 * (par + 1)],
                    tc[:, 8:9], tc[:, 9:10], ALU.mult, ALU.subtract),
                    reads=[bxs, B["tmpc"]], writes=[bo], tag=f"oc3{i}")
                for hh in range(2):
                    pn, poff = ph[hh]
                    s.op("vector", lambda e, pn=pn, poff=poff, par=par, hh=hh: e.tensor_add(
                        st["OUTS"][:, 1024 * par + 512 * hh:1024 * par + 512 * (hh + 1)],
                        st["OUTS"][:, 1024 * par + 512 * hh:1024 * par + 512 * (hh + 1)],
                        ps[pn][:, poff:poff + 512]),
                        reads=[bps[hh], bo], writes=[bo], tag=f"oadd{i}_{hh}")
                s.dma(ch_out[par], lambda e, i=i, par=par: e.dma_start(
                    out=out_d[128 * i:128 * (i + 1), :],
                    in_=st["OUTS"][:, 1024 * par:1024 * (par + 1)]),
                    reads=[bo], tag=f"oout{i}")
            odefer[0] = oblock
        odefer[0]()

    ORDER = ["S0", "S1a", "S1b", "S2", "S4", "S5", "S6"]
    stage_n = ORDER.index(upto)
    S0_consts()
    S0_wquant_late = stage_n >= 2
    if not S0_wquant_late:
        S0_wquant()
    s.barrier("bar_S0")
    carve_S1()
    if stage_n == 1:
        S1a()
    if stage_n >= 2:
        wqg = S0_wquant_gen()
        s1ag = S1a_gen()
        S1b2(wq=wqg, s1a=s1ag)
        for _ in s1ag:
            pass
        for _ in wqg:
            pass
    if stage_n >= 3:
        s.barrier("bar_S2")
        carve_S2()
        S2(0)
        gn0 = GN_gen(0)
        S2(1, gn_prev=gn0)
        for _ in gn0:
            pass
        gn1 = GN_gen(1)
        if stage_n == 3:
            for _ in gn1:
                pass
    if stage_n >= 4:
        s.barrier("bar_S4")
        carve_S4()
        S4(gn=gn1)
        for _ in gn1:
            pass
    if stage_n >= 5:
        s.barrier("bar_S5")
        carve_S5()
        S5()
    if stage_n >= 6:
        s.barrier("bar_S6")
        carve_S6()
        S6()

    if debug:
        s.barrier("bar_dbg")
        bdb = s.buf("dbg")
        for k, name in enumerate(debug):
            src = dr[name]
            shp = list(src.tensor.shape)
            dt = src.tensor.dtype
            o = nc.dram_tensor("dbg_" + name, shp, dt, kind="ExternalOutput").ap()
            n = int(np.prod(shp))
            s.dma(ch_out[k % 2], lambda e, src=src, o=o, n=n: e.dma_start(
                out=bass.AP(o.tensor, 0, [[n, 1], [1, n]]),
                in_=bass.AP(src.tensor, 0, [[n, 1], [1, n]])),
                reads=[bdb], tag=f"dbg_{name}")

    if stage_n < 6:
        s.barrier("bar_out")
        bz = s.buf("zout")
        s.op("vector", lambda e: e.memset(st["rmsb"][:], 0.0),
             reads=[B["rmsb"]], writes=[bz], tag="zz")
        for i in range(NTT):
            s.dma(ch_out[i % 2], lambda e, i=i: e.dma_start(
                out=bass.AP(out_d.tensor, 128 * i * C, [[C, 128], [1, C]]),
                in_=st["rmsb"][:]),
                reads=[bz], tag=f"zo{i}")

    s.barrier("bar_final")

    with ExitStack() as stack:
        e = stack.enter_context
        sems = {}
        for eng in ENGINES:
            sems[("eng", eng)] = e(nc.semaphore(f"sem_{eng}"))
        for c in s.chans:
            sems[("chan", c.name)] = e(nc.semaphore(f"semc_{c.name}"))
        block = e(nc.Block())
        s.emit(block, sems)
    return nc


def _prep(inputs):
    consts = _host_consts()
    tiled = {
        "w1t": _tile_oc(np.ascontiguousarray(
            np.asarray(inputs["L0_attn_w_qkvff"], dtype=np.float32).T), 40, 8),
        "wgt": _tile_oc(np.ascontiguousarray(
            np.asarray(inputs["L0_attn_w_gated"], dtype=np.float32).T), 8, 8),
        "wpt": np.ascontiguousarray(np.asarray(inputs["L0_attn_w_proj"], dtype=np.float32).T),
        "wft": np.ascontiguousarray(np.asarray(inputs["L0_attn_w_ff"], dtype=np.float32).T),
        "wg": np.ascontiguousarray(np.asarray(inputs["L0_ffn_w_gate"], dtype=np.float32).T),
        "wu": np.ascontiguousarray(np.asarray(inputs["L0_ffn_w_up"], dtype=np.float32).T),
        "wd": np.ascontiguousarray(np.asarray(inputs["L0_ffn_w_down"], dtype=np.float32).T),
        "b_gated": np.asarray(inputs["L0_attn_b_gated"], dtype=np.float32).reshape(1, C),
        "b_proj": np.asarray(inputs["L0_attn_b_proj"], dtype=np.float32).reshape(1, C),
        "gn_w": np.asarray(inputs["L0_attn_gn_w"], dtype=np.float32).reshape(1, C),
        "gn_b": np.asarray(inputs["L0_attn_gn_b"], dtype=np.float32).reshape(1, C),
        "rms_w": np.asarray(inputs["L0_attn_rms_w"], dtype=np.float32).reshape(1, C),
    }
    tiled.update(_xpos_decay_tables())
    return consts, tiled


def kernel(**inputs):
    consts, tiled = _prep(inputs)
    nc = build_program()
    in_maps = []
    for core in range(8):
        m = {"x": np.ascontiguousarray(
            np.asarray(inputs["L0_attn_x"][2 * core:2 * core + 2],
                       dtype=np.float32).reshape(NT, C))}
        m.update(tiled)
        m.update(consts)
        in_maps.append(m)
    res = run_bass_kernel_spmd(nc, in_maps, core_ids=list(range(8)))
    out = np.concatenate([np.asarray(r["out"]).reshape(2, T, C)
                          for r in res.results], axis=0)
    return out.astype(np.float32)


# version 20
# speedup vs baseline: 1.0176x; 1.0045x over previous
import math
from contextlib import ExitStack

import numpy as np

import concourse.bass as bass
import concourse.mybir as mybir
from concourse.bass_utils import run_bass_kernel_spmd

F32 = mybir.dt.float32
BF16 = mybir.dt.bfloat16
AF = mybir.ActivationFunctionType
ALU = mybir.AluOpType
AX = mybir.AxisListType.X

SEQ, T, C, H, HD = 2, 2048, 1024, 16, 64
NT = SEQ * T
FF2, INNER = 2048, 4096
NTT = NT // 128
WIN = 512
C1 = float(1.5 * 2.0 ** 23)
CD = float(1.5 * 2.0 ** 52)
GN_EPS, LN_EPS = 1e-5, 1e-5
S1B_MODE = "full"
CARVE = 49152
CARVE_END = 229376

ENGINES = ("tensor", "vector", "scalar", "gpsimd", "sync")


class _Buf:
    __slots__ = ("name", "lw", "rd", "psum")

    def __init__(self, name, psum=False):
        self.name, self.lw, self.rd, self.psum = name, None, [], psum


class _Chan:
    __slots__ = ("name", "eng", "cum", "ops", "last", "waiters")

    def __init__(self, name, eng):
        assert eng in ("sync", "gpsimd")
        self.name, self.eng, self.cum, self.ops, self.last = name, eng, 0, [], None
        self.waiters = {}


class _Op:
    __slots__ = ("eng", "fn", "idx", "deps", "need_inc", "inc_val", "chan",
                 "cum_after", "waits", "tag")

    def __init__(self, eng, fn, idx, tag):
        self.eng, self.fn, self.idx, self.tag = eng, fn, idx, tag
        self.deps, self.need_inc, self.inc_val = [], False, None
        self.chan, self.cum_after, self.waits = None, None, None


class Sched:
    def __init__(self):
        self.ops, self.chans, self.last_on = [], [], {e: None for e in ENGINES}

    def buf(self, name="", psum=False):
        return _Buf(name, psum)

    def chan(self, name, eng):
        c = _Chan(name, eng)
        self.chans.append(c)
        return c

    def _record(self, op, reads, writes):
        deps = set()
        for b in reads:
            if b.lw is not None:
                deps.add(b.lw)
            if b.psum:
                for r in b.rd:
                    if r.eng != op.eng:
                        deps.add(r)
        for b in writes:
            if b.lw is not None:
                deps.add(b.lw)
            for r in b.rd:
                deps.add(r)
        op.deps = [d for d in deps if d is not op and not (
            d.eng == "tensor" and op.eng == "tensor"
            and d.chan is None and op.chan is None)]
        for d in op.deps:
            if d.chan is not None:
                d.chan.waiters[op.eng] = op
        for b in writes:
            b.lw, b.rd = op, []
        for b in reads:
            b.rd.append(op)
        self.ops.append(op)
        self.last_on[op.eng] = op
        return op

    def op(self, eng, fn, reads=(), writes=(), tag=""):
        assert eng in ("tensor", "vector", "scalar", "gpsimd")
        return self._record(_Op(eng, fn, len(self.ops), tag), reads, writes)

    def dma(self, chan, fn, reads=(), writes=(), tag=""):
        op = _Op(chan.eng, fn, len(self.ops), tag)
        op.chan = chan
        chan.cum += 16
        op.cum_after = chan.cum
        chan.ops.append(op)
        chan.last = op
        extra = [w for w in chan.waiters.values() if w is not op]
        ret = self._record(op, reads, writes)
        for w in extra:
            if w not in op.deps:
                op.deps.append(w)
        return ret

    def barrier(self, tag="barrier"):
        prods = [o for o in self.last_on.values() if o is not None]
        prods += [c.last for c in self.chans if c.last is not None]
        for eng in ENGINES:
            op = _Op(eng, None, len(self.ops), tag)
            op.deps = [p for p in prods if p is not op]
            self.ops.append(op)

    def emit(self, block, sems):
        for op in self.ops:
            for d in op.deps:
                d.need_inc = True
        inc_count = {e: 0 for e in ENGINES}
        for op in self.ops:
            if op.chan is None and op.need_inc:
                inc_count[op.eng] += 1
                op.inc_val = inc_count[op.eng]
        for op in self.ops:
            waits = {}
            for d in op.deps:
                if d.chan is not None:
                    key = ("chan", d.chan.name)
                    val = d.cum_after
                    for other in d.chan.ops:
                        if other.idx < op.idx and other.cum_after > val:
                            val = other.cum_after
                else:
                    key, val = ("eng", d.eng), d.inc_val
                waits[key] = max(waits.get(key, 0), val)
            op.waits = waits
        self._verify_schedulable()

        self._emit_streams(block, sems)

    def _verify_schedulable(self):
        queues = {e: [op for op in self.ops if op.eng == e] for e in ENGINES}
        heads = {e: 0 for e in ENGINES}
        esem = {e: 0 for e in ENGINES}
        csem = {c.name: 0 for c in self.chans}

        def ok(op):
            for (kind, nm), val in op.waits.items():
                if kind == "eng" and esem[nm] < val:
                    return False
                if kind == "chan" and csem[nm] < val:
                    return False
            return True

        progress = True
        while progress:
            progress = False
            for e in ENGINES:
                q = queues[e]
                while heads[e] < len(q):
                    op = q[heads[e]]
                    if not ok(op):
                        break
                    if op.chan is not None:
                        csem[op.chan.name] += 16
                    elif op.inc_val is not None:
                        esem[e] = op.inc_val
                    heads[e] += 1
                    progress = True
        stuck = [(e, queues[e][heads[e]]) for e in ENGINES if heads[e] < len(queues[e])]
        if stuck:
            msg = "; ".join(
                f"{e}:{op.tag}@{op.idx} waits {op.waits}" for e, op in stuck[:4])
            raise RuntimeError("schedule deadlock: " + msg)

    def _emit_streams(self, block, sems):
        def stream(engobj, ename):
            hw = {}
            for op in self.ops:
                if op.eng != ename:
                    continue
                for key, val in sorted(op.waits.items(), key=lambda kv: str(kv[0])):
                    if hw.get(key, 0) >= val:
                        continue
                    engobj.wait_ge(sems[key], val)
                    hw[key] = val
                if op.fn is not None:
                    inst = op.fn(engobj)
                    if op.chan is not None:
                        inst.then_inc(sems[("chan", op.chan.name)], 16)
                    elif op.need_inc:
                        inst.then_inc(sems[("eng", op.eng)], 1)
                elif op.need_inc:
                    raise RuntimeError(f"fn-less op {op.tag} cannot carry inc")

        block.tensor(lambda e: stream(e, "tensor"))
        block.vector(lambda e: stream(e, "vector"))
        block.scalar(lambda e: stream(e, "scalar"))
        block.gpsimd(lambda e: stream(e, "gpsimd"))
        block.sync(lambda e: stream(e, "sync"))


def _host_consts():
    c = {}
    c["identm"] = np.eye(128, dtype=np.float32)
    P = np.zeros((128, 128), dtype=np.float32)
    for i in range(64):
        P[2 * i, 2 * i + 1] = 1.0
        P[2 * i + 1, 2 * i] = 1.0
    c["pswapm"] = P
    tri = np.zeros((128, 4 * WIN), dtype=np.float32)
    for r in range(4):
        for j in range(4):
            if j > r:
                tri[:, WIN * r + 128 * j:WIN * r + 128 * (j + 1)] = 1.0
            elif j == r:
                tri[:, WIN * r + 128 * j:WIN * r + 128 * (j + 1)] = \
                    np.tril(np.ones((128, 128), dtype=np.float32)).T
    c["trim"] = tri * 0.125
    p = ((np.arange(T, dtype=np.float64) - 1024.0) / 512.0).astype(np.float32)
    c["pbcm"] = np.broadcast_to(p[None, :], (128, T)).copy()
    gam = (1.0 - 2.0 ** (-5.0 - np.arange(H, dtype=np.float64))).astype(np.float32)
    gcols = np.zeros((128, 8), dtype=np.float32)
    for cc in range(8):
        for pp in range(128):
            gcols[pp, cc] = gam[(128 * cc + pp) // 64]
    c["gcolsm"] = gcols
    c["hrampm"] = (np.arange(128, dtype=np.float32) // 2).reshape(128, 1)
    c["sgnm"] = np.where(np.arange(128) % 2 == 0, -1.0, 1.0
                         ).astype(np.float32).reshape(128, 1)
    c["onesm"] = np.ones((128, 1), dtype=np.float32)
    c["halfpim"] = np.full((128, 1), np.float32(np.pi / 2), dtype=np.float32)
    gl = np.zeros((128, 8 * 64), dtype=np.float32)
    for hp in range(8):
        gl[0:64, 64 * hp + 2 * hp] = 1.0 / 64.0
        gl[64:128, 64 * hp + 2 * hp + 1] = 1.0 / 64.0
    c["gnlm"] = gl
    return c


def _tile_oc(w_t, nblk_o, nblk_c):
    ci, co = w_t.shape
    assert ci == nblk_c * 128 and co == nblk_o * 128
    return np.ascontiguousarray(
        w_t.reshape(nblk_c, 128, nblk_o, 128).transpose(2, 1, 0, 3))



def _xpos_decay_tables():
    import jax
    cpu = jax.local_devices(backend="cpu")[0]
    import jax.numpy as jnp
    with jax.default_device(cpu):
        half = C // 2
        sv = (jnp.arange(0, C, 2, dtype=jnp.float32) + 0.4 * C) / (1.4 * C)
        mn = -(T) // 2
        power = jnp.arange(mn, T + mn, dtype=jnp.float32) / 512
        scale = sv[None, :] ** power[:, None]
        invf = 1.0 / 10000.0 ** (jnp.arange(half, dtype=jnp.float32) / half)
        sinus = jnp.arange(T, dtype=jnp.float32)[:, None] * invf[None, :]
        sin = jnp.sin(sinus)
        cos = jnp.cos(sinus)
        cos_q = jnp.repeat(cos * scale, 2, axis=-1)
        sin_q = jnp.repeat(sin * scale, 2, axis=-1)
        iscale = 1.0 / scale
        cos_k = jnp.repeat(cos * iscale, 2, axis=-1)
        sin_k = jnp.repeat(sin * iscale, 2, axis=-1)
        cos_q = np.asarray(cos_q, dtype=np.float32)
        sin_q = np.asarray(sin_q, dtype=np.float32)
        cos_k = np.asarray(cos_k, dtype=np.float32)
        sin_k = np.asarray(sin_k, dtype=np.float32)
    sgn = np.where(np.arange(C) % 2 == 0, -1.0, 1.0).astype(np.float32)
    gam = (1.0 - 2.0 ** (-5.0 - np.arange(H, dtype=np.float64)))
    gc = gam[np.arange(C) // 64]
    tpow = np.arange(T, dtype=np.float64)
    g_t = (gc[:, None] ** tpow[None, :])
    g_ti = (gc[:, None] ** (-tpow[None, :]))
    out = {
        "cqm": (cos_q.T.astype(np.float64) * g_t).astype(np.float32),
        "sqm": ((sin_q.T * sgn[:, None]).astype(np.float64) * g_t).astype(np.float32),
        "ckm": (cos_k.T.astype(np.float64) * g_ti).astype(np.float32),
        "skm": ((sin_k.T * sgn[:, None]).astype(np.float64) * g_ti).astype(np.float32),
    }
    return {k: np.ascontiguousarray(v) for k, v in out.items()}


def build_program(upto="S6", debug=()):
    nc = bass.Bass("TRN2", target_bir_lowering=False)
    s = Sched()
    dr, st, B = {}, {}, {}

    def din(name, shape, dtype=F32):
        dr[name] = nc.dram_tensor(name, list(shape), dtype, kind="ExternalInput").ap()
        return dr[name]

    def dsc(name, shape, dtype):
        dr[name] = nc.dram_tensor(name, list(shape), dtype, kind="Internal").ap()
        return dr[name]

    x_d = din("x", (NT, C))
    w1t_d = din("w1t", (40, 128, 8, 128))
    wgt_d = din("wgt", (8, 128, 8, 128))
    wpt_d = din("wpt", (C, C))
    wft_d = din("wft", (FF2, C))
    wg_d = din("wg", (C, INNER))
    wu_d = din("wu", (C, INNER))
    wd_d = din("wd", (INNER, C))
    bgat_d = din("b_gated", (1, C))
    bprj_d = din("b_proj", (1, C))
    gnw_d = din("gn_w", (1, C))
    gnb_d = din("gn_b", (1, C))
    rms_d = din("rms_w", (1, C))
    identm_d = din("identm", (128, 128))
    pswap_d = din("pswapm", (128, 128))
    tri_d = din("trim", (128, 4 * WIN))
    pbc_d = din("pbcm", (128, T))
    gcols_d = din("gcolsm", (128, 8))
    hramp_d = din("hrampm", (128, 1))
    sgn_d = din("sgnm", (128, 1))
    ones_d = din("onesm", (128, 1))
    halfpi_d = din("halfpim", (128, 1))
    gnl_d = din("gnlm", (128, 8 * 64))
    cqm_d = din("cqm", (C, T))
    sqm_d = din("sqm", (C, T))
    ckm_d = din("ckm", (C, T))
    skm_d = din("skm", (C, T))

    out_d = nc.dram_tensor("out", [NT, C], F32, kind="ExternalOutput").ap()

    scq_d = dsc("s_cq", (C, T), F32)
    ssq_d = dsc("s_sq", (C, T), F32)
    sck_d = dsc("s_ck", (C, T), F32)
    ssk_d = dsc("s_sk", (C, T), F32)
    sq_d = dsc("s_q", (SEQ, C, T), BF16)
    sk_d = dsc("s_k", (SEQ, C, T), BF16)
    sv_d = dsc("s_v", (SEQ, T, C), BF16)
    sgel_d = dsc("s_gelu", (SEQ, 128, 16, 16, 128), BF16)
    sxn_d = dsc("s_xn", (128, 8, NT), BF16)
    sxq_d = dsc("s_xq", (NTT, 128, 8 * 128), BF16)
    sy2a_d = dsc("s_y2a", (SEQ, 128, 8, T), BF16)
    sy2_d = dsc("s_y2", (128, NTT, 8, 128), BF16)
    shq_d = dsc("s_hq", (NTT, 4, 128, 8 * 128), BF16)
    swgq_d = dsc("s_wgq", (C, INNER), BF16)
    swuq_d = dsc("s_wuq", (C, INNER), BF16)
    swdq_d = dsc("s_wdq", (INNER, C), BF16)
    sdbg_d = dsc("s_dbgspill", (40, 128, NT), BF16)

    def sb(key, shape, dtype=F32):
        st[key] = nc.alloc_sbuf_tensor("sb_" + key, list(shape), dtype)
        B[key] = s.buf(key)
        return st[key]

    def sb_at(key, shape, dtype, off):
        nbytes = int(np.prod(shape[1:])) * mybir.dt.size(dtype)
        assert CARVE + off + nbytes <= CARVE_END, (key, off, nbytes)
        st[key] = nc.alloc_sbuf_tensor_at("cv_" + key, list(shape), dtype,
                                          offset=CARVE + off)
        B[key] = s.buf(key)
        return st[key]

    sb("identb", (128, 128), BF16)
    sb("identf", (128, 128), F32)
    sb("pswapb", (128, 128), BF16)
    sb("gcols", (128, 8))
    sb("hramp", (128, 1))
    sb("sgncol", (128, 1))
    sb("onescol", (128, 1))
    sb("halfpi", (128, 1))
    sb("vecs", (128, 5 * 8))
    sb("rowbuf", (1, 2 * C))
    sb("rmsb", (128, C))
    sb("ceff", (128, 4))
    sb("wsc", (128, 16))
    sb("stats", (128, 9 * NTT))
    sb("tmpc", (128, 64))
    sb("gnl", (128, 8 * 64))
    sb("tri", (128, 4 * WIN))
    sentinel = nc.alloc_sbuf_tensor("sb_sentinel", [128, 4], F32)
    assert nc.lookup_mloc(sentinel).addr + 16 <= CARVE, \
        f"persistent sbuf overflow: {nc.lookup_mloc(sentinel).addr}"

    ps = {}
    for nm, shape, dt in (("PA", (128, 512), F32), ("PB", (128, 512), F32),
                          ("PC", (128, 512), F32), ("PD", (128, 512), F32),
                          ("PE2", (128, 1024), F32), ("PT", (128, 1024), BF16),
                          ("PG", (128, 512), F32)):
        ps[nm] = nc.alloc_psum_tensor(nm, list(shape), dt)
        B[nm] = s.buf(nm, psum=True)
    B["PE2A"] = s.buf("PE2A", psum=True)
    B["PE2B"] = s.buf("PE2B", psum=True)

    def pap(pname):
        if pname == "PE2A":
            return ps["PE2"][:, 0:512]
        if pname == "PE2B":
            return ps["PE2"][:, 512:1024]
        return ps[pname][:]

    ch_in = [s.chan(f"in{i}", "sync") for i in range(2)]
    ch_w = [s.chan(f"w{i}", "gpsimd") for i in range(2)]
    ch_sp = [s.chan(f"sp{i}", "gpsimd") for i in range(8)]
    ch_rd = [s.chan(f"rd{i}", "sync") for i in range(4)]
    ch_out = [s.chan(f"out{i}", "sync") for i in range(2)]
    ch_cst = s.chan("cst", "sync")
    ch_cw = s.chan("cw", "gpsimd")

    def S0_consts():
        for key, d, chn in (("identb", identm_d, ch_cw), ("pswapb", pswap_d, ch_cw),
                            ("identf", identm_d, ch_cst),
                            ("tri", tri_d, ch_cst), ("gcols", gcols_d, ch_cst),
                            ("hramp", hramp_d, ch_cst), ("sgncol", sgn_d, ch_cst),
                            ("onescol", ones_d, ch_cst), ("gnl", gnl_d, ch_cst),
                            ("halfpi", halfpi_d, ch_cst)):
            s.dma(chn, lambda e, key=key, d=d: e.dma_start(out=st[key][:], in_=d),
                  writes=[B[key]], tag=f"ld_{key}")
        for k, d in enumerate((bgat_d, bprj_d, gnw_d, gnb_d, rms_d)):
            s.dma(ch_cst, lambda e, k=k, d=d: e.dma_start(
                out=st["vecs"][:, 8 * k:8 * (k + 1)],
                in_=bass.AP(d.tensor, 0, [[1, 128], [128, 8]]),
                allow_slow_non_contiguous=True),
                writes=[B["vecs"]], tag=f"veccol{k}")
        s.dma(ch_cst, lambda e: e.dma_start(out=st["rowbuf"][0:1, 0:C], in_=bprj_d),
              writes=[B["rowbuf"]], tag="bprow")
        s.dma(ch_cst, lambda e: e.dma_start(out=st["rowbuf"][0:1, C:2 * C], in_=rms_d),
              writes=[B["rowbuf"]], tag="rmsrow")

        s.op("vector", lambda e: e.tensor_scalar(
            st["ceff"][:, 0:1], st["onescol"][:], 2.0 ** -30, 1.0, ALU.add, ALU.subtract),
            reads=[B["onescol"]], writes=[B["ceff"]], tag="ind")
        s.op("vector", lambda e: e.tensor_scalar(
            st["ceff"][:, 1:2], st["ceff"][:, 0:1], 2.0 ** 30 * CD, C1, ALU.mult, ALU.add),
            reads=[B["ceff"]], writes=[B["ceff"]], tag="ceff")

        s.op("vector", lambda e: e.memset(st["tmpc"][0:1, 0:16], 1.0),
             writes=[B["tmpc"]], tag="onesrow16")
        onesrow = sb("onesrow", (1, 128))
        s.op("vector", lambda e: e.memset(st["onesrow"][:], 1.0),
             writes=[B["onesrow"]], tag="onesrow")
        for half in range(2):
            s.op("tensor", lambda e, half=half: e.matmul(
                ps["PE2"][:, 512 * half:512 * (half + 1)], st["onesrow"][0:1, :],
                st["rowbuf"][0:1, C + 512 * half:C + 512 * (half + 1)],
                start=True, stop=True),
                reads=[B["onesrow"], B["rowbuf"]],
                writes=[B["PE2A"] if half == 0 else B["PE2B"]], tag=f"rmsmm{half}")
        s.op("vector", lambda e: e.tensor_copy(st["rmsb"][:], ps["PE2"][:, 0:C]),
             reads=[B["PE2A"], B["PE2B"]], writes=[B["rmsb"]], tag="rmsbc")

    def S0_wquant_gen():
        sb_at("wload0", (128, 4096), F32, 132 * 1024)
        sb_at("wload1", (128, 4096), F32, 148 * 1024)
        sb_at("wqout0", (128, 4096), BF16, 164 * 1024)
        acc = st["wsc"]
        for wname, wsrc, wdst, nchunk, wcols, wi in (
                ("wg", wg_d, swgq_d, 8, INNER, 0),
                ("wu", wu_d, swuq_d, 8, INNER, 1),
                ("wd", wd_d, swdq_d, 32, C, 2)):
            for k in range(nchunk):
                par = k % 2
                bw = B[f"wload{par}"]
                s.dma(ch_out[par], lambda e, k=k, par=par, wsrc=wsrc, wcols=wcols:
                      e.dma_start(out=st[f"wload{par}"][:, 0:wcols],
                                  in_=wsrc[128 * k:128 * (k + 1), :]),
                      writes=[bw], tag=f"{wname}ld{k}")
                s.op("vector", lambda e, par=par, wcols=wcols: e.reduce_sum(
                    acc[:, 8:9], st[f"wload{par}"][:, 0:wcols], AX,
                    apply_absolute_value=True),
                    reads=[bw, B["wsc"]], writes=[B["wsc"]], tag=f"{wname}abs{k}")
                if k == 0:
                    s.op("vector", lambda e: e.tensor_copy(acc[:, 9:10], acc[:, 8:9]),
                         reads=[B["wsc"]], writes=[B["wsc"]], tag=f"{wname}a0")
                else:
                    s.op("vector", lambda e: e.tensor_add(acc[:, 9:10], acc[:, 9:10], acc[:, 8:9]),
                         reads=[B["wsc"]], writes=[B["wsc"]], tag=f"{wname}a{k}")
                yield
            s.op("tensor", lambda e: e.matmul(
                ps["PE2"][0:1, 0:1], st["onescol"][:], acc[:, 9:10], start=True, stop=True),
                reads=[B["onescol"], B["wsc"]], writes=[B["PE2A"]], tag=f"{wname}tot")
            ncount = float(nchunk * 128 * wcols)
            s.op("vector", lambda e, ncount=ncount: e.tensor_scalar(
                acc[0:1, 10:11], ps["PE2"][0:1, 0:1], 1.0 / ncount, 1e-5, ALU.mult, ALU.max),
                reads=[B["PE2A"]], writes=[B["wsc"]], tag=f"{wname}mean")
            s.op("tensor", lambda e: e.matmul(
                ps["PE2"][:, 4:5], st["onesrow"][0:1, :], acc[0:1, 10:11],
                start=True, stop=True),
                reads=[B["wsc"], B["onesrow"]], writes=[B["PE2A"]], tag=f"{wname}bc")
            s.op("vector", lambda e, wi=wi: e.tensor_copy(
                st["wsc"][:, wi:wi + 1], ps["PE2"][:, 4:5]),
                reads=[B["PE2A"]], writes=[B["wsc"]], tag=f"{wname}wsc")
            s.op("scalar", lambda e, wi=wi: e.activation(
                st["wsc"][:, 4 + wi:5 + wi], st["wsc"][:, wi:wi + 1], AF.Ln),
                reads=[B["wsc"]], writes=[B["wsc"]], tag=f"{wname}ln")
            s.op("scalar", lambda e, wi=wi: e.activation(
                st["wsc"][:, 4 + wi:5 + wi], st["wsc"][:, 4 + wi:5 + wi], AF.Exp, scale=-1.0),
                reads=[B["wsc"]], writes=[B["wsc"]], tag=f"{wname}exp")
            for k in range(nchunk):
                par = k % 2
                bw, bq = B[f"wload{par}"], B["wqout0"]
                s.dma(ch_in[par], lambda e, k=k, par=par, wsrc=wsrc, wcols=wcols:
                      e.dma_start(out=st[f"wload{par}"][:, 0:wcols],
                                  in_=wsrc[128 * k:128 * (k + 1), :]),
                      writes=[bw], tag=f"{wname}l2_{k}")
                s.op("scalar", lambda e, par=par, wcols=wcols, wi=wi: e.activation(
                    st[f"wload{par}"][:, 0:wcols], st[f"wload{par}"][:, 0:wcols],
                    AF.Copy, scale=st["wsc"][:, 4 + wi:5 + wi]),
                    reads=[bw, B["wsc"]], writes=[bw], tag=f"{wname}sc{k}")
                s.op("vector", lambda e, par=par, wcols=wcols: e.tensor_scalar(
                    st[f"wload{par}"][:, 0:wcols], st[f"wload{par}"][:, 0:wcols],
                    st["ceff"][:, 1:2], st["ceff"][:, 1:2], ALU.add, ALU.subtract),
                    reads=[bw, B["ceff"]], writes=[bw], tag=f"{wname}ri{k}")
                s.op("vector", lambda e, par=par, wcols=wcols: e.tensor_scalar(
                    st["wqout0"][:, 0:wcols], st[f"wload{par}"][:, 0:wcols],
                    -1.0, 1.0, ALU.max, ALU.min),
                    reads=[bw], writes=[bq], tag=f"{wname}cl{k}")
                s.dma(ch_sp[4 + par], lambda e, k=k, par=par, wdst=wdst, wcols=wcols:
                      e.dma_start(out=wdst[128 * k:128 * (k + 1), :],
                                  in_=st["wqout0"][:, 0:wcols]),
                      reads=[bq], tag=f"{wname}st{k}")
                yield

    def S0_wquant():
        for _ in S0_wquant_gen():
            pass

    def S0b():
        sb_at("pbc", (128, T), F32, 56 * 1024)
        for nm, off in (("tbC", 64), ("tbS", 72), ("tbU", 80), ("tbA", 88),
                        ("tbB", 96), ("tbG", 104), ("tbH", 112)):
            sb_at(nm, (128, T), F32, off * 1024)
        sb_at("tcol", (128, 16), F32, 120 * 1024)
        s.dma(ch_cst, lambda e: e.dma_start(out=st["pbc"][:], in_=pbc_d),
              writes=[B["pbc"]], tag="ld_pbc")
        LNIF = -math.log(10000.0) / 512.0
        tc = st["tcol"]
        bcol = B["tcol"]
        for cc in range(8):
            bC, bS, bU, bA, bB_, bG, bH = (B["tbC"], B["tbS"], B["tbU"], B["tbA"],
                                           B["tbB"], B["tbG"], B["tbH"])
            s.op("vector", lambda e, cc=cc: e.tensor_scalar_add(
                tc[:, 0:1], st["hramp"][:], float(64 * cc)),
                reads=[B["hramp"]], writes=[bcol], tag=f"t{cc}jc")
            s.op("scalar", lambda e: e.activation(tc[:, 1:2], tc[:, 0:1], AF.Exp, scale=LNIF),
                 reads=[bcol], writes=[bcol], tag=f"t{cc}f")
            s.op("vector", lambda e: e.tensor_scalar(
                tc[:, 2:3], tc[:, 0:1], 2.0 / 1433.6, 409.6 / 1433.6, ALU.mult, ALU.add),
                reads=[bcol], writes=[bcol], tag=f"t{cc}sv")
            s.op("scalar", lambda e: e.activation(tc[:, 3:4], tc[:, 2:3], AF.Ln),
                 reads=[bcol], writes=[bcol], tag=f"t{cc}L")
            s.op("vector", lambda e: e.tensor_scalar(
                st["tbU"][:], st["pbc"][:], tc[:, 3:4], None, ALU.mult),
                reads=[bcol, B["pbc"]], writes=[bU], tag=f"t{cc}U")
            s.op("scalar", lambda e: e.activation(st["tbA"][:], st["tbU"][:], AF.Exp),
                 reads=[bU], writes=[bA], tag=f"t{cc}scq")
            s.op("scalar", lambda e: e.activation(st["tbB"][:], st["tbU"][:], AF.Exp, scale=-1.0),
                 reads=[bU], writes=[bB_], tag=f"t{cc}sck")
            s.op("vector", lambda e: e.memset(st["tbC"][:, 0:1], 1.0),
                 writes=[bC], tag=f"t{cc}c0")
            s.op("vector", lambda e: e.memset(st["tbS"][:, 0:1], 0.0),
                 writes=[bS], tag=f"t{cc}s0")
            s.op("vector", lambda e: e.tensor_scalar(
                tc[:, 12:13], tc[:, 1:2], -1.0, 1.5707963705062866, ALU.mult, ALU.add),
                reads=[bcol], writes=[bcol], tag=f"t{cc}ch1")
            s.op("vector", lambda e: e.tensor_scalar_add(
                tc[:, 12:13], tc[:, 12:13], -4.371138828673793e-08),
                reads=[bcol], writes=[bcol], tag=f"t{cc}ch2")
            s.op("scalar", lambda e: e.activation(st["tbC"][:, 1:2], tc[:, 12:13], AF.Sin),
                 reads=[bcol], writes=[bC], tag=f"t{cc}c1")
            s.op("scalar", lambda e: e.activation(st["tbS"][:, 1:2], tc[:, 1:2], AF.Sin),
                 reads=[bcol], writes=[bS], tag=f"t{cc}s1")
            s.op("vector", lambda e: e.tensor_copy(tc[:, 4:5], st["tbC"][:, 1:2]),
                 reads=[bC], writes=[bcol], tag=f"t{cc}r0")
            s.op("vector", lambda e: e.tensor_copy(tc[:, 5:6], st["tbS"][:, 1:2]),
                 reads=[bS], writes=[bcol], tag=f"t{cc}r1")
            w = 2
            while w < T:
                s.op("vector", lambda e: e.tensor_mul(tc[:, 6:7], tc[:, 4:5], tc[:, 4:5]),
                     reads=[bcol], writes=[bcol], tag=f"t{cc}ra{w}")
                s.op("vector", lambda e: e.tensor_mul(tc[:, 7:8], tc[:, 5:6], tc[:, 5:6]),
                     reads=[bcol], writes=[bcol], tag=f"t{cc}rb{w}")
                s.op("vector", lambda e: e.tensor_mul(tc[:, 8:9], tc[:, 4:5], tc[:, 5:6]),
                     reads=[bcol], writes=[bcol], tag=f"t{cc}rc{w}")
                s.op("vector", lambda e: e.tensor_sub(tc[:, 4:5], tc[:, 6:7], tc[:, 7:8]),
                     reads=[bcol], writes=[bcol], tag=f"t{cc}rd{w}")
                s.op("vector", lambda e: e.tensor_scalar_mul(tc[:, 5:6], tc[:, 8:9], 2.0),
                     reads=[bcol], writes=[bcol], tag=f"t{cc}re{w}")
                s.op("vector", lambda e, w=w: e.tensor_scalar(
                    st["tbH"][:, 0:w], st["tbS"][:, 0:w], tc[:, 5:6], None, ALU.mult),
                    reads=[bS, bcol], writes=[bH], tag=f"t{cc}da{w}")
                s.op("vector", lambda e, w=w: e.scalar_tensor_tensor(
                    st["tbC"][:, w:2 * w], st["tbC"][:, 0:w], tc[:, 4:5],
                    st["tbH"][:, 0:w], ALU.mult, ALU.subtract),
                    reads=[bC, bcol, bH], writes=[bC], tag=f"t{cc}db{w}")
                s.op("vector", lambda e, w=w: e.tensor_scalar(
                    st["tbH"][:, 0:w], st["tbC"][:, 0:w], tc[:, 5:6], None, ALU.mult),
                    reads=[bC, bcol], writes=[bH], tag=f"t{cc}dc{w}")
                s.op("vector", lambda e, w=w: e.scalar_tensor_tensor(
                    st["tbS"][:, w:2 * w], st["tbS"][:, 0:w], tc[:, 4:5],
                    st["tbH"][:, 0:w], ALU.mult, ALU.add),
                    reads=[bS, bcol, bH], writes=[bS], tag=f"t{cc}dd{w}")
                w *= 2
            s.op("vector", lambda e, cc=cc: e.tensor_copy(
                tc[:, 9:10], st["gcols"][:, cc:cc + 1]),
                reads=[B["gcols"]], writes=[bcol], tag=f"t{cc}g")
            s.op("vector", lambda e: e.reciprocal(tc[:, 10:11], tc[:, 9:10]),
                 reads=[bcol], writes=[bcol], tag=f"t{cc}gi")
            for dst, bD, col in ((st["tbG"], bG, 9), (st["tbH"], bH, 10)):
                s.op("vector", lambda e, dst=dst: e.memset(dst[:, 0:1], 1.0),
                     writes=[bD], tag=f"t{cc}g0_{col}")
                s.op("vector", lambda e, dst=dst, col=col: e.tensor_copy(
                    dst[:, 1:2], tc[:, col:col + 1]),
                    reads=[bcol], writes=[bD], tag=f"t{cc}g1_{col}")
                s.op("vector", lambda e, col=col: e.tensor_copy(
                    tc[:, 11:12], tc[:, col:col + 1]),
                    reads=[bcol], writes=[bcol], tag=f"t{cc}gr_{col}")
                w = 2
                while w < T:
                    s.op("vector", lambda e: e.tensor_mul(
                        tc[:, 11:12], tc[:, 11:12], tc[:, 11:12]),
                        reads=[bcol], writes=[bcol], tag=f"t{cc}gs{w}_{col}")
                    s.op("vector", lambda e, w=w, dst=dst: e.tensor_scalar(
                        dst[:, w:2 * w], dst[:, 0:w], tc[:, 11:12], None, ALU.mult),
                        reads=[bD, bcol], writes=[bD], tag=f"t{cc}gm{w}_{col}")
                    w *= 2
            s.op("vector", lambda e: e.tensor_mul(st["tbA"][:], st["tbA"][:], st["tbG"][:]),
                 reads=[bA, bG], writes=[bA], tag=f"t{cc}aq")
            s.op("vector", lambda e: e.tensor_mul(st["tbB"][:], st["tbB"][:], st["tbH"][:]),
                 reads=[bB_, bH], writes=[bB_], tag=f"t{cc}ak")
            for ji, (dst_d, srcCS, bsrc, srcA, bamp, use_sgn) in enumerate((
                    (scq_d, st["tbC"], bC, st["tbA"], bA, False),
                    (ssq_d, st["tbS"], bS, st["tbA"], bA, True),
                    (sck_d, st["tbC"], bC, st["tbB"], bB_, False),
                    (ssk_d, st["tbS"], bS, st["tbB"], bB_, True))):
                s.op("vector", lambda e, srcCS=srcCS, srcA=srcA: e.tensor_mul(
                    st["tbU"][:], srcCS[:], srcA[:]),
                    reads=[bsrc, bamp], writes=[bU], tag=f"t{cc}f{ji}")
                if use_sgn:
                    s.op("vector", lambda e: e.tensor_scalar(
                        st["tbU"][:], st["tbU"][:], st["sgncol"][:], None, ALU.mult),
                        reads=[bU, B["sgncol"]], writes=[bU], tag=f"t{cc}fs{ji}")
                s.dma(ch_sp[ji % 2], lambda e, dst_d=dst_d, cc=cc: e.dma_start(
                    out=dst_d[128 * cc:128 * (cc + 1), :], in_=st["tbU"][:]),
                    reads=[bU], tag=f"t{cc}st{ji}")

    def carve_S1():
        sb_at("XHFM", (128, 8 * NT), BF16, 0)
        for g in range(8):
            B[f"XHFM{g}"] = s.buf(f"XHFM{g}")
        sb_at("xt", (128, 2 * C), F32, 64 * 1024)
        sb_at("xnf", (128, 2 * C), F32, 72 * 1024)
        sb_at("tmp1", (128, 2 * C), F32, 80 * 1024)
        sb_at("xps", (128, 2 * WIN), F32, 88 * 1024)
        sb_at("xhb", (128, 2 * C), BF16, 92 * 1024)
        sb_at("xnb", (128, 2 * C), BF16, 96 * 1024)
        sb_at("xqb", (128, 2 * C), BF16, 100 * 1024)
        sb_at("stg", (128, 2 * 2 * C), BF16, 104 * 1024)
        sb_at("wbuf", (128, 2 * 1024), BF16, 112 * 1024)
        sb_at("tabs", (128, 2 * 2 * WIN), F32, 116 * 1024)
        sb_at("qkst", (128, 2 * 2 * WIN), BF16, 124 * 1024)
        sb_at("xsq", (128, C), F32, 128 * 1024)
        for nm in ("xt", "xnf", "tmp1", "xps", "xhb", "xnb", "xqb", "stg",
                   "qkst", "tabs", "wbuf"):
            B[nm + "0"] = s.buf(nm + "0")
            B[nm + "1"] = s.buf(nm + "1")

    def S1a():
        for _ in S1a_gen():
            pass

    def S1a_gen():
        xhfm = st["XHFM"]
        cS = st["stats"]
        tc = st["tmpc"]
        for fam in ("stSUM", "stSSQ", "stXMX", "stXMN"):
            if fam not in B:
                B[fam] = s.buf(fam)
        G_ORDER = [g for gp in zip(range(4), range(4, 8)) for g in gp]
        SUM, SSQ, XMX, XMN = 5 * NTT, 6 * NTT, 7 * NTT, 8 * NTT
        for g in G_ORDER:
            tiles = list(range(4 * g, 4 * g + 4))
            for i in tiles:
                par = i % 2
                bxt = B[f"xt{par}"]
                s.dma(ch_in[par], lambda e, i=i, par=par: e.dma_start(
                    out=st["xt"][:, C * par:C * (par + 1)],
                    in_=x_d[128 * i:128 * (i + 1), :]),
                    writes=[bxt], tag=f"xld{i}")
                s.op("scalar", lambda e, par=par, i=i: e.activation(
                    st["xsq"][:], st["xt"][:, C * par:C * (par + 1)], AF.Copy,
                    accum_out=cS[:, SUM + i:SUM + i + 1]),
                    reads=[bxt], writes=[B["xsq"], B["stSUM"]], tag=f"rs{i}")
                s.op("scalar", lambda e, par=par, i=i: e.activation(
                    st["xsq"][:], st["xt"][:, C * par:C * (par + 1)], AF.Square,
                    accum_out=cS[:, SSQ + i:SSQ + i + 1]),
                    reads=[bxt], writes=[B["xsq"], B["stSSQ"]], tag=f"sq{i}")
                s.op("vector", lambda e, par=par, i=i: e.reduce_max(
                    cS[:, XMX + i:XMX + i + 1],
                    st["xt"][:, C * par:C * (par + 1)], AX),
                    reads=[bxt], writes=[B["stXMX"]], tag=f"mx{i}")
                s.op("vector", lambda e, par=par, i=i: e.tensor_reduce(
                    cS[:, XMN + i:XMN + i + 1],
                    st["xt"][:, C * par:C * (par + 1)], AX, ALU.min),
                    reads=[bxt], writes=[B["stXMN"]], tag=f"mn{i}")
            a, b = 4 * g, 4 * g + 4
            ch = [
                ("vector", [B["stSUM"]], lambda e, a=a, b=b: e.tensor_scalar(
                    cS[:, a:b], cS[:, SUM + a:SUM + b], 1.0 / C, None, ALU.mult), "m"),
                ("vector", [B["stSSQ"]], lambda e, a=a, b=b: e.tensor_scalar(
                    tc[:, 16:20], cS[:, SSQ + a:SSQ + b], 1.0 / C, None, ALU.mult), "e2"),
                ("vector", [], lambda e, a=a, b=b: e.tensor_mul(
                    tc[:, 20:24], cS[:, a:b], cS[:, a:b]), "mm"),
                ("vector", [], lambda e: e.tensor_sub(
                    tc[:, 24:28], tc[:, 16:20], tc[:, 20:24]), "v"),
                ("vector", [], lambda e: e.tensor_scalar_add(
                    tc[:, 28:32], tc[:, 24:28], LN_EPS), "u"),
                ("scalar", [], lambda e: e.activation(
                    tc[:, 32:36], tc[:, 28:32], AF.Sqrt), "sqr"),
                ("vector", [], lambda e: e.reciprocal(tc[:, 36:40], tc[:, 32:36]), "b0"),
                ("vector", [], lambda e: e.tensor_mul(
                    tc[:, 40:44], tc[:, 36:40], tc[:, 36:40]), "n1"),
                ("vector", [], lambda e: e.tensor_mul(
                    tc[:, 40:44], tc[:, 40:44], tc[:, 28:32]), "n2"),
                ("vector", [], lambda e: e.tensor_scalar(
                    tc[:, 40:44], tc[:, 40:44], -0.5, 1.5, ALU.mult, ALU.add), "n3"),
                ("vector", [], lambda e, a=a, b=b: e.tensor_mul(
                    cS[:, NTT + a:NTT + b], tc[:, 40:44], tc[:, 36:40]), "beta"),
                ("vector", [], lambda e: e.reciprocal(tc[:, 44:48], tc[:, 28:32]), "ru"),
                ("vector", [], lambda e: e.tensor_mul(
                    tc[:, 44:48], tc[:, 44:48], tc[:, 24:28]), "w"),
                ("scalar", [], lambda e: e.activation(
                    tc[:, 48:52], tc[:, 44:48], AF.Sqrt), "nr"),
                ("vector", [], lambda e: e.tensor_scalar_max(
                    tc[:, 48:52], tc[:, 48:52], 1e-8), "mxr"),
                ("vector", [], lambda e: e.reciprocal(tc[:, 52:56], tc[:, 48:52]), "rg"),
                ("vector", [], lambda e, a=a, b=b: e.tensor_mul(
                    cS[:, 2 * NTT + a:2 * NTT + b],
                    cS[:, NTT + a:NTT + b], tc[:, 52:56]), "al"),
                ("vector", [B["stXMX"]], lambda e, a=a, b=b: e.tensor_sub(
                    tc[:, 56:60], cS[:, XMX + a:XMX + b], cS[:, a:b]), "t1"),
                ("vector", [B["stXMN"]], lambda e, a=a, b=b: e.tensor_sub(
                    tc[:, 60:64], cS[:, a:b], cS[:, XMN + a:XMN + b]), "t2"),
                ("vector", [], lambda e: e.tensor_max(
                    tc[:, 56:60], tc[:, 56:60], tc[:, 60:64]), "t3"),
                ("vector", [], lambda e, a=a, b=b: e.tensor_mul(
                    tc[:, 56:60], tc[:, 56:60], cS[:, NTT + a:NTT + b]), "axn"),
                ("vector", [], lambda e: e.tensor_scalar_max(
                    tc[:, 56:60], tc[:, 56:60], 1e-5), "cl"),
                ("vector", [], lambda e: e.reciprocal(tc[:, 60:64], tc[:, 56:60]), "rc"),
                ("vector", [], lambda e, a=a, b=b: e.tensor_scalar(
                    cS[:, 3 * NTT + a:3 * NTT + b],
                    tc[:, 60:64], 127.0, None, ALU.mult), "sa"),
                ("vector", [], lambda e, a=a, b=b: e.tensor_scalar(
                    cS[:, 4 * NTT + a:4 * NTT + b],
                    tc[:, 56:60], 1.0 / 127.0, None, ALU.mult), "sar"),
            ]
            for eng, xreads, fn, nm in ch:
                s.op(eng, fn, reads=[B["stats"], B["tmpc"]] + xreads,
                     writes=[B["stats"], B["tmpc"]], tag=f"gc_{nm}{g}")
            for i in tiles:
                par = i % 2
                bxt, bxn, btm = B[f"xt{par}"], B[f"xnf{par}"], B[f"tmp1{par}"]
                bxh, bxq, bnb = B[f"xhb{par}"], B[f"xqb{par}"], B[f"xnb{par}"]
                s.dma(ch_in[par], lambda e, i=i, par=par: e.dma_start(
                    out=st["xt"][:, C * par:C * (par + 1)],
                    in_=x_d[128 * i:128 * (i + 1), :]),
                    writes=[bxt], tag=f"xl2_{i}")
                s.op("vector", lambda e, i=i, par=par: e.tensor_scalar(
                    st["xnf"][:, C * par:C * (par + 1)],
                    st["xt"][:, C * par:C * (par + 1)],
                    cS[:, i:i + 1], cS[:, NTT + i:NTT + i + 1],
                    ALU.subtract, ALU.mult),
                    reads=[bxt, B["stats"]], writes=[bxn], tag=f"xn{i}")
                s.op("vector", lambda e, i=i, par=par: e.tensor_scalar(
                    st["tmp1"][:, C * par:C * (par + 1)],
                    st["xt"][:, C * par:C * (par + 1)],
                    cS[:, i:i + 1], cS[:, 2 * NTT + i:2 * NTT + i + 1],
                    ALU.subtract, ALU.mult),
                    reads=[bxt, B["stats"]], writes=[btm], tag=f"xh{i}")
                s.op("vector", lambda e, par=par: e.tensor_mul(
                    st["xhb"][:, C * par:C * (par + 1)],
                    st["tmp1"][:, C * par:C * (par + 1)], st["rmsb"][:]),
                    reads=[btm, B["rmsb"]], writes=[bxh], tag=f"xhr{i}")
                s.op("vector", lambda e, i=i, par=par: e.tensor_scalar(
                    st["tmp1"][:, C * par:C * (par + 1)],
                    st["xnf"][:, C * par:C * (par + 1)],
                    cS[:, 3 * NTT + i:3 * NTT + i + 1], None, ALU.mult),
                    reads=[bxn, B["stats"]], writes=[btm], tag=f"xqs{i}")
                s.op("vector", lambda e, par=par: e.tensor_scalar(
                    st["xqb"][:, C * par:C * (par + 1)],
                    st["tmp1"][:, C * par:C * (par + 1)],
                    st["ceff"][:, 1:2], st["ceff"][:, 1:2], ALU.add, ALU.subtract),
                    reads=[btm, B["ceff"]], writes=[bxq], tag=f"xqr{i}")
                s.op("vector", lambda e, par=par: e.tensor_copy(
                    st["xnb"][:, C * par:C * (par + 1)],
                    st["xnf"][:, C * par:C * (par + 1)]),
                    reads=[bxn], writes=[bnb], tag=f"xnb{i}")
                for kind, src2, bsrc in (("fm", st["xhb"], bxh), ("xn", st["xnb"], bnb),
                                         ("xq", st["xqb"], bxq)):
                    for cchunk in range(8):
                        s.op("tensor", lambda e, src2=src2, par=par, cchunk=cchunk:
                             e.transpose(ps["PT"][:, 128 * cchunk:128 * (cchunk + 1)],
                                         src2[:, C * par + 128 * cchunk:
                                              C * par + 128 * (cchunk + 1)],
                                         st["identb"][:]),
                             reads=[bsrc, B["identb"]], writes=[B["PT"]],
                             tag=f"tr_{kind}{i}_{cchunk}")
                    if kind == "fm":
                        s.op("vector", lambda e, i=i: e.tensor_copy(
                            bass.AP(xhfm, 128 * i, [[8 * NT, 128], [NT, 8], [1, 128]]),
                            bass.AP(ps["PT"], 0, [[1024, 128], [128, 8], [1, 128]])),
                            reads=[B["PT"]], writes=[B[f"XHFM{i // 4}"]],
                            tag=f"ev_{kind}{i}")
                    else:
                        koff = 0 if kind == "xn" else 1
                        reg = 2 * C * par + C * koff
                        bstg = B[f"stg{par}"]
                        s.op("vector", lambda e, reg=reg: e.tensor_copy(
                            st["stg"][:, reg:reg + 1024], ps["PT"][:]),
                            reads=[B["PT"]], writes=[bstg], tag=f"sg_{kind}{i}")
                        if kind == "xq":
                            s.dma(ch_sp[2 + (i % 2)],
                                  lambda e, i=i, reg=reg: e.dma_start(
                                      out=sxq_d[i], in_=st["stg"][:, reg:reg + 1024]),
                                  reads=[bstg], tag=f"sp_xq{i}")
                        else:
                            s.dma(ch_sp[(i % 2)],
                                  lambda e, i=i, reg=reg: e.dma_start(
                                      out=sxn_d[:, :, 128 * i:128 * (i + 1)],
                                      in_=st["stg"][:, reg:reg + 1024]),
                                  reads=[bstg], tag=f"sp_xn{i}")
            yield

    def S1b2(wq=None, s1a=None):
        xhfm = st["XHFM"]
        o_order = []
        for cp in range(8):
            o_order += [cp, cp + 8]
        o_order += list(range(16, 40))
        prot = ["PA", "PB", "PC"]
        rrot = ["PD", "PG"]
        cnt = [0]
        for oi, o in enumerate(o_order):
            par = oi % 2
            bw = B[f"wbuf{par}"]
            s.dma(ch_w[par], lambda e, o=o, par=par: e.dma_start(
                out=st["wbuf"][:, 1024 * par:1024 * (par + 1)], in_=w1t_d[o]),
                writes=[bw], tag=f"wld{o}")
            is_q, is_k, is_v = o < 8, 8 <= o < 16, 16 <= o < 24
            cchunk = o % 8 if (o < 16) else None
            for w in range(4):
                if s1a is not None and oi == 0:
                    next(s1a, None)
                    next(s1a, None)
                tpar = w % 2
                btab = B[f"tabs{tpar}"]
                if is_q or is_k:
                    ctab_d = cqm_d if is_q else ckm_d
                    stab_d = sqm_d if is_q else skm_d
                    s.dma(ch_rd[tpar], lambda e, ctab_d=ctab_d, cchunk=cchunk, w=w, tpar=tpar:
                          e.dma_start(
                              out=st["tabs"][:, 2 * WIN * tpar:2 * WIN * tpar + WIN],
                              in_=ctab_d[128 * cchunk:128 * (cchunk + 1),
                                         WIN * w:WIN * (w + 1)]),
                          writes=[btab], tag=f"tc{o}_{w}")
                    s.dma(ch_rd[2 + tpar], lambda e, stab_d=stab_d, cchunk=cchunk, w=w, tpar=tpar:
                          e.dma_start(
                              out=st["tabs"][:, 2 * WIN * tpar + WIN:2 * WIN * (tpar + 1)],
                              in_=stab_d[128 * cchunk:128 * (cchunk + 1),
                                         WIN * w:WIN * (w + 1)]),
                          writes=[btab], tag=f"ts{o}_{w}")
                for sq in range(SEQ):
                    tw = T * sq + WIN * w
                    twin = WIN * w
                    pname = prot[cnt[0] % 3]
                    rname = rrot[cnt[0] % 2]
                    cnt[0] += 1
                    bp = B[pname]
                    for cci in range(8):
                        s.op("tensor", lambda e, pname=pname, par=par, cci=cci, tw=tw:
                             e.matmul(pap(pname),
                                      st["wbuf"][:, 1024 * par + 128 * cci:
                                                 1024 * par + 128 * (cci + 1)],
                                      bass.AP(xhfm, cci * NT + tw, [[8 * NT, 128], [1, WIN]]),
                                      start=(cci == 0), stop=(cci == 7)),
                             reads=[bw, B[f"XHFM{tw // WIN}"]], writes=[bp],
                             tag=f"mm{o}_{w}_{sq}_{cci}")
                    spar = (2 * w + sq) % 2
                    bqk = B[f"qkst{spar}"]
                    bxp = B[f"xps{spar}"]
                    if S1B_MODE == "mm_only":
                        s.op("scalar", lambda e, pname=pname, spar=spar: e.activation(
                            st["qkst"][:, 2 * WIN * spar:2 * WIN * spar + WIN],
                            ps[pname][:], AF.Copy),
                            reads=[bp], writes=[bqk], tag=f"dqb{o}_{w}_{sq}")
                        s.dma(ch_sp[4 + ((2 * w + sq) % 2)],
                              lambda e, o=o, tw=tw, spar=spar: e.dma_start(
                                  out=sdbg_d[o, :, tw:tw + WIN],
                                  in_=st["qkst"][:, 2 * WIN * spar:2 * WIN * spar + WIN]),
                              reads=[bqk], tag=f"dsp{o}_{w}_{sq}")
                        continue
                    if S1B_MODE == "rot" and (is_q or is_k):
                        s.op("scalar", lambda e, pname=pname, spar=spar: e.activation(
                            st["qkst"][:, 2 * WIN * spar:2 * WIN * spar + WIN],
                            ps[pname][:], AF.Copy),
                            reads=[bp], writes=[bqk], tag=f"dqb{o}_{w}_{sq}")
                        s.op("tensor", lambda e, spar=spar: e.matmul(
                            ps["PD"][:], st["pswapb"][:],
                            st["qkst"][:, 2 * WIN * spar:2 * WIN * spar + WIN],
                            start=True, stop=True),
                            reads=[bqk, B["pswapb"]], writes=[B["PD"]],
                            tag=f"drot{o}_{w}_{sq}")
                        s.op("vector", lambda e, spar=spar: e.tensor_copy(
                            st["qkst"][:, 2 * WIN * spar + WIN:2 * WIN * (spar + 1)],
                            bass.AP(ps["PD"], 0, [[512, 128], [1, WIN]])),
                            reads=[B["PD"]], writes=[bqk], tag=f"dre{o}_{w}_{sq}")
                        s.dma(ch_sp[4 + ((2 * w + sq) % 2)],
                              lambda e, o=o, tw=tw, spar=spar: e.dma_start(
                                  out=sdbg_d[o, :, tw:tw + WIN],
                                  in_=st["qkst"][:, 2 * WIN * spar + WIN:2 * WIN * (spar + 1)]),
                              reads=[bqk], tag=f"dsp{o}_{w}_{sq}")
                        continue
                    if S1B_MODE == "rot":
                        continue
                    if S1B_MODE in ("xpos", "xpos1", "xpos2", "xpos3", "xpos4") and not (is_q or is_k):
                        continue
                    if S1B_MODE == "xpos3" and (is_q or is_k):
                        s.op("vector", lambda e, pname=pname, tpar=tpar, spar=spar: e.tensor_mul(
                            st["xps"][:, WIN * spar:WIN * (spar + 1)], ps[pname][:],
                            st["tabs"][:, 2 * WIN * tpar:2 * WIN * tpar + WIN]),
                            reads=[bp, btab], writes=[bxp], tag=f"3xc{o}_{w}_{sq}")
                        btm = B[f"tmp1{spar}"]
                        s.op("vector", lambda e, pname=pname, tpar=tpar, spar=spar: e.tensor_mul(
                            st["tmp1"][:, C * spar:C * spar + WIN], ps[pname][:],
                            st["tabs"][:, 2 * WIN * tpar + WIN:2 * WIN * (tpar + 1)]),
                            reads=[bp, btab], writes=[btm], tag=f"3xs{o}_{w}_{sq}")
                        s.op("vector", lambda e, spar=spar: e.tensor_add(
                            st["qkst"][:, 2 * WIN * spar + WIN:2 * WIN * (spar + 1)],
                            st["xps"][:, WIN * spar:WIN * (spar + 1)],
                            st["tmp1"][:, C * spar:C * spar + WIN]),
                            reads=[bxp, btm], writes=[bqk], tag=f"3xa{o}_{w}_{sq}")
                        s.dma(ch_sp[4 + ((2 * w + sq) % 2)],
                              lambda e, o=o, tw=tw, spar=spar: e.dma_start(
                                  out=sdbg_d[o, :, tw:tw + WIN],
                                  in_=st["qkst"][:, 2 * WIN * spar + WIN:2 * WIN * (spar + 1)]),
                              reads=[bqk], tag=f"dsp{o}_{w}_{sq}")
                        continue
                    if S1B_MODE == "xpos4" and (is_q or is_k):
                        s.op("scalar", lambda e, pname=pname, spar=spar: e.activation(
                            st["qkst"][:, 2 * WIN * spar:2 * WIN * spar + WIN],
                            ps[pname][:], AF.Copy),
                            reads=[bp], writes=[bqk], tag=f"4qb{o}_{w}_{sq}")
                        s.op("tensor", lambda e, spar=spar: e.matmul(
                            ps["PD"][:], st["pswapb"][:],
                            st["qkst"][:, 2 * WIN * spar:2 * WIN * spar + WIN],
                            start=True, stop=True),
                            reads=[bqk, B["pswapb"]], writes=[B["PD"]],
                            tag=f"4rot{o}_{w}_{sq}")
                        btm = B[f"tmp1{spar}"]
                        s.op("vector", lambda e, tpar=tpar, spar=spar: e.tensor_mul(
                            st["tmp1"][:, C * spar:C * spar + WIN],
                            bass.AP(ps["PD"], 0, [[512, 128], [1, WIN]]),
                            st["tabs"][:, 2 * WIN * tpar + WIN:2 * WIN * (tpar + 1)]),
                            reads=[B["PD"], btab], writes=[btm], tag=f"4xs{o}_{w}_{sq}")
                        s.op("vector", lambda e, spar=spar: e.tensor_copy(
                            st["qkst"][:, 2 * WIN * spar + WIN:2 * WIN * (spar + 1)],
                            st["tmp1"][:, C * spar:C * spar + WIN]),
                            reads=[btm], writes=[bqk], tag=f"4cp{o}_{w}_{sq}")
                        s.dma(ch_sp[4 + ((2 * w + sq) % 2)],
                              lambda e, o=o, tw=tw, spar=spar: e.dma_start(
                                  out=sdbg_d[o, :, tw:tw + WIN],
                                  in_=st["qkst"][:, 2 * WIN * spar + WIN:2 * WIN * (spar + 1)]),
                              reads=[bqk], tag=f"dsp{o}_{w}_{sq}")
                        continue
                    if S1B_MODE == "xpos1" and (is_q or is_k):
                        s.op("vector", lambda e, pname=pname, tpar=tpar, spar=spar: e.tensor_mul(
                            st["xps"][:, WIN * spar:WIN * (spar + 1)], ps[pname][:],
                            st["tabs"][:, 2 * WIN * tpar:2 * WIN * tpar + WIN]),
                            reads=[bp, btab], writes=[bxp], tag=f"dxc{o}_{w}_{sq}")
                        s.op("scalar", lambda e, spar=spar: e.activation(
                            st["qkst"][:, 2 * WIN * spar:2 * WIN * spar + WIN],
                            st["xps"][:, WIN * spar:WIN * (spar + 1)], AF.Copy),
                            reads=[bxp], writes=[bqk], tag=f"dcp{o}_{w}_{sq}")
                        s.dma(ch_sp[4 + ((2 * w + sq) % 2)],
                              lambda e, o=o, tw=tw, spar=spar: e.dma_start(
                                  out=sdbg_d[o, :, tw:tw + WIN],
                                  in_=st["qkst"][:, 2 * WIN * spar:2 * WIN * spar + WIN]),
                              reads=[bqk], tag=f"dsp{o}_{w}_{sq}")
                        continue
                    if S1B_MODE == "xpos2" and (is_q or is_k):
                        s.op("scalar", lambda e, pname=pname, spar=spar: e.activation(
                            st["qkst"][:, 2 * WIN * spar:2 * WIN * spar + WIN],
                            ps[pname][:], AF.Copy),
                            reads=[bp], writes=[bqk], tag=f"qb{o}_{w}_{sq}")
                        s.op("tensor", lambda e, spar=spar, rname=rname: e.matmul(
                            ps[rname][:], st["pswapb"][:],
                            st["qkst"][:, 2 * WIN * spar:2 * WIN * spar + WIN],
                            start=True, stop=True),
                            reads=[bqk, B["pswapb"]], writes=[B[rname]],
                            tag=f"rot{o}_{w}_{sq}")
                        s.op("vector", lambda e, pname=pname, tpar=tpar, spar=spar: e.tensor_mul(
                            st["xps"][:, WIN * spar:WIN * (spar + 1)], ps[pname][:],
                            st["tabs"][:, 2 * WIN * tpar:2 * WIN * tpar + WIN]),
                            reads=[bp, btab], writes=[bxp], tag=f"xc{o}_{w}_{sq}")
                        btm = B[f"tmp1{spar}"]
                        s.op("vector", lambda e, tpar=tpar, spar=spar: e.tensor_mul(
                            st["tmp1"][:, C * spar:C * spar + WIN],
                            bass.AP(ps["PD"], 0, [[512, 128], [1, WIN]]),
                            st["tabs"][:, 2 * WIN * tpar + WIN:2 * WIN * (tpar + 1)]),
                            reads=[B["PD"], btab], writes=[btm], tag=f"xs{o}_{w}_{sq}")
                        s.op("vector", lambda e, spar=spar: e.tensor_add(
                            st["qkst"][:, 2 * WIN * spar + WIN:2 * WIN * (spar + 1)],
                            st["xps"][:, WIN * spar:WIN * (spar + 1)],
                            st["tmp1"][:, C * spar:C * spar + WIN]),
                            reads=[bxp, btm], writes=[bqk], tag=f"xa{o}_{w}_{sq}")
                        s.dma(ch_sp[4 + ((2 * w + sq) % 2)],
                              lambda e, o=o, tw=tw, spar=spar: e.dma_start(
                                  out=sdbg_d[o, :, tw:tw + WIN],
                                  in_=st["qkst"][:, 2 * WIN * spar + WIN:2 * WIN * (spar + 1)]),
                              reads=[bqk], tag=f"dsp{o}_{w}_{sq}")
                        continue
                    if S1B_MODE == "vff" and (is_q or is_k):
                        s.op("scalar", lambda e, pname=pname, spar=spar: e.activation(
                            st["qkst"][:, 2 * WIN * spar:2 * WIN * spar + WIN],
                            ps[pname][:], AF.Copy),
                            reads=[bp], writes=[bqk], tag=f"dqb{o}_{w}_{sq}")
                        s.dma(ch_sp[4 + ((2 * w + sq) % 2)],
                              lambda e, o=o, tw=tw, spar=spar: e.dma_start(
                                  out=sdbg_d[o, :, tw:tw + WIN],
                                  in_=st["qkst"][:, 2 * WIN * spar:2 * WIN * spar + WIN]),
                              reads=[bqk], tag=f"dsp{o}_{w}_{sq}")
                        continue
                    if is_q or is_k:
                        evac_op = s.op("scalar", lambda e, pname=pname, spar=spar: e.activation(
                            st["qkst"][:, 2 * WIN * spar:2 * WIN * spar + WIN],
                            pap(pname), AF.Copy),
                            reads=[bp], writes=[bqk], tag=f"qb{o}_{w}_{sq}")
                        s.op("tensor", lambda e, spar=spar, rname=rname: e.matmul(
                            ps[rname][:], st["pswapb"][:],
                            st["qkst"][:, 2 * WIN * spar:2 * WIN * spar + WIN],
                            start=True, stop=True),
                            reads=[bqk, B["pswapb"]], writes=[B[rname]],
                            tag=f"rot{o}_{w}_{sq}")
                        xc_op = s.op("vector", lambda e, pname=pname, tpar=tpar, spar=spar: e.tensor_mul(
                            st["xps"][:, WIN * spar:WIN * (spar + 1)], pap(pname),
                            st["tabs"][:, 2 * WIN * tpar:2 * WIN * tpar + WIN]),
                            reads=[bp, btab], writes=[bxp], tag=f"xc{o}_{w}_{sq}")

                        btm = B[f"tmp1{spar}"]
                        s.op("vector", lambda e, tpar=tpar, spar=spar, rname=rname: e.tensor_mul(
                            st["tmp1"][:, C * spar:C * spar + WIN], ps[rname][:],
                            st["tabs"][:, 2 * WIN * tpar + WIN:2 * WIN * (tpar + 1)]),
                            reads=[B[rname], btab], writes=[btm], tag=f"xs{o}_{w}_{sq}")
                        s.op("vector", lambda e, spar=spar: e.tensor_add(
                            st["qkst"][:, 2 * WIN * spar + WIN:2 * WIN * (spar + 1)],
                            st["xps"][:, WIN * spar:WIN * (spar + 1)],
                            st["tmp1"][:, C * spar:C * spar + WIN]),
                            reads=[bxp, btm], writes=[bqk], tag=f"xa{o}_{w}_{sq}")
                        dst_d = sq_d if is_q else sk_d
                        s.dma(ch_sp[4 + ((2 * w + sq) % 2)],
                              lambda e, dst_d=dst_d, sq=sq, cchunk=cchunk, twin=twin, spar=spar:
                              e.dma_start(
                                  out=dst_d[sq, 128 * cchunk:128 * (cchunk + 1),
                                            twin:twin + WIN],
                                  in_=st["qkst"][:, 2 * WIN * spar + WIN:
                                                 2 * WIN * (spar + 1)]),
                              reads=[bqk], tag=f"qko{o}_{w}_{sq}")
                    elif is_v:
                        s.op("scalar", lambda e, pname=pname, spar=spar: e.activation(
                            st["qkst"][:, 2 * WIN * spar:2 * WIN * spar + WIN],
                            pap(pname), AF.Copy),
                            reads=[bp], writes=[bqk], tag=f"vb{o}_{w}_{sq}")
                        for j in range(4):
                            s.op("tensor", lambda e, j=j, spar=spar: e.transpose(
                                ps["PT"][:, 128 * j:128 * (j + 1)],
                                st["qkst"][:, 2 * WIN * spar + 128 * j:
                                           2 * WIN * spar + 128 * (j + 1)],
                                st["identb"][:]),
                                reads=[bqk, B["identb"]], writes=[B["PT"]],
                                tag=f"vt{o}_{w}_{sq}_{j}")
                        s.op("scalar", lambda e, spar=spar: e.activation(
                            st["qkst"][:, 2 * WIN * spar + WIN:2 * WIN * (spar + 1)],
                            ps["PT"][:, 0:512], AF.Copy),
                            reads=[B["PT"]], writes=[bqk], tag=f"ve{o}_{w}_{sq}")
                        for j in range(4):
                            s.dma(ch_sp[6 + ((2 * w + sq + j) % 2)],
                                  lambda e, sq=sq, o=o, twin=twin, j=j, spar=spar:
                                  e.dma_start(
                                      out=sv_d[sq, twin + 128 * j:twin + 128 * (j + 1),
                                               128 * (o - 16):128 * (o - 15)],
                                      in_=st["qkst"][:, 2 * WIN * spar + WIN + 128 * j:
                                                     2 * WIN * spar + WIN + 128 * (j + 1)]),
                                  reads=[bqk], tag=f"vo{o}_{w}_{sq}_{j}")
                    else:
                        s.op("scalar", lambda e, pname=pname, spar=spar: e.activation(
                            st["qkst"][:, 2 * WIN * spar:2 * WIN * spar + WIN],
                            pap(pname), AF.Gelu),
                            reads=[bp], writes=[bqk], tag=f"gel{o}_{w}_{sq}")
                        s.dma(ch_sp[4 + ((2 * w + sq) % 2)],
                              lambda e, sq=sq, o=o, twin=twin, spar=spar: e.dma_start(
                                  out=sgel_d[sq, :, twin // 128:twin // 128 + 4,
                                             o - 24, :],
                                  in_=st["qkst"][:, 2 * WIN * spar:2 * WIN * spar + WIN]),
                              reads=[bqk], tag=f"go{o}_{w}_{sq}")
                if wq is not None and oi >= 16:
                    next(wq, None)
                    if oi >= 28:
                        next(wq, None)


    def carve_S2():
        sb_at("KK", (128, 8 * T), BF16, 0)
        sb_at("VV", (128, 16 * C), BF16, 32 * 1024)
        sb_at("Y", (128, 8 * T), F32, 64 * 1024)
        sb_at("QT", (128, 2 * WIN), BF16, 128 * 1024)
        sb_at("QB", (128, 2 * WIN), BF16, 159 * 1024)
        sb_at("SC", (128, 4 * WIN), BF16, 130 * 1024)
        sb_at("YT", (128, 2 * C), F32, 134 * 1024)
        sb_at("YSQT", (128, 2 * C), F32, 142 * 1024)
        sb_at("YN", (128, 2 * C), BF16, 150 * 1024)
        sb_at("MU", (128, 16 * 5), F32, 154 * 1024)
        sb_at("Y2S", (128, 2 * C), BF16, 155 * 1024)
        for nm in ("QT", "QB", "SC", "YT", "YSQT", "YN", "Y2S"):
            B[nm + "0"] = s.buf(nm + "0")
            B[nm + "1"] = s.buf(nm + "1")
        for nm in ("KK", "VV", "Y", "MU"):
            if nm not in B:
                B[nm] = s.buf(nm)
        for g in range(4):
            if f"Ycol{g}" not in B:
                B[f"Ycol{g}"] = s.buf(f"Ycol{g}")
        for sq in range(SEQ):
            for g in range(4):
                if f"Y2Asp{sq}_{g}" not in B:
                    B[f"Y2Asp{sq}_{g}"] = s.buf(f"Y2Asp{sq}_{g}")

    def S2(sq, gn_prev=None):
        if sq == 0:
            s.op("vector", lambda e: e.memset(st["QT"][64:128, :], 0.0),
                 writes=[B["QT0"], B["QT1"]], tag="qz0")
            s.op("vector", lambda e: e.memset(st["QB"][0:64, :], 0.0),
                 writes=[B["QB0"], B["QB1"]], tag="qz1")
        for cc in range(8):
            s.dma(ch_rd[cc % 2], lambda e, cc=cc, sq=sq: e.dma_start(
                out=st["KK"][:, T * cc:T * (cc + 1)],
                in_=sk_d[sq, 128 * cc:128 * (cc + 1), :]),
                writes=[B["KK"]], tag=f"Kld{sq}_{cc}")
        for stile in range(16):
            s.dma(ch_rd[2 + stile % 2], lambda e, stile=stile, sq=sq: e.dma_start(
                out=st["VV"][:, C * stile:C * (stile + 1)],
                in_=sv_d[sq, 128 * stile:128 * (stile + 1), :]),
                writes=[B["VV"]], tag=f"Vld{sq}_{stile}")
        scA = ["PA", "PB"]
        scB = ["PC", "PC"]
        for w in range(4):
            if gn_prev is not None:
                next(gn_prev, None)
            for hp in range(8):
                qpar = (4 * w + hp) % 2
                bq = B[f"QT{qpar}"]
                bqb = B[f"QB{qpar}"]
                s.dma(ch_rd[qpar], lambda e, sq=sq, hp=hp, w=w, qpar=qpar: e.dma_start(
                    out=st["QT"][0:64, WIN * qpar:WIN * (qpar + 1)],
                    in_=sq_d[sq, 128 * hp:128 * hp + 64, WIN * w:WIN * (w + 1)]),
                    writes=[bq], tag=f"Qld{sq}_{w}_{hp}")
                s.dma(ch_rd[qpar], lambda e, sq=sq, hp=hp, w=w, qpar=qpar: e.dma_start(
                    out=st["QB"][64:128, WIN * qpar:WIN * (qpar + 1)],
                    in_=sq_d[sq, 128 * hp + 64:128 * (hp + 1), WIN * w:WIN * (w + 1)]),
                    writes=[bqb], tag=f"Qlb{sq}_{w}_{hp}")
                nst = 4 * w + 4
                avdef = [None]
                for stt in range(nst):
                    pa = scA[stt % 2]
                    pb = scB[stt % 2]
                    s.op("tensor", lambda e, pa=pa, hp=hp, stt=stt, qpar=qpar: e.matmul(
                        ps[pa][:], st["KK"][:, T * hp + 128 * stt:T * hp + 128 * (stt + 1)],
                        st["QT"][:, WIN * qpar:WIN * (qpar + 1)],
                        start=True, stop=True),
                        reads=[B["KK"], bq], writes=[B[pa]], tag=f"sMA{sq}_{w}_{hp}_{stt}")
                    s.op("tensor", lambda e, pb=pb, hp=hp, stt=stt, qpar=qpar: e.matmul(
                        ps[pb][:], st["KK"][:, T * hp + 128 * stt:T * hp + 128 * (stt + 1)],
                        st["QB"][:, WIN * qpar:WIN * (qpar + 1)],
                        start=True, stop=True),
                        reads=[B["KK"], bqb], writes=[B[pb]], tag=f"sMB{sq}_{w}_{hp}_{stt}")
                    diag = stt >= 4 * w
                    r = stt - 4 * w
                    bsc = B[f"SC{stt % 2}"]
                    if diag:
                        s.op("vector", lambda e, pa=pa, stt=stt, r=r: e.tensor_mul(
                            st["SC"][:, WIN * (stt % 2):WIN * (stt % 2 + 1)],
                            ps[pa][:], st["tri"][:, WIN * r:WIN * (r + 1)]),
                            reads=[B[pa], B["tri"]], writes=[bsc], tag=f"eA{sq}_{w}_{hp}_{stt}")
                        s.op("vector", lambda e, pb=pb, stt=stt, r=r: e.tensor_mul(
                            st["SC"][:, WIN * (2 + stt % 2):WIN * (3 + stt % 2)],
                            ps[pb][:], st["tri"][:, WIN * r:WIN * (r + 1)]),
                            reads=[B[pb], B["tri"]], writes=[bsc], tag=f"eB{sq}_{w}_{hp}_{stt}")
                    else:
                        s.op("scalar", lambda e, pa=pa, stt=stt: e.activation(
                            st["SC"][:, WIN * (stt % 2):WIN * (stt % 2 + 1)],
                            ps[pa][:], AF.Copy, scale=0.125),
                            reads=[B[pa]], writes=[bsc], tag=f"eA{sq}_{w}_{hp}_{stt}")
                        s.op("vector", lambda e, pb=pb, stt=stt: e.tensor_scalar(
                            st["SC"][:, WIN * (2 + stt % 2):WIN * (3 + stt % 2)],
                            ps[pb][:], 0.125, None, ALU.mult),
                            reads=[B[pb]], writes=[bsc], tag=f"eB{sq}_{w}_{hp}_{stt}")
                    if avdef[0] is not None:
                        avdef[0]()
                    def avblock(hp=hp, stt=stt, nst=nst):
                        s.op("tensor", lambda e, hp=hp, stt=stt, nst=nst: e.matmul(
                            ps["PE2"][0:64, 0:512],
                            st["VV"][:, C * stt + 128 * hp:C * stt + 128 * hp + 64],
                            st["SC"][:, WIN * (stt % 2):WIN * (stt % 2 + 1)],
                            start=(stt == 0), stop=(stt == nst - 1)),
                            reads=[B["VV"], B[f"SC{stt % 2}"]], writes=[B["PE2A"]],
                            tag=f"aMA{sq}_{w}_{hp}_{stt}")
                        s.op("tensor", lambda e, hp=hp, stt=stt, nst=nst: e.matmul(
                            ps["PE2"][0:64, 512:1024],
                            st["VV"][:, C * stt + 128 * hp + 64:C * stt + 128 * (hp + 1)],
                            st["SC"][:, WIN * (2 + stt % 2):WIN * (3 + stt % 2)],
                            start=(stt == 0), stop=(stt == nst - 1)),
                            reads=[B["VV"], B[f"SC{stt % 2}"]], writes=[B["PE2B"]],
                            tag=f"aMB{sq}_{w}_{hp}_{stt}")
                    avdef[0] = avblock
                avdef[0]()
                s.op("scalar", lambda e, hp=hp, w=w: e.activation(
                    st["Y"][0:64, T * hp + WIN * w:T * hp + WIN * (w + 1)],
                    ps["PE2"][0:64, 0:512], AF.Copy),
                    reads=[B["PE2A"]], writes=[B[f"Ycol{w}"]], tag=f"yA{sq}_{w}_{hp}")
                s.op("scalar", lambda e, hp=hp, w=w: e.activation(
                    st["Y"][64:128, T * hp + WIN * w:T * hp + WIN * (w + 1)],
                    ps["PE2"][0:64, 512:1024], AF.Copy),
                    reads=[B["PE2B"]], writes=[B[f"Ycol{w}"]], tag=f"yB{sq}_{w}_{hp}")
    def GN_gen(sq):
        for tt in range(16):
            par = tt % 2
            byt, bysq, byn, by2 = (B[f"YT{par}"], B[f"YSQT{par}"],
                                   B[f"YN{par}"], B[f"Y2S{par}"])
            byc = B[f"Ycol{tt // 4}"]
            for gg in range(2):
                pname = "PD" if gg == 0 else "PG"
                for j in range(4):
                    cc = 4 * gg + j
                    s.op("tensor", lambda e, cc=cc, j=j, tt=tt, pname=pname: e.transpose(
                        ps[pname][:, 128 * j:128 * (j + 1)],
                        st["Y"][:, T * cc + 128 * tt:T * cc + 128 * (tt + 1)],
                        st["identf"][:]),
                        reads=[byc, B["identf"]], writes=[B[pname]],
                        tag=f"gt{sq}_{tt}_{cc}")
                s.op("scalar", lambda e, gg=gg, par=par, pname=pname: e.activation(
                    st["YT"][:, C * par + 512 * gg:C * par + 512 * (gg + 1)],
                    ps[pname][:], AF.Copy),
                    reads=[B[pname]], writes=[byt], tag=f"ge{sq}_{tt}_{gg}")
            s.op("vector", lambda e, par=par, tt=tt: e.reduce_sum(
                st["MU"][:, 16 * (tt % 2):16 * (tt % 2) + 16],
                bass.AP(st["YT"], C * par, [[2 * C, 128], [64, 16], [1, 64]]), AX),
                reads=[byt, B["MU"]], writes=[B["MU"]], tag=f"gmu{sq}_{tt}")
            s.op("scalar", lambda e, par=par: e.activation(
                st["YSQT"][:, C * par:C * (par + 1)],
                st["YT"][:, C * par:C * (par + 1)], AF.Square),
                reads=[byt], writes=[bysq], tag=f"gsq{sq}_{tt}")
            s.op("vector", lambda e, par=par, tt=tt: e.reduce_sum(
                st["MU"][:, 32 + 16 * (tt % 2):32 + 16 * (tt % 2) + 16],
                bass.AP(st["YSQT"], C * par, [[2 * C, 128], [64, 16], [1, 64]]), AX),
                reads=[bysq, B["MU"]], writes=[B["MU"]], tag=f"gs2{sq}_{tt}")
            mcol = slice(16 * (tt % 2), 16 * (tt % 2) + 16)
            scol = slice(32 + 16 * (tt % 2), 32 + 16 * (tt % 2) + 16)
            s.op("vector", lambda e, mcol=mcol: e.tensor_scalar(
                st["MU"][:, mcol], st["MU"][:, mcol], 1.0 / 64.0, None, ALU.mult),
                reads=[B["MU"]], writes=[B["MU"]], tag=f"gm1{sq}_{tt}")
            s.op("vector", lambda e, mcol=mcol, scol=scol: e.tensor_mul(
                st["MU"][:, 64:80], st["MU"][:, mcol], st["MU"][:, mcol]),
                reads=[B["MU"]], writes=[B["MU"]], tag=f"gm2{sq}_{tt}")
            s.op("vector", lambda e, scol=scol: e.tensor_scalar(
                st["MU"][:, scol], st["MU"][:, scol], 1.0 / 64.0, GN_EPS, ALU.mult, ALU.add),
                reads=[B["MU"]], writes=[B["MU"]], tag=f"gm3{sq}_{tt}")
            s.op("vector", lambda e, scol=scol: e.tensor_sub(
                st["MU"][:, scol], st["MU"][:, scol], st["MU"][:, 64:80]),
                reads=[B["MU"]], writes=[B["MU"]], tag=f"gm4{sq}_{tt}")
            s.op("scalar", lambda e, scol=scol: e.activation(
                st["MU"][:, 64:80], st["MU"][:, scol], AF.Sqrt),
                reads=[B["MU"]], writes=[B["MU"]], tag=f"gm7{sq}_{tt}")
            s.op("vector", lambda e: e.reciprocal(st["MU"][:, 64:80], st["MU"][:, 64:80]),
                reads=[B["MU"]], writes=[B["MU"]], tag=f"gm8{sq}_{tt}")
            for h in range(16):
                s.op("vector", lambda e, h=h, par=par, mcol=mcol: e.tensor_scalar(
                    st["YN"][:, C * par + 64 * h:C * par + 64 * (h + 1)],
                    st["YT"][:, C * par + 64 * h:C * par + 64 * (h + 1)],
                    st["MU"][:, mcol.start + h:mcol.start + h + 1],
                    st["MU"][:, 64 + h:64 + h + 1], ALU.subtract, ALU.mult),
                    reads=[byt, B["MU"]], writes=[byn], tag=f"gn{sq}_{tt}_{h}")
            for cc in range(8):
                s.op("tensor", lambda e, cc=cc, par=par: e.transpose(
                    ps["PT"][:, 128 * cc:128 * (cc + 1)],
                    st["YN"][:, C * par + 128 * cc:C * par + 128 * (cc + 1)],
                    st["identb"][:]),
                    reads=[byn, B["identb"]], writes=[B["PT"]], tag=f"gb{sq}_{tt}_{cc}")
            for cc in range(8):
                s.op("vector", lambda e, cc=cc, par=par: e.tensor_scalar(
                    st["Y2S"][:, C * par + 128 * cc:C * par + 128 * (cc + 1)],
                    ps["PT"][:, 128 * cc:128 * (cc + 1)],
                    st["vecs"][:, 16 + cc:17 + cc], st["vecs"][:, 24 + cc:25 + cc],
                    ALU.mult, ALU.add),
                    reads=[B["PT"], B["vecs"]], writes=[by2], tag=f"ga{sq}_{tt}_{cc}")
            s.dma(ch_sp[4 + (tt % 2)], lambda e, tt=tt, sq=sq, par=par:
                  e.dma_start(
                      out=sy2a_d[sq, :, :, 128 * tt:128 * (tt + 1)],
                      in_=st["Y2S"][:, C * par:C * (par + 1)]),
                  reads=[by2], writes=[B[f"Y2Asp{sq}_{tt // 4}"]], tag=f"gsp{sq}_{tt}")
            if tt % 4 == 3:
                yield

    def carve_S4():
        sb_at("WGT", (128, 64 * 128), BF16, 0)
        sb_at("XNM", (128, 2 * 8 * WIN), BF16, 16 * 1024)
        sb_at("YH", (128, 2 * WIN), BF16, 32 * 1024)
        sb_at("GS", (128, 2 * WIN), BF16, 34 * 1024)
        sb_at("Y2O", (128, 2 * WIN), BF16, 36 * 1024)
        for nm in ("XNM", "YH", "GS", "Y2O"):
            B[nm + "0"] = s.buf(nm + "0")
            B[nm + "1"] = s.buf(nm + "1")
        B["WGT"] = s.buf("WGT")

    def S4(gn=None):
        for o in range(8):
            s.dma(ch_w[o % 2], lambda e, o=o: e.dma_start(
                out=st["WGT"][:, 1024 * o:1024 * (o + 1)], in_=wgt_d[o]),
                writes=[B["WGT"]], tag=f"wgld{o}")
        pp = ["PA", "PB", "PC"]
        for sq in range(SEQ):
            for w in range(4):
                if gn is not None and sq == 0:
                    next(gn, None)
                wpar = (4 * sq + w) % 2
                tw = T * sq + WIN * w
                bxn = B[f"XNM{wpar}"]
                for cc in range(8):
                    s.dma(ch_rd[cc % 2], lambda e, cc=cc, tw=tw, wpar=wpar: e.dma_start(
                        out=st["XNM"][:, 8 * WIN * wpar + WIN * cc:
                                      8 * WIN * wpar + WIN * (cc + 1)],
                        in_=sxn_d[:, cc, tw:tw + WIN]),
                        writes=[bxn], tag=f"xnl{sq}_{w}_{cc}")
                for o in range(8):
                    par = o % 2
                    p3 = o % 3
                    byh, bgs, by2 = B[f"YH{par}"], B[f"GS{par}"], B[f"Y2O{par}"]
                    s.dma(ch_rd[2 + par], lambda e, o=o, sq=sq, w=w, par=par: e.dma_start(
                        out=st["YH"][:, WIN * par:WIN * (par + 1)],
                        in_=sy2a_d[sq, :, o, WIN * w:WIN * (w + 1)]),
                        reads=[B[f"Y2Asp{sq}_{w}"]],
                        writes=[byh], tag=f"yhl{o}_{sq}_{w}")
                    pname = pp[p3]
                    for cc in range(8):
                        s.op("tensor", lambda e, pname=pname, o=o, cc=cc, wpar=wpar:
                             e.matmul(ps[pname][:],
                                      st["WGT"][:, 1024 * o + 128 * cc:
                                                 1024 * o + 128 * (cc + 1)],
                                      st["XNM"][:, 8 * WIN * wpar + WIN * cc:
                                                 8 * WIN * wpar + WIN * (cc + 1)],
                                      start=(cc == 0), stop=(cc == 7)),
                             reads=[B["WGT"], bxn], writes=[B[pname]],
                             tag=f"gmm{o}_{sq}_{w}_{cc}")
                    s.op("scalar", lambda e, pname=pname, o=o, par=par: e.activation(
                        st["GS"][:, WIN * par:WIN * (par + 1)], ps[pname][:],
                        AF.Silu, bias=st["vecs"][:, o:o + 1]),
                        reads=[B[pname], B["vecs"]], writes=[bgs], tag=f"gsl{o}_{sq}_{w}")
                    s.op("vector", lambda e, par=par: e.tensor_mul(
                        st["Y2O"][:, WIN * par:WIN * (par + 1)],
                        st["GS"][:, WIN * par:WIN * (par + 1)],
                        st["YH"][:, WIN * par:WIN * (par + 1)]),
                        reads=[bgs, byh], writes=[by2], tag=f"gy2{o}_{sq}_{w}")
                    s.dma(ch_sp[par], lambda e, o=o, tw=tw, par=par: e.dma_start(
                        out=sy2_d[:, tw // 128:tw // 128 + 4, o, :],
                        in_=st["Y2O"][:, WIN * par:WIN * (par + 1)]),
                        reads=[by2], tag=f"y2o{o}_{sq}_{w}")

    def carve_S5():
        sb_at("WGQ", (128, 8 * INNER), BF16, 0)
        sb_at("WUQ", (128, 8 * INNER), BF16, 64 * 1024)
        sb_at("XQS", (128, 2 * 1024), BF16, 128 * 1024)
        sb_at("HH", (128, INNER), F32, 132 * 1024)
        sb_at("VT", (128, INNER), F32, 148 * 1024)
        sb_at("HQT", (128, INNER), BF16, 164 * 1024)
        sb_at("HQS", (128, 1024), BF16, 172 * 1024)
        for nm in ("XQS",):
            B[nm + "0"] = s.buf(nm + "0")
            B[nm + "1"] = s.buf(nm + "1")
        for nm in ("WGQ", "WUQ", "HH", "VT", "HQT", "HQS"):
            B[nm] = s.buf(nm)

    def S5():
        for cc in range(8):
            s.dma(ch_rd[cc % 2], lambda e, cc=cc: e.dma_start(
                out=st["WGQ"][:, INNER * cc:INNER * (cc + 1)],
                in_=swgq_d[128 * cc:128 * (cc + 1), :]),
                writes=[B["WGQ"]], tag=f"wgq{cc}")
            s.dma(ch_rd[2 + cc % 2], lambda e, cc=cc: e.dma_start(
                out=st["WUQ"][:, INNER * cc:INNER * (cc + 1)],
                in_=swuq_d[128 * cc:128 * (cc + 1), :]),
                writes=[B["WUQ"]], tag=f"wuq{cc}")
        tc = st["tmpc"]
        tdefer = [None]
        for i in range(NTT):
            par = i % 2
            bxq = B[f"XQS{par}"]
            s.dma(ch_in[par], lambda e, i=i, par=par: e.dma_start(
                out=st["XQS"][:, 1024 * par:1024 * (par + 1)], in_=sxq_d[i]),
                writes=[bxq], tag=f"xql{i}")
            s.op("vector", lambda e, i=i: e.tensor_scalar(
                tc[:, 0:1], st["stats"][:, 4 * NTT + i:4 * NTT + i + 1],
                st["wsc"][:, 0:1], None, ALU.mult),
                reads=[B["stats"], B["wsc"], B["tmpc"]], writes=[B["tmpc"]], tag=f"dqg{i}")
            s.op("vector", lambda e, i=i: e.tensor_scalar(
                tc[:, 1:2], st["stats"][:, 4 * NTT + i:4 * NTT + i + 1],
                st["wsc"][:, 1:2], None, ALU.mult),
                reads=[B["stats"], B["wsc"], B["tmpc"]], writes=[B["tmpc"]], tag=f"dqu{i}")
            pg = ["PA", "PC"]
            pu = ["PB", "PD", "PG"]
            for iw in range(8):
                psg, psu = pg[iw % 2], pu[iw % 3]
                for cc in range(8):
                    s.op("tensor", lambda e, psg=psg, cc=cc, iw=iw, par=par: e.matmul(
                        pap(psg),
                        st["XQS"][:, 1024 * par + 128 * cc:1024 * par + 128 * (cc + 1)],
                        st["WGQ"][:, INNER * cc + WIN * iw:INNER * cc + WIN * (iw + 1)],
                        start=(cc == 0), stop=(cc == 7)),
                        reads=[bxq, B["WGQ"]], writes=[B[psg]], tag=f"fg{i}_{iw}_{cc}")
                    s.op("tensor", lambda e, psu=psu, cc=cc, iw=iw, par=par: e.matmul(
                        pap(psu),
                        st["XQS"][:, 1024 * par + 128 * cc:1024 * par + 128 * (cc + 1)],
                        st["WUQ"][:, INNER * cc + WIN * iw:INNER * cc + WIN * (iw + 1)],
                        start=(cc == 0), stop=(cc == 7)),
                        reads=[bxq, B["WUQ"]], writes=[B[psu]], tag=f"fu{i}_{iw}_{cc}")
                s.op("scalar", lambda e, psg=psg, iw=iw: e.activation(
                    st["VT"][:, WIN * iw:WIN * (iw + 1)], pap(psg), AF.Silu,
                    scale=tc[:, 0:1]),
                    reads=[B[psg], B["tmpc"]], writes=[B["VT"]], tag=f"fsl{i}_{iw}")
                s.op("vector", lambda e, psu=psu, iw=iw: e.tensor_mul(
                    st["HH"][:, WIN * iw:WIN * (iw + 1)],
                    st["VT"][:, WIN * iw:WIN * (iw + 1)], pap(psu)),
                    reads=[B[psu], B["VT"]], writes=[B["HH"]], tag=f"fh{i}_{iw}")
            if tdefer[0] is not None:
                tdefer[0]()
            s.op("vector", lambda e: e.reduce_max(tc[:, 2:3], st["HH"][:], AX,
                                                  apply_absolute_value=True),
                reads=[B["HH"], B["tmpc"]], writes=[B["tmpc"]], tag=f"fam{i}")
            s.op("vector", lambda e: e.tensor_mul(tc[:, 2:3], tc[:, 2:3], tc[:, 1:2]),
                reads=[B["tmpc"]], writes=[B["tmpc"]], tag=f"fm1{i}")
            s.op("vector", lambda e: e.tensor_scalar_max(tc[:, 2:3], tc[:, 2:3], 1e-5),
                reads=[B["tmpc"]], writes=[B["tmpc"]], tag=f"fm2{i}")
            s.op("vector", lambda e: e.reciprocal(tc[:, 3:4], tc[:, 2:3]),
                reads=[B["tmpc"]], writes=[B["tmpc"]], tag=f"fm3{i}")
            s.op("vector", lambda e: e.tensor_scalar(
                tc[:, 4:5], tc[:, 3:4], 127.0, None, ALU.mult),
                reads=[B["tmpc"]], writes=[B["tmpc"]], tag=f"fm4{i}")
            s.op("vector", lambda e: e.tensor_mul(tc[:, 4:5], tc[:, 4:5], tc[:, 1:2]),
                reads=[B["tmpc"]], writes=[B["tmpc"]], tag=f"fm5{i}")
            s.op("vector", lambda e: e.tensor_scalar(
                tc[:, 5:6], tc[:, 2:3], 1.0 / 127.0, None, ALU.mult),
                reads=[B["tmpc"]], writes=[B["tmpc"]], tag=f"fm6{i}")
            s.op("vector", lambda e: e.tensor_scalar(
                tc[:, 5:6], tc[:, 5:6], st["wsc"][:, 2:3], None, ALU.mult),
                reads=[B["tmpc"], B["wsc"]], writes=[B["tmpc"]], tag=f"fm7{i}")
            s.op("vector", lambda e: e.tensor_scalar(
                st["VT"][:], st["HH"][:], tc[:, 4:5], st["ceff"][:, 1:2],
                ALU.mult, ALU.add),
                reads=[B["HH"], B["tmpc"], B["ceff"]], writes=[B["VT"]], tag=f"fq1{i}")
            s.op("vector", lambda e: e.tensor_scalar(
                st["HH"][:], st["VT"][:], st["ceff"][:, 1:2], None, ALU.subtract),
                reads=[B["VT"], B["ceff"]], writes=[B["HH"]], tag=f"fq2{i}")
            hqt_t = st["HQT"]
            hqt_b = B["HQT"]
            s.op("vector", lambda e, hqt_t=hqt_t: e.tensor_scalar(
                hqt_t[:], st["HH"][:], tc[:, 5:6], None, ALU.mult),
                reads=[B["HH"], B["tmpc"]], writes=[hqt_b], tag=f"fq3{i}")

            def tblock(i=i, hqt_t=hqt_t, hqt_b=hqt_b):
                for r in range(4):
                    for g in range(8):
                        ic = 8 * r + g
                        s.op("tensor", lambda e, ic=ic, g=g, hqt_t=hqt_t: e.transpose(
                            ps["PT"][:, 128 * g:128 * (g + 1)],
                            hqt_t[:, 128 * ic:128 * (ic + 1)], st["identb"][:]),
                            reads=[hqt_b, B["identb"]], writes=[B["PT"]],
                            tag=f"ftr{i}_{ic}")
                    s.op("vector", lambda e: e.tensor_copy(st["HQS"][:], ps["PT"][:]),
                         reads=[B["PT"]], writes=[B["HQS"]], tag=f"fte{i}_{r}")
                    s.dma(ch_sp[4 + r % 2], lambda e, i=i, r=r: e.dma_start(
                        out=shq_d[i, r], in_=st["HQS"][:]),
                        reads=[B["HQS"]], tag=f"fsp{i}_{r}")
            tdefer[0] = tblock

        tdefer[0]()

    def carve_S6():
        sb_at("WDQ", (128, 32 * 1024), BF16, 0)
        sb_at("WPT", (128, 8 * 1024), BF16, 64 * 1024)
        sb_at("WFT", (128, 16 * 1024), BF16, 80 * 1024)
        sb_at("HQ6", (128, 2 * 4096), BF16, 112 * 1024)
        sb_at("Y2T", (128, 2 * 1024), BF16, 128 * 1024)
        sb_at("GET", (128, 2 * 2048), BF16, 132 * 1024)
        sb_at("XS", (128, 2 * 1024), F32, 140 * 1024)
        sb_at("OUTS", (128, 2 * 1024), F32, 148 * 1024)
        for nm in ("HQ6", "Y2T", "GET", "XS", "OUTS"):
            B[nm + "0"] = s.buf(nm + "0")
            B[nm + "1"] = s.buf(nm + "1")
        for nm in ("WDQ", "WPT", "WFT"):
            B[nm] = s.buf(nm)

    def S6():
        for ic in range(32):
            s.dma(ch_rd[ic % 2], lambda e, ic=ic: e.dma_start(
                out=st["WDQ"][:, 1024 * ic:1024 * (ic + 1)],
                in_=swdq_d[128 * ic:128 * (ic + 1), :]),
                writes=[B["WDQ"]], tag=f"wdq{ic}")
        for cc in range(8):
            s.dma(ch_w[cc % 2], lambda e, cc=cc: e.dma_start(
                out=st["WPT"][:, 1024 * cc:1024 * (cc + 1)],
                in_=wpt_d[128 * cc:128 * (cc + 1), :]),
                writes=[B["WPT"]], tag=f"wptl{cc}")
        for ec in range(16):
            s.dma(ch_w[ec % 2], lambda e, ec=ec: e.dma_start(
                out=st["WFT"][:, 1024 * ec:1024 * (ec + 1)],
                in_=wft_d[128 * ec:128 * (ec + 1), :]),
                writes=[B["WFT"]], tag=f"wftl{ec}")
        tc = st["tmpc"]
        odefer = [None]
        for i in range(NTT):
            par = i % 2
            sq, tt = i // 16, i % 16
            bhq, by2, bge, bxs, bo = (B[f"HQ6{par}"], B[f"Y2T{par}"],
                                      B[f"GET{par}"], B[f"XS{par}"], B[f"OUTS{par}"])
            s.dma(ch_in[par], lambda e, i=i, par=par: e.dma_start(
                out=st["XS"][:, 1024 * par:1024 * (par + 1)],
                in_=x_d[128 * i:128 * (i + 1), :]),
                writes=[bxs], tag=f"oxl{i}")
            for r in range(4):
                s.dma(ch_rd[2 + r % 2], lambda e, i=i, r=r, par=par: e.dma_start(
                    out=st["HQ6"][:, 4096 * par + 1024 * r:4096 * par + 1024 * (r + 1)],
                    in_=shq_d[i, r]),
                    writes=[bhq], tag=f"ohl{i}_{r}")
            s.dma(ch_rd[i % 2], lambda e, i=i, par=par: e.dma_start(
                out=st["Y2T"][:, 1024 * par:1024 * (par + 1)],
                in_=sy2_d[:, i, :, :]),
                writes=[by2], tag=f"oyl{i}")
            s.dma(ch_sp[6 + i % 2], lambda e, sq=sq, tt=tt, par=par: e.dma_start(
                out=st["GET"][:, 2048 * par:2048 * (par + 1)],
                in_=sgel_d[sq, :, tt, :, :]),
                writes=[bge], tag=f"ogl{i}")
            if par == 0:
                ph = [("PE2", 0), ("PE2", 512)]
                bps = [B["PE2A"], B["PE2B"]]
            else:
                ph = [("PA", 0), ("PB", 0)]
                bps = [B["PA"], B["PB"]]

            def mmpair(lhs_ap, rhs_base, rhs_tensor, brd, bstat, start, stop, tag):
                for hh in range(2):
                    pn, poff = ph[hh]
                    s.op("tensor", lambda e, pn=pn, poff=poff, lhs_ap=lhs_ap,
                         rhs_base=rhs_base, rhs_tensor=rhs_tensor, hh=hh,
                         start=start, stop=stop:
                         e.matmul(ps[pn][:, poff:poff + 512], lhs_ap(),
                                  rhs_tensor[:, rhs_base + 512 * hh:rhs_base + 512 * (hh + 1)],
                                  start=start, stop=stop),
                         reads=[brd, bstat],
                         writes=[bps[hh]], tag=f"{tag}_{hh}")

            for hh in range(2):
                pn, poff = ph[hh]
                s.op("tensor", lambda e, pn=pn, poff=poff, hh=hh: e.matmul(
                    ps[pn][:, poff:poff + 512], st["onesrow"][0:1, :],
                    st["rowbuf"][0:1, 512 * hh:512 * (hh + 1)],
                    start=True, stop=False),
                    reads=[B["onesrow"], B["rowbuf"]], writes=[bps[hh]],
                    tag=f"obias{i}_{hh}")
            for cc in range(8):
                mmpair(lambda cc=cc, par=par: st["Y2T"][:, 1024 * par + 128 * cc:
                                                        1024 * par + 128 * (cc + 1)],
                       1024 * cc, st["WPT"], B["WPT"], by2, False, False, f"omp{i}_{cc}")
            for ec in range(16):
                mmpair(lambda ec=ec, par=par: st["GET"][:, 2048 * par + 128 * ec:
                                                        2048 * par + 128 * (ec + 1)],
                       1024 * ec, st["WFT"], B["WFT"], bge, False, False, f"omf{i}_{ec}")
            for ic in range(32):
                mmpair(lambda ic=ic, par=par: st["HQ6"][:, 4096 * par + 128 * ic:
                                                         4096 * par + 128 * (ic + 1)],
                       1024 * ic, st["WDQ"], B["WDQ"], bhq, False, ic == 31, f"omd{i}_{ic}")
            if odefer[0] is not None:
                odefer[0]()
            def oblock(i=i, par=par, ph=ph, bps=bps, bxs=bxs, bo=bo):
                s.op("vector", lambda e, i=i: e.tensor_scalar(
                    tc[:, 8:9], st["stats"][:, NTT + i:NTT + i + 1], 1.0, None, ALU.add),
                    reads=[B["stats"], B["tmpc"]], writes=[B["tmpc"]], tag=f"oc1{i}")
                s.op("vector", lambda e, i=i: e.tensor_mul(
                    tc[:, 9:10], st["stats"][:, i:i + 1], st["stats"][:, NTT + i:NTT + i + 1]),
                    reads=[B["stats"], B["tmpc"]], writes=[B["tmpc"]], tag=f"oc2{i}")
                s.op("vector", lambda e, par=par: e.tensor_scalar(
                    st["OUTS"][:, 1024 * par:1024 * (par + 1)],
                    st["XS"][:, 1024 * par:1024 * (par + 1)],
                    tc[:, 8:9], tc[:, 9:10], ALU.mult, ALU.subtract),
                    reads=[bxs, B["tmpc"]], writes=[bo], tag=f"oc3{i}")
                for hh in range(2):
                    pn, poff = ph[hh]
                    s.op("vector", lambda e, pn=pn, poff=poff, par=par, hh=hh: e.tensor_add(
                        st["OUTS"][:, 1024 * par + 512 * hh:1024 * par + 512 * (hh + 1)],
                        st["OUTS"][:, 1024 * par + 512 * hh:1024 * par + 512 * (hh + 1)],
                        ps[pn][:, poff:poff + 512]),
                        reads=[bps[hh], bo], writes=[bo], tag=f"oadd{i}_{hh}")
                s.dma(ch_out[par], lambda e, i=i, par=par: e.dma_start(
                    out=out_d[128 * i:128 * (i + 1), :],
                    in_=st["OUTS"][:, 1024 * par:1024 * (par + 1)]),
                    reads=[bo], tag=f"oout{i}")
            odefer[0] = oblock
        odefer[0]()

    ORDER = ["S0", "S1a", "S1b", "S2", "S4", "S5", "S6"]
    stage_n = ORDER.index(upto)
    S0_consts()
    S0_wquant_late = stage_n >= 2
    if not S0_wquant_late:
        S0_wquant()
    s.barrier("bar_S0")
    carve_S1()
    if stage_n == 1:
        S1a()
    if stage_n >= 2:
        wqg = S0_wquant_gen()
        s1ag = S1a_gen()
        S1b2(wq=wqg, s1a=s1ag)
        for _ in s1ag:
            pass
        for _ in wqg:
            pass
    if stage_n >= 3:
        s.barrier("bar_S2")
        carve_S2()
        S2(0)
        gn0 = GN_gen(0)
        S2(1, gn_prev=gn0)
        for _ in gn0:
            pass
        gn1 = GN_gen(1)
        if stage_n == 3:
            for _ in gn1:
                pass
    if stage_n >= 4:
        s.barrier("bar_S4")
        carve_S4()
        S4(gn=gn1)
        for _ in gn1:
            pass
    if stage_n >= 5:
        s.barrier("bar_S5")
        carve_S5()
        S5()
    if stage_n >= 6:
        s.barrier("bar_S6")
        carve_S6()
        S6()

    if debug:
        s.barrier("bar_dbg")
        bdb = s.buf("dbg")
        for k, name in enumerate(debug):
            src = dr[name]
            shp = list(src.tensor.shape)
            dt = src.tensor.dtype
            o = nc.dram_tensor("dbg_" + name, shp, dt, kind="ExternalOutput").ap()
            n = int(np.prod(shp))
            s.dma(ch_out[k % 2], lambda e, src=src, o=o, n=n: e.dma_start(
                out=bass.AP(o.tensor, 0, [[n, 1], [1, n]]),
                in_=bass.AP(src.tensor, 0, [[n, 1], [1, n]])),
                reads=[bdb], tag=f"dbg_{name}")

    if stage_n < 6:
        s.barrier("bar_out")
        bz = s.buf("zout")
        s.op("vector", lambda e: e.memset(st["rmsb"][:], 0.0),
             reads=[B["rmsb"]], writes=[bz], tag="zz")
        for i in range(NTT):
            s.dma(ch_out[i % 2], lambda e, i=i: e.dma_start(
                out=bass.AP(out_d.tensor, 128 * i * C, [[C, 128], [1, C]]),
                in_=st["rmsb"][:]),
                reads=[bz], tag=f"zo{i}")

    s.barrier("bar_final")

    with ExitStack() as stack:
        e = stack.enter_context
        sems = {}
        for eng in ENGINES:
            sems[("eng", eng)] = e(nc.semaphore(f"sem_{eng}"))
        for c in s.chans:
            sems[("chan", c.name)] = e(nc.semaphore(f"semc_{c.name}"))
        block = e(nc.Block())
        s.emit(block, sems)
    return nc


def _prep(inputs):
    consts = _host_consts()
    tiled = {
        "w1t": _tile_oc(np.ascontiguousarray(
            np.asarray(inputs["L0_attn_w_qkvff"], dtype=np.float32).T), 40, 8),
        "wgt": _tile_oc(np.ascontiguousarray(
            np.asarray(inputs["L0_attn_w_gated"], dtype=np.float32).T), 8, 8),
        "wpt": np.ascontiguousarray(np.asarray(inputs["L0_attn_w_proj"], dtype=np.float32).T),
        "wft": np.ascontiguousarray(np.asarray(inputs["L0_attn_w_ff"], dtype=np.float32).T),
        "wg": np.ascontiguousarray(np.asarray(inputs["L0_ffn_w_gate"], dtype=np.float32).T),
        "wu": np.ascontiguousarray(np.asarray(inputs["L0_ffn_w_up"], dtype=np.float32).T),
        "wd": np.ascontiguousarray(np.asarray(inputs["L0_ffn_w_down"], dtype=np.float32).T),
        "b_gated": np.asarray(inputs["L0_attn_b_gated"], dtype=np.float32).reshape(1, C),
        "b_proj": np.asarray(inputs["L0_attn_b_proj"], dtype=np.float32).reshape(1, C),
        "gn_w": np.asarray(inputs["L0_attn_gn_w"], dtype=np.float32).reshape(1, C),
        "gn_b": np.asarray(inputs["L0_attn_gn_b"], dtype=np.float32).reshape(1, C),
        "rms_w": np.asarray(inputs["L0_attn_rms_w"], dtype=np.float32).reshape(1, C),
    }
    tiled.update(_xpos_decay_tables())
    return consts, tiled


def kernel(**inputs):
    consts, tiled = _prep(inputs)
    nc = build_program()
    in_maps = []
    for core in range(8):
        m = {"x": np.ascontiguousarray(
            np.asarray(inputs["L0_attn_x"][2 * core:2 * core + 2],
                       dtype=np.float32).reshape(NT, C))}
        m.update(tiled)
        m.update(consts)
        in_maps.append(m)
    res = run_bass_kernel_spmd(nc, in_maps, core_ids=list(range(8)))
    out = np.concatenate([np.asarray(r["out"]).reshape(2, T, C)
                          for r in res.results], axis=0)
    return out.astype(np.float32)


# version 22
# speedup vs baseline: 1.0380x; 1.0201x over previous
import math
from contextlib import ExitStack

import numpy as np

import concourse.bass as bass
import concourse.mybir as mybir
from concourse.bass_utils import run_bass_kernel_spmd

F32 = mybir.dt.float32
BF16 = mybir.dt.bfloat16
AF = mybir.ActivationFunctionType
ALU = mybir.AluOpType
AX = mybir.AxisListType.X

SEQ, T, C, H, HD = 2, 2048, 1024, 16, 64
NT = SEQ * T
FF2, INNER = 2048, 4096
NTT = NT // 128
WIN = 512
C1 = float(1.5 * 2.0 ** 23)
CD = float(1.5 * 2.0 ** 52)
GN_EPS, LN_EPS = 1e-5, 1e-5
S1B_MODE = "full"
CARVE = 49152
CARVE_END = 229376

ENGINES = ("tensor", "vector", "scalar", "gpsimd", "sync")


class _Buf:
    __slots__ = ("name", "lw", "rd", "psum")

    def __init__(self, name, psum=False):
        self.name, self.lw, self.rd, self.psum = name, None, [], psum


class _Chan:
    __slots__ = ("name", "eng", "cum", "ops", "last", "waiters")

    def __init__(self, name, eng):
        assert eng in ("sync", "gpsimd")
        self.name, self.eng, self.cum, self.ops, self.last = name, eng, 0, [], None
        self.waiters = {}


class _Op:
    __slots__ = ("eng", "fn", "idx", "deps", "need_inc", "inc_val", "chan",
                 "cum_after", "waits", "tag")

    def __init__(self, eng, fn, idx, tag):
        self.eng, self.fn, self.idx, self.tag = eng, fn, idx, tag
        self.deps, self.need_inc, self.inc_val = [], False, None
        self.chan, self.cum_after, self.waits = None, None, None


class Sched:
    def __init__(self):
        self.ops, self.chans, self.last_on = [], [], {e: None for e in ENGINES}

    def buf(self, name="", psum=False):
        return _Buf(name, psum)

    def chan(self, name, eng):
        c = _Chan(name, eng)
        self.chans.append(c)
        return c

    def _record(self, op, reads, writes):
        deps = set()
        for b in reads:
            if b.lw is not None:
                deps.add(b.lw)
            if b.psum:
                for r in b.rd:
                    if r.eng != op.eng:
                        deps.add(r)
        for b in writes:
            if b.lw is not None:
                deps.add(b.lw)
            for r in b.rd:
                deps.add(r)
        op.deps = [d for d in deps if d is not op and not (
            d.eng == "tensor" and op.eng == "tensor"
            and d.chan is None and op.chan is None)]
        for d in op.deps:
            if d.chan is not None:
                d.chan.waiters[op.eng] = op
        for b in writes:
            b.lw, b.rd = op, []
        for b in reads:
            b.rd.append(op)
        self.ops.append(op)
        self.last_on[op.eng] = op
        return op

    def op(self, eng, fn, reads=(), writes=(), tag=""):
        assert eng in ("tensor", "vector", "scalar", "gpsimd")
        return self._record(_Op(eng, fn, len(self.ops), tag), reads, writes)

    def dma(self, chan, fn, reads=(), writes=(), tag=""):
        op = _Op(chan.eng, fn, len(self.ops), tag)
        op.chan = chan
        chan.cum += 16
        op.cum_after = chan.cum
        chan.ops.append(op)
        chan.last = op
        extra = [w for w in chan.waiters.values() if w is not op]
        ret = self._record(op, reads, writes)
        for w in extra:
            if w not in op.deps:
                op.deps.append(w)
        return ret

    def barrier(self, tag="barrier"):
        prods = [o for o in self.last_on.values() if o is not None]
        prods += [c.last for c in self.chans if c.last is not None]
        for eng in ENGINES:
            op = _Op(eng, None, len(self.ops), tag)
            op.deps = [p for p in prods if p is not op]
            self.ops.append(op)

    def emit(self, block, sems):
        for op in self.ops:
            for d in op.deps:
                d.need_inc = True
        inc_count = {e: 0 for e in ENGINES}
        for op in self.ops:
            if op.chan is None and op.need_inc:
                inc_count[op.eng] += 1
                op.inc_val = inc_count[op.eng]
        for op in self.ops:
            waits = {}
            for d in op.deps:
                if d.chan is not None:
                    key = ("chan", d.chan.name)
                    val = d.cum_after
                    for other in d.chan.ops:
                        if other.idx < op.idx and other.cum_after > val:
                            val = other.cum_after
                else:
                    key, val = ("eng", d.eng), d.inc_val
                waits[key] = max(waits.get(key, 0), val)
            op.waits = waits
        self._verify_schedulable()

        self._emit_streams(block, sems)

    def _verify_schedulable(self):
        queues = {e: [op for op in self.ops if op.eng == e] for e in ENGINES}
        heads = {e: 0 for e in ENGINES}
        esem = {e: 0 for e in ENGINES}
        csem = {c.name: 0 for c in self.chans}

        def ok(op):
            for (kind, nm), val in op.waits.items():
                if kind == "eng" and esem[nm] < val:
                    return False
                if kind == "chan" and csem[nm] < val:
                    return False
            return True

        progress = True
        while progress:
            progress = False
            for e in ENGINES:
                q = queues[e]
                while heads[e] < len(q):
                    op = q[heads[e]]
                    if not ok(op):
                        break
                    if op.chan is not None:
                        csem[op.chan.name] += 16
                    elif op.inc_val is not None:
                        esem[e] = op.inc_val
                    heads[e] += 1
                    progress = True
        stuck = [(e, queues[e][heads[e]]) for e in ENGINES if heads[e] < len(queues[e])]
        if stuck:
            msg = "; ".join(
                f"{e}:{op.tag}@{op.idx} waits {op.waits}" for e, op in stuck[:4])
            raise RuntimeError("schedule deadlock: " + msg)

    def _emit_streams(self, block, sems):
        def stream(engobj, ename):
            hw = {}
            for op in self.ops:
                if op.eng != ename:
                    continue
                for key, val in sorted(op.waits.items(), key=lambda kv: str(kv[0])):
                    if hw.get(key, 0) >= val:
                        continue
                    engobj.wait_ge(sems[key], val)
                    hw[key] = val
                if op.fn is not None:
                    inst = op.fn(engobj)
                    if op.chan is not None:
                        inst.then_inc(sems[("chan", op.chan.name)], 16)
                    elif op.need_inc:
                        inst.then_inc(sems[("eng", op.eng)], 1)
                elif op.need_inc:
                    raise RuntimeError(f"fn-less op {op.tag} cannot carry inc")

        block.tensor(lambda e: stream(e, "tensor"))
        block.vector(lambda e: stream(e, "vector"))
        block.scalar(lambda e: stream(e, "scalar"))
        block.gpsimd(lambda e: stream(e, "gpsimd"))
        block.sync(lambda e: stream(e, "sync"))


def _host_consts():
    c = {}
    c["identm"] = np.eye(128, dtype=np.float32)
    P = np.zeros((128, 128), dtype=np.float32)
    for i in range(64):
        P[2 * i, 2 * i + 1] = 1.0
        P[2 * i + 1, 2 * i] = 1.0
    c["pswapm"] = P
    tri = np.zeros((128, 4 * WIN), dtype=np.float32)
    for r in range(4):
        for j in range(4):
            if j > r:
                tri[:, WIN * r + 128 * j:WIN * r + 128 * (j + 1)] = 1.0
            elif j == r:
                tri[:, WIN * r + 128 * j:WIN * r + 128 * (j + 1)] = \
                    np.tril(np.ones((128, 128), dtype=np.float32)).T
    c["trim"] = tri * 0.125
    p = ((np.arange(T, dtype=np.float64) - 1024.0) / 512.0).astype(np.float32)
    c["pbcm"] = np.broadcast_to(p[None, :], (128, T)).copy()
    gam = (1.0 - 2.0 ** (-5.0 - np.arange(H, dtype=np.float64))).astype(np.float32)
    gcols = np.zeros((128, 8), dtype=np.float32)
    for cc in range(8):
        for pp in range(128):
            gcols[pp, cc] = gam[(128 * cc + pp) // 64]
    c["gcolsm"] = gcols
    c["hrampm"] = (np.arange(128, dtype=np.float32) // 2).reshape(128, 1)
    c["sgnm"] = np.where(np.arange(128) % 2 == 0, -1.0, 1.0
                         ).astype(np.float32).reshape(128, 1)
    c["onesm"] = np.ones((128, 1), dtype=np.float32)
    c["halfpim"] = np.full((128, 1), np.float32(np.pi / 2), dtype=np.float32)
    gl = np.zeros((128, 8 * 64), dtype=np.float32)
    for hp in range(8):
        gl[0:64, 64 * hp + 2 * hp] = 1.0 / 64.0
        gl[64:128, 64 * hp + 2 * hp + 1] = 1.0 / 64.0
    c["gnlm"] = gl
    return c


def _tile_oc(w_t, nblk_o, nblk_c):
    ci, co = w_t.shape
    assert ci == nblk_c * 128 and co == nblk_o * 128
    return np.ascontiguousarray(
        w_t.reshape(nblk_c, 128, nblk_o, 128).transpose(2, 1, 0, 3))



def _xpos_decay_tables():
    import jax
    cpu = jax.local_devices(backend="cpu")[0]
    import jax.numpy as jnp
    with jax.default_device(cpu):
        half = C // 2
        sv = (jnp.arange(0, C, 2, dtype=jnp.float32) + 0.4 * C) / (1.4 * C)
        mn = -(T) // 2
        power = jnp.arange(mn, T + mn, dtype=jnp.float32) / 512
        scale = sv[None, :] ** power[:, None]
        invf = 1.0 / 10000.0 ** (jnp.arange(half, dtype=jnp.float32) / half)
        sinus = jnp.arange(T, dtype=jnp.float32)[:, None] * invf[None, :]
        sin = jnp.sin(sinus)
        cos = jnp.cos(sinus)
        cos_q = jnp.repeat(cos * scale, 2, axis=-1)
        sin_q = jnp.repeat(sin * scale, 2, axis=-1)
        iscale = 1.0 / scale
        cos_k = jnp.repeat(cos * iscale, 2, axis=-1)
        sin_k = jnp.repeat(sin * iscale, 2, axis=-1)
        cos_q = np.asarray(cos_q, dtype=np.float32)
        sin_q = np.asarray(sin_q, dtype=np.float32)
        cos_k = np.asarray(cos_k, dtype=np.float32)
        sin_k = np.asarray(sin_k, dtype=np.float32)
    sgn = np.where(np.arange(C) % 2 == 0, -1.0, 1.0).astype(np.float32)
    gam = (1.0 - 2.0 ** (-5.0 - np.arange(H, dtype=np.float64)))
    gc = gam[np.arange(C) // 64]
    tpow = np.arange(T, dtype=np.float64)
    g_t = (gc[:, None] ** tpow[None, :])
    g_ti = (gc[:, None] ** (-tpow[None, :]))
    out = {
        "cqm": (cos_q.T.astype(np.float64) * g_t).astype(np.float32),
        "sqm": ((sin_q.T * sgn[:, None]).astype(np.float64) * g_t).astype(np.float32),
        "ckm": (cos_k.T.astype(np.float64) * g_ti).astype(np.float32),
        "skm": ((sin_k.T * sgn[:, None]).astype(np.float64) * g_ti).astype(np.float32),
    }
    return {k: np.ascontiguousarray(v) for k, v in out.items()}


def build_program(upto="S6", debug=()):
    nc = bass.Bass("TRN2", target_bir_lowering=False)
    s = Sched()
    dr, st, B = {}, {}, {}

    def din(name, shape, dtype=F32):
        dr[name] = nc.dram_tensor(name, list(shape), dtype, kind="ExternalInput").ap()
        return dr[name]

    def dsc(name, shape, dtype):
        dr[name] = nc.dram_tensor(name, list(shape), dtype, kind="Internal").ap()
        return dr[name]

    x_d = din("x", (NT, C))
    w1t_d = din("w1t", (40, 128, 8, 128))
    wgt_d = din("wgt", (8, 128, 8, 128))
    wpt_d = din("wpt", (C, C))
    wft_d = din("wft", (FF2, C))
    wg_d = din("wg", (C, INNER))
    wu_d = din("wu", (C, INNER))
    wd_d = din("wd", (INNER, C))
    bgat_d = din("b_gated", (1, C))
    bprj_d = din("b_proj", (1, C))
    gnw_d = din("gn_w", (1, C))
    gnb_d = din("gn_b", (1, C))
    rms_d = din("rms_w", (1, C))
    identm_d = din("identm", (128, 128))
    pswap_d = din("pswapm", (128, 128))
    tri_d = din("trim", (128, 4 * WIN))
    pbc_d = din("pbcm", (128, T))
    gcols_d = din("gcolsm", (128, 8))
    hramp_d = din("hrampm", (128, 1))
    sgn_d = din("sgnm", (128, 1))
    ones_d = din("onesm", (128, 1))
    halfpi_d = din("halfpim", (128, 1))
    gnl_d = din("gnlm", (128, 8 * 64))
    cqm_d = din("cqm", (C, T))
    sqm_d = din("sqm", (C, T))
    ckm_d = din("ckm", (C, T))
    skm_d = din("skm", (C, T))

    out_d = nc.dram_tensor("out", [NT, C], F32, kind="ExternalOutput").ap()

    scq_d = dsc("s_cq", (C, T), F32)
    ssq_d = dsc("s_sq", (C, T), F32)
    sck_d = dsc("s_ck", (C, T), F32)
    ssk_d = dsc("s_sk", (C, T), F32)
    sq_d = dsc("s_q", (SEQ, C, T), BF16)
    sk_d = dsc("s_k", (SEQ, C, T), BF16)
    sv_d = dsc("s_v", (SEQ, T, C), BF16)
    sgel_d = dsc("s_gelu", (SEQ, 128, 16, 16, 128), BF16)
    sxn_d = dsc("s_xn", (128, 8, NT), BF16)
    sxq_d = dsc("s_xq", (NTT, 128, 8 * 128), BF16)
    sy2a_d = dsc("s_y2a", (SEQ, 128, 8, T), BF16)
    sy2_d = dsc("s_y2", (128, NTT, 8, 128), BF16)
    shq_d = dsc("s_hq", (NTT, 4, 128, 8 * 128), BF16)
    swgq_d = dsc("s_wgq", (C, INNER), BF16)
    swuq_d = dsc("s_wuq", (C, INNER), BF16)
    swdq_d = dsc("s_wdq", (INNER, C), BF16)
    sdbg_d = dsc("s_dbgspill", (40, 128, NT), BF16)

    def sb(key, shape, dtype=F32):
        st[key] = nc.alloc_sbuf_tensor("sb_" + key, list(shape), dtype)
        B[key] = s.buf(key)
        return st[key]

    def sb_at(key, shape, dtype, off):
        nbytes = int(np.prod(shape[1:])) * mybir.dt.size(dtype)
        assert CARVE + off + nbytes <= CARVE_END, (key, off, nbytes)
        st[key] = nc.alloc_sbuf_tensor_at("cv_" + key, list(shape), dtype,
                                          offset=CARVE + off)
        B[key] = s.buf(key)
        return st[key]

    sb("identb", (128, 128), BF16)
    sb("identf", (128, 128), F32)
    sb("pswapb", (128, 128), BF16)
    sb("gcols", (128, 8))
    sb("hramp", (128, 1))
    sb("sgncol", (128, 1))
    sb("onescol", (128, 1))
    sb("halfpi", (128, 1))
    sb("vecs", (128, 5 * 8))
    sb("rowbuf", (1, 2 * C))
    sb("rmsb", (128, C))
    sb("ceff", (128, 4))
    sb("wsc", (128, 16))
    sb("stats", (128, 9 * NTT))
    sb("tmpc", (128, 64))
    sb("gnl", (128, 8 * 64))
    sb("tri", (128, 4 * WIN))
    sentinel = nc.alloc_sbuf_tensor("sb_sentinel", [128, 4], F32)
    assert nc.lookup_mloc(sentinel).addr + 16 <= CARVE, \
        f"persistent sbuf overflow: {nc.lookup_mloc(sentinel).addr}"

    ps = {}
    for nm, shape, dt in (("PA", (128, 512), F32), ("PB", (128, 512), F32),
                          ("PC", (128, 512), F32), ("PD", (128, 512), F32),
                          ("PE2", (128, 1024), F32), ("PT", (128, 1024), BF16),
                          ("PG", (128, 512), F32)):
        ps[nm] = nc.alloc_psum_tensor(nm, list(shape), dt)
        B[nm] = s.buf(nm, psum=True)
    B["PE2A"] = s.buf("PE2A", psum=True)
    B["PE2B"] = s.buf("PE2B", psum=True)

    def pap(pname):
        if pname == "PE2A":
            return ps["PE2"][:, 0:512]
        if pname == "PE2B":
            return ps["PE2"][:, 512:1024]
        return ps[pname][:]

    ch_in = [s.chan(f"in{i}", "sync") for i in range(2)]
    ch_w = [s.chan(f"w{i}", "gpsimd") for i in range(2)]
    ch_sp = [s.chan(f"sp{i}", "gpsimd") for i in range(8)]
    ch_rd = [s.chan(f"rd{i}", "sync") for i in range(4)]
    ch_out = [s.chan(f"out{i}", "sync") for i in range(2)]
    ch_cst = s.chan("cst", "sync")
    ch_cw = s.chan("cw", "gpsimd")

    def S0_consts():
        for key, d, chn in (("identb", identm_d, ch_cw), ("pswapb", pswap_d, ch_cw),
                            ("identf", identm_d, ch_cst),
                            ("tri", tri_d, ch_cst), ("gcols", gcols_d, ch_cst),
                            ("hramp", hramp_d, ch_cst), ("sgncol", sgn_d, ch_cst),
                            ("onescol", ones_d, ch_cst), ("gnl", gnl_d, ch_cst),
                            ("halfpi", halfpi_d, ch_cst)):
            s.dma(chn, lambda e, key=key, d=d: e.dma_start(out=st[key][:], in_=d),
                  writes=[B[key]], tag=f"ld_{key}")
        for k, d in enumerate((bgat_d, bprj_d, gnw_d, gnb_d, rms_d)):
            s.dma(ch_cst, lambda e, k=k, d=d: e.dma_start(
                out=st["vecs"][:, 8 * k:8 * (k + 1)],
                in_=bass.AP(d.tensor, 0, [[1, 128], [128, 8]]),
                allow_slow_non_contiguous=True),
                writes=[B["vecs"]], tag=f"veccol{k}")
        s.dma(ch_cst, lambda e: e.dma_start(out=st["rowbuf"][0:1, 0:C], in_=bprj_d),
              writes=[B["rowbuf"]], tag="bprow")
        s.dma(ch_cst, lambda e: e.dma_start(out=st["rowbuf"][0:1, C:2 * C], in_=rms_d),
              writes=[B["rowbuf"]], tag="rmsrow")

        s.op("vector", lambda e: e.tensor_scalar(
            st["ceff"][:, 0:1], st["onescol"][:], 2.0 ** -30, 1.0, ALU.add, ALU.subtract),
            reads=[B["onescol"]], writes=[B["ceff"]], tag="ind")
        s.op("vector", lambda e: e.tensor_scalar(
            st["ceff"][:, 1:2], st["ceff"][:, 0:1], 2.0 ** 30 * CD, C1, ALU.mult, ALU.add),
            reads=[B["ceff"]], writes=[B["ceff"]], tag="ceff")

        s.op("vector", lambda e: e.memset(st["tmpc"][0:1, 0:16], 1.0),
             writes=[B["tmpc"]], tag="onesrow16")
        onesrow = sb("onesrow", (1, 128))
        s.op("vector", lambda e: e.memset(st["onesrow"][:], 1.0),
             writes=[B["onesrow"]], tag="onesrow")
        for half in range(2):
            s.op("tensor", lambda e, half=half: e.matmul(
                ps["PE2"][:, 512 * half:512 * (half + 1)], st["onesrow"][0:1, :],
                st["rowbuf"][0:1, C + 512 * half:C + 512 * (half + 1)],
                start=True, stop=True),
                reads=[B["onesrow"], B["rowbuf"]],
                writes=[B["PE2A"] if half == 0 else B["PE2B"]], tag=f"rmsmm{half}")
        s.op("vector", lambda e: e.tensor_copy(st["rmsb"][:], ps["PE2"][:, 0:C]),
             reads=[B["PE2A"], B["PE2B"]], writes=[B["rmsb"]], tag="rmsbc")

    def S0_wquant_gen():
        sb_at("wload0", (128, 4096), F32, 132 * 1024)
        sb_at("wload1", (128, 4096), F32, 148 * 1024)
        sb_at("wqout0", (128, 4096), BF16, 164 * 1024)
        acc = st["wsc"]
        for wname, wsrc, wdst, nchunk, wcols, wi in (
                ("wg", wg_d, swgq_d, 8, INNER, 0),
                ("wu", wu_d, swuq_d, 8, INNER, 1),
                ("wd", wd_d, swdq_d, 32, C, 2)):
            for k in range(nchunk):
                par = k % 2
                bw = B[f"wload{par}"]
                s.dma(ch_out[par], lambda e, k=k, par=par, wsrc=wsrc, wcols=wcols:
                      e.dma_start(out=st[f"wload{par}"][:, 0:wcols],
                                  in_=wsrc[128 * k:128 * (k + 1), :]),
                      writes=[bw], tag=f"{wname}ld{k}")
                s.op("vector", lambda e, par=par, wcols=wcols: e.reduce_sum(
                    acc[:, 8:9], st[f"wload{par}"][:, 0:wcols], AX,
                    apply_absolute_value=True),
                    reads=[bw, B["wsc"]], writes=[B["wsc"]], tag=f"{wname}abs{k}")
                if k == 0:
                    s.op("vector", lambda e: e.tensor_copy(acc[:, 9:10], acc[:, 8:9]),
                         reads=[B["wsc"]], writes=[B["wsc"]], tag=f"{wname}a0")
                else:
                    s.op("vector", lambda e: e.tensor_add(acc[:, 9:10], acc[:, 9:10], acc[:, 8:9]),
                         reads=[B["wsc"]], writes=[B["wsc"]], tag=f"{wname}a{k}")
                yield
            s.op("tensor", lambda e: e.matmul(
                ps["PE2"][0:1, 0:1], st["onescol"][:], acc[:, 9:10], start=True, stop=True),
                reads=[B["onescol"], B["wsc"]], writes=[B["PE2A"]], tag=f"{wname}tot")
            ncount = float(nchunk * 128 * wcols)
            s.op("vector", lambda e, ncount=ncount: e.tensor_scalar(
                acc[0:1, 10:11], ps["PE2"][0:1, 0:1], 1.0 / ncount, 1e-5, ALU.mult, ALU.max),
                reads=[B["PE2A"]], writes=[B["wsc"]], tag=f"{wname}mean")
            s.op("tensor", lambda e: e.matmul(
                ps["PE2"][:, 4:5], st["onesrow"][0:1, :], acc[0:1, 10:11],
                start=True, stop=True),
                reads=[B["wsc"], B["onesrow"]], writes=[B["PE2A"]], tag=f"{wname}bc")
            s.op("vector", lambda e, wi=wi: e.tensor_copy(
                st["wsc"][:, wi:wi + 1], ps["PE2"][:, 4:5]),
                reads=[B["PE2A"]], writes=[B["wsc"]], tag=f"{wname}wsc")
            s.op("scalar", lambda e, wi=wi: e.activation(
                st["wsc"][:, 4 + wi:5 + wi], st["wsc"][:, wi:wi + 1], AF.Ln),
                reads=[B["wsc"]], writes=[B["wsc"]], tag=f"{wname}ln")
            s.op("scalar", lambda e, wi=wi: e.activation(
                st["wsc"][:, 4 + wi:5 + wi], st["wsc"][:, 4 + wi:5 + wi], AF.Exp, scale=-1.0),
                reads=[B["wsc"]], writes=[B["wsc"]], tag=f"{wname}exp")
            for k in range(nchunk):
                par = k % 2
                bw, bq = B[f"wload{par}"], B["wqout0"]
                s.dma(ch_in[par], lambda e, k=k, par=par, wsrc=wsrc, wcols=wcols:
                      e.dma_start(out=st[f"wload{par}"][:, 0:wcols],
                                  in_=wsrc[128 * k:128 * (k + 1), :]),
                      writes=[bw], tag=f"{wname}l2_{k}")
                s.op("scalar", lambda e, par=par, wcols=wcols, wi=wi: e.activation(
                    st[f"wload{par}"][:, 0:wcols], st[f"wload{par}"][:, 0:wcols],
                    AF.Copy, scale=st["wsc"][:, 4 + wi:5 + wi]),
                    reads=[bw, B["wsc"]], writes=[bw], tag=f"{wname}sc{k}")
                s.op("vector", lambda e, par=par, wcols=wcols: e.tensor_scalar(
                    st[f"wload{par}"][:, 0:wcols], st[f"wload{par}"][:, 0:wcols],
                    st["ceff"][:, 1:2], st["ceff"][:, 1:2], ALU.add, ALU.subtract),
                    reads=[bw, B["ceff"]], writes=[bw], tag=f"{wname}ri{k}")
                s.op("vector", lambda e, par=par, wcols=wcols: e.tensor_scalar(
                    st["wqout0"][:, 0:wcols], st[f"wload{par}"][:, 0:wcols],
                    -1.0, 1.0, ALU.max, ALU.min),
                    reads=[bw], writes=[bq], tag=f"{wname}cl{k}")
                s.dma(ch_sp[4 + par], lambda e, k=k, par=par, wdst=wdst, wcols=wcols:
                      e.dma_start(out=wdst[128 * k:128 * (k + 1), :],
                                  in_=st["wqout0"][:, 0:wcols]),
                      reads=[bq], tag=f"{wname}st{k}")
                yield

    def S0_wquant():
        for _ in S0_wquant_gen():
            pass

    def S0b():
        sb_at("pbc", (128, T), F32, 56 * 1024)
        for nm, off in (("tbC", 64), ("tbS", 72), ("tbU", 80), ("tbA", 88),
                        ("tbB", 96), ("tbG", 104), ("tbH", 112)):
            sb_at(nm, (128, T), F32, off * 1024)
        sb_at("tcol", (128, 16), F32, 120 * 1024)
        s.dma(ch_cst, lambda e: e.dma_start(out=st["pbc"][:], in_=pbc_d),
              writes=[B["pbc"]], tag="ld_pbc")
        LNIF = -math.log(10000.0) / 512.0
        tc = st["tcol"]
        bcol = B["tcol"]
        for cc in range(8):
            bC, bS, bU, bA, bB_, bG, bH = (B["tbC"], B["tbS"], B["tbU"], B["tbA"],
                                           B["tbB"], B["tbG"], B["tbH"])
            s.op("vector", lambda e, cc=cc: e.tensor_scalar_add(
                tc[:, 0:1], st["hramp"][:], float(64 * cc)),
                reads=[B["hramp"]], writes=[bcol], tag=f"t{cc}jc")
            s.op("scalar", lambda e: e.activation(tc[:, 1:2], tc[:, 0:1], AF.Exp, scale=LNIF),
                 reads=[bcol], writes=[bcol], tag=f"t{cc}f")
            s.op("vector", lambda e: e.tensor_scalar(
                tc[:, 2:3], tc[:, 0:1], 2.0 / 1433.6, 409.6 / 1433.6, ALU.mult, ALU.add),
                reads=[bcol], writes=[bcol], tag=f"t{cc}sv")
            s.op("scalar", lambda e: e.activation(tc[:, 3:4], tc[:, 2:3], AF.Ln),
                 reads=[bcol], writes=[bcol], tag=f"t{cc}L")
            s.op("vector", lambda e: e.tensor_scalar(
                st["tbU"][:], st["pbc"][:], tc[:, 3:4], None, ALU.mult),
                reads=[bcol, B["pbc"]], writes=[bU], tag=f"t{cc}U")
            s.op("scalar", lambda e: e.activation(st["tbA"][:], st["tbU"][:], AF.Exp),
                 reads=[bU], writes=[bA], tag=f"t{cc}scq")
            s.op("scalar", lambda e: e.activation(st["tbB"][:], st["tbU"][:], AF.Exp, scale=-1.0),
                 reads=[bU], writes=[bB_], tag=f"t{cc}sck")
            s.op("vector", lambda e: e.memset(st["tbC"][:, 0:1], 1.0),
                 writes=[bC], tag=f"t{cc}c0")
            s.op("vector", lambda e: e.memset(st["tbS"][:, 0:1], 0.0),
                 writes=[bS], tag=f"t{cc}s0")
            s.op("vector", lambda e: e.tensor_scalar(
                tc[:, 12:13], tc[:, 1:2], -1.0, 1.5707963705062866, ALU.mult, ALU.add),
                reads=[bcol], writes=[bcol], tag=f"t{cc}ch1")
            s.op("vector", lambda e: e.tensor_scalar_add(
                tc[:, 12:13], tc[:, 12:13], -4.371138828673793e-08),
                reads=[bcol], writes=[bcol], tag=f"t{cc}ch2")
            s.op("scalar", lambda e: e.activation(st["tbC"][:, 1:2], tc[:, 12:13], AF.Sin),
                 reads=[bcol], writes=[bC], tag=f"t{cc}c1")
            s.op("scalar", lambda e: e.activation(st["tbS"][:, 1:2], tc[:, 1:2], AF.Sin),
                 reads=[bcol], writes=[bS], tag=f"t{cc}s1")
            s.op("vector", lambda e: e.tensor_copy(tc[:, 4:5], st["tbC"][:, 1:2]),
                 reads=[bC], writes=[bcol], tag=f"t{cc}r0")
            s.op("vector", lambda e: e.tensor_copy(tc[:, 5:6], st["tbS"][:, 1:2]),
                 reads=[bS], writes=[bcol], tag=f"t{cc}r1")
            w = 2
            while w < T:
                s.op("vector", lambda e: e.tensor_mul(tc[:, 6:7], tc[:, 4:5], tc[:, 4:5]),
                     reads=[bcol], writes=[bcol], tag=f"t{cc}ra{w}")
                s.op("vector", lambda e: e.tensor_mul(tc[:, 7:8], tc[:, 5:6], tc[:, 5:6]),
                     reads=[bcol], writes=[bcol], tag=f"t{cc}rb{w}")
                s.op("vector", lambda e: e.tensor_mul(tc[:, 8:9], tc[:, 4:5], tc[:, 5:6]),
                     reads=[bcol], writes=[bcol], tag=f"t{cc}rc{w}")
                s.op("vector", lambda e: e.tensor_sub(tc[:, 4:5], tc[:, 6:7], tc[:, 7:8]),
                     reads=[bcol], writes=[bcol], tag=f"t{cc}rd{w}")
                s.op("vector", lambda e: e.tensor_scalar_mul(tc[:, 5:6], tc[:, 8:9], 2.0),
                     reads=[bcol], writes=[bcol], tag=f"t{cc}re{w}")
                s.op("vector", lambda e, w=w: e.tensor_scalar(
                    st["tbH"][:, 0:w], st["tbS"][:, 0:w], tc[:, 5:6], None, ALU.mult),
                    reads=[bS, bcol], writes=[bH], tag=f"t{cc}da{w}")
                s.op("vector", lambda e, w=w: e.scalar_tensor_tensor(
                    st["tbC"][:, w:2 * w], st["tbC"][:, 0:w], tc[:, 4:5],
                    st["tbH"][:, 0:w], ALU.mult, ALU.subtract),
                    reads=[bC, bcol, bH], writes=[bC], tag=f"t{cc}db{w}")
                s.op("vector", lambda e, w=w: e.tensor_scalar(
                    st["tbH"][:, 0:w], st["tbC"][:, 0:w], tc[:, 5:6], None, ALU.mult),
                    reads=[bC, bcol], writes=[bH], tag=f"t{cc}dc{w}")
                s.op("vector", lambda e, w=w: e.scalar_tensor_tensor(
                    st["tbS"][:, w:2 * w], st["tbS"][:, 0:w], tc[:, 4:5],
                    st["tbH"][:, 0:w], ALU.mult, ALU.add),
                    reads=[bS, bcol, bH], writes=[bS], tag=f"t{cc}dd{w}")
                w *= 2
            s.op("vector", lambda e, cc=cc: e.tensor_copy(
                tc[:, 9:10], st["gcols"][:, cc:cc + 1]),
                reads=[B["gcols"]], writes=[bcol], tag=f"t{cc}g")
            s.op("vector", lambda e: e.reciprocal(tc[:, 10:11], tc[:, 9:10]),
                 reads=[bcol], writes=[bcol], tag=f"t{cc}gi")
            for dst, bD, col in ((st["tbG"], bG, 9), (st["tbH"], bH, 10)):
                s.op("vector", lambda e, dst=dst: e.memset(dst[:, 0:1], 1.0),
                     writes=[bD], tag=f"t{cc}g0_{col}")
                s.op("vector", lambda e, dst=dst, col=col: e.tensor_copy(
                    dst[:, 1:2], tc[:, col:col + 1]),
                    reads=[bcol], writes=[bD], tag=f"t{cc}g1_{col}")
                s.op("vector", lambda e, col=col: e.tensor_copy(
                    tc[:, 11:12], tc[:, col:col + 1]),
                    reads=[bcol], writes=[bcol], tag=f"t{cc}gr_{col}")
                w = 2
                while w < T:
                    s.op("vector", lambda e: e.tensor_mul(
                        tc[:, 11:12], tc[:, 11:12], tc[:, 11:12]),
                        reads=[bcol], writes=[bcol], tag=f"t{cc}gs{w}_{col}")
                    s.op("vector", lambda e, w=w, dst=dst: e.tensor_scalar(
                        dst[:, w:2 * w], dst[:, 0:w], tc[:, 11:12], None, ALU.mult),
                        reads=[bD, bcol], writes=[bD], tag=f"t{cc}gm{w}_{col}")
                    w *= 2
            s.op("vector", lambda e: e.tensor_mul(st["tbA"][:], st["tbA"][:], st["tbG"][:]),
                 reads=[bA, bG], writes=[bA], tag=f"t{cc}aq")
            s.op("vector", lambda e: e.tensor_mul(st["tbB"][:], st["tbB"][:], st["tbH"][:]),
                 reads=[bB_, bH], writes=[bB_], tag=f"t{cc}ak")
            for ji, (dst_d, srcCS, bsrc, srcA, bamp, use_sgn) in enumerate((
                    (scq_d, st["tbC"], bC, st["tbA"], bA, False),
                    (ssq_d, st["tbS"], bS, st["tbA"], bA, True),
                    (sck_d, st["tbC"], bC, st["tbB"], bB_, False),
                    (ssk_d, st["tbS"], bS, st["tbB"], bB_, True))):
                s.op("vector", lambda e, srcCS=srcCS, srcA=srcA: e.tensor_mul(
                    st["tbU"][:], srcCS[:], srcA[:]),
                    reads=[bsrc, bamp], writes=[bU], tag=f"t{cc}f{ji}")
                if use_sgn:
                    s.op("vector", lambda e: e.tensor_scalar(
                        st["tbU"][:], st["tbU"][:], st["sgncol"][:], None, ALU.mult),
                        reads=[bU, B["sgncol"]], writes=[bU], tag=f"t{cc}fs{ji}")
                s.dma(ch_sp[ji % 2], lambda e, dst_d=dst_d, cc=cc: e.dma_start(
                    out=dst_d[128 * cc:128 * (cc + 1), :], in_=st["tbU"][:]),
                    reads=[bU], tag=f"t{cc}st{ji}")

    def carve_S1():
        sb_at("XHFM", (128, 8 * NT), BF16, 0)
        for g in range(8):
            B[f"XHFM{g}"] = s.buf(f"XHFM{g}")
        sb_at("xt", (128, 2 * C), F32, 64 * 1024)
        sb_at("xnf", (128, 2 * C), F32, 72 * 1024)
        sb_at("tmp1", (128, 2 * C), F32, 80 * 1024)
        sb_at("xps", (128, 2 * WIN), F32, 88 * 1024)
        sb_at("xhb", (128, 2 * C), BF16, 92 * 1024)
        sb_at("xnb", (128, 2 * C), BF16, 96 * 1024)
        sb_at("xqb", (128, 2 * C), BF16, 100 * 1024)
        sb_at("stg", (128, 2 * 2 * C), BF16, 104 * 1024)
        sb_at("wbuf", (128, 2 * 1024), BF16, 112 * 1024)
        sb_at("tabs", (128, 2 * 2 * WIN), F32, 116 * 1024)
        sb_at("qkst", (128, 2 * 2 * WIN), BF16, 124 * 1024)
        sb_at("xsq", (128, C), F32, 128 * 1024)
        for nm in ("xt", "xnf", "tmp1", "xps", "xhb", "xnb", "xqb", "stg",
                   "qkst", "tabs", "wbuf"):
            B[nm + "0"] = s.buf(nm + "0")
            B[nm + "1"] = s.buf(nm + "1")

    def S1a():
        for _ in S1a_gen():
            pass

    def S1a_gen():
        xhfm = st["XHFM"]
        cS = st["stats"]
        tc = st["tmpc"]
        for fam in ("stSUM", "stSSQ", "stXMX", "stXMN"):
            if fam not in B:
                B[fam] = s.buf(fam)
        G_ORDER = [g for gp in zip(range(4), range(4, 8)) for g in gp]
        SUM, SSQ, XMX, XMN = 5 * NTT, 6 * NTT, 7 * NTT, 8 * NTT
        for g in G_ORDER:
            tiles = list(range(4 * g, 4 * g + 4))
            for i in tiles:
                par = i % 2
                bxt = B[f"xt{par}"]
                s.dma(ch_in[par], lambda e, i=i, par=par: e.dma_start(
                    out=st["xt"][:, C * par:C * (par + 1)],
                    in_=x_d[128 * i:128 * (i + 1), :]),
                    writes=[bxt], tag=f"xld{i}")
                s.op("scalar", lambda e, par=par, i=i: e.activation(
                    st["xsq"][:], st["xt"][:, C * par:C * (par + 1)], AF.Copy,
                    accum_out=cS[:, SUM + i:SUM + i + 1]),
                    reads=[bxt], writes=[B["xsq"], B["stSUM"]], tag=f"rs{i}")
                s.op("scalar", lambda e, par=par, i=i: e.activation(
                    st["xsq"][:], st["xt"][:, C * par:C * (par + 1)], AF.Square,
                    accum_out=cS[:, SSQ + i:SSQ + i + 1]),
                    reads=[bxt], writes=[B["xsq"], B["stSSQ"]], tag=f"sq{i}")
                s.op("vector", lambda e, par=par, i=i: e.reduce_max(
                    cS[:, XMX + i:XMX + i + 1],
                    st["xt"][:, C * par:C * (par + 1)], AX),
                    reads=[bxt], writes=[B["stXMX"]], tag=f"mx{i}")
                s.op("vector", lambda e, par=par, i=i: e.tensor_reduce(
                    cS[:, XMN + i:XMN + i + 1],
                    st["xt"][:, C * par:C * (par + 1)], AX, ALU.min),
                    reads=[bxt], writes=[B["stXMN"]], tag=f"mn{i}")
            a, b = 4 * g, 4 * g + 4
            ch = [
                ("vector", [B["stSUM"]], lambda e, a=a, b=b: e.tensor_scalar(
                    cS[:, a:b], cS[:, SUM + a:SUM + b], 1.0 / C, None, ALU.mult), "m"),
                ("vector", [B["stSSQ"]], lambda e, a=a, b=b: e.tensor_scalar(
                    tc[:, 16:20], cS[:, SSQ + a:SSQ + b], 1.0 / C, None, ALU.mult), "e2"),
                ("vector", [], lambda e, a=a, b=b: e.tensor_mul(
                    tc[:, 20:24], cS[:, a:b], cS[:, a:b]), "mm"),
                ("vector", [], lambda e: e.tensor_sub(
                    tc[:, 24:28], tc[:, 16:20], tc[:, 20:24]), "v"),
                ("vector", [], lambda e: e.tensor_scalar_add(
                    tc[:, 28:32], tc[:, 24:28], LN_EPS), "u"),
                ("scalar", [], lambda e: e.activation(
                    tc[:, 32:36], tc[:, 28:32], AF.Sqrt), "sqr"),
                ("vector", [], lambda e: e.reciprocal(tc[:, 36:40], tc[:, 32:36]), "b0"),
                ("vector", [], lambda e: e.tensor_mul(
                    tc[:, 40:44], tc[:, 36:40], tc[:, 36:40]), "n1"),
                ("vector", [], lambda e: e.tensor_mul(
                    tc[:, 40:44], tc[:, 40:44], tc[:, 28:32]), "n2"),
                ("vector", [], lambda e: e.tensor_scalar(
                    tc[:, 40:44], tc[:, 40:44], -0.5, 1.5, ALU.mult, ALU.add), "n3"),
                ("vector", [], lambda e, a=a, b=b: e.tensor_mul(
                    cS[:, NTT + a:NTT + b], tc[:, 40:44], tc[:, 36:40]), "beta"),
                ("vector", [], lambda e: e.reciprocal(tc[:, 44:48], tc[:, 28:32]), "ru"),
                ("vector", [], lambda e: e.tensor_mul(
                    tc[:, 44:48], tc[:, 44:48], tc[:, 24:28]), "w"),
                ("scalar", [], lambda e: e.activation(
                    tc[:, 48:52], tc[:, 44:48], AF.Sqrt), "nr"),
                ("vector", [], lambda e: e.tensor_scalar_max(
                    tc[:, 48:52], tc[:, 48:52], 1e-8), "mxr"),
                ("vector", [], lambda e: e.reciprocal(tc[:, 52:56], tc[:, 48:52]), "rg"),
                ("vector", [], lambda e, a=a, b=b: e.tensor_mul(
                    cS[:, 2 * NTT + a:2 * NTT + b],
                    cS[:, NTT + a:NTT + b], tc[:, 52:56]), "al"),
                ("vector", [B["stXMX"]], lambda e, a=a, b=b: e.tensor_sub(
                    tc[:, 56:60], cS[:, XMX + a:XMX + b], cS[:, a:b]), "t1"),
                ("vector", [B["stXMN"]], lambda e, a=a, b=b: e.tensor_sub(
                    tc[:, 60:64], cS[:, a:b], cS[:, XMN + a:XMN + b]), "t2"),
                ("vector", [], lambda e: e.tensor_max(
                    tc[:, 56:60], tc[:, 56:60], tc[:, 60:64]), "t3"),
                ("vector", [], lambda e, a=a, b=b: e.tensor_mul(
                    tc[:, 56:60], tc[:, 56:60], cS[:, NTT + a:NTT + b]), "axn"),
                ("vector", [], lambda e: e.tensor_scalar_max(
                    tc[:, 56:60], tc[:, 56:60], 1e-5), "cl"),
                ("vector", [], lambda e: e.reciprocal(tc[:, 60:64], tc[:, 56:60]), "rc"),
                ("vector", [], lambda e, a=a, b=b: e.tensor_scalar(
                    cS[:, 3 * NTT + a:3 * NTT + b],
                    tc[:, 60:64], 127.0, None, ALU.mult), "sa"),
                ("vector", [], lambda e, a=a, b=b: e.tensor_scalar(
                    cS[:, 4 * NTT + a:4 * NTT + b],
                    tc[:, 56:60], 1.0 / 127.0, None, ALU.mult), "sar"),
            ]
            for eng, xreads, fn, nm in ch:
                s.op(eng, fn, reads=[B["stats"], B["tmpc"]] + xreads,
                     writes=[B["stats"], B["tmpc"]], tag=f"gc_{nm}{g}")
            for i in tiles:
                par = i % 2
                bxt, bxn, btm = B[f"xt{par}"], B[f"xnf{par}"], B[f"tmp1{par}"]
                bxh, bxq, bnb = B[f"xhb{par}"], B[f"xqb{par}"], B[f"xnb{par}"]
                s.dma(ch_in[par], lambda e, i=i, par=par: e.dma_start(
                    out=st["xt"][:, C * par:C * (par + 1)],
                    in_=x_d[128 * i:128 * (i + 1), :]),
                    writes=[bxt], tag=f"xl2_{i}")
                s.op("vector", lambda e, i=i, par=par: e.tensor_scalar(
                    st["xnf"][:, C * par:C * (par + 1)],
                    st["xt"][:, C * par:C * (par + 1)],
                    cS[:, i:i + 1], cS[:, NTT + i:NTT + i + 1],
                    ALU.subtract, ALU.mult),
                    reads=[bxt, B["stats"]], writes=[bxn], tag=f"xn{i}")
                s.op("vector", lambda e, i=i, par=par: e.tensor_scalar(
                    st["tmp1"][:, C * par:C * (par + 1)],
                    st["xt"][:, C * par:C * (par + 1)],
                    cS[:, i:i + 1], cS[:, 2 * NTT + i:2 * NTT + i + 1],
                    ALU.subtract, ALU.mult),
                    reads=[bxt, B["stats"]], writes=[btm], tag=f"xh{i}")
                s.op("vector", lambda e, par=par: e.tensor_mul(
                    st["xhb"][:, C * par:C * (par + 1)],
                    st["tmp1"][:, C * par:C * (par + 1)], st["rmsb"][:]),
                    reads=[btm, B["rmsb"]], writes=[bxh], tag=f"xhr{i}")
                s.op("vector", lambda e, i=i, par=par: e.tensor_scalar(
                    st["tmp1"][:, C * par:C * (par + 1)],
                    st["xnf"][:, C * par:C * (par + 1)],
                    cS[:, 3 * NTT + i:3 * NTT + i + 1], None, ALU.mult),
                    reads=[bxn, B["stats"]], writes=[btm], tag=f"xqs{i}")
                s.op("vector", lambda e, par=par: e.tensor_scalar(
                    st["xqb"][:, C * par:C * (par + 1)],
                    st["tmp1"][:, C * par:C * (par + 1)],
                    st["ceff"][:, 1:2], st["ceff"][:, 1:2], ALU.add, ALU.subtract),
                    reads=[btm, B["ceff"]], writes=[bxq], tag=f"xqr{i}")
                s.op("scalar", lambda e, par=par: e.activation(
                    st["xnb"][:, C * par:C * (par + 1)],
                    st["xnf"][:, C * par:C * (par + 1)], AF.Copy),
                    reads=[bxn], writes=[bnb], tag=f"xnb{i}")
                for kind, src2, bsrc in (("fm", st["xhb"], bxh), ("xn", st["xnb"], bnb),
                                         ("xq", st["xqb"], bxq)):
                    for cchunk in range(8):
                        s.op("tensor", lambda e, src2=src2, par=par, cchunk=cchunk:
                             e.transpose(ps["PT"][:, 128 * cchunk:128 * (cchunk + 1)],
                                         src2[:, C * par + 128 * cchunk:
                                              C * par + 128 * (cchunk + 1)],
                                         st["identb"][:]),
                             reads=[bsrc, B["identb"]], writes=[B["PT"]],
                             tag=f"tr_{kind}{i}_{cchunk}")
                    if kind == "fm":
                        s.op("vector", lambda e, i=i: e.tensor_copy(
                            bass.AP(xhfm, 128 * i, [[8 * NT, 128], [NT, 8], [1, 128]]),
                            bass.AP(ps["PT"], 0, [[1024, 128], [128, 8], [1, 128]])),
                            reads=[B["PT"]], writes=[B[f"XHFM{i // 4}"]],
                            tag=f"ev_{kind}{i}")
                    else:
                        koff = 0 if kind == "xn" else 1
                        reg = 2 * C * par + C * koff
                        bstg = B[f"stg{par}"]
                        s.op("scalar", lambda e, reg=reg: e.activation(
                            st["stg"][:, reg:reg + 1024], ps["PT"][:], AF.Copy),
                            reads=[B["PT"]], writes=[bstg], tag=f"sg_{kind}{i}")
                        if kind == "xq":
                            s.dma(ch_sp[2 + (i % 2)],
                                  lambda e, i=i, reg=reg: e.dma_start(
                                      out=sxq_d[i], in_=st["stg"][:, reg:reg + 1024]),
                                  reads=[bstg], tag=f"sp_xq{i}")
                        else:
                            s.dma(ch_sp[(i % 2)],
                                  lambda e, i=i, reg=reg: e.dma_start(
                                      out=sxn_d[:, :, 128 * i:128 * (i + 1)],
                                      in_=st["stg"][:, reg:reg + 1024]),
                                  reads=[bstg], tag=f"sp_xn{i}")
            yield

    def S1b2(wq=None, s1a=None):
        xhfm = st["XHFM"]
        o_order = []
        for cp in range(8):
            o_order += [cp, cp + 8]
        o_order += list(range(16, 40))
        prot = ["PA", "PB", "PC"]
        rrot = ["PD", "PG"]
        cnt = [0]
        for oi, o in enumerate(o_order):
            par = oi % 2
            bw = B[f"wbuf{par}"]
            s.dma(ch_w[par], lambda e, o=o, par=par: e.dma_start(
                out=st["wbuf"][:, 1024 * par:1024 * (par + 1)], in_=w1t_d[o]),
                writes=[bw], tag=f"wld{o}")
            is_q, is_k, is_v = o < 8, 8 <= o < 16, 16 <= o < 24
            cchunk = o % 8 if (o < 16) else None
            for w in range(4):
                if s1a is not None and oi == 0:
                    next(s1a, None)
                    next(s1a, None)
                tpar = w % 2
                btab = B[f"tabs{tpar}"]
                if is_q or is_k:
                    ctab_d = cqm_d if is_q else ckm_d
                    stab_d = sqm_d if is_q else skm_d
                    s.dma(ch_rd[tpar], lambda e, ctab_d=ctab_d, cchunk=cchunk, w=w, tpar=tpar:
                          e.dma_start(
                              out=st["tabs"][:, 2 * WIN * tpar:2 * WIN * tpar + WIN],
                              in_=ctab_d[128 * cchunk:128 * (cchunk + 1),
                                         WIN * w:WIN * (w + 1)]),
                          writes=[btab], tag=f"tc{o}_{w}")
                    s.dma(ch_rd[2 + tpar], lambda e, stab_d=stab_d, cchunk=cchunk, w=w, tpar=tpar:
                          e.dma_start(
                              out=st["tabs"][:, 2 * WIN * tpar + WIN:2 * WIN * (tpar + 1)],
                              in_=stab_d[128 * cchunk:128 * (cchunk + 1),
                                         WIN * w:WIN * (w + 1)]),
                          writes=[btab], tag=f"ts{o}_{w}")
                for sq in range(SEQ):
                    tw = T * sq + WIN * w
                    twin = WIN * w
                    pname = prot[cnt[0] % 3]
                    rname = rrot[cnt[0] % 2]
                    cnt[0] += 1
                    bp = B[pname]
                    for cci in range(8):
                        s.op("tensor", lambda e, pname=pname, par=par, cci=cci, tw=tw:
                             e.matmul(pap(pname),
                                      st["wbuf"][:, 1024 * par + 128 * cci:
                                                 1024 * par + 128 * (cci + 1)],
                                      bass.AP(xhfm, cci * NT + tw, [[8 * NT, 128], [1, WIN]]),
                                      start=(cci == 0), stop=(cci == 7)),
                             reads=[bw, B[f"XHFM{tw // WIN}"]], writes=[bp],
                             tag=f"mm{o}_{w}_{sq}_{cci}")
                    spar = (2 * w + sq) % 2
                    bqk = B[f"qkst{spar}"]
                    bxp = B[f"xps{spar}"]
                    if S1B_MODE == "mm_only":
                        s.op("scalar", lambda e, pname=pname, spar=spar: e.activation(
                            st["qkst"][:, 2 * WIN * spar:2 * WIN * spar + WIN],
                            ps[pname][:], AF.Copy),
                            reads=[bp], writes=[bqk], tag=f"dqb{o}_{w}_{sq}")
                        s.dma(ch_sp[4 + ((2 * w + sq) % 2)],
                              lambda e, o=o, tw=tw, spar=spar: e.dma_start(
                                  out=sdbg_d[o, :, tw:tw + WIN],
                                  in_=st["qkst"][:, 2 * WIN * spar:2 * WIN * spar + WIN]),
                              reads=[bqk], tag=f"dsp{o}_{w}_{sq}")
                        continue
                    if S1B_MODE == "rot" and (is_q or is_k):
                        s.op("scalar", lambda e, pname=pname, spar=spar: e.activation(
                            st["qkst"][:, 2 * WIN * spar:2 * WIN * spar + WIN],
                            ps[pname][:], AF.Copy),
                            reads=[bp], writes=[bqk], tag=f"dqb{o}_{w}_{sq}")
                        s.op("tensor", lambda e, spar=spar: e.matmul(
                            ps["PD"][:], st["pswapb"][:],
                            st["qkst"][:, 2 * WIN * spar:2 * WIN * spar + WIN],
                            start=True, stop=True),
                            reads=[bqk, B["pswapb"]], writes=[B["PD"]],
                            tag=f"drot{o}_{w}_{sq}")
                        s.op("vector", lambda e, spar=spar: e.tensor_copy(
                            st["qkst"][:, 2 * WIN * spar + WIN:2 * WIN * (spar + 1)],
                            bass.AP(ps["PD"], 0, [[512, 128], [1, WIN]])),
                            reads=[B["PD"]], writes=[bqk], tag=f"dre{o}_{w}_{sq}")
                        s.dma(ch_sp[4 + ((2 * w + sq) % 2)],
                              lambda e, o=o, tw=tw, spar=spar: e.dma_start(
                                  out=sdbg_d[o, :, tw:tw + WIN],
                                  in_=st["qkst"][:, 2 * WIN * spar + WIN:2 * WIN * (spar + 1)]),
                              reads=[bqk], tag=f"dsp{o}_{w}_{sq}")
                        continue
                    if S1B_MODE == "rot":
                        continue
                    if S1B_MODE in ("xpos", "xpos1", "xpos2", "xpos3", "xpos4") and not (is_q or is_k):
                        continue
                    if S1B_MODE == "xpos3" and (is_q or is_k):
                        s.op("vector", lambda e, pname=pname, tpar=tpar, spar=spar: e.tensor_mul(
                            st["xps"][:, WIN * spar:WIN * (spar + 1)], ps[pname][:],
                            st["tabs"][:, 2 * WIN * tpar:2 * WIN * tpar + WIN]),
                            reads=[bp, btab], writes=[bxp], tag=f"3xc{o}_{w}_{sq}")
                        btm = B[f"tmp1{spar}"]
                        s.op("vector", lambda e, pname=pname, tpar=tpar, spar=spar: e.tensor_mul(
                            st["tmp1"][:, C * spar:C * spar + WIN], ps[pname][:],
                            st["tabs"][:, 2 * WIN * tpar + WIN:2 * WIN * (tpar + 1)]),
                            reads=[bp, btab], writes=[btm], tag=f"3xs{o}_{w}_{sq}")
                        s.op("vector", lambda e, spar=spar: e.tensor_add(
                            st["qkst"][:, 2 * WIN * spar + WIN:2 * WIN * (spar + 1)],
                            st["xps"][:, WIN * spar:WIN * (spar + 1)],
                            st["tmp1"][:, C * spar:C * spar + WIN]),
                            reads=[bxp, btm], writes=[bqk], tag=f"3xa{o}_{w}_{sq}")
                        s.dma(ch_sp[4 + ((2 * w + sq) % 2)],
                              lambda e, o=o, tw=tw, spar=spar: e.dma_start(
                                  out=sdbg_d[o, :, tw:tw + WIN],
                                  in_=st["qkst"][:, 2 * WIN * spar + WIN:2 * WIN * (spar + 1)]),
                              reads=[bqk], tag=f"dsp{o}_{w}_{sq}")
                        continue
                    if S1B_MODE == "xpos4" and (is_q or is_k):
                        s.op("scalar", lambda e, pname=pname, spar=spar: e.activation(
                            st["qkst"][:, 2 * WIN * spar:2 * WIN * spar + WIN],
                            ps[pname][:], AF.Copy),
                            reads=[bp], writes=[bqk], tag=f"4qb{o}_{w}_{sq}")
                        s.op("tensor", lambda e, spar=spar: e.matmul(
                            ps["PD"][:], st["pswapb"][:],
                            st["qkst"][:, 2 * WIN * spar:2 * WIN * spar + WIN],
                            start=True, stop=True),
                            reads=[bqk, B["pswapb"]], writes=[B["PD"]],
                            tag=f"4rot{o}_{w}_{sq}")
                        btm = B[f"tmp1{spar}"]
                        s.op("vector", lambda e, tpar=tpar, spar=spar: e.tensor_mul(
                            st["tmp1"][:, C * spar:C * spar + WIN],
                            bass.AP(ps["PD"], 0, [[512, 128], [1, WIN]]),
                            st["tabs"][:, 2 * WIN * tpar + WIN:2 * WIN * (tpar + 1)]),
                            reads=[B["PD"], btab], writes=[btm], tag=f"4xs{o}_{w}_{sq}")
                        s.op("vector", lambda e, spar=spar: e.tensor_copy(
                            st["qkst"][:, 2 * WIN * spar + WIN:2 * WIN * (spar + 1)],
                            st["tmp1"][:, C * spar:C * spar + WIN]),
                            reads=[btm], writes=[bqk], tag=f"4cp{o}_{w}_{sq}")
                        s.dma(ch_sp[4 + ((2 * w + sq) % 2)],
                              lambda e, o=o, tw=tw, spar=spar: e.dma_start(
                                  out=sdbg_d[o, :, tw:tw + WIN],
                                  in_=st["qkst"][:, 2 * WIN * spar + WIN:2 * WIN * (spar + 1)]),
                              reads=[bqk], tag=f"dsp{o}_{w}_{sq}")
                        continue
                    if S1B_MODE == "xpos1" and (is_q or is_k):
                        s.op("vector", lambda e, pname=pname, tpar=tpar, spar=spar: e.tensor_mul(
                            st["xps"][:, WIN * spar:WIN * (spar + 1)], ps[pname][:],
                            st["tabs"][:, 2 * WIN * tpar:2 * WIN * tpar + WIN]),
                            reads=[bp, btab], writes=[bxp], tag=f"dxc{o}_{w}_{sq}")
                        s.op("scalar", lambda e, spar=spar: e.activation(
                            st["qkst"][:, 2 * WIN * spar:2 * WIN * spar + WIN],
                            st["xps"][:, WIN * spar:WIN * (spar + 1)], AF.Copy),
                            reads=[bxp], writes=[bqk], tag=f"dcp{o}_{w}_{sq}")
                        s.dma(ch_sp[4 + ((2 * w + sq) % 2)],
                              lambda e, o=o, tw=tw, spar=spar: e.dma_start(
                                  out=sdbg_d[o, :, tw:tw + WIN],
                                  in_=st["qkst"][:, 2 * WIN * spar:2 * WIN * spar + WIN]),
                              reads=[bqk], tag=f"dsp{o}_{w}_{sq}")
                        continue
                    if S1B_MODE == "xpos2" and (is_q or is_k):
                        s.op("scalar", lambda e, pname=pname, spar=spar: e.activation(
                            st["qkst"][:, 2 * WIN * spar:2 * WIN * spar + WIN],
                            ps[pname][:], AF.Copy),
                            reads=[bp], writes=[bqk], tag=f"qb{o}_{w}_{sq}")
                        s.op("tensor", lambda e, spar=spar, rname=rname: e.matmul(
                            ps[rname][:], st["pswapb"][:],
                            st["qkst"][:, 2 * WIN * spar:2 * WIN * spar + WIN],
                            start=True, stop=True),
                            reads=[bqk, B["pswapb"]], writes=[B[rname]],
                            tag=f"rot{o}_{w}_{sq}")
                        s.op("vector", lambda e, pname=pname, tpar=tpar, spar=spar: e.tensor_mul(
                            st["xps"][:, WIN * spar:WIN * (spar + 1)], ps[pname][:],
                            st["tabs"][:, 2 * WIN * tpar:2 * WIN * tpar + WIN]),
                            reads=[bp, btab], writes=[bxp], tag=f"xc{o}_{w}_{sq}")
                        btm = B[f"tmp1{spar}"]
                        s.op("vector", lambda e, tpar=tpar, spar=spar: e.tensor_mul(
                            st["tmp1"][:, C * spar:C * spar + WIN],
                            bass.AP(ps["PD"], 0, [[512, 128], [1, WIN]]),
                            st["tabs"][:, 2 * WIN * tpar + WIN:2 * WIN * (tpar + 1)]),
                            reads=[B["PD"], btab], writes=[btm], tag=f"xs{o}_{w}_{sq}")
                        s.op("vector", lambda e, spar=spar: e.tensor_add(
                            st["qkst"][:, 2 * WIN * spar + WIN:2 * WIN * (spar + 1)],
                            st["xps"][:, WIN * spar:WIN * (spar + 1)],
                            st["tmp1"][:, C * spar:C * spar + WIN]),
                            reads=[bxp, btm], writes=[bqk], tag=f"xa{o}_{w}_{sq}")
                        s.dma(ch_sp[4 + ((2 * w + sq) % 2)],
                              lambda e, o=o, tw=tw, spar=spar: e.dma_start(
                                  out=sdbg_d[o, :, tw:tw + WIN],
                                  in_=st["qkst"][:, 2 * WIN * spar + WIN:2 * WIN * (spar + 1)]),
                              reads=[bqk], tag=f"dsp{o}_{w}_{sq}")
                        continue
                    if S1B_MODE == "vff" and (is_q or is_k):
                        s.op("scalar", lambda e, pname=pname, spar=spar: e.activation(
                            st["qkst"][:, 2 * WIN * spar:2 * WIN * spar + WIN],
                            ps[pname][:], AF.Copy),
                            reads=[bp], writes=[bqk], tag=f"dqb{o}_{w}_{sq}")
                        s.dma(ch_sp[4 + ((2 * w + sq) % 2)],
                              lambda e, o=o, tw=tw, spar=spar: e.dma_start(
                                  out=sdbg_d[o, :, tw:tw + WIN],
                                  in_=st["qkst"][:, 2 * WIN * spar:2 * WIN * spar + WIN]),
                              reads=[bqk], tag=f"dsp{o}_{w}_{sq}")
                        continue
                    if is_q or is_k:
                        evac_op = s.op("scalar", lambda e, pname=pname, spar=spar: e.activation(
                            st["qkst"][:, 2 * WIN * spar:2 * WIN * spar + WIN],
                            pap(pname), AF.Copy),
                            reads=[bp], writes=[bqk], tag=f"qb{o}_{w}_{sq}")
                        s.op("tensor", lambda e, spar=spar, rname=rname: e.matmul(
                            ps[rname][:], st["pswapb"][:],
                            st["qkst"][:, 2 * WIN * spar:2 * WIN * spar + WIN],
                            start=True, stop=True),
                            reads=[bqk, B["pswapb"]], writes=[B[rname]],
                            tag=f"rot{o}_{w}_{sq}")
                        xc_op = s.op("vector", lambda e, pname=pname, tpar=tpar, spar=spar: e.tensor_mul(
                            st["xps"][:, WIN * spar:WIN * (spar + 1)], pap(pname),
                            st["tabs"][:, 2 * WIN * tpar:2 * WIN * tpar + WIN]),
                            reads=[bp, btab], writes=[bxp], tag=f"xc{o}_{w}_{sq}")

                        btm = B[f"tmp1{spar}"]
                        s.op("vector", lambda e, tpar=tpar, spar=spar, rname=rname: e.tensor_mul(
                            st["tmp1"][:, C * spar:C * spar + WIN], ps[rname][:],
                            st["tabs"][:, 2 * WIN * tpar + WIN:2 * WIN * (tpar + 1)]),
                            reads=[B[rname], btab], writes=[btm], tag=f"xs{o}_{w}_{sq}")
                        s.op("vector", lambda e, spar=spar: e.tensor_add(
                            st["qkst"][:, 2 * WIN * spar + WIN:2 * WIN * (spar + 1)],
                            st["xps"][:, WIN * spar:WIN * (spar + 1)],
                            st["tmp1"][:, C * spar:C * spar + WIN]),
                            reads=[bxp, btm], writes=[bqk], tag=f"xa{o}_{w}_{sq}")
                        dst_d = sq_d if is_q else sk_d
                        s.dma(ch_sp[4 + ((2 * w + sq) % 2)],
                              lambda e, dst_d=dst_d, sq=sq, cchunk=cchunk, twin=twin, spar=spar:
                              e.dma_start(
                                  out=dst_d[sq, 128 * cchunk:128 * (cchunk + 1),
                                            twin:twin + WIN],
                                  in_=st["qkst"][:, 2 * WIN * spar + WIN:
                                                 2 * WIN * (spar + 1)]),
                              reads=[bqk], tag=f"qko{o}_{w}_{sq}")
                    elif is_v:
                        s.op("scalar", lambda e, pname=pname, spar=spar: e.activation(
                            st["qkst"][:, 2 * WIN * spar:2 * WIN * spar + WIN],
                            pap(pname), AF.Copy),
                            reads=[bp], writes=[bqk], tag=f"vb{o}_{w}_{sq}")
                        for j in range(4):
                            s.op("tensor", lambda e, j=j, spar=spar: e.transpose(
                                ps["PT"][:, 128 * j:128 * (j + 1)],
                                st["qkst"][:, 2 * WIN * spar + 128 * j:
                                           2 * WIN * spar + 128 * (j + 1)],
                                st["identb"][:]),
                                reads=[bqk, B["identb"]], writes=[B["PT"]],
                                tag=f"vt{o}_{w}_{sq}_{j}")
                        s.op("scalar", lambda e, spar=spar: e.activation(
                            st["qkst"][:, 2 * WIN * spar + WIN:2 * WIN * (spar + 1)],
                            ps["PT"][:, 0:512], AF.Copy),
                            reads=[B["PT"]], writes=[bqk], tag=f"ve{o}_{w}_{sq}")
                        for j in range(4):
                            s.dma(ch_sp[6 + ((2 * w + sq + j) % 2)],
                                  lambda e, sq=sq, o=o, twin=twin, j=j, spar=spar:
                                  e.dma_start(
                                      out=sv_d[sq, twin + 128 * j:twin + 128 * (j + 1),
                                               128 * (o - 16):128 * (o - 15)],
                                      in_=st["qkst"][:, 2 * WIN * spar + WIN + 128 * j:
                                                     2 * WIN * spar + WIN + 128 * (j + 1)]),
                                  reads=[bqk], tag=f"vo{o}_{w}_{sq}_{j}")
                    else:
                        s.op("scalar", lambda e, pname=pname, spar=spar: e.activation(
                            st["qkst"][:, 2 * WIN * spar:2 * WIN * spar + WIN],
                            pap(pname), AF.Gelu),
                            reads=[bp], writes=[bqk], tag=f"gel{o}_{w}_{sq}")
                        s.dma(ch_sp[4 + ((2 * w + sq) % 2)],
                              lambda e, sq=sq, o=o, twin=twin, spar=spar: e.dma_start(
                                  out=sgel_d[sq, :, twin // 128:twin // 128 + 4,
                                             o - 24, :],
                                  in_=st["qkst"][:, 2 * WIN * spar:2 * WIN * spar + WIN]),
                              reads=[bqk], tag=f"go{o}_{w}_{sq}")
                if wq is not None and oi >= 16:
                    next(wq, None)
                    if oi >= 28:
                        next(wq, None)


    def carve_S2():
        sb_at("KK", (128, 8 * T), BF16, 0)
        sb_at("VV", (128, 16 * C), BF16, 32 * 1024)
        sb_at("Y", (128, 8 * T), F32, 64 * 1024)
        sb_at("QT", (128, 2 * WIN), BF16, 128 * 1024)
        sb_at("QB", (128, 2 * WIN), BF16, 159 * 1024)
        sb_at("SC", (128, 4 * WIN), BF16, 130 * 1024)
        sb_at("YT", (128, 2 * C), F32, 134 * 1024)
        sb_at("YSQT", (128, 2 * C), F32, 142 * 1024)
        sb_at("YN", (128, 2 * C), BF16, 150 * 1024)
        sb_at("MU", (128, 16 * 5), F32, 154 * 1024)
        sb_at("Y2S", (128, 2 * C), BF16, 155 * 1024)
        for nm in ("QT", "QB", "SC", "YT", "YSQT", "YN", "Y2S"):
            B[nm + "0"] = s.buf(nm + "0")
            B[nm + "1"] = s.buf(nm + "1")
        for nm in ("KK", "VV", "Y", "MU"):
            if nm not in B:
                B[nm] = s.buf(nm)
        for g in range(4):
            if f"Ycol{g}" not in B:
                B[f"Ycol{g}"] = s.buf(f"Ycol{g}")
        for sq in range(SEQ):
            for g in range(4):
                if f"Y2Asp{sq}_{g}" not in B:
                    B[f"Y2Asp{sq}_{g}"] = s.buf(f"Y2Asp{sq}_{g}")

    def S2(sq, gn_prev=None):
        if sq == 0:
            s.op("vector", lambda e: e.memset(st["QT"][64:128, :], 0.0),
                 writes=[B["QT0"], B["QT1"]], tag="qz0")
            s.op("vector", lambda e: e.memset(st["QB"][0:64, :], 0.0),
                 writes=[B["QB0"], B["QB1"]], tag="qz1")
        for cc in range(8):
            s.dma(ch_rd[cc % 2], lambda e, cc=cc, sq=sq: e.dma_start(
                out=st["KK"][:, T * cc:T * (cc + 1)],
                in_=sk_d[sq, 128 * cc:128 * (cc + 1), :]),
                writes=[B["KK"]], tag=f"Kld{sq}_{cc}")
        for stile in range(16):
            s.dma(ch_rd[2 + stile % 2], lambda e, stile=stile, sq=sq: e.dma_start(
                out=st["VV"][:, C * stile:C * (stile + 1)],
                in_=sv_d[sq, 128 * stile:128 * (stile + 1), :]),
                writes=[B["VV"]], tag=f"Vld{sq}_{stile}")
        scA = ["PA", "PB"]
        scB = ["PC", "PC"]
        for w in range(4):
            if gn_prev is not None:
                next(gn_prev, None)
            for hp in range(8):
                qpar = (4 * w + hp) % 2
                bq = B[f"QT{qpar}"]
                bqb = B[f"QB{qpar}"]
                s.dma(ch_rd[qpar], lambda e, sq=sq, hp=hp, w=w, qpar=qpar: e.dma_start(
                    out=st["QT"][0:64, WIN * qpar:WIN * (qpar + 1)],
                    in_=sq_d[sq, 128 * hp:128 * hp + 64, WIN * w:WIN * (w + 1)]),
                    writes=[bq], tag=f"Qld{sq}_{w}_{hp}")
                s.dma(ch_rd[qpar], lambda e, sq=sq, hp=hp, w=w, qpar=qpar: e.dma_start(
                    out=st["QB"][64:128, WIN * qpar:WIN * (qpar + 1)],
                    in_=sq_d[sq, 128 * hp + 64:128 * (hp + 1), WIN * w:WIN * (w + 1)]),
                    writes=[bqb], tag=f"Qlb{sq}_{w}_{hp}")
                nst = 4 * w + 4
                avdef = [None]
                for stt in range(nst):
                    pa = scA[stt % 2]
                    pb = scB[stt % 2]
                    s.op("tensor", lambda e, pa=pa, hp=hp, stt=stt, qpar=qpar: e.matmul(
                        ps[pa][:], st["KK"][:, T * hp + 128 * stt:T * hp + 128 * (stt + 1)],
                        st["QT"][:, WIN * qpar:WIN * (qpar + 1)],
                        start=True, stop=True),
                        reads=[B["KK"], bq], writes=[B[pa]], tag=f"sMA{sq}_{w}_{hp}_{stt}")
                    s.op("tensor", lambda e, pb=pb, hp=hp, stt=stt, qpar=qpar: e.matmul(
                        ps[pb][:], st["KK"][:, T * hp + 128 * stt:T * hp + 128 * (stt + 1)],
                        st["QB"][:, WIN * qpar:WIN * (qpar + 1)],
                        start=True, stop=True),
                        reads=[B["KK"], bqb], writes=[B[pb]], tag=f"sMB{sq}_{w}_{hp}_{stt}")
                    diag = stt >= 4 * w
                    r = stt - 4 * w
                    bsc = B[f"SC{stt % 2}"]
                    if diag:
                        s.op("vector", lambda e, pa=pa, stt=stt, r=r: e.tensor_mul(
                            st["SC"][:, WIN * (stt % 2):WIN * (stt % 2 + 1)],
                            ps[pa][:], st["tri"][:, WIN * r:WIN * (r + 1)]),
                            reads=[B[pa], B["tri"]], writes=[bsc], tag=f"eA{sq}_{w}_{hp}_{stt}")
                        s.op("vector", lambda e, pb=pb, stt=stt, r=r: e.tensor_mul(
                            st["SC"][:, WIN * (2 + stt % 2):WIN * (3 + stt % 2)],
                            ps[pb][:], st["tri"][:, WIN * r:WIN * (r + 1)]),
                            reads=[B[pb], B["tri"]], writes=[bsc], tag=f"eB{sq}_{w}_{hp}_{stt}")
                    else:
                        s.op("scalar", lambda e, pa=pa, stt=stt: e.activation(
                            st["SC"][:, WIN * (stt % 2):WIN * (stt % 2 + 1)],
                            ps[pa][:], AF.Copy, scale=0.125),
                            reads=[B[pa]], writes=[bsc], tag=f"eA{sq}_{w}_{hp}_{stt}")
                        s.op("vector", lambda e, pb=pb, stt=stt: e.tensor_scalar(
                            st["SC"][:, WIN * (2 + stt % 2):WIN * (3 + stt % 2)],
                            ps[pb][:], 0.125, None, ALU.mult),
                            reads=[B[pb]], writes=[bsc], tag=f"eB{sq}_{w}_{hp}_{stt}")
                    if avdef[0] is not None:
                        avdef[0]()
                    def avblock(hp=hp, stt=stt, nst=nst):
                        s.op("tensor", lambda e, hp=hp, stt=stt, nst=nst: e.matmul(
                            ps["PE2"][0:64, 0:512],
                            st["VV"][:, C * stt + 128 * hp:C * stt + 128 * hp + 64],
                            st["SC"][:, WIN * (stt % 2):WIN * (stt % 2 + 1)],
                            start=(stt == 0), stop=(stt == nst - 1)),
                            reads=[B["VV"], B[f"SC{stt % 2}"]], writes=[B["PE2A"]],
                            tag=f"aMA{sq}_{w}_{hp}_{stt}")
                        s.op("tensor", lambda e, hp=hp, stt=stt, nst=nst: e.matmul(
                            ps["PE2"][0:64, 512:1024],
                            st["VV"][:, C * stt + 128 * hp + 64:C * stt + 128 * (hp + 1)],
                            st["SC"][:, WIN * (2 + stt % 2):WIN * (3 + stt % 2)],
                            start=(stt == 0), stop=(stt == nst - 1)),
                            reads=[B["VV"], B[f"SC{stt % 2}"]], writes=[B["PE2B"]],
                            tag=f"aMB{sq}_{w}_{hp}_{stt}")
                    avdef[0] = avblock
                avdef[0]()
                s.op("scalar", lambda e, hp=hp, w=w: e.activation(
                    st["Y"][0:64, T * hp + WIN * w:T * hp + WIN * (w + 1)],
                    ps["PE2"][0:64, 0:512], AF.Copy),
                    reads=[B["PE2A"]], writes=[B[f"Ycol{w}"]], tag=f"yA{sq}_{w}_{hp}")
                s.op("scalar", lambda e, hp=hp, w=w: e.activation(
                    st["Y"][64:128, T * hp + WIN * w:T * hp + WIN * (w + 1)],
                    ps["PE2"][0:64, 512:1024], AF.Copy),
                    reads=[B["PE2B"]], writes=[B[f"Ycol{w}"]], tag=f"yB{sq}_{w}_{hp}")
    def GN_gen(sq):
        gdefer = [None]
        for tt in range(16):
            par = tt % 2
            byt, bysq, byn, by2 = (B[f"YT{par}"], B[f"YSQT{par}"],
                                   B[f"YN{par}"], B[f"Y2S{par}"])
            byc = B[f"Ycol{tt // 4}"]
            for gg in range(2):
                pname = "PD" if gg == 0 else "PG"
                for j in range(4):
                    cc = 4 * gg + j
                    s.op("tensor", lambda e, cc=cc, j=j, tt=tt, pname=pname: e.transpose(
                        ps[pname][:, 128 * j:128 * (j + 1)],
                        st["Y"][:, T * cc + 128 * tt:T * cc + 128 * (tt + 1)],
                        st["identf"][:]),
                        reads=[byc, B["identf"]], writes=[B[pname]],
                        tag=f"gt{sq}_{tt}_{cc}")
                s.op("scalar", lambda e, gg=gg, par=par, pname=pname: e.activation(
                    st["YT"][:, C * par + 512 * gg:C * par + 512 * (gg + 1)],
                    ps[pname][:], AF.Copy),
                    reads=[B[pname]], writes=[byt], tag=f"ge{sq}_{tt}_{gg}")
            s.op("vector", lambda e, par=par, tt=tt: e.reduce_sum(
                st["MU"][:, 16 * (tt % 2):16 * (tt % 2) + 16],
                bass.AP(st["YT"], C * par, [[2 * C, 128], [64, 16], [1, 64]]), AX),
                reads=[byt, B["MU"]], writes=[B["MU"]], tag=f"gmu{sq}_{tt}")
            s.op("scalar", lambda e, par=par: e.activation(
                st["YSQT"][:, C * par:C * (par + 1)],
                st["YT"][:, C * par:C * (par + 1)], AF.Square),
                reads=[byt], writes=[bysq], tag=f"gsq{sq}_{tt}")
            s.op("vector", lambda e, par=par, tt=tt: e.reduce_sum(
                st["MU"][:, 32 + 16 * (tt % 2):32 + 16 * (tt % 2) + 16],
                bass.AP(st["YSQT"], C * par, [[2 * C, 128], [64, 16], [1, 64]]), AX),
                reads=[bysq, B["MU"]], writes=[B["MU"]], tag=f"gs2{sq}_{tt}")
            mcol = slice(16 * (tt % 2), 16 * (tt % 2) + 16)
            scol = slice(32 + 16 * (tt % 2), 32 + 16 * (tt % 2) + 16)
            s.op("vector", lambda e, mcol=mcol: e.tensor_scalar(
                st["MU"][:, mcol], st["MU"][:, mcol], 1.0 / 64.0, None, ALU.mult),
                reads=[B["MU"]], writes=[B["MU"]], tag=f"gm1{sq}_{tt}")
            s.op("vector", lambda e, mcol=mcol, scol=scol: e.tensor_mul(
                st["MU"][:, 64:80], st["MU"][:, mcol], st["MU"][:, mcol]),
                reads=[B["MU"]], writes=[B["MU"]], tag=f"gm2{sq}_{tt}")
            s.op("vector", lambda e, scol=scol: e.tensor_scalar(
                st["MU"][:, scol], st["MU"][:, scol], 1.0 / 64.0, GN_EPS, ALU.mult, ALU.add),
                reads=[B["MU"]], writes=[B["MU"]], tag=f"gm3{sq}_{tt}")
            s.op("vector", lambda e, scol=scol: e.tensor_sub(
                st["MU"][:, scol], st["MU"][:, scol], st["MU"][:, 64:80]),
                reads=[B["MU"]], writes=[B["MU"]], tag=f"gm4{sq}_{tt}")
            s.op("scalar", lambda e, scol=scol: e.activation(
                st["MU"][:, 64:80], st["MU"][:, scol], AF.Sqrt),
                reads=[B["MU"]], writes=[B["MU"]], tag=f"gm7{sq}_{tt}")
            s.op("vector", lambda e: e.reciprocal(st["MU"][:, 64:80], st["MU"][:, 64:80]),
                reads=[B["MU"]], writes=[B["MU"]], tag=f"gm8{sq}_{tt}")
            for h in range(16):
                s.op("vector", lambda e, h=h, par=par, mcol=mcol: e.tensor_scalar(
                    st["YN"][:, C * par + 64 * h:C * par + 64 * (h + 1)],
                    st["YT"][:, C * par + 64 * h:C * par + 64 * (h + 1)],
                    st["MU"][:, mcol.start + h:mcol.start + h + 1],
                    st["MU"][:, 64 + h:64 + h + 1], ALU.subtract, ALU.mult),
                    reads=[byt, B["MU"]], writes=[byn], tag=f"gn{sq}_{tt}_{h}")
            if gdefer[0] is not None:
                gdefer[0]()
            def gblock(tt=tt, par=par, byn=byn, by2=by2, sq=sq):
                for cc in range(8):
                    s.op("tensor", lambda e, cc=cc, par=par: e.transpose(
                        ps["PT"][:, 128 * cc:128 * (cc + 1)],
                        st["YN"][:, C * par + 128 * cc:C * par + 128 * (cc + 1)],
                        st["identb"][:]),
                        reads=[byn, B["identb"]], writes=[B["PT"]], tag=f"gb{sq}_{tt}_{cc}")
                for cc in range(8):
                    s.op("vector", lambda e, cc=cc, par=par: e.tensor_scalar(
                        st["Y2S"][:, C * par + 128 * cc:C * par + 128 * (cc + 1)],
                        ps["PT"][:, 128 * cc:128 * (cc + 1)],
                        st["vecs"][:, 16 + cc:17 + cc], st["vecs"][:, 24 + cc:25 + cc],
                        ALU.mult, ALU.add),
                        reads=[B["PT"], B["vecs"]], writes=[by2], tag=f"ga{sq}_{tt}_{cc}")
                s.dma(ch_sp[4 + (tt % 2)], lambda e, tt=tt, sq=sq, par=par:
                      e.dma_start(
                          out=sy2a_d[sq, :, :, 128 * tt:128 * (tt + 1)],
                          in_=st["Y2S"][:, C * par:C * (par + 1)]),
                      reads=[by2], writes=[B[f"Y2Asp{sq}_{tt // 4}"]], tag=f"gsp{sq}_{tt}")
            gdefer[0] = gblock
            if tt % 4 == 3:
                yield
        gdefer[0]()

    def carve_S4():
        sb_at("WGT", (128, 64 * 128), BF16, 0)
        sb_at("XNM", (128, 2 * 8 * WIN), BF16, 16 * 1024)
        sb_at("YH", (128, 2 * WIN), BF16, 32 * 1024)
        sb_at("GS", (128, 2 * WIN), BF16, 34 * 1024)
        sb_at("Y2O", (128, 2 * WIN), BF16, 36 * 1024)
        for nm in ("XNM", "YH", "GS", "Y2O"):
            B[nm + "0"] = s.buf(nm + "0")
            B[nm + "1"] = s.buf(nm + "1")
        B["WGT"] = s.buf("WGT")

    def S4(gn=None):
        for o in range(8):
            s.dma(ch_w[o % 2], lambda e, o=o: e.dma_start(
                out=st["WGT"][:, 1024 * o:1024 * (o + 1)], in_=wgt_d[o]),
                writes=[B["WGT"]], tag=f"wgld{o}")
        pp = ["PA", "PB", "PC"]
        for sq in range(SEQ):
            for w in range(4):
                if gn is not None and (sq == 0 or w == 0):
                    next(gn, None)
                wpar = (4 * sq + w) % 2
                tw = T * sq + WIN * w
                bxn = B[f"XNM{wpar}"]
                for cc in range(8):
                    s.dma(ch_rd[cc % 2], lambda e, cc=cc, tw=tw, wpar=wpar: e.dma_start(
                        out=st["XNM"][:, 8 * WIN * wpar + WIN * cc:
                                      8 * WIN * wpar + WIN * (cc + 1)],
                        in_=sxn_d[:, cc, tw:tw + WIN]),
                        writes=[bxn], tag=f"xnl{sq}_{w}_{cc}")
                for o in range(8):
                    par = o % 2
                    p3 = o % 3
                    byh, bgs, by2 = B[f"YH{par}"], B[f"GS{par}"], B[f"Y2O{par}"]
                    s.dma(ch_rd[2 + par], lambda e, o=o, sq=sq, w=w, par=par: e.dma_start(
                        out=st["YH"][:, WIN * par:WIN * (par + 1)],
                        in_=sy2a_d[sq, :, o, WIN * w:WIN * (w + 1)]),
                        reads=[B[f"Y2Asp{sq}_{w}"]],
                        writes=[byh], tag=f"yhl{o}_{sq}_{w}")
                    pname = pp[p3]
                    for cc in range(8):
                        s.op("tensor", lambda e, pname=pname, o=o, cc=cc, wpar=wpar:
                             e.matmul(ps[pname][:],
                                      st["WGT"][:, 1024 * o + 128 * cc:
                                                 1024 * o + 128 * (cc + 1)],
                                      st["XNM"][:, 8 * WIN * wpar + WIN * cc:
                                                 8 * WIN * wpar + WIN * (cc + 1)],
                                      start=(cc == 0), stop=(cc == 7)),
                             reads=[B["WGT"], bxn], writes=[B[pname]],
                             tag=f"gmm{o}_{sq}_{w}_{cc}")
                    s.op("scalar", lambda e, pname=pname, o=o, par=par: e.activation(
                        st["GS"][:, WIN * par:WIN * (par + 1)], ps[pname][:],
                        AF.Silu, bias=st["vecs"][:, o:o + 1]),
                        reads=[B[pname], B["vecs"]], writes=[bgs], tag=f"gsl{o}_{sq}_{w}")
                    s.op("vector", lambda e, par=par: e.tensor_mul(
                        st["Y2O"][:, WIN * par:WIN * (par + 1)],
                        st["GS"][:, WIN * par:WIN * (par + 1)],
                        st["YH"][:, WIN * par:WIN * (par + 1)]),
                        reads=[bgs, byh], writes=[by2], tag=f"gy2{o}_{sq}_{w}")
                    s.dma(ch_sp[par], lambda e, o=o, tw=tw, par=par: e.dma_start(
                        out=sy2_d[:, tw // 128:tw // 128 + 4, o, :],
                        in_=st["Y2O"][:, WIN * par:WIN * (par + 1)]),
                        reads=[by2], tag=f"y2o{o}_{sq}_{w}")

    def carve_S5():
        sb_at("WGQ", (128, 8 * INNER), BF16, 0)
        sb_at("WUQ", (128, 8 * INNER), BF16, 64 * 1024)
        sb_at("XQS", (128, 2 * 1024), BF16, 128 * 1024)
        sb_at("HH", (128, INNER), F32, 132 * 1024)
        sb_at("VT", (128, INNER), F32, 148 * 1024)
        sb_at("HQT", (128, INNER), BF16, 164 * 1024)
        sb_at("HQS", (128, 1024), BF16, 172 * 1024)
        for nm in ("XQS",):
            B[nm + "0"] = s.buf(nm + "0")
            B[nm + "1"] = s.buf(nm + "1")
        for nm in ("WGQ", "WUQ", "HH", "VT", "HQT", "HQS"):
            B[nm] = s.buf(nm)

    def S5():
        for cc in range(8):
            s.dma(ch_rd[cc % 2], lambda e, cc=cc: e.dma_start(
                out=st["WGQ"][:, INNER * cc:INNER * (cc + 1)],
                in_=swgq_d[128 * cc:128 * (cc + 1), :]),
                writes=[B["WGQ"]], tag=f"wgq{cc}")
            s.dma(ch_rd[2 + cc % 2], lambda e, cc=cc: e.dma_start(
                out=st["WUQ"][:, INNER * cc:INNER * (cc + 1)],
                in_=swuq_d[128 * cc:128 * (cc + 1), :]),
                writes=[B["WUQ"]], tag=f"wuq{cc}")
        tc = st["tmpc"]
        tdefer = [None]
        for i in range(NTT):
            par = i % 2
            bxq = B[f"XQS{par}"]
            s.dma(ch_in[par], lambda e, i=i, par=par: e.dma_start(
                out=st["XQS"][:, 1024 * par:1024 * (par + 1)], in_=sxq_d[i]),
                writes=[bxq], tag=f"xql{i}")
            s.op("vector", lambda e, i=i: e.tensor_scalar(
                tc[:, 0:1], st["stats"][:, 4 * NTT + i:4 * NTT + i + 1],
                st["wsc"][:, 0:1], None, ALU.mult),
                reads=[B["stats"], B["wsc"], B["tmpc"]], writes=[B["tmpc"]], tag=f"dqg{i}")
            s.op("vector", lambda e, i=i: e.tensor_scalar(
                tc[:, 1:2], st["stats"][:, 4 * NTT + i:4 * NTT + i + 1],
                st["wsc"][:, 1:2], None, ALU.mult),
                reads=[B["stats"], B["wsc"], B["tmpc"]], writes=[B["tmpc"]], tag=f"dqu{i}")
            pg = ["PA", "PC"]
            pu = ["PB", "PD", "PG"]
            for iw in range(8):
                psg, psu = pg[iw % 2], pu[iw % 3]
                for cc in range(8):
                    s.op("tensor", lambda e, psg=psg, cc=cc, iw=iw, par=par: e.matmul(
                        pap(psg),
                        st["XQS"][:, 1024 * par + 128 * cc:1024 * par + 128 * (cc + 1)],
                        st["WGQ"][:, INNER * cc + WIN * iw:INNER * cc + WIN * (iw + 1)],
                        start=(cc == 0), stop=(cc == 7)),
                        reads=[bxq, B["WGQ"]], writes=[B[psg]], tag=f"fg{i}_{iw}_{cc}")
                    s.op("tensor", lambda e, psu=psu, cc=cc, iw=iw, par=par: e.matmul(
                        pap(psu),
                        st["XQS"][:, 1024 * par + 128 * cc:1024 * par + 128 * (cc + 1)],
                        st["WUQ"][:, INNER * cc + WIN * iw:INNER * cc + WIN * (iw + 1)],
                        start=(cc == 0), stop=(cc == 7)),
                        reads=[bxq, B["WUQ"]], writes=[B[psu]], tag=f"fu{i}_{iw}_{cc}")
                s.op("scalar", lambda e, psg=psg, iw=iw: e.activation(
                    st["VT"][:, WIN * iw:WIN * (iw + 1)], pap(psg), AF.Silu,
                    scale=tc[:, 0:1]),
                    reads=[B[psg], B["tmpc"]], writes=[B["VT"]], tag=f"fsl{i}_{iw}")
                s.op("vector", lambda e, psu=psu, iw=iw: e.tensor_mul(
                    st["HH"][:, WIN * iw:WIN * (iw + 1)],
                    st["VT"][:, WIN * iw:WIN * (iw + 1)], pap(psu)),
                    reads=[B[psu], B["VT"]], writes=[B["HH"]], tag=f"fh{i}_{iw}")
            if tdefer[0] is not None:
                tdefer[0]()
            s.op("vector", lambda e: e.reduce_max(tc[:, 2:3], st["HH"][:], AX,
                                                  apply_absolute_value=True),
                reads=[B["HH"], B["tmpc"]], writes=[B["tmpc"]], tag=f"fam{i}")
            s.op("vector", lambda e: e.tensor_mul(tc[:, 2:3], tc[:, 2:3], tc[:, 1:2]),
                reads=[B["tmpc"]], writes=[B["tmpc"]], tag=f"fm1{i}")
            s.op("vector", lambda e: e.tensor_scalar_max(tc[:, 2:3], tc[:, 2:3], 1e-5),
                reads=[B["tmpc"]], writes=[B["tmpc"]], tag=f"fm2{i}")
            s.op("vector", lambda e: e.reciprocal(tc[:, 3:4], tc[:, 2:3]),
                reads=[B["tmpc"]], writes=[B["tmpc"]], tag=f"fm3{i}")
            s.op("vector", lambda e: e.tensor_scalar(
                tc[:, 4:5], tc[:, 3:4], 127.0, None, ALU.mult),
                reads=[B["tmpc"]], writes=[B["tmpc"]], tag=f"fm4{i}")
            s.op("vector", lambda e: e.tensor_mul(tc[:, 4:5], tc[:, 4:5], tc[:, 1:2]),
                reads=[B["tmpc"]], writes=[B["tmpc"]], tag=f"fm5{i}")
            s.op("vector", lambda e: e.tensor_scalar(
                tc[:, 5:6], tc[:, 2:3], 1.0 / 127.0, None, ALU.mult),
                reads=[B["tmpc"]], writes=[B["tmpc"]], tag=f"fm6{i}")
            s.op("vector", lambda e: e.tensor_scalar(
                tc[:, 5:6], tc[:, 5:6], st["wsc"][:, 2:3], None, ALU.mult),
                reads=[B["tmpc"], B["wsc"]], writes=[B["tmpc"]], tag=f"fm7{i}")
            s.op("vector", lambda e: e.tensor_scalar(
                st["VT"][:], st["HH"][:], tc[:, 4:5], st["ceff"][:, 1:2],
                ALU.mult, ALU.add),
                reads=[B["HH"], B["tmpc"], B["ceff"]], writes=[B["VT"]], tag=f"fq1{i}")
            s.op("vector", lambda e: e.tensor_scalar(
                st["HH"][:], st["VT"][:], st["ceff"][:, 1:2], None, ALU.subtract),
                reads=[B["VT"], B["ceff"]], writes=[B["HH"]], tag=f"fq2{i}")
            hqt_t = st["HQT"]
            hqt_b = B["HQT"]
            s.op("vector", lambda e, hqt_t=hqt_t: e.tensor_scalar(
                hqt_t[:], st["HH"][:], tc[:, 5:6], None, ALU.mult),
                reads=[B["HH"], B["tmpc"]], writes=[hqt_b], tag=f"fq3{i}")

            def tblock(i=i, hqt_t=hqt_t, hqt_b=hqt_b):
                for r in range(4):
                    for g in range(8):
                        ic = 8 * r + g
                        s.op("tensor", lambda e, ic=ic, g=g, hqt_t=hqt_t: e.transpose(
                            ps["PT"][:, 128 * g:128 * (g + 1)],
                            hqt_t[:, 128 * ic:128 * (ic + 1)], st["identb"][:]),
                            reads=[hqt_b, B["identb"]], writes=[B["PT"]],
                            tag=f"ftr{i}_{ic}")
                    s.op("vector", lambda e: e.tensor_copy(st["HQS"][:], ps["PT"][:]),
                         reads=[B["PT"]], writes=[B["HQS"]], tag=f"fte{i}_{r}")
                    s.dma(ch_sp[4 + r % 2], lambda e, i=i, r=r: e.dma_start(
                        out=shq_d[i, r], in_=st["HQS"][:]),
                        reads=[B["HQS"]], tag=f"fsp{i}_{r}")
            tdefer[0] = tblock

        tdefer[0]()

    def carve_S6():
        sb_at("WDQ", (128, 32 * 1024), BF16, 0)
        sb_at("WPT", (128, 8 * 1024), BF16, 64 * 1024)
        sb_at("WFT", (128, 16 * 1024), BF16, 80 * 1024)
        sb_at("HQ6", (128, 2 * 4096), BF16, 112 * 1024)
        sb_at("Y2T", (128, 2 * 1024), BF16, 128 * 1024)
        sb_at("GET", (128, 2 * 2048), BF16, 132 * 1024)
        sb_at("XS", (128, 2 * 1024), F32, 140 * 1024)
        sb_at("OUTS", (128, 2 * 1024), F32, 148 * 1024)
        for nm in ("HQ6", "Y2T", "GET", "XS", "OUTS"):
            B[nm + "0"] = s.buf(nm + "0")
            B[nm + "1"] = s.buf(nm + "1")
        for nm in ("WDQ", "WPT", "WFT"):
            B[nm] = s.buf(nm)

    def S6():
        for ic in range(32):
            s.dma(ch_rd[ic % 2], lambda e, ic=ic: e.dma_start(
                out=st["WDQ"][:, 1024 * ic:1024 * (ic + 1)],
                in_=swdq_d[128 * ic:128 * (ic + 1), :]),
                writes=[B["WDQ"]], tag=f"wdq{ic}")
        for cc in range(8):
            s.dma(ch_w[cc % 2], lambda e, cc=cc: e.dma_start(
                out=st["WPT"][:, 1024 * cc:1024 * (cc + 1)],
                in_=wpt_d[128 * cc:128 * (cc + 1), :]),
                writes=[B["WPT"]], tag=f"wptl{cc}")
        for ec in range(16):
            s.dma(ch_w[ec % 2], lambda e, ec=ec: e.dma_start(
                out=st["WFT"][:, 1024 * ec:1024 * (ec + 1)],
                in_=wft_d[128 * ec:128 * (ec + 1), :]),
                writes=[B["WFT"]], tag=f"wftl{ec}")
        tc = st["tmpc"]
        odefer = [None]
        for i in range(NTT):
            par = i % 2
            sq, tt = i // 16, i % 16
            bhq, by2, bge, bxs, bo = (B[f"HQ6{par}"], B[f"Y2T{par}"],
                                      B[f"GET{par}"], B[f"XS{par}"], B[f"OUTS{par}"])
            s.dma(ch_in[par], lambda e, i=i, par=par: e.dma_start(
                out=st["XS"][:, 1024 * par:1024 * (par + 1)],
                in_=x_d[128 * i:128 * (i + 1), :]),
                writes=[bxs], tag=f"oxl{i}")
            for r in range(4):
                s.dma(ch_rd[2 + r % 2], lambda e, i=i, r=r, par=par: e.dma_start(
                    out=st["HQ6"][:, 4096 * par + 1024 * r:4096 * par + 1024 * (r + 1)],
                    in_=shq_d[i, r]),
                    writes=[bhq], tag=f"ohl{i}_{r}")
            s.dma(ch_rd[i % 2], lambda e, i=i, par=par: e.dma_start(
                out=st["Y2T"][:, 1024 * par:1024 * (par + 1)],
                in_=sy2_d[:, i, :, :]),
                writes=[by2], tag=f"oyl{i}")
            s.dma(ch_sp[6 + i % 2], lambda e, sq=sq, tt=tt, par=par: e.dma_start(
                out=st["GET"][:, 2048 * par:2048 * (par + 1)],
                in_=sgel_d[sq, :, tt, :, :]),
                writes=[bge], tag=f"ogl{i}")
            if par == 0:
                ph = [("PE2", 0), ("PE2", 512)]
                bps = [B["PE2A"], B["PE2B"]]
            else:
                ph = [("PA", 0), ("PB", 0)]
                bps = [B["PA"], B["PB"]]

            def mmpair(lhs_ap, rhs_base, rhs_tensor, brd, bstat, start, stop, tag):
                for hh in range(2):
                    pn, poff = ph[hh]
                    s.op("tensor", lambda e, pn=pn, poff=poff, lhs_ap=lhs_ap,
                         rhs_base=rhs_base, rhs_tensor=rhs_tensor, hh=hh,
                         start=start, stop=stop:
                         e.matmul(ps[pn][:, poff:poff + 512], lhs_ap(),
                                  rhs_tensor[:, rhs_base + 512 * hh:rhs_base + 512 * (hh + 1)],
                                  start=start, stop=stop),
                         reads=[brd, bstat],
                         writes=[bps[hh]], tag=f"{tag}_{hh}")

            for hh in range(2):
                pn, poff = ph[hh]
                s.op("tensor", lambda e, pn=pn, poff=poff, hh=hh: e.matmul(
                    ps[pn][:, poff:poff + 512], st["onesrow"][0:1, :],
                    st["rowbuf"][0:1, 512 * hh:512 * (hh + 1)],
                    start=True, stop=False),
                    reads=[B["onesrow"], B["rowbuf"]], writes=[bps[hh]],
                    tag=f"obias{i}_{hh}")
            for cc in range(8):
                mmpair(lambda cc=cc, par=par: st["Y2T"][:, 1024 * par + 128 * cc:
                                                        1024 * par + 128 * (cc + 1)],
                       1024 * cc, st["WPT"], B["WPT"], by2, False, False, f"omp{i}_{cc}")
            for ec in range(16):
                mmpair(lambda ec=ec, par=par: st["GET"][:, 2048 * par + 128 * ec:
                                                        2048 * par + 128 * (ec + 1)],
                       1024 * ec, st["WFT"], B["WFT"], bge, False, False, f"omf{i}_{ec}")
            for ic in range(32):
                mmpair(lambda ic=ic, par=par: st["HQ6"][:, 4096 * par + 128 * ic:
                                                         4096 * par + 128 * (ic + 1)],
                       1024 * ic, st["WDQ"], B["WDQ"], bhq, False, ic == 31, f"omd{i}_{ic}")
            if odefer[0] is not None:
                odefer[0]()
            def oblock(i=i, par=par, ph=ph, bps=bps, bxs=bxs, bo=bo):
                s.op("vector", lambda e, i=i: e.tensor_scalar(
                    tc[:, 8:9], st["stats"][:, NTT + i:NTT + i + 1], 1.0, None, ALU.add),
                    reads=[B["stats"], B["tmpc"]], writes=[B["tmpc"]], tag=f"oc1{i}")
                s.op("vector", lambda e, i=i: e.tensor_mul(
                    tc[:, 9:10], st["stats"][:, i:i + 1], st["stats"][:, NTT + i:NTT + i + 1]),
                    reads=[B["stats"], B["tmpc"]], writes=[B["tmpc"]], tag=f"oc2{i}")
                s.op("vector", lambda e, par=par: e.tensor_scalar(
                    st["OUTS"][:, 1024 * par:1024 * (par + 1)],
                    st["XS"][:, 1024 * par:1024 * (par + 1)],
                    tc[:, 8:9], tc[:, 9:10], ALU.mult, ALU.subtract),
                    reads=[bxs, B["tmpc"]], writes=[bo], tag=f"oc3{i}")
                for hh in range(2):
                    pn, poff = ph[hh]
                    s.op("vector", lambda e, pn=pn, poff=poff, par=par, hh=hh: e.tensor_add(
                        st["OUTS"][:, 1024 * par + 512 * hh:1024 * par + 512 * (hh + 1)],
                        st["OUTS"][:, 1024 * par + 512 * hh:1024 * par + 512 * (hh + 1)],
                        ps[pn][:, poff:poff + 512]),
                        reads=[bps[hh], bo], writes=[bo], tag=f"oadd{i}_{hh}")
                s.dma(ch_out[par], lambda e, i=i, par=par: e.dma_start(
                    out=out_d[128 * i:128 * (i + 1), :],
                    in_=st["OUTS"][:, 1024 * par:1024 * (par + 1)]),
                    reads=[bo], tag=f"oout{i}")
            odefer[0] = oblock
        odefer[0]()

    ORDER = ["S0", "S1a", "S1b", "S2", "S4", "S5", "S6"]
    stage_n = ORDER.index(upto)
    S0_consts()
    S0_wquant_late = stage_n >= 2
    if not S0_wquant_late:
        S0_wquant()
    s.barrier("bar_S0")
    carve_S1()
    if stage_n == 1:
        S1a()
    if stage_n >= 2:
        wqg = S0_wquant_gen()
        s1ag = S1a_gen()
        S1b2(wq=wqg, s1a=s1ag)
        for _ in s1ag:
            pass
        for _ in wqg:
            pass
    if stage_n >= 3:
        s.barrier("bar_S2")
        carve_S2()
        S2(0)
        gn0 = GN_gen(0)
        S2(1, gn_prev=gn0)
        for _ in gn0:
            pass
        gn1 = GN_gen(1)
        if stage_n == 3:
            for _ in gn1:
                pass
    if stage_n >= 4:
        s.barrier("bar_S4")
        carve_S4()
        S4(gn=gn1)
        for _ in gn1:
            pass
    if stage_n >= 5:
        s.barrier("bar_S5")
        carve_S5()
        S5()
    if stage_n >= 6:
        s.barrier("bar_S6")
        carve_S6()
        S6()

    if debug:
        s.barrier("bar_dbg")
        bdb = s.buf("dbg")
        for k, name in enumerate(debug):
            src = dr[name]
            shp = list(src.tensor.shape)
            dt = src.tensor.dtype
            o = nc.dram_tensor("dbg_" + name, shp, dt, kind="ExternalOutput").ap()
            n = int(np.prod(shp))
            s.dma(ch_out[k % 2], lambda e, src=src, o=o, n=n: e.dma_start(
                out=bass.AP(o.tensor, 0, [[n, 1], [1, n]]),
                in_=bass.AP(src.tensor, 0, [[n, 1], [1, n]])),
                reads=[bdb], tag=f"dbg_{name}")

    if stage_n < 6:
        s.barrier("bar_out")
        bz = s.buf("zout")
        s.op("vector", lambda e: e.memset(st["rmsb"][:], 0.0),
             reads=[B["rmsb"]], writes=[bz], tag="zz")
        for i in range(NTT):
            s.dma(ch_out[i % 2], lambda e, i=i: e.dma_start(
                out=bass.AP(out_d.tensor, 128 * i * C, [[C, 128], [1, C]]),
                in_=st["rmsb"][:]),
                reads=[bz], tag=f"zo{i}")

    s.barrier("bar_final")

    with ExitStack() as stack:
        e = stack.enter_context
        sems = {}
        for eng in ENGINES:
            sems[("eng", eng)] = e(nc.semaphore(f"sem_{eng}"))
        for c in s.chans:
            sems[("chan", c.name)] = e(nc.semaphore(f"semc_{c.name}"))
        block = e(nc.Block())
        s.emit(block, sems)
    return nc


def _prep(inputs):
    consts = _host_consts()
    tiled = {
        "w1t": _tile_oc(np.ascontiguousarray(
            np.asarray(inputs["L0_attn_w_qkvff"], dtype=np.float32).T), 40, 8),
        "wgt": _tile_oc(np.ascontiguousarray(
            np.asarray(inputs["L0_attn_w_gated"], dtype=np.float32).T), 8, 8),
        "wpt": np.ascontiguousarray(np.asarray(inputs["L0_attn_w_proj"], dtype=np.float32).T),
        "wft": np.ascontiguousarray(np.asarray(inputs["L0_attn_w_ff"], dtype=np.float32).T),
        "wg": np.ascontiguousarray(np.asarray(inputs["L0_ffn_w_gate"], dtype=np.float32).T),
        "wu": np.ascontiguousarray(np.asarray(inputs["L0_ffn_w_up"], dtype=np.float32).T),
        "wd": np.ascontiguousarray(np.asarray(inputs["L0_ffn_w_down"], dtype=np.float32).T),
        "b_gated": np.asarray(inputs["L0_attn_b_gated"], dtype=np.float32).reshape(1, C),
        "b_proj": np.asarray(inputs["L0_attn_b_proj"], dtype=np.float32).reshape(1, C),
        "gn_w": np.asarray(inputs["L0_attn_gn_w"], dtype=np.float32).reshape(1, C),
        "gn_b": np.asarray(inputs["L0_attn_gn_b"], dtype=np.float32).reshape(1, C),
        "rms_w": np.asarray(inputs["L0_attn_rms_w"], dtype=np.float32).reshape(1, C),
    }
    tiled.update(_xpos_decay_tables())
    return consts, tiled


def kernel(**inputs):
    consts, tiled = _prep(inputs)
    nc = build_program()
    in_maps = []
    for core in range(8):
        m = {"x": np.ascontiguousarray(
            np.asarray(inputs["L0_attn_x"][2 * core:2 * core + 2],
                       dtype=np.float32).reshape(NT, C))}
        m.update(tiled)
        m.update(consts)
        in_maps.append(m)
    res = run_bass_kernel_spmd(nc, in_maps, core_ids=list(range(8)))
    out = np.concatenate([np.asarray(r["out"]).reshape(2, T, C)
                          for r in res.results], axis=0)
    return out.astype(np.float32)
